# version 08
# baseline (speedup 1.0000x reference)
import contextlib
import numpy as np
import ml_dtypes
import concourse.bass as bass
import concourse.mybir as mybir
from concourse.bass_utils import run_bass_kernel_spmd

F32 = mybir.dt.float32
BF16 = mybir.dt.bfloat16
U8 = mybir.dt.uint8
I32 = mybir.dt.int32
AF = mybir.ActivationFunctionType
ALU = mybir.AluOpType
AX = mybir.AxisListType

B, L, D = 4, 4096, 2048
HID = 5632
NCH = 32
T = 2048
NT = 16
HB = 44
TH = 1024
PI = float(np.pi)
TWO_PI = float(2.0 * np.pi)
EPS_LN = 1e-5
KB = 1024


class Ctx:

    KEYS = ("d_par", "d_u", "d_u2", "d_x0", "d_x1w", "d_x1w2", "d_x1r",
            "d_xnw", "d_xnw2", "d_xnr", "d_w", "d_w2", "d_wdl", "d_wdh", "d_yw",
            "d_yw2", "d_yr", "d_out", "d_out2", "v", "s", "t", "g")

    def __init__(self):
        self.marks = {}
        self.pass_num = 0
        self.cnt = {}
        self.sems = None

    def begin(self, pass_num, sems=None):
        self.pass_num = pass_num
        self.cnt = {k: 0 for k in self.KEYS}
        self.sems = sems

    def inc(self, inst, key, by=1, mark=None):
        if self.pass_num == 2:
            inst.then_inc(self.sems[key], by)
        self.cnt[key] += by
        if mark is not None:
            if self.pass_num == 1:
                assert mark not in self.marks, f"dup mark {mark}"
                self.marks[mark] = (key, self.cnt[key])
            else:
                assert self.marks[mark] == (key, self.cnt[key]), \
                    f"mark drift {mark}"
        return self.cnt[key]

    def wait(self, eng, mark, delta=0):
        if self.pass_num == 2:
            key, val = self.marks[mark]
            if val + delta > 0:
                eng.wait_ge(self.sems[key], val + delta)

    def wait_cnt(self, eng, key, val):
        if self.pass_num == 2 and val > 0:
            eng.wait_ge(self.sems[key], val)

    def selfwait(self, eng, key):
        if self.pass_num == 2 and self.cnt[key] > 0:
            eng.wait_ge(self.sems[key], self.cnt[key])


class _FakeInst:
    def then_inc(self, *a, **k):
        return self


class _FakeEng:
    def __getattr__(self, name):
        return lambda *a, **k: _FakeInst()


def build_program():
    nc = bass.Bass("TRN2", target_bir_lowering=False, debug=False)

    d_in = {}

    def din(name, shape, dt):
        d_in[name] = nc.dram_tensor(name, shape, dt, kind="ExternalInput").ap()

    din("u", [T, D], F32)
    din("uprev", [T, D], F32)
    din("wg", [HB, 128, D], U8)
    din("wu", [HB, 128, D], U8)
    din("wd", [HB, 128, D], U8)
    din("sg", [128, HB], F32)
    din("su", [128, HB], F32)
    din("sd", [1, D], F32)
    din("bre", [NCH, D], F32)
    din("bim", [NCH, D], F32)
    din("cret", [NCH, D], F32)
    din("cimt", [NCH, D], F32)
    din("drow", [1, D], F32)
    din("lunc", [NCH, 1], F32)
    din("lim", [NCH, 1], F32)
    din("logd", [NCH, 1], F32)
    out_hbm = nc.dram_tensor("out", [T, D], F32, kind="ExternalOutput").ap()
    x1_spill = nc.dram_tensor("x1sp", [T, D], F32, kind="Internal").ap()
    xnt_spill = nc.dram_tensor("xntsp", [128, 16 * TH], BF16,
                               kind="Internal").ap()
    y_spill = nc.dram_tensor("ysp", [TH, D], F32, kind="Internal").ap()

    st = contextlib.ExitStack()
    ARENA = 207 * KB
    arena = st.enter_context(nc.sbuf_tensor("arena", [128, ARENA], U8)).ap()

    def vf32(off_kb, n):
        off = int(off_kb * KB)
        assert off % 4 == 0 and off + 4 * n <= ARENA, (off_kb, n)
        return arena[:, off:off + 4 * n].bitcast(F32)

    def vbf(off_kb, n):
        off = int(off_kb * KB)
        assert off % 2 == 0 and off + 2 * n <= ARENA, (off_kb, n)
        return arena[:, off:off + 2 * n].bitcast(BF16)

    def vi32(off_kb, n):
        off = int(off_kb * KB)
        return arena[:, off:off + 4 * n].bitcast(I32)

    t1n_full = vbf(0, 32768)
    wd_lo = vbf(0, 11264)
    wd_hi = vbf(22, 11264)
    wgbuf = [vbf(44, 2048), vbf(48, 2048)]
    wubuf = [vbf(52, 2048), vbf(56, 2048)]
    us_s1 = vbf(60, 1024)
    yrbuf = vf32(0, 2048)
    x1rbuf = vf32(8, 2048)
    outbuf = [vf32(16, 2048), vf32(24, 2048)]

    xnt = vbf(64, 16384)
    drow_f = vf32(70, 2048)[0:1, :]
    sdrow = vf32(78, 2048)[0:1, :]
    drow_bf = vbf(86, 2048)[0:1, :]

    ubuf = [vf32(96, 2048), vf32(104, 2048)]
    t1nT = [vbf(112, 2048), vbf(116, 2048)]
    bu_re_m = vf32(120, 2048)[0:NCH, :]
    bu_im_m = vf32(128, 2048)[0:NCH, :]
    bu_re_p = vf32(136, 2048)[0:NCH, :]
    bu_im_p = vf32(144, 2048)[0:NCH, :]
    cosT = vf32(152, 2048)[0:NCH, :]
    sinT = vf32(160, 2048)[0:NCH, :]
    it_fwd = vf32(168, 2048)[0:NCH, :]
    rp_v = vf32(168, 2048)[0:NCH, :]
    c2re_v = vbf(176, 2048)[0:NCH, :]
    c2im_v = vbf(180, 2048)[0:NCH, :]
    bbar_sA = vf32(96, 2048)[0:NCH, :]
    bbar_sB = vf32(104, 2048)[0:NCH, :]
    bbar_re = vbf(112, 2048)[0:NCH, :]
    bbar_im = vbf(116, 2048)[0:NCH, :]
    ph_p = vf32(120, 2048)[0:NCH, :]
    ph_q = vf32(128, 2048)[0:NCH, :]
    ph_qi = vi32(136, 2048)[0:NCH, :]
    ph_r = vf32(144, 2048)[0:NCH, :]
    tA = vf32(96, 2048)[0:NCH, :]
    tB = vf32(104, 2048)[0:NCH, :]
    tC = vf32(112, 2048)[0:NCH, :]
    bt_re = vf32(136, 2048)[0:NCH, :]
    bt_im = vf32(144, 2048)[0:NCH, :]
    xs_re = vbf(120, 2048)[0:NCH, :]
    xs_im = vbf(124, 2048)[0:NCH, :]
    tmpD_p = [vf32(96, 2048), vf32(104, 2048)]
    ybuf_p = [vf32(112, 2048), vf32(160, 2048)]
    x1buf = [vf32(128, 2048), vf32(136, 2048)]
    x0buf1 = vf32(144, 2048)
    xnbuf = vbf(152, 2048)
    xnt_stage = vbf(156, 2048)
    Hbuf = vbf(96, 45056)

    d_rep_bf = vbf(184, 2048)
    sdrep = vf32(188, 2048)

    id_bf = vbf(196, 128)
    onesrow_bf = vbf(196.25, 128)[0:1, :]
    onesrow_f = vf32(196.5, 128)[0:1, :]
    sgbuf = vf32(197, HB)
    subuf = vf32(197.25, HB)
    colbuf = vf32(197.5, 96)
    statbuf = vf32(197.875, 24)
    bret_bf = vbf(198.25, 512)
    bimt_bf = vbf(199.25, 512)
    ystage = [vf32(200.25, 512), vf32(202.25, 512)]
    ln4stats = vf32(204.25, 192)
    h2buf = vbf(205, 1024)

    pA = st.enter_context(nc.psum_tensor("pA", [128, 2048], F32)).ap()
    pB = st.enter_context(nc.psum_tensor("pB", [128, 1024], F32)).ap()
    pC = st.enter_context(nc.psum_tensor("pC", [128, 512], F32)).ap()
    pD = st.enter_context(nc.psum_tensor("pD", [128, 512], F32)).ap()
    pBbf = pB.bitcast(BF16)

    sems = {k: st.enter_context(nc.semaphore(k)) for k in Ctx.KEYS}
    C = Ctx()

    p_lunc = colbuf[0:NCH, 0:1]
    p_lim = colbuf[0:NCH, 1:2]
    p_logd = colbuf[0:NCH, 2:3]
    c_delta = colbuf[0:NCH, 3:4]
    c_lamre = colbuf[0:NCH, 4:5]
    c_lnr = colbuf[0:NCH, 5:6]
    c_theta = colbuf[0:NCH, 6:7]
    c_r = colbuf[0:NCH, 7:8]
    c_are = colbuf[0:NCH, 8:9]
    c_aim = colbuf[0:NCH, 9:10]
    c_t1 = colbuf[0:NCH, 10:11]
    c_t2 = colbuf[0:NCH, 11:12]
    c_t3 = colbuf[0:NCH, 12:13]
    c_t4 = colbuf[0:NCH, 13:14]
    c_dre = colbuf[0:NCH, 14:15]
    c_dim = colbuf[0:NCH, 15:16]
    c_eps = colbuf[:, 16:17]
    c_cre = colbuf[0:NCH, 17:18]
    c_cim = colbuf[0:NCH, 18:19]
    c_sr = colbuf[0:NCH, 19:20]
    c_si = colbuf[0:NCH, 20:21]
    c_negm = colbuf[:, 21:22]
    c_rstd = colbuf[:, 22:23]
    c_sd = colbuf[:, 23:24]
    c_Tlnr = colbuf[0:NCH, 24:25]
    c_i32 = colbuf.bitcast(I32)[0:NCH, 25:26]
    c_sinth = colbuf[0:NCH, 26:27]
    c_sh = colbuf[0:NCH, 27:28]
    c_costh = colbuf[0:NCH, 28:29]
    c_thred = colbuf[0:NCH, 29:30]
    mvbuf = colbuf[:, 30:32]
    mv_p = [colbuf[:, 32:34], colbuf[:, 34:36]]
    sd_p = [colbuf[:, 36:37], colbuf[:, 37:38]]
    rstd_p = [colbuf[:, 38:39], colbuf[:, 39:40]]
    nmr_p = [colbuf[:, 40:41], colbuf[:, 41:42]]
    mvy_p = [colbuf[:, 42:44], colbuf[:, 44:46]]
    sdy_p = [colbuf[:, 46:47], colbuf[:, 47:48]]
    rsy_p = [colbuf[:, 48:49], colbuf[:, 49:50]]
    nmy_p = [colbuf[:, 50:51], colbuf[:, 51:52]]
    mvx_p = [colbuf[:, 52:54], colbuf[:, 54:56]]
    sdx_p = [colbuf[:, 56:57], colbuf[:, 57:58]]
    rsx_p = [colbuf[:, 58:59], colbuf[:, 59:60]]
    nmx_p = [colbuf[:, 60:61], colbuf[:, 61:62]]
    l4mv = [colbuf[:, 62 + 2 * t_: 64 + 2 * t_] for t_ in range(8)]
    l4rs = [colbuf[:, 78 + t_: 79 + t_] for t_ in range(8)]
    l4nm = [colbuf[:, 86 + t_: 87 + t_] for t_ in range(8)]
    p_bre = bu_re_m
    p_bim = bu_im_m
    p_cret = cosT
    p_cimt = sinT

    def Hcols(hb, tg):
        return Hbuf[:, hb * TH + tg * 512: hb * TH + (tg + 1) * 512]

    def bu_dsts(k):
        i = k % NT
        bre = (bu_re_p if k < NT else bu_re_m)[:, i * 128:(i + 1) * 128]
        bim = (bu_im_p if k < NT else bu_im_m)[:, i * 128:(i + 1) * 128]
        return bre, bim

    def xnev_dst(i, gq):
        if i < 8:
            return xnt.rearrange("p (k tt) -> p k tt", k=16)[
                :, 4 * gq:4 * (gq + 1), i * 128:(i + 1) * 128]
        return xnt_stage[:, 512 * gq:512 * (gq + 1)]

    def xnev_src(i, gq):
        s = pBbf[:, 512 * gq:512 * (gq + 1)]
        if i < 8:
            return s.rearrange("p (k tt) -> p k tt", k=4)
        return s

    def emit_sync(q):
        C.inc(q.dma_start(p_lunc, d_in["lunc"]), "d_par", 16)
        C.inc(q.dma_start(p_lim, d_in["lim"]), "d_par", 16)
        C.inc(q.dma_start(p_logd, d_in["logd"]), "d_par", 16)
        C.inc(q.dma_start(p_bre, d_in["bre"]), "d_par", 16)
        C.inc(q.dma_start(p_bim, d_in["bim"]), "d_par", 16)
        C.inc(q.dma_start(p_cret, d_in["cret"]), "d_par", 16)
        C.inc(q.dma_start(p_cimt, d_in["cimt"]), "d_par", 16)
        C.inc(q.dma_start(drow_f, d_in["drow"]), "d_par", 16)
        C.inc(q.dma_start(sdrow, d_in["sd"]), "d_par", 16)
        C.inc(q.dma_start(sgbuf[:, :], d_in["sg"]), "d_par", 16)
        C.inc(q.dma_start(subuf[:, :], d_in["su"]), "d_par", 16,
              mark="d_params")

        for j in range(2 * NT):
            src = d_in["uprev"] if j < NT else d_in["u"]
            i = j % NT
            if j < 2:
                C.wait(q, "v_bbar")
            else:
                C.wait(q, f"uapply_{j-2}")
            C.inc(q.dma_start(ubuf[j % 2][:, :],
                              src[i * 128:(i + 1) * 128, :]),
                  "d_u" if j % 2 == 0 else "d_u2", 16, mark=f"du_{j}")

        for i in range(NT):
            if i == 0:
                C.wait(q, "v_scan_done")
            else:
                C.wait(q, f"x1_{i-1}")
            C.inc(q.dma_start(x0buf1[:, :],
                              d_in["u"][i * 128:(i + 1) * 128, :]),
                  "d_x0", 16, mark=f"dx0_{i}")
            if i >= 1:
                C.inc(q.dma_start(x1_spill[(i - 1) * 128:i * 128, :],
                                  x1buf[(i - 1) % 2][:, :]),
                      "d_x1w" if (i - 1) % 2 == 0 else "d_x1w2", 16,
                      mark=f"dx1w_{i-1}")
        C.wait(q, f"x1_{NT-1}")
        C.inc(q.dma_start(x1_spill[(NT - 1) * 128:NT * 128, :],
                          x1buf[(NT - 1) % 2][:, :]),
              "d_x1w" if (NT - 1) % 2 == 0 else "d_x1w2", 16,
              mark=f"dx1w_{NT-1}")

        for X in range(2):
            if X == 0:
                C.wait(q, "s1mm_0_43")
                C.inc(q.dma_start(xnt[:, :], xnt_spill[:, :]), "d_xnr", 16,
                      mark="dxnr")
            for dg in range(4):
                for ti in range(8):
                    par = (dg * 8 + ti) % 2
                    C.wait(q, f"ys_{X}_{dg}_{ti}")
                    C.inc(q.dma_start(
                        y_spill[ti * 128:(ti + 1) * 128,
                                dg * 512:(dg + 1) * 512],
                        ystage[par][:, :]),
                        "d_yw" if par == 0 else "d_yw2", 16,
                        mark=f"dyw_{X}_{dg}_{ti}")
            C.wait(q, f"s2f_{X}_3_7")
            C.wait(q, f"dyw_{X}_3_6")
            C.wait(q, f"dyw_{X}_3_7")
            for ti in range(8):
                gt = X * 8 + ti
                if ti >= 1:
                    C.wait(q, f"out_{X}_{ti-1}")
                C.inc(q.dma_start(yrbuf[:, :],
                                  y_spill[ti * 128:(ti + 1) * 128, :]),
                      "d_yr", 16, mark=f"dyr_{X}_{ti}")
                if ti >= 1:
                    C.wait(q, f"out_{X}_{ti-1}")
                C.inc(q.dma_start(x1rbuf[:, :],
                                  x1_spill[gt * 128:(gt + 1) * 128, :]),
                      "d_x1r", 16, mark=f"dx1r_{X}_{ti}")
                C.wait(q, f"out_{X}_{ti}")
                C.inc(q.dma_start(out_hbm[gt * 128:(gt + 1) * 128, :],
                                  outbuf[ti % 2][:, :]),
                      "d_out" if ti % 2 == 0 else "d_out2", 16,
                      mark=f"dout_{X}_{ti}")
        C.wait_cnt(q, "d_out", 16 * 8)
        C.wait_cnt(q, "d_out2", 16 * 8)

    def emit_gpsimd(g):
        C.inc(g.iota(it_fwd[:, :], pattern=[[1, T]], base=1,
                     channel_multiplier=0,
                     allow_small_or_imprecise_dtypes=True), "g", mark="g_itf")
        C.inc(g.memset(id_bf[:, :], 1.0), "g")
        C.selfwait(g, "g")
        C.inc(g.affine_select(id_bf[:, :], id_bf[:, :], pattern=[[-1, 128]],
                              compare_op=ALU.is_equal, fill=0.0, base=0,
                              channel_multiplier=1), "g", mark="g_id")
        C.inc(g.memset(onesrow_bf[:, :], 1.0), "g")
        C.inc(g.memset(onesrow_f[:, :], 1.0), "g")
        C.inc(g.memset(c_eps, EPS_LN), "g", mark="g_setup")

        for k in range(8):
            C.wait(g, f"xnevv_{k + 8}")
            C.wait(g, f"xnevs_{k + 8}")
            C.inc(g.dma_start(
                xnt_spill.rearrange("p (kk ti tt) -> p kk ti tt",
                                    k=16, ti=8, kk=16)[:, :, k, :],
                xnt_stage.rearrange("p (kk tt) -> p kk tt", kk=16)),
                "d_xnw" if k % 2 == 0 else "d_xnw2", 16,
                mark=f"dxnw_{k}")

        for X in range(2):
            for hb in range(HB):
                par = hb % 2
                if X == 0 and hb < 2:
                    C.wait(g, "yev_15")
                else:
                    pX, phb = (X, hb - 2) if hb >= 2 else (0, HB - 2 + hb)
                    C.wait(g, f"s1mm_{pX}_{phb}")
                wkey = "d_w" if par == 0 else "d_w2"
                C.inc(g.dma_start(wgbuf[par][:, :], d_in["wg"][hb, :, :]),
                      wkey, 16)
                C.inc(g.dma_start(wubuf[par][:, :], d_in["wu"][hb, :, :]),
                      wkey, 16, mark=f"wconv_{X}_{hb}")
            for dg in range(4):
                if dg == 0:
                    if X == 0:
                        C.wait(g, "yev_15")
                    else:
                        C.wait_cnt(g, "d_out", 16 * 4)
                        C.wait_cnt(g, "d_out2", 16 * 4)
                        C.wait(g, "s2lo_0_3")
                else:
                    C.wait(g, f"s2lo_{X}_{dg-1}")
                C.inc(g.dma_start(
                    wd_lo.rearrange("p (h dd) -> p h dd", h=22),
                    d_in["wd"][0:22, :, dg * 512:(dg + 1) * 512].rearrange(
                        "h p dd -> p h dd")), "d_wdl", 16,
                    mark=f"dwdlo_{X}_{dg}")
                if dg == 0:
                    if X == 1:
                        C.wait(g, "s2f_0_3_7")
                else:
                    C.wait(g, f"s2f_{X}_{dg-1}_7")
                C.inc(g.dma_start(
                    wd_hi.rearrange("p (h dd) -> p h dd", h=22),
                    d_in["wd"][22:44, :, dg * 512:(dg + 1) * 512].rearrange(
                        "h p dd -> p h dd")), "d_wdh", 16,
                    mark=f"dwdhi_{X}_{dg}")

    def emit_scalar(s):
        C.wait(s, "d_params")
        C.inc(s.activation(c_delta, p_logd, AF.Exp), "s")
        C.inc(s.activation(c_t1, p_lunc, AF.Abs), "s")
        C.selfwait(s, "s")
        C.inc(s.activation(c_t2, c_t1, AF.Exp, scale=-1.0), "s")
        C.selfwait(s, "s")
        C.inc(s.activation(c_t3, c_t2, AF.Ln, bias=1.0), "s", mark="s_p1")
        C.wait(s, "v_lnr")
        C.inc(s.activation(c_r, c_lnr, AF.Exp), "s", mark="s_r")
        C.wait(s, "v_thred")
        C.inc(s.activation(c_sinth, c_thred, AF.Sin), "s")
        C.inc(s.activation(c_sh, c_thred, AF.Sin, scale=0.5), "s",
              mark="s_sincos")
        C.wait(s, "v_pred")
        C.inc(s.activation(sinT[:, :], ph_p[:, :], AF.Sin), "s")
        C.inc(s.activation(ph_q[:, :], ph_p[:, :], AF.Abs), "s",
              mark="s_abs")
        C.wait(s, "v_carg")
        C.inc(s.activation(cosT[:, :], ph_r[:, :], AF.Sin), "s",
              mark="s_tables")
        C.inc(s.activation(drow_bf, drow_f, AF.Copy), "s", mark="s_dbf")

        for sl in range(2 * NT + 2):
            j = sl
            if j < 2 * NT:
                C.wait(s, f"aggr_{j}")
                C.inc(s.activation(sd_p[j % 2], mv_p[j % 2][:, 1:2],
                                   AF.Sqrt, bias=c_eps), "s",
                      mark=f"sqrt_{j}")
            k = sl - 1
            if 0 <= k < 2 * NT:
                C.wait(s, f"lnc_{k}")
                i = k % NT
                C.inc(s.activation(t1n_full[:, i * 2048:(i + 1) * 2048],
                                   ubuf[k % 2][:, :], AF.Identity,
                                   scale=rstd_p[k % 2], bias=nmr_p[k % 2]),
                      "s", mark=f"uapply_{k}")
            k = sl - 2
            if 0 <= k < 2 * NT:
                C.wait(s, f"tr_{k}")
                for gq in range(2, 4):
                    C.inc(s.activation(
                        t1nT[k % 2][:, 512 * gq:512 * (gq + 1)],
                        pBbf[:, 512 * gq:512 * (gq + 1)], AF.Copy), "s",
                        mark=(f"t1nTs_{k}" if gq == 3 else None))
        C.wait(s, "v_rpe")
        C.inc(s.activation(rp_v[:, :], tC[:, :], AF.Exp, scale=-1.0,
                           bias=c_Tlnr), "s", mark="s_rpow")
        for sl in range(NT + 5):
            k = sl - 1
            if 0 <= k < NT:
                C.wait(s, f"yaggr_{k}")
                C.inc(s.activation(sdy_p[k % 2], mvy_p[k % 2][:, 1:2],
                                   AF.Sqrt, bias=c_eps), "s",
                      mark=f"ysqrt_{k}")
            k = sl - 2
            if 0 <= k < NT:
                C.wait(s, f"ylnc_{k}")
                C.inc(s.activation(ybuf_p[k % 2][:, :], ybuf_p[k % 2][:, :],
                                   AF.Identity, scale=rsy_p[k % 2],
                                   bias=nmy_p[k % 2]), "s", mark=f"ln2_{k}")
            k = sl - 3
            if 0 <= k < NT:
                C.wait(s, f"x1aggr_{k}")
                C.inc(s.activation(sdx_p[k % 2], mvx_p[k % 2][:, 1:2],
                                   AF.Sqrt, bias=c_eps), "s",
                      mark=f"x1sqrt_{k}")
            k = sl - 4
            if 0 <= k < NT:
                C.wait(s, f"xlnc_{k}")
                if k >= 1:
                    C.wait(s, f"xntr_{k-1}")
                C.inc(s.activation(xnbuf[:, :], x1buf[k % 2][:, :],
                                   AF.Identity, scale=rsx_p[k % 2],
                                   bias=nmx_p[k % 2]), "s",
                      mark=f"xnapply_{k}")
            k = sl - 5
            if 0 <= k < NT:
                C.wait(s, f"xntr_{k}")
                if k >= 9:
                    C.wait(s, f"dxnw_{k-9}")
                for gq in range(2, 4):
                    C.inc(s.activation(xnev_dst(k, gq), xnev_src(k, gq),
                                       AF.Copy), "s",
                          mark=(f"xnevs_{k}" if gq == 3 else None))
        for X in range(2):
            for hb in range(HB):
                C.wait(s, f"gmm_{X}_{hb}")
                if hb >= 1 or X == 1:
                    pX, phb = (X, hb - 1) if hb >= 1 else (0, HB - 1)
                    C.wait(s, f"h_{pX}_{phb}")
                gsrc = pA if hb % 2 == 0 else pB
                for tg in range(2):
                    C.inc(s.activation(h2buf[:, tg * 512:(tg + 1) * 512],
                                       gsrc[:, tg * 512:(tg + 1) * 512],
                                       AF.Silu, scale=sgbuf[:, hb:hb + 1]),
                          "s", mark=(f"gs_{X}_{hb}" if tg == 1 else None))
            for dg in range(4):
                C.wait(s, f"dwdlo_{X}_{dg}")
                C.inc(s.activation(wd_lo[:, :], wd_lo[:, :], AF.Copy,
                                   bias=-128.0), "s", mark=f"wdlo_{X}_{dg}")
                C.wait(s, f"dwdhi_{X}_{dg}")
                C.inc(s.activation(wd_hi[:, :], wd_hi[:, :], AF.Copy,
                                   bias=-128.0), "s", mark=f"wdhi_{X}_{dg}")
            for ti in range(8):
                C.wait(s, f"l4aggr_{X}_{ti}")
                C.inc(s.activation(l4nm[ti], l4mv[ti][:, 1:2], AF.Sqrt,
                                   bias=c_eps), "s", mark=f"l4sqrt_{X}_{ti}")

    def emit_tensor(t):
        C.wait(t, "g_id")
        C.wait(t, "v_bbar")
        for kt in range(16):
            C.inc(t.transpose(pBbf[:, kt * 32:(kt + 1) * 32],
                              bbar_re[:, kt * 128:(kt + 1) * 128],
                              id_bf[0:NCH, 0:NCH]), "t")
        for kt in range(16):
            C.inc(t.transpose(pBbf[:, 512 + kt * 32:512 + (kt + 1) * 32],
                              bbar_im[:, kt * 128:(kt + 1) * 128],
                              id_bf[0:NCH, 0:NCH]), "t",
                  mark=("t_btr" if kt == 15 else None))
        C.wait(t, "s_dbf")
        for dg in range(4):
            C.inc(t.matmul(pA[:, dg * 512:(dg + 1) * 512], onesrow_bf,
                           drow_bf[:, dg * 512:(dg + 1) * 512],
                           start=True, stop=True), "t",
                  mark=("t_drep" if dg == 3 else None))
        C.wait(t, "v_drep")
        for dg in range(4):
            C.inc(t.matmul(pA[:, dg * 512:(dg + 1) * 512], onesrow_f,
                           sdrow[:, dg * 512:(dg + 1) * 512],
                           start=True, stop=True), "t",
                  mark=("t_sdrep" if dg == 3 else None))

        for j in range(2 * NT):
            C.wait(t, f"uapply_{j}")
            if j >= 1:
                C.wait(t, f"t1nTv_{j-1}")
                C.wait(t, f"t1nTs_{j-1}")
            i = j % NT
            src = t1n_full[:, i * 2048:(i + 1) * 2048]
            for kt in range(16):
                C.inc(t.transpose(pBbf[:, kt * 128:(kt + 1) * 128],
                                  src[:, kt * 128:(kt + 1) * 128],
                                  id_bf[:, :]), "t",
                      mark=(f"tr_{j}" if kt == 15 else None))
            C.wait(t, f"t1nTv_{j}")
            C.wait(t, f"t1nTs_{j}")
            if j >= 2:
                C.wait(t, f"buev_{j-2}")
            pb = pC if j % 2 == 0 else pD
            for kt in range(16):
                C.inc(t.matmul(pb[0:NCH, 0:128],
                               bret_bf[:, kt * 32:(kt + 1) * 32],
                               t1nT[j % 2][:, kt * 128:(kt + 1) * 128],
                               start=(kt == 0), stop=(kt == 15)), "t")
            for kt in range(16):
                C.inc(t.matmul(pb[0:NCH, 128:256],
                               bimt_bf[:, kt * 32:(kt + 1) * 32],
                               t1nT[j % 2][:, kt * 128:(kt + 1) * 128],
                               start=(kt == 0), stop=(kt == 15)), "t",
                      mark=(f"bu_{j}" if kt == 15 else None))

        def xntr_block(k):
            C.wait(t, f"xnapply_{k}")
            if k == 0:
                C.wait(t, f"buev_{2*NT-1}")
            else:
                C.wait(t, f"xnevv_{k-1}")
                C.wait(t, f"xnevs_{k-1}")
            for kt in range(16):
                C.inc(t.transpose(pBbf[:, kt * 128:(kt + 1) * 128],
                                  xnbuf[:, kt * 128:(kt + 1) * 128],
                                  id_bf[:, :]), "t",
                      mark=(f"xntr_{k}" if kt == 15 else None))

        C.wait(t, "v_scan_done")
        C.wait(t, "v_sdrep")
        for i in range(NT):
            if i >= 1:
                C.wait(t, f"yev_{i-1}")
            for dg in range(4):
                C.inc(t.matmul(pA[:, dg * 512:(dg + 1) * 512],
                               xs_re[:, i * 128:(i + 1) * 128],
                               c2re_v[:, dg * 512:(dg + 1) * 512],
                               start=True, stop=False), "t")
                C.inc(t.matmul(pA[:, dg * 512:(dg + 1) * 512],
                               xs_im[:, i * 128:(i + 1) * 128],
                               c2im_v[:, dg * 512:(dg + 1) * 512],
                               start=False, stop=True), "t",
                      mark=(f"ymm_{i}" if dg == 3 else None))
            if i >= 5:
                xntr_block(i - 5)
        for k in range(NT - 5, NT):
            xntr_block(k)

        PS2 = [pA[:, 0:512], pA[:, 512:1024], pA[:, 1024:1536],
               pA[:, 1536:2048], pB[:, 0:512], pB[:, 512:1024], pC, pD]
        for X in range(2):
            for hb in range(HB):
                par = hb % 2
                C.wait(t, f"wconv_{X}_{hb}")
                if X == 0 and hb == 0:
                    C.wait(t, "yev_15")
                if X == 0 and hb == 1:
                    C.wait(t, "xnevv_15")
                    C.wait(t, "xnevs_15")
                if X == 1 and hb == 0:
                    C.wait(t, "dxnr")
                    C.wait(t, "gs_0_42")
                    C.wait(t, "uev_0_42")
                    C.wait(t, "ys_0_3_3")
                if X == 1 and hb == 1:
                    C.wait(t, "gs_0_43")
                    C.wait(t, "uev_0_43")
                    C.wait(t, "ys_0_3_7")
                if hb >= 2:
                    C.wait(t, f"gs_{X}_{hb-2}")
                    C.wait(t, f"uev_{X}_{hb-2}")
                if par == 0:
                    gps = [pA[:, 0:512], pA[:, 512:1024]]
                    ups = [pA[:, 1024:1536], pA[:, 1536:2048]]
                else:
                    gps = [pB[:, 0:512], pB[:, 512:1024]]
                    ups = [pC[:, :], pD[:, :]]
                for tg in range(2):
                    for kt in range(16):
                        C.inc(t.matmul(
                            gps[tg],
                            wgbuf[par][:, kt * 128:(kt + 1) * 128],
                            xnt[:, kt * TH + tg * 512:
                                kt * TH + (tg + 1) * 512],
                            start=(kt == 0), stop=(kt == 15)), "t",
                            mark=(f"gmm_{X}_{hb}"
                                  if tg == 1 and kt == 15 else None))
                for tg in range(2):
                    for kt in range(16):
                        C.inc(t.matmul(
                            ups[tg],
                            wubuf[par][:, kt * 128:(kt + 1) * 128],
                            xnt[:, kt * TH + tg * 512:
                                kt * TH + (tg + 1) * 512],
                            start=(kt == 0), stop=(kt == 15)), "t",
                            mark=(f"s1mm_{X}_{hb}"
                                  if tg == 1 and kt == 15 else None))
            for dg in range(4):
                for hb in range(HB):
                    if hb == 0:
                        C.wait(t, f"wdlo_{X}_{dg}")
                    if hb == 22:
                        C.wait(t, f"wdhi_{X}_{dg}")
                    if dg == 0:
                        C.wait(t, f"h_{X}_{hb}")
                    wsrc = wd_lo if hb < 22 else wd_hi
                    wof = hb if hb < 22 else hb - 22
                    for ti in range(8):
                        if hb == 0:
                            if dg == 0:
                                C.wait(t, f"gs_{X}_{HB-1}")
                                C.wait(t, f"uev_{X}_{HB-1}")
                            else:
                                C.wait(t, f"ys_{X}_{dg-1}_{ti}")
                        C.inc(t.matmul(
                            PS2[ti],
                            Hbuf[:, hb * TH + ti * 128:
                                 hb * TH + (ti + 1) * 128],
                            wsrc[:, wof * 512:(wof + 1) * 512],
                            start=(hb == 0), stop=(hb == HB - 1)), "t",
                            mark=(f"s2f_{X}_{dg}_{ti}" if hb == HB - 1
                                  else (f"s2lo_{X}_{dg}"
                                        if hb == 21 and ti == 7 else None)))

    def emit_vector(v):
        C.wait(v, "d_params")
        C.wait(v, "g_setup")
        C.inc(v.tensor_scalar(c2re_v[:, :], p_cret, 2.0, None, ALU.mult),
              "v")
        C.inc(v.tensor_scalar(c2im_v[:, :], p_cimt, -2.0, None, ALU.mult),
              "v", mark="v_c2")
        C.wait(v, "s_p1")
        C.inc(v.tensor_scalar(c_t1, p_lunc, 0.0, None, ALU.max), "v")
        C.selfwait(v, "v")
        C.inc(v.tensor_tensor(c_t1, c_t1, c_t3, ALU.add), "v")
        C.selfwait(v, "v")
        C.inc(v.tensor_scalar(c_lamre, c_t1, 0.0101, -1.0, ALU.add,
                              ALU.mult), "v")
        C.selfwait(v, "v")
        C.inc(v.tensor_tensor(c_lnr, c_lamre, c_delta, ALU.mult), "v",
              mark="v_lnr")
        C.selfwait(v, "v")
        C.inc(v.tensor_scalar(c_Tlnr, c_lnr, float(T), None, ALU.mult), "v")
        C.inc(v.tensor_tensor(c_theta, p_lim, c_delta, ALU.mult), "v")
        C.selfwait(v, "v")
        C.inc(v.tensor_scalar(c_t2, c_theta, 1.0 / TWO_PI, None, ALU.mult),
              "v")
        C.selfwait(v, "v")
        C.inc(v.tensor_copy(c_i32, c_t2), "v")
        C.selfwait(v, "v")
        C.inc(v.tensor_copy(c_t2, c_i32), "v")
        C.selfwait(v, "v")
        C.inc(v.scalar_tensor_tensor(c_thred, c_t2, -TWO_PI, c_theta,
                                     ALU.mult, ALU.add), "v")
        C.selfwait(v, "v")
        C.inc(v.tensor_scalar(c_t2, c_thred, PI, None, ALU.is_gt), "v")
        C.selfwait(v, "v")
        C.inc(v.scalar_tensor_tensor(c_thred, c_t2, -TWO_PI, c_thred,
                                     ALU.mult, ALU.add), "v")
        C.selfwait(v, "v")
        C.inc(v.tensor_scalar(c_t2, c_thred, -PI, None, ALU.is_lt), "v")
        C.selfwait(v, "v")
        C.inc(v.scalar_tensor_tensor(c_thred, c_t2, TWO_PI, c_thred,
                                     ALU.mult, ALU.add), "v", mark="v_thred")
        C.wait(v, "s_sincos")
        C.inc(v.tensor_tensor(c_t2, c_sh, c_sh, ALU.mult), "v")
        C.selfwait(v, "v")
        C.inc(v.tensor_scalar(c_costh, c_t2, -2.0, 1.0, ALU.mult, ALU.add),
              "v")
        C.wait(v, "s_r")
        C.selfwait(v, "v")
        C.inc(v.tensor_tensor(c_are, c_r, c_costh, ALU.mult), "v")
        C.inc(v.tensor_tensor(c_aim, c_r, c_sinth, ALU.mult), "v")
        C.selfwait(v, "v")
        C.inc(v.tensor_scalar(c_t1, c_are, -1.0, None, ALU.add), "v")
        C.inc(v.tensor_tensor(c_t2, c_lamre, c_lamre, ALU.mult), "v")
        C.inc(v.tensor_tensor(c_t3, p_lim, p_lim, ALU.mult), "v")
        C.selfwait(v, "v")
        C.inc(v.tensor_tensor(c_t2, c_t2, c_t3, ALU.add), "v")
        C.selfwait(v, "v")
        C.inc(v.reciprocal(c_t2, c_t2), "v")
        C.selfwait(v, "v")
        C.inc(v.tensor_tensor(c_t3, c_t1, c_lamre, ALU.mult), "v")
        C.inc(v.tensor_tensor(c_t4, c_aim, p_lim, ALU.mult), "v")
        C.selfwait(v, "v")
        C.inc(v.tensor_tensor(c_t3, c_t3, c_t4, ALU.add), "v")
        C.selfwait(v, "v")
        C.inc(v.tensor_tensor(c_dre, c_t3, c_t2, ALU.mult), "v")
        C.selfwait(v, "v")
        C.inc(v.tensor_tensor(c_t3, c_aim, c_lamre, ALU.mult), "v")
        C.inc(v.tensor_tensor(c_t4, c_t1, p_lim, ALU.mult), "v")
        C.selfwait(v, "v")
        C.inc(v.tensor_tensor(c_t3, c_t3, c_t4, ALU.subtract), "v")
        C.selfwait(v, "v")
        C.inc(v.tensor_tensor(c_dim, c_t3, c_t2, ALU.mult), "v")
        C.selfwait(v, "v")
        C.inc(v.tensor_scalar(bbar_sA[:, :], p_bre, c_dre, None, ALU.mult),
              "v")
        C.inc(v.tensor_scalar(bbar_sB[:, :], p_bim, c_dim, None, ALU.mult),
              "v")
        C.selfwait(v, "v")
        C.inc(v.tensor_tensor(bbar_re[:, :], bbar_sA[:, :], bbar_sB[:, :],
                              ALU.subtract), "v")
        C.selfwait(v, "v")
        C.inc(v.tensor_scalar(bbar_sA[:, :], p_bim, c_dre, None, ALU.mult),
              "v")
        C.inc(v.tensor_scalar(bbar_sB[:, :], p_bre, c_dim, None, ALU.mult),
              "v")
        C.selfwait(v, "v")
        C.inc(v.tensor_tensor(bbar_im[:, :], bbar_sA[:, :], bbar_sB[:, :],
                              ALU.add), "v", mark="v_bbar")
        C.inc(v.tensor_scalar(ph_p[:, :], it_fwd[:, :], c_theta, None,
                              ALU.mult), "v")
        C.selfwait(v, "v")
        C.inc(v.tensor_scalar(ph_q[:, :], ph_p[:, :], 1.0 / TWO_PI, None,
                              ALU.mult), "v")
        C.selfwait(v, "v")
        C.inc(v.tensor_copy(ph_qi, ph_q[:, :]), "v")
        C.selfwait(v, "v")
        C.inc(v.tensor_copy(ph_r[:, :], ph_qi), "v")
        C.selfwait(v, "v")
        C.inc(v.scalar_tensor_tensor(ph_p[:, :], ph_r[:, :], -TWO_PI,
                                     ph_p[:, :], ALU.mult, ALU.add), "v")
        C.selfwait(v, "v")
        C.inc(v.tensor_scalar(ph_r[:, :], ph_p[:, :], PI, None, ALU.is_gt),
              "v")
        C.selfwait(v, "v")
        C.inc(v.scalar_tensor_tensor(ph_p[:, :], ph_r[:, :], -TWO_PI,
                                     ph_p[:, :], ALU.mult, ALU.add), "v")
        C.selfwait(v, "v")
        C.inc(v.tensor_scalar(ph_r[:, :], ph_p[:, :], -PI, None, ALU.is_lt),
              "v")
        C.selfwait(v, "v")
        C.inc(v.scalar_tensor_tensor(ph_p[:, :], ph_r[:, :], TWO_PI,
                                     ph_p[:, :], ALU.mult, ALU.add), "v",
              mark="v_pred")
        C.wait(v, "s_abs")
        C.selfwait(v, "v")
        C.inc(v.tensor_scalar(ph_r[:, :], ph_q[:, :], -1.0, PI / 2.0,
                              ALU.mult, ALU.add), "v", mark="v_carg")
        C.wait(v, "t_btr")
        C.inc(v.tensor_copy(bret_bf[:, :], pBbf[:, 0:512]), "v")
        C.inc(v.tensor_copy(bimt_bf[:, :], pBbf[:, 512:1024]), "v",
              mark="v_btrev")
        C.wait(v, "t_drep")
        C.inc(v.tensor_copy(d_rep_bf[:, :], pA[:, :]), "v", mark="v_drep")
        C.wait(v, "t_sdrep")
        C.inc(v.tensor_copy(sdrep[:, :], pA[:, :]), "v", mark="v_sdrep")

        NTOT = 2 * NT
        for sl in range(NTOT + 3):
            j = sl
            if j < NTOT:
                C.wait(v, f"du_{j}")
                C.selfwait(v, "v")
                for gq in range(4):
                    C.inc(v.bn_stats(statbuf[:, 6 * gq:6 * (gq + 1)],
                                     ubuf[j % 2][:, 512 * gq:512 * (gq + 1)]),
                          "v")
                C.selfwait(v, "v")
                C.inc(v.bn_aggr(mv_p[j % 2], statbuf[:, :]), "v",
                      mark=f"aggr_{j}")
            k = sl - 1
            if 0 <= k < NTOT:
                C.wait(v, f"sqrt_{k}")
                C.selfwait(v, "v")
                C.inc(v.reciprocal(rstd_p[k % 2], sd_p[k % 2]), "v")
                C.inc(v.tensor_scalar(c_negm, mv_p[k % 2][:, 0:1], -1.0,
                                      None, ALU.mult), "v")
                C.selfwait(v, "v")
                C.inc(v.tensor_tensor(nmr_p[k % 2], c_negm, rstd_p[k % 2],
                                      ALU.mult), "v", mark=f"lnc_{k}")
            k = sl - 2
            if 0 <= k < NTOT:
                C.wait(v, f"tr_{k}")
                for gq in range(2):
                    C.inc(v.tensor_copy(
                        t1nT[k % 2][:, 512 * gq:512 * (gq + 1)],
                        pBbf[:, 512 * gq:512 * (gq + 1)]), "v",
                        mark=(f"t1nTv_{k}" if gq == 1 else None))
            k = sl - 3
            if 0 <= k < NTOT:
                C.wait(v, f"bu_{k}")
                pb = pC if k % 2 == 0 else pD
                bre_dst, bim_dst = bu_dsts(k)
                C.inc(v.tensor_copy(bre_dst, pb[0:NCH, 0:128]), "v")
                C.inc(v.tensor_copy(bim_dst, pb[0:NCH, 128:256]), "v",
                      mark=f"buev_{k}")

        C.inc(v.tensor_scalar(tC[:, :], it_fwd[:, :], c_lnr, None, ALU.mult),
              "v", mark="v_rpe")
        C.wait(v, "s_rpow")
        C.inc(v.tensor_tensor(tA[:, :], cosT[:, :], rp_v[:, :], ALU.mult),
              "v")
        C.inc(v.tensor_tensor(tB[:, :], sinT[:, :], rp_v[:, :], ALU.mult),
              "v")
        C.selfwait(v, "v")
        C.inc(v.tensor_tensor(tC[:, :], tA[:, :], bu_re_p[:, :], ALU.mult),
              "v")
        C.selfwait(v, "v")
        C.inc(v.reduce_sum(c_t1, tC[:, :], axis=AX.X), "v")
        C.selfwait(v, "v")
        C.inc(v.tensor_tensor(tC[:, :], tB[:, :], bu_im_p[:, :], ALU.mult),
              "v")
        C.selfwait(v, "v")
        C.inc(v.reduce_sum(c_t2, tC[:, :], axis=AX.X), "v")
        C.selfwait(v, "v")
        C.inc(v.tensor_tensor(tC[:, :], tA[:, :], bu_im_p[:, :], ALU.mult),
              "v")
        C.selfwait(v, "v")
        C.inc(v.reduce_sum(c_t3, tC[:, :], axis=AX.X), "v")
        C.selfwait(v, "v")
        C.inc(v.tensor_tensor(tC[:, :], tB[:, :], bu_re_p[:, :], ALU.mult),
              "v")
        C.selfwait(v, "v")
        C.inc(v.reduce_sum(c_t4, tC[:, :], axis=AX.X), "v")
        C.selfwait(v, "v")
        C.inc(v.tensor_tensor(c_sr, c_t1, c_t2, ALU.add), "v")
        C.inc(v.tensor_tensor(c_si, c_t3, c_t4, ALU.subtract), "v")
        C.selfwait(v, "v")
        C.inc(v.tensor_tensor(c_t1, cosT[:, T - 1:T], c_sr, ALU.mult), "v")
        C.inc(v.tensor_tensor(c_t2, sinT[:, T - 1:T], c_si, ALU.mult), "v")
        C.inc(v.tensor_tensor(c_t3, sinT[:, T - 1:T], c_sr, ALU.mult), "v")
        C.inc(v.tensor_tensor(c_t4, cosT[:, T - 1:T], c_si, ALU.mult), "v")
        C.selfwait(v, "v")
        C.inc(v.tensor_tensor(c_cre, c_t1, c_t2, ALU.subtract), "v")
        C.inc(v.tensor_tensor(c_cim, c_t3, c_t4, ALU.add), "v")
        C.selfwait(v, "v")
        C.inc(v.tensor_tensor(tA[:, :], cosT[:, :], bu_re_m[:, :], ALU.mult),
              "v")
        C.inc(v.tensor_tensor(tB[:, :], sinT[:, :], bu_im_m[:, :], ALU.mult),
              "v")
        C.selfwait(v, "v")
        C.inc(v.tensor_tensor(bt_re[:, :], tA[:, :], tB[:, :], ALU.add), "v")
        C.selfwait(v, "v")
        C.inc(v.tensor_tensor(tA[:, :], cosT[:, :], bu_im_m[:, :], ALU.mult),
              "v")
        C.inc(v.tensor_tensor(tB[:, :], sinT[:, :], bu_re_m[:, :], ALU.mult),
              "v")
        C.selfwait(v, "v")
        C.inc(v.tensor_tensor(bt_im[:, :], tA[:, :], tB[:, :], ALU.subtract),
              "v")
        C.inc(v.memset(tC[:, :], 1.0), "v")
        C.selfwait(v, "v")
        C.inc(v.tensor_scalar(tC[:, :], tC[:, :], c_r, None, ALU.mult), "v")
        C.selfwait(v, "v")
        C.inc(v.tensor_tensor_scan(tA[:, :], tC[:, :], bt_re[:, :], c_cre,
                                   ALU.mult, ALU.add), "v")
        C.inc(v.tensor_tensor_scan(tB[:, :], tC[:, :], bt_im[:, :], c_cim,
                                   ALU.mult, ALU.add), "v")
        C.selfwait(v, "v")
        C.inc(v.tensor_tensor(tC[:, :], cosT[:, :], tA[:, :], ALU.mult), "v")
        C.inc(v.tensor_tensor(bt_re[:, :], sinT[:, :], tB[:, :], ALU.mult),
              "v")
        C.selfwait(v, "v")
        C.inc(v.tensor_tensor(xs_re[:, :], tC[:, :], bt_re[:, :],
                              ALU.subtract), "v")
        C.selfwait(v, "v")
        C.inc(v.tensor_tensor(tC[:, :], cosT[:, :], tB[:, :], ALU.mult), "v")
        C.inc(v.tensor_tensor(bt_im[:, :], sinT[:, :], tA[:, :], ALU.mult),
              "v")
        C.selfwait(v, "v")
        C.inc(v.tensor_tensor(xs_im[:, :], tC[:, :], bt_im[:, :], ALU.add),
              "v", mark="v_scan_done")

        for sl in range(NT + 5):
            i = sl
            if i < NT:
                C.selfwait(v, "v")
                C.inc(v.tensor_tensor(tmpD_p[i % 2][:, :], d_rep_bf[:, :],
                                      t1n_full[:, i * 2048:(i + 1) * 2048],
                                      ALU.mult), "v", mark=f"td_{i}")
            k = sl - 2
            if 0 <= k < NT:
                C.wait(v, f"ysqrt_{k}")
                C.selfwait(v, "v")
                C.inc(v.reciprocal(rsy_p[k % 2], sdy_p[k % 2]), "v")
                C.inc(v.tensor_scalar(c_negm, mvy_p[k % 2][:, 0:1], -1.0,
                                      None, ALU.mult), "v")
                C.selfwait(v, "v")
                C.inc(v.tensor_tensor(nmy_p[k % 2], c_negm, rsy_p[k % 2],
                                      ALU.mult), "v", mark=f"ylnc_{k}")
            k = sl - 4
            if 0 <= k < NT:
                C.wait(v, f"x1sqrt_{k}")
                C.selfwait(v, "v")
                C.inc(v.reciprocal(rsx_p[k % 2], sdx_p[k % 2]), "v")
                C.inc(v.tensor_scalar(c_negm, mvx_p[k % 2][:, 0:1], -1.0,
                                      None, ALU.mult), "v")
                C.selfwait(v, "v")
                C.inc(v.tensor_tensor(nmx_p[k % 2], c_negm, rsx_p[k % 2],
                                      ALU.mult), "v", mark=f"xlnc_{k}")
            k = sl - 3
            if 0 <= k < NT:
                C.wait(v, f"ln2_{k}")
                C.wait(v, f"dx0_{k}")
                if k >= 2:
                    C.wait(v, f"dx1w_{k-2}")
                    C.wait(v, f"xnapply_{k-2}")
                C.selfwait(v, "v")
                C.inc(v.tensor_tensor(x1buf[k % 2][:, :],
                                      ybuf_p[k % 2][:, :], x0buf1[:, :],
                                      ALU.add), "v", mark=f"x1_{k}")
                C.selfwait(v, "v")
                for gq in range(4):
                    C.inc(v.bn_stats(
                        statbuf[:, 6 * gq:6 * (gq + 1)],
                        x1buf[k % 2][:, 512 * gq:512 * (gq + 1)]), "v")
                C.selfwait(v, "v")
                C.inc(v.bn_aggr(mvx_p[k % 2], statbuf[:, :]), "v",
                      mark=f"x1aggr_{k}")
            k = sl - 1
            if 0 <= k < NT:
                C.wait(v, f"ymm_{k}")
                if k >= 2:
                    C.wait(v, f"ln2_{k-2}")
                for gq in range(4):
                    C.inc(v.tensor_tensor(
                        ybuf_p[k % 2][:, 512 * gq:512 * (gq + 1)],
                        pA[:, 512 * gq:512 * (gq + 1)],
                        tmpD_p[k % 2][:, 512 * gq:512 * (gq + 1)],
                        ALU.add), "v",
                        mark=(f"yev_{k}" if gq == 3 else None))
                C.selfwait(v, "v")
                for gq in range(4):
                    C.inc(v.bn_stats(
                        statbuf[:, 6 * gq:6 * (gq + 1)],
                        ybuf_p[k % 2][:, 512 * gq:512 * (gq + 1)]), "v")
                C.selfwait(v, "v")
                C.inc(v.bn_aggr(mvy_p[k % 2], statbuf[:, :]), "v",
                      mark=f"yaggr_{k}")
            k = sl - 5
            if 0 <= k < NT:
                C.wait(v, f"xntr_{k}")
                if k >= 9:
                    C.wait(v, f"dxnw_{k-9}")
                for gq in range(2):
                    C.inc(v.tensor_copy(xnev_dst(k, gq), xnev_src(k, gq)),
                          "v", mark=(f"xnevv_{k}" if gq == 1 else None))

        for X in range(2):
            for hb in range(HB):
                if X == 0 and hb == 0:
                    C.wait(v, "dxnw_6")
                    C.wait(v, "dxnw_7")
                    C.wait(v, f"dx1w_{NT-2}")
                    C.wait(v, f"dx1w_{NT-1}")
                C.wait(v, f"s1mm_{X}_{hb}")
                C.selfwait(v, "v")
                if hb % 2 == 0:
                    ups = [pA[:, 1024:1536], pA[:, 1536:2048]]
                else:
                    ups = [pC[:, :], pD[:, :]]
                for tg in range(2):
                    C.inc(v.tensor_scalar(
                        us_s1[:, tg * 512:(tg + 1) * 512], ups[tg],
                        subuf[:, hb:hb + 1], None, ALU.mult), "v",
                        mark=(f"uev_{X}_{hb}" if tg == 1 else None))
                C.wait(v, f"gs_{X}_{hb}")
                C.selfwait(v, "v")
                for tg in range(2):
                    C.inc(v.tensor_tensor(
                        Hcols(hb, tg), h2buf[:, tg * 512:(tg + 1) * 512],
                        us_s1[:, tg * 512:(tg + 1) * 512], ALU.mult), "v",
                        mark=(f"h_{X}_{hb}" if tg == 1 else None))
            PS2v = [pA[:, 0:512], pA[:, 512:1024], pA[:, 1024:1536],
                    pA[:, 1536:2048], pB[:, 0:512], pB[:, 512:1024],
                    pC[:, :], pD[:, :]]
            for dg in range(4):
                for ti in range(8):
                    par = (dg * 8 + ti) % 2
                    C.wait(v, f"s2f_{X}_{dg}_{ti}")
                    k = dg * 8 + ti
                    if k >= 2:
                        pk = k - 2
                        C.wait(v, f"dyw_{X}_{pk // 8}_{pk % 8}")
                    elif X == 1:
                        C.wait(v, f"dyw_0_3_{6 + ti}")
                    C.inc(v.tensor_tensor(ystage[par][:, :], PS2v[ti],
                                          sdrep[:, dg * 512:(dg + 1) * 512],
                                          ALU.mult), "v")
                    C.selfwait(v, "v")
                    C.inc(v.bn_stats(
                        ln4stats[:, ti * 24 + dg * 6:ti * 24 + dg * 6 + 6],
                        ystage[par][:, :]), "v", mark=f"ys_{X}_{dg}_{ti}")
            C.selfwait(v, "v")
            for ti in range(8):
                C.inc(v.bn_aggr(l4mv[ti], ln4stats[:, ti * 24:(ti + 1) * 24]),
                      "v", mark=f"l4aggr_{X}_{ti}")
            for ti in range(8):
                C.wait(v, f"l4sqrt_{X}_{ti}")
                C.selfwait(v, "v")
                C.inc(v.reciprocal(l4rs[ti], l4nm[ti]), "v")
                C.inc(v.tensor_scalar(c_negm, l4mv[ti][:, 0:1], -1.0, None,
                                      ALU.mult), "v")
                C.selfwait(v, "v")
                C.inc(v.tensor_tensor(l4nm[ti], c_negm, l4rs[ti], ALU.mult),
                      "v", mark=f"l4c_{X}_{ti}")
            for ti in range(8):
                C.wait(v, f"dyr_{X}_{ti}")
                C.inc(v.tensor_scalar(yrbuf[:, :], yrbuf[:, :], l4nm[ti],
                                      l4rs[ti], ALU.add, ALU.mult), "v",
                      mark=f"ln4apply_{X}_{ti}")
                C.wait(v, f"dx1r_{X}_{ti}")
                gk = X * 8 + ti
                if gk >= 2:
                    pgk = gk - 2
                    C.wait(v, f"dout_{pgk // 8}_{pgk % 8}")
                C.selfwait(v, "v")
                C.inc(v.tensor_tensor(outbuf[ti % 2][:, :], yrbuf[:, :],
                                      x1rbuf[:, :], ALU.add), "v",
                      mark=f"out_{X}_{ti}")

    def run_pass(pn):
        C.begin(pn, sems if pn == 2 else None)
        if pn == 1:
            emit_sync(_FakeEng())
            emit_gpsimd(_FakeEng())
            emit_scalar(_FakeEng())
            emit_vector(_FakeEng())
            emit_tensor(_FakeEng())
            return
        with nc.Block() as block:
            @block.sync
            def _(q):
                emit_sync(q)

            @block.gpsimd
            def _(g):
                emit_gpsimd(g)

            @block.scalar
            def _(s):
                emit_scalar(s)

            @block.vector
            def _(v):
                emit_vector(v)

            @block.tensor
            def _(t):
                emit_tensor(t)

    run_pass(1)
    cnt1 = dict(C.cnt)
    run_pass(2)
    assert C.cnt == cnt1, (C.cnt, cnt1)
    st.close()
    return nc


def _u8view(wq_int32):
    a = np.ascontiguousarray(np.asarray(wq_int32))
    assert a.dtype == np.int32
    return np.ascontiguousarray(a.view(np.uint8)[..., ::4])


def _stage_w_hd(wq):
    a = _u8view(wq).reshape(HB, 128, 16, 128)
    a = np.ascontiguousarray(a.transpose(0, 3, 2, 1))
    return a.reshape(HB, 128, 2048)


def _stage_w_dh(wq):
    a = np.ascontiguousarray(_u8view(wq).T)
    return a.reshape(HB, 128, 2048)


def kernel(L0_attn_u, L0_attn_Lambda_unconstrained_re, L0_attn_Lambda_im,
           L0_attn_B_re, L0_attn_B_im, L0_attn_C_re, L0_attn_C_im,
           L0_attn_D, L0_attn_log_Delta, L0_ffn_wq_gate, L0_ffn_s_gate,
           L0_ffn_wq_up, L0_ffn_s_up, L0_ffn_wq_down, L0_ffn_s_down):
    u = np.ascontiguousarray(np.asarray(L0_attn_u, dtype=np.float32))
    wg = _stage_w_hd(L0_ffn_wq_gate)
    wu = _stage_w_hd(L0_ffn_wq_up)
    wd = _stage_w_dh(L0_ffn_wq_down)
    sg = np.ascontiguousarray(
        np.asarray(L0_ffn_s_gate, np.float32).reshape(HB, 128).T)
    su = np.ascontiguousarray(
        np.asarray(L0_ffn_s_up, np.float32).reshape(HB, 128).T)
    sd = np.ascontiguousarray(np.asarray(L0_ffn_s_down,
                                         np.float32).reshape(1, D))
    bre = np.ascontiguousarray(np.asarray(L0_attn_B_re, np.float32))
    bim = np.ascontiguousarray(np.asarray(L0_attn_B_im, np.float32))
    cret = np.ascontiguousarray(np.asarray(L0_attn_C_re, np.float32).T)
    cimt = np.ascontiguousarray(np.asarray(L0_attn_C_im, np.float32).T)
    drow = np.ascontiguousarray(np.asarray(L0_attn_D,
                                           np.float32).reshape(1, D))
    lunc = np.ascontiguousarray(
        np.asarray(L0_attn_Lambda_unconstrained_re,
                   np.float32).reshape(NCH, 1))
    lim = np.ascontiguousarray(np.asarray(L0_attn_Lambda_im,
                                          np.float32).reshape(NCH, 1))
    logd = np.ascontiguousarray(
        np.asarray(L0_attn_log_Delta, np.float32)[:NCH].reshape(NCH, 1))
    zeros_u = np.zeros((T, D), np.float32)

    nc = build_program()
    in_maps = []
    for c in range(8):
        b, h = c // 2, c % 2
        mine = np.ascontiguousarray(u[b, h * T:(h + 1) * T, :])
        prev = np.ascontiguousarray(u[b, 0:T, :]) if h == 1 else zeros_u
        in_maps.append(dict(
            u=mine, uprev=prev, wg=wg, wu=wu, wd=wd, sg=sg, su=su, sd=sd,
            bre=bre, bim=bim, cret=cret, cimt=cimt, drow=drow, lunc=lunc,
            lim=lim, logd=logd))
    res = run_bass_kernel_spmd(nc, in_maps, core_ids=list(range(8)))
    out = np.empty((B, L, D), np.float32)
    for c in range(8):
        b, h = c // 2, c % 2
        out[b, h * T:(h + 1) * T, :] = np.asarray(res.results[c]["out"])
    return out


# version 09
# speedup vs baseline: 1.2635x; 1.2635x over previous
import contextlib
import numpy as np
import ml_dtypes
import concourse.bass as bass
import concourse.mybir as mybir
from concourse.bass_utils import run_bass_kernel_spmd

F32 = mybir.dt.float32
BF16 = mybir.dt.bfloat16
U8 = mybir.dt.uint8
I32 = mybir.dt.int32
AF = mybir.ActivationFunctionType
ALU = mybir.AluOpType
AX = mybir.AxisListType

B, L, D = 4, 4096, 2048
HID = 5632
NCH = 32
T = 2048
NT = 16
HB = 44
TH = 1024
PI = float(np.pi)
TWO_PI = float(2.0 * np.pi)
EPS_LN = 1e-5
KB = 1024


class Ctx:

    KEYS = ("d_par", "d_u", "d_u2", "d_u3", "d_x0", "d_x02", "d_x1w",
            "d_x1w2", "d_x1r",
            "d_xnw", "d_xnw2", "d_xnr", "d_w", "d_w2", "d_wdl", "d_wdh", "d_yw",
            "d_yw2", "d_yr", "d_out", "d_out2", "v", "s", "t", "g")

    def __init__(self):
        self.marks = {}
        self.pass_num = 0
        self.cnt = {}
        self.sems = None

    def begin(self, pass_num, sems=None):
        self.pass_num = pass_num
        self.cnt = {k: 0 for k in self.KEYS}
        self.sems = sems

    def inc(self, inst, key, by=1, mark=None):
        if self.pass_num == 2:
            inst.then_inc(self.sems[key], by)
        self.cnt[key] += by
        if mark is not None:
            if self.pass_num == 1:
                assert mark not in self.marks, f"dup mark {mark}"
                self.marks[mark] = (key, self.cnt[key])
            else:
                assert self.marks[mark] == (key, self.cnt[key]), \
                    f"mark drift {mark}"
        return self.cnt[key]

    def wait(self, eng, mark, delta=0):
        if self.pass_num == 2:
            key, val = self.marks[mark]
            if val + delta > 0:
                eng.wait_ge(self.sems[key], val + delta)

    def wait_cnt(self, eng, key, val):
        if self.pass_num == 2 and val > 0:
            eng.wait_ge(self.sems[key], val)

    def selfwait(self, eng, key):
        if self.pass_num == 2 and self.cnt[key] > 0:
            eng.wait_ge(self.sems[key], self.cnt[key])


class _FakeInst:
    def then_inc(self, *a, **k):
        return self


class _FakeEng:
    def __getattr__(self, name):
        return lambda *a, **k: _FakeInst()


def build_program():
    nc = bass.Bass("TRN2", target_bir_lowering=False, debug=False)

    d_in = {}

    def din(name, shape, dt):
        d_in[name] = nc.dram_tensor(name, shape, dt, kind="ExternalInput").ap()

    din("u", [T, D], F32)
    din("uprev", [T, D], F32)
    din("wg", [HB, 128, D], U8)
    din("wu", [HB, 128, D], U8)
    din("wd", [HB, 128, D], U8)
    din("sg", [128, HB], F32)
    din("su", [128, HB], F32)
    din("sd", [1, D], F32)
    din("bre", [NCH, D], F32)
    din("bim", [NCH, D], F32)
    din("cret", [NCH, D], F32)
    din("cimt", [NCH, D], F32)
    din("drow", [1, D], F32)
    din("lunc", [NCH, 1], F32)
    din("lim", [NCH, 1], F32)
    din("logd", [NCH, 1], F32)
    out_hbm = nc.dram_tensor("out", [T, D], F32, kind="ExternalOutput").ap()
    x1_spill = nc.dram_tensor("x1sp", [T, D], F32, kind="Internal").ap()
    xnt_spill = nc.dram_tensor("xntsp", [128, 16 * TH], BF16,
                               kind="Internal").ap()
    y_spill = nc.dram_tensor("ysp", [TH, D], F32, kind="Internal").ap()

    st = contextlib.ExitStack()
    ARENA = 207 * KB
    arena = st.enter_context(nc.sbuf_tensor("arena", [128, ARENA], U8)).ap()

    def vf32(off_kb, n):
        off = int(off_kb * KB)
        assert off % 4 == 0 and off + 4 * n <= ARENA, (off_kb, n)
        return arena[:, off:off + 4 * n].bitcast(F32)

    def vbf(off_kb, n):
        off = int(off_kb * KB)
        assert off % 2 == 0 and off + 2 * n <= ARENA, (off_kb, n)
        return arena[:, off:off + 2 * n].bitcast(BF16)

    def vi32(off_kb, n):
        off = int(off_kb * KB)
        return arena[:, off:off + 4 * n].bitcast(I32)

    t1n_full = vbf(0, 32768)
    wd_lo = vbf(0, 11264)
    wd_hi = vbf(22, 11264)
    wgbuf = [vbf(44, 2048), vbf(48, 2048)]
    wubuf = [vbf(52, 2048), vbf(56, 2048)]
    us_s1 = vbf(60, 1024)
    yrbuf = vf32(0, 2048)
    x1rbuf = vf32(8, 2048)
    outbuf = [vf32(16, 2048), vf32(24, 2048)]

    xnt = vbf(64, 16384)
    drow_f = vf32(64, 2048)[0:1, :]
    sdrow = vf32(72, 2048)[0:1, :]
    drow_bf = vbf(80, 2048)[0:1, :]

    ubuf = [vf32(96, 2048), vf32(104, 2048), vf32(84, 2048)]
    t1nT = [vbf(112, 2048), vbf(116, 2048)]
    bu_re_m = vf32(120, 2048)[0:NCH, :]
    bu_im_m = vf32(128, 2048)[0:NCH, :]
    bu_re_p = vf32(136, 2048)[0:NCH, :]
    bu_im_p = vf32(144, 2048)[0:NCH, :]
    cosT = vf32(152, 2048)[0:NCH, :]
    sinT = vf32(160, 2048)[0:NCH, :]
    it_fwd = vf32(168, 2048)[0:NCH, :]
    rp_v = vf32(168, 2048)[0:NCH, :]
    c2re_v = vbf(176, 2048)[0:NCH, :]
    c2im_v = vbf(180, 2048)[0:NCH, :]
    bbar_sA = vf32(96, 2048)[0:NCH, :]
    bbar_sB = vf32(104, 2048)[0:NCH, :]
    bbar_re = vbf(112, 2048)[0:NCH, :]
    bbar_im = vbf(116, 2048)[0:NCH, :]
    ph_p = vf32(120, 2048)[0:NCH, :]
    ph_q = vf32(128, 2048)[0:NCH, :]
    ph_qi = vi32(136, 2048)[0:NCH, :]
    ph_r = vf32(144, 2048)[0:NCH, :]
    tA = vf32(96, 2048)[0:NCH, :]
    tB = vf32(104, 2048)[0:NCH, :]
    tC = vf32(112, 2048)[0:NCH, :]
    bt_re = vf32(136, 2048)[0:NCH, :]
    bt_im = vf32(144, 2048)[0:NCH, :]
    xs_re = vbf(120, 2048)[0:NCH, :]
    xs_im = vbf(124, 2048)[0:NCH, :]
    tmpD_p = [vf32(96, 2048), vf32(104, 2048)]
    ybuf_p = [vf32(112, 2048), vf32(160, 2048)]
    x1buf = [vf32(128, 2048), vf32(136, 2048)]
    x0buf_p = [vf32(144, 2048), vf32(168, 2048)]
    xnbuf = vbf(152, 2048)
    xnt_stage = vbf(156, 2048)
    Hbuf = vbf(96, 45056)

    d_rep_bf = vbf(184, 2048)
    sdrep = vf32(188, 2048)

    id_bf = vbf(196, 128)
    onesrow_bf = vbf(196.25, 128)[0:1, :]
    onesrow_f = vf32(196.5, 128)[0:1, :]
    sgbuf = vf32(197, HB)
    subuf = vf32(197.25, HB)
    colbuf = vf32(197.5, 96)
    statbuf = vf32(197.875, 24)
    bret_bf = vbf(198.25, 512)
    bimt_bf = vbf(199.25, 512)
    ystage = [vf32(200.25, 512), vf32(202.25, 512)]
    ln4stats = vf32(204.25, 192)
    h2buf = vbf(205, 1024)

    pA = st.enter_context(nc.psum_tensor("pA", [128, 2048], F32)).ap()
    pB = st.enter_context(nc.psum_tensor("pB", [128, 1024], F32)).ap()
    pC = st.enter_context(nc.psum_tensor("pC", [128, 512], F32)).ap()
    pD = st.enter_context(nc.psum_tensor("pD", [128, 512], F32)).ap()
    pBbf = pB.bitcast(BF16)

    sems = {k: st.enter_context(nc.semaphore(k)) for k in Ctx.KEYS}
    C = Ctx()

    p_lunc = colbuf[0:NCH, 0:1]
    p_lim = colbuf[0:NCH, 1:2]
    p_logd = colbuf[0:NCH, 2:3]
    c_delta = colbuf[0:NCH, 3:4]
    c_lamre = colbuf[0:NCH, 4:5]
    c_lnr = colbuf[0:NCH, 5:6]
    c_theta = colbuf[0:NCH, 6:7]
    c_r = colbuf[0:NCH, 7:8]
    c_are = colbuf[0:NCH, 8:9]
    c_aim = colbuf[0:NCH, 9:10]
    c_t1 = colbuf[0:NCH, 10:11]
    c_t2 = colbuf[0:NCH, 11:12]
    c_t3 = colbuf[0:NCH, 12:13]
    c_t4 = colbuf[0:NCH, 13:14]
    c_dre = colbuf[0:NCH, 14:15]
    c_dim = colbuf[0:NCH, 15:16]
    c_eps = colbuf[:, 16:17]
    c_cre = colbuf[0:NCH, 17:18]
    c_cim = colbuf[0:NCH, 18:19]
    c_sr = colbuf[0:NCH, 19:20]
    c_si = colbuf[0:NCH, 20:21]
    c_negm = colbuf[:, 21:22]
    c_rstd = colbuf[:, 22:23]
    c_sd = colbuf[:, 23:24]
    c_Tlnr = colbuf[0:NCH, 24:25]
    c_i32 = colbuf.bitcast(I32)[0:NCH, 25:26]
    c_sinth = colbuf[0:NCH, 26:27]
    c_sh = colbuf[0:NCH, 27:28]
    c_costh = colbuf[0:NCH, 28:29]
    c_thred = colbuf[0:NCH, 29:30]
    mvbuf = colbuf[:, 30:32]
    mv_p = [colbuf[:, 32:34], colbuf[:, 34:36]]
    sd_p = [colbuf[:, 36:37], colbuf[:, 37:38]]
    rstd_p = [colbuf[:, 38:39], colbuf[:, 39:40]]
    nmr_p = [colbuf[:, 40:41], colbuf[:, 41:42]]
    mvy_p = [colbuf[:, 42:44], colbuf[:, 44:46]]
    sdy_p = [colbuf[:, 46:47], colbuf[:, 47:48]]
    rsy_p = [colbuf[:, 48:49], colbuf[:, 49:50]]
    nmy_p = [colbuf[:, 50:51], colbuf[:, 51:52]]
    mvx_p = [colbuf[:, 52:54], colbuf[:, 54:56]]
    sdx_p = [colbuf[:, 56:57], colbuf[:, 57:58]]
    rsx_p = [colbuf[:, 58:59], colbuf[:, 59:60]]
    nmx_p = [colbuf[:, 60:61], colbuf[:, 61:62]]
    l4mv = [colbuf[:, 62 + 2 * t_: 64 + 2 * t_] for t_ in range(8)]
    l4rs = [colbuf[:, 78 + t_: 79 + t_] for t_ in range(8)]
    l4nm = [colbuf[:, 86 + t_: 87 + t_] for t_ in range(8)]
    p_bre = bu_re_m
    p_bim = bu_im_m
    p_cret = cosT
    p_cimt = sinT

    def Hcols(hb, tg):
        return Hbuf[:, hb * TH + tg * 512: hb * TH + (tg + 1) * 512]

    def bu_dsts(k):
        i = k % NT
        bre = (bu_re_p if k < NT else bu_re_m)[:, i * 128:(i + 1) * 128]
        bim = (bu_im_p if k < NT else bu_im_m)[:, i * 128:(i + 1) * 128]
        return bre, bim

    def xnev_dst(i, gq):
        if i < 8:
            return xnt.rearrange("p (k tt) -> p k tt", k=16)[
                :, 4 * gq:4 * (gq + 1), i * 128:(i + 1) * 128]
        return xnt_stage[:, 512 * gq:512 * (gq + 1)]

    def xnev_src(i, gq):
        s = pBbf[:, 512 * gq:512 * (gq + 1)]
        if i < 8:
            return s.rearrange("p (k tt) -> p k tt", k=4)
        return s

    def emit_sync(q):
        C.inc(q.dma_start(p_lunc, d_in["lunc"]), "d_par", 16)
        C.inc(q.dma_start(p_lim, d_in["lim"]), "d_par", 16)
        C.inc(q.dma_start(p_logd, d_in["logd"]), "d_par", 16)
        C.inc(q.dma_start(p_bre, d_in["bre"]), "d_par", 16)
        C.inc(q.dma_start(p_bim, d_in["bim"]), "d_par", 16)
        C.inc(q.dma_start(p_cret, d_in["cret"]), "d_par", 16)
        C.inc(q.dma_start(p_cimt, d_in["cimt"]), "d_par", 16)
        C.inc(q.dma_start(drow_f, d_in["drow"]), "d_par", 16)
        C.inc(q.dma_start(sdrow, d_in["sd"]), "d_par", 16)
        C.inc(q.dma_start(sgbuf[:, :], d_in["sg"]), "d_par", 16)
        C.inc(q.dma_start(subuf[:, :], d_in["su"]), "d_par", 16,
              mark="d_params")

        for j in range(2 * NT):
            src = d_in["uprev"] if j < NT else d_in["u"]
            i = j % NT
            if j < 2:
                C.wait(q, "v_bbar")
            elif j >= 3:
                C.wait(q, f"uapply_{j-3}")
            C.inc(q.dma_start(ubuf[j % 3][:, :],
                              src[i * 128:(i + 1) * 128, :]),
                  ("d_u", "d_u2", "d_u3")[j % 3], 16, mark=f"du_{j}")

        for i in range(NT):
            if i < 2:
                C.wait(q, "v_scan_done")
            else:
                C.wait(q, f"x1_{i-2}")
            C.inc(q.dma_start(x0buf_p[i % 2][:, :],
                              d_in["u"][i * 128:(i + 1) * 128, :]),
                  "d_x0" if i % 2 == 0 else "d_x02", 16, mark=f"dx0_{i}")
            if i >= 1:
                C.wait(q, f"x1_{i-1}")
                C.inc(q.dma_start(x1_spill[(i - 1) * 128:i * 128, :],
                                  x1buf[(i - 1) % 2][:, :]),
                      "d_x1w" if (i - 1) % 2 == 0 else "d_x1w2", 16,
                      mark=f"dx1w_{i-1}")
        C.wait(q, f"x1_{NT-1}")
        C.inc(q.dma_start(x1_spill[(NT - 1) * 128:NT * 128, :],
                          x1buf[(NT - 1) % 2][:, :]),
              "d_x1w" if (NT - 1) % 2 == 0 else "d_x1w2", 16,
              mark=f"dx1w_{NT-1}")

        for X in range(2):
            if X == 0:
                C.wait(q, "s1mm_0_43")
                C.inc(q.dma_start(xnt[:, :], xnt_spill[:, :]), "d_xnr", 16,
                      mark="dxnr")
            for dg in range(4):
                for ti in range(8):
                    par = (dg * 8 + ti) % 2
                    C.wait(q, f"ys_{X}_{dg}_{ti}")
                    C.inc(q.dma_start(
                        y_spill[ti * 128:(ti + 1) * 128,
                                dg * 512:(dg + 1) * 512],
                        ystage[par][:, :]),
                        "d_yw" if par == 0 else "d_yw2", 16,
                        mark=f"dyw_{X}_{dg}_{ti}")
            C.wait(q, f"s2f_{X}_3_7")
            C.wait(q, f"dyw_{X}_3_6")
            C.wait(q, f"dyw_{X}_3_7")
            for ti in range(8):
                gt = X * 8 + ti
                if ti >= 1:
                    C.wait(q, f"out_{X}_{ti-1}")
                C.inc(q.dma_start(yrbuf[:, :],
                                  y_spill[ti * 128:(ti + 1) * 128, :]),
                      "d_yr", 16, mark=f"dyr_{X}_{ti}")
                if ti >= 1:
                    C.wait(q, f"out_{X}_{ti-1}")
                C.inc(q.dma_start(x1rbuf[:, :],
                                  x1_spill[gt * 128:(gt + 1) * 128, :]),
                      "d_x1r", 16, mark=f"dx1r_{X}_{ti}")
                C.wait(q, f"out_{X}_{ti}")
                C.inc(q.dma_start(out_hbm[gt * 128:(gt + 1) * 128, :],
                                  outbuf[ti % 2][:, :]),
                      "d_out" if ti % 2 == 0 else "d_out2", 16,
                      mark=f"dout_{X}_{ti}")
        C.wait_cnt(q, "d_out", 16 * 8)
        C.wait_cnt(q, "d_out2", 16 * 8)

    def emit_gpsimd(g):
        C.inc(g.iota(it_fwd[:, :], pattern=[[1, T]], base=1,
                     channel_multiplier=0,
                     allow_small_or_imprecise_dtypes=True), "g", mark="g_itf")
        C.inc(g.memset(id_bf[:, :], 1.0), "g")
        C.selfwait(g, "g")
        C.inc(g.affine_select(id_bf[:, :], id_bf[:, :], pattern=[[-1, 128]],
                              compare_op=ALU.is_equal, fill=0.0, base=0,
                              channel_multiplier=1), "g", mark="g_id")
        C.inc(g.memset(onesrow_bf[:, :], 1.0), "g")
        C.inc(g.memset(onesrow_f[:, :], 1.0), "g")
        C.inc(g.memset(c_eps, EPS_LN), "g", mark="g_setup")

        for k in range(8):
            C.wait(g, f"xnevv_{k + 8}")
            C.wait(g, f"xnevs_{k + 8}")
            C.inc(g.dma_start(
                xnt_spill.rearrange("p (kk ti tt) -> p kk ti tt",
                                    k=16, ti=8, kk=16)[:, :, k, :],
                xnt_stage.rearrange("p (kk tt) -> p kk tt", kk=16)),
                "d_xnw" if k % 2 == 0 else "d_xnw2", 16,
                mark=f"dxnw_{k}")

        for X in range(2):
            for hb in range(HB):
                par = hb % 2
                if X == 0 and hb < 2:
                    C.wait(g, "yev_15")
                else:
                    pX, phb = (X, hb - 2) if hb >= 2 else (0, HB - 2 + hb)
                    C.wait(g, f"s1mm_{pX}_{phb}")
                wkey = "d_w" if par == 0 else "d_w2"
                C.inc(g.dma_start(wgbuf[par][:, :], d_in["wg"][hb, :, :]),
                      wkey, 16)
                C.inc(g.dma_start(wubuf[par][:, :], d_in["wu"][hb, :, :]),
                      wkey, 16, mark=f"wconv_{X}_{hb}")
            for dg in range(4):
                if dg == 0:
                    if X == 0:
                        C.wait(g, "yev_15")
                    else:
                        C.wait_cnt(g, "d_out", 16 * 4)
                        C.wait_cnt(g, "d_out2", 16 * 4)
                        C.wait(g, "s2lo_0_3")
                else:
                    C.wait(g, f"s2lo_{X}_{dg-1}")
                C.inc(g.dma_start(
                    wd_lo.rearrange("p (h dd) -> p h dd", h=22),
                    d_in["wd"][0:22, :, dg * 512:(dg + 1) * 512].rearrange(
                        "h p dd -> p h dd")), "d_wdl", 16,
                    mark=f"dwdlo_{X}_{dg}")
                if dg == 0:
                    if X == 1:
                        C.wait(g, "s2f_0_3_7")
                else:
                    C.wait(g, f"s2f_{X}_{dg-1}_7")
                C.inc(g.dma_start(
                    wd_hi.rearrange("p (h dd) -> p h dd", h=22),
                    d_in["wd"][22:44, :, dg * 512:(dg + 1) * 512].rearrange(
                        "h p dd -> p h dd")), "d_wdh", 16,
                    mark=f"dwdhi_{X}_{dg}")

    def emit_scalar(s):
        C.wait(s, "d_params")
        C.inc(s.activation(c_delta, p_logd, AF.Exp), "s")
        C.inc(s.activation(c_t1, p_lunc, AF.Abs), "s")
        C.selfwait(s, "s")
        C.inc(s.activation(c_t2, c_t1, AF.Exp, scale=-1.0), "s")
        C.selfwait(s, "s")
        C.inc(s.activation(c_t3, c_t2, AF.Ln, bias=1.0), "s", mark="s_p1")
        C.wait(s, "v_lnr")
        C.inc(s.activation(c_r, c_lnr, AF.Exp), "s", mark="s_r")
        C.wait(s, "v_thred")
        C.inc(s.activation(c_sinth, c_thred, AF.Sin), "s")
        C.inc(s.activation(c_sh, c_thred, AF.Sin, scale=0.5), "s",
              mark="s_sincos")
        C.wait(s, "v_pred")
        C.inc(s.activation(sinT[:, :], ph_p[:, :], AF.Sin), "s")
        C.inc(s.activation(ph_q[:, :], ph_p[:, :], AF.Abs), "s",
              mark="s_abs")
        C.wait(s, "v_carg")
        C.inc(s.activation(cosT[:, :], ph_r[:, :], AF.Sin), "s",
              mark="s_tables")
        C.inc(s.activation(drow_bf, drow_f, AF.Copy), "s", mark="s_dbf")

        for sl in range(2 * NT + 2):
            j = sl
            if j < 2 * NT:
                C.wait(s, f"aggr_{j}")
                C.inc(s.activation(sd_p[j % 2], mv_p[j % 2][:, 1:2],
                                   AF.Sqrt, bias=c_eps), "s",
                      mark=f"sqrt_{j}")
            k = sl - 1
            if 0 <= k < 2 * NT:
                C.wait(s, f"lnc_{k}")
                i = k % NT
                C.inc(s.activation(t1n_full[:, i * 2048:(i + 1) * 2048],
                                   ubuf[k % 3][:, :], AF.Identity,
                                   scale=rstd_p[k % 2], bias=nmr_p[k % 2]),
                      "s", mark=f"uapply_{k}")
            k = sl - 2
            if 0 <= k < 2 * NT:
                C.wait(s, f"tr_{k}")
                for gq in range(2, 4):
                    C.inc(s.activation(
                        t1nT[k % 2][:, 512 * gq:512 * (gq + 1)],
                        pBbf[:, 512 * gq:512 * (gq + 1)], AF.Copy), "s",
                        mark=(f"t1nTs_{k}" if gq == 3 else None))
        C.wait(s, "v_rpe")
        C.inc(s.activation(rp_v[:, :], tC[:, :], AF.Exp, scale=-1.0,
                           bias=c_Tlnr), "s", mark="s_rpow")
        for sl in range(NT + 5):
            k = sl - 1
            if 0 <= k < NT:
                C.wait(s, f"yaggr_{k}")
                C.inc(s.activation(sdy_p[k % 2], mvy_p[k % 2][:, 1:2],
                                   AF.Sqrt, bias=c_eps), "s",
                      mark=f"ysqrt_{k}")
            k = sl - 2
            if 0 <= k < NT:
                C.wait(s, f"ylnc_{k}")
                C.inc(s.activation(ybuf_p[k % 2][:, :], ybuf_p[k % 2][:, :],
                                   AF.Identity, scale=rsy_p[k % 2],
                                   bias=nmy_p[k % 2]), "s", mark=f"ln2_{k}")
            k = sl - 3
            if 0 <= k < NT:
                C.wait(s, f"x1aggr_{k}")
                C.inc(s.activation(sdx_p[k % 2], mvx_p[k % 2][:, 1:2],
                                   AF.Sqrt, bias=c_eps), "s",
                      mark=f"x1sqrt_{k}")
            k = sl - 4
            if 0 <= k < NT:
                C.wait(s, f"xlnc_{k}")
                if k >= 1:
                    C.wait(s, f"xntr_{k-1}")
                C.inc(s.activation(xnbuf[:, :], x1buf[k % 2][:, :],
                                   AF.Identity, scale=rsx_p[k % 2],
                                   bias=nmx_p[k % 2]), "s",
                      mark=f"xnapply_{k}")
            k = sl - 5
            if 0 <= k < NT:
                C.wait(s, f"xntr_{k}")
                if k >= 9:
                    C.wait(s, f"dxnw_{k-9}")
                for gq in range(2, 4):
                    C.inc(s.activation(xnev_dst(k, gq), xnev_src(k, gq),
                                       AF.Copy), "s",
                          mark=(f"xnevs_{k}" if gq == 3 else None))
        for X in range(2):
            for hb in range(HB):
                C.wait(s, f"gmm_{X}_{hb}")
                if hb >= 1 or X == 1:
                    pX, phb = (X, hb - 1) if hb >= 1 else (0, HB - 1)
                    C.wait(s, f"h_{pX}_{phb}")
                gsrc = pA if hb % 2 == 0 else pB
                for tg in range(2):
                    C.inc(s.activation(h2buf[:, tg * 512:(tg + 1) * 512],
                                       gsrc[:, tg * 512:(tg + 1) * 512],
                                       AF.Silu, scale=sgbuf[:, hb:hb + 1]),
                          "s", mark=(f"gs_{X}_{hb}" if tg == 1 else None))
            for dg in range(4):
                C.wait(s, f"dwdlo_{X}_{dg}")
                C.inc(s.activation(wd_lo[:, :], wd_lo[:, :], AF.Copy,
                                   bias=-128.0), "s", mark=f"wdlo_{X}_{dg}")
                C.wait(s, f"dwdhi_{X}_{dg}")
                C.inc(s.activation(wd_hi[:, :], wd_hi[:, :], AF.Copy,
                                   bias=-128.0), "s", mark=f"wdhi_{X}_{dg}")
            for ti in range(8):
                C.wait(s, f"l4aggr_{X}_{ti}")
                C.inc(s.activation(l4nm[ti], l4mv[ti][:, 1:2], AF.Sqrt,
                                   bias=c_eps), "s", mark=f"l4sqrt_{X}_{ti}")

    def emit_tensor(t):
        C.wait(t, "g_id")
        C.wait(t, "v_bbar")
        for kt in range(16):
            C.inc(t.transpose(pBbf[:, kt * 32:(kt + 1) * 32],
                              bbar_re[:, kt * 128:(kt + 1) * 128],
                              id_bf[0:NCH, 0:NCH]), "t")
        for kt in range(16):
            C.inc(t.transpose(pBbf[:, 512 + kt * 32:512 + (kt + 1) * 32],
                              bbar_im[:, kt * 128:(kt + 1) * 128],
                              id_bf[0:NCH, 0:NCH]), "t",
                  mark=("t_btr" if kt == 15 else None))
        C.wait(t, "s_dbf")
        for dg in range(4):
            C.inc(t.matmul(pA[:, dg * 512:(dg + 1) * 512], onesrow_bf,
                           drow_bf[:, dg * 512:(dg + 1) * 512],
                           start=True, stop=True), "t",
                  mark=("t_drep" if dg == 3 else None))
        C.wait(t, "v_drep")
        for dg in range(4):
            C.inc(t.matmul(pA[:, dg * 512:(dg + 1) * 512], onesrow_f,
                           sdrow[:, dg * 512:(dg + 1) * 512],
                           start=True, stop=True), "t",
                  mark=("t_sdrep" if dg == 3 else None))

        for j in range(2 * NT):
            C.wait(t, f"uapply_{j}")
            if j >= 1:
                C.wait(t, f"t1nTv_{j-1}")
                C.wait(t, f"t1nTs_{j-1}")
            i = j % NT
            src = t1n_full[:, i * 2048:(i + 1) * 2048]
            for kt in range(16):
                C.inc(t.transpose(pBbf[:, kt * 128:(kt + 1) * 128],
                                  src[:, kt * 128:(kt + 1) * 128],
                                  id_bf[:, :]), "t",
                      mark=(f"tr_{j}" if kt == 15 else None))
            C.wait(t, f"t1nTv_{j}")
            C.wait(t, f"t1nTs_{j}")
            if j >= 2:
                C.wait(t, f"buev_{j-2}")
            pb = pC if j % 2 == 0 else pD
            for kt in range(16):
                C.inc(t.matmul(pb[0:NCH, 0:128],
                               bret_bf[:, kt * 32:(kt + 1) * 32],
                               t1nT[j % 2][:, kt * 128:(kt + 1) * 128],
                               start=(kt == 0), stop=(kt == 15)), "t")
            for kt in range(16):
                C.inc(t.matmul(pb[0:NCH, 128:256],
                               bimt_bf[:, kt * 32:(kt + 1) * 32],
                               t1nT[j % 2][:, kt * 128:(kt + 1) * 128],
                               start=(kt == 0), stop=(kt == 15)), "t",
                      mark=(f"bu_{j}" if kt == 15 else None))

        def xntr_block(k):
            C.wait(t, f"xnapply_{k}")
            if k == 0:
                C.wait(t, f"buev_{2*NT-1}")
            else:
                C.wait(t, f"xnevv_{k-1}")
                C.wait(t, f"xnevs_{k-1}")
            for kt in range(16):
                C.inc(t.transpose(pBbf[:, kt * 128:(kt + 1) * 128],
                                  xnbuf[:, kt * 128:(kt + 1) * 128],
                                  id_bf[:, :]), "t",
                      mark=(f"xntr_{k}" if kt == 15 else None))

        C.wait(t, "v_scan_done")
        C.wait(t, "v_sdrep")
        for i in range(NT):
            if i >= 1:
                C.wait(t, f"yev_{i-1}")
            for dg in range(4):
                C.inc(t.matmul(pA[:, dg * 512:(dg + 1) * 512],
                               xs_re[:, i * 128:(i + 1) * 128],
                               c2re_v[:, dg * 512:(dg + 1) * 512],
                               start=True, stop=False), "t")
                C.inc(t.matmul(pA[:, dg * 512:(dg + 1) * 512],
                               xs_im[:, i * 128:(i + 1) * 128],
                               c2im_v[:, dg * 512:(dg + 1) * 512],
                               start=False, stop=True), "t",
                      mark=(f"ymm_{i}" if dg == 3 else None))
            if i >= 5:
                xntr_block(i - 5)
        for k in range(NT - 5, NT):
            xntr_block(k)

        PS2 = [pA[:, 0:512], pA[:, 512:1024], pA[:, 1024:1536],
               pA[:, 1536:2048], pB[:, 0:512], pB[:, 512:1024], pC, pD]
        for X in range(2):
            for hb in range(HB):
                par = hb % 2
                C.wait(t, f"wconv_{X}_{hb}")
                if X == 0 and hb == 0:
                    C.wait(t, "yev_15")
                if X == 0 and hb == 1:
                    C.wait(t, "xnevv_15")
                    C.wait(t, "xnevs_15")
                if X == 1 and hb == 0:
                    C.wait(t, "dxnr")
                    C.wait(t, "gs_0_42")
                    C.wait(t, "uev_0_42")
                    C.wait(t, "ys_0_3_3")
                if X == 1 and hb == 1:
                    C.wait(t, "gs_0_43")
                    C.wait(t, "uev_0_43")
                    C.wait(t, "ys_0_3_7")
                if hb >= 2:
                    C.wait(t, f"gs_{X}_{hb-2}")
                    C.wait(t, f"uev_{X}_{hb-2}")
                if par == 0:
                    gps = [pA[:, 0:512], pA[:, 512:1024]]
                    ups = [pA[:, 1024:1536], pA[:, 1536:2048]]
                else:
                    gps = [pB[:, 0:512], pB[:, 512:1024]]
                    ups = [pC[:, :], pD[:, :]]
                for tg in range(2):
                    for kt in range(16):
                        C.inc(t.matmul(
                            gps[tg],
                            wgbuf[par][:, kt * 128:(kt + 1) * 128],
                            xnt[:, kt * TH + tg * 512:
                                kt * TH + (tg + 1) * 512],
                            start=(kt == 0), stop=(kt == 15)), "t",
                            mark=(f"gmm_{X}_{hb}"
                                  if tg == 1 and kt == 15 else None))
                for tg in range(2):
                    for kt in range(16):
                        C.inc(t.matmul(
                            ups[tg],
                            wubuf[par][:, kt * 128:(kt + 1) * 128],
                            xnt[:, kt * TH + tg * 512:
                                kt * TH + (tg + 1) * 512],
                            start=(kt == 0), stop=(kt == 15)), "t",
                            mark=(f"s1mm_{X}_{hb}"
                                  if tg == 1 and kt == 15 else None))
            for dg in range(4):
                for hb in range(HB):
                    if hb == 0:
                        C.wait(t, f"wdlo_{X}_{dg}")
                    if hb == 22:
                        C.wait(t, f"wdhi_{X}_{dg}")
                    if dg == 0:
                        C.wait(t, f"h_{X}_{hb}")
                    wsrc = wd_lo if hb < 22 else wd_hi
                    wof = hb if hb < 22 else hb - 22
                    for ti in range(8):
                        if hb == 0:
                            if dg == 0:
                                C.wait(t, f"gs_{X}_{HB-1}")
                                C.wait(t, f"uev_{X}_{HB-1}")
                            else:
                                C.wait(t, f"ys_{X}_{dg-1}_{ti}")
                        C.inc(t.matmul(
                            PS2[ti],
                            Hbuf[:, hb * TH + ti * 128:
                                 hb * TH + (ti + 1) * 128],
                            wsrc[:, wof * 512:(wof + 1) * 512],
                            start=(hb == 0), stop=(hb == HB - 1)), "t",
                            mark=(f"s2f_{X}_{dg}_{ti}" if hb == HB - 1
                                  else (f"s2lo_{X}_{dg}"
                                        if hb == 21 and ti == 7 else None)))

    def emit_vector(v):
        C.wait(v, "d_params")
        C.wait(v, "g_setup")
        C.inc(v.tensor_scalar(c2re_v[:, :], p_cret, 2.0, None, ALU.mult),
              "v")
        C.inc(v.tensor_scalar(c2im_v[:, :], p_cimt, -2.0, None, ALU.mult),
              "v", mark="v_c2")
        C.wait(v, "s_p1")
        C.inc(v.tensor_scalar(c_t1, p_lunc, 0.0, None, ALU.max), "v")
        C.selfwait(v, "v")
        C.inc(v.tensor_tensor(c_t1, c_t1, c_t3, ALU.add), "v")
        C.selfwait(v, "v")
        C.inc(v.tensor_scalar(c_lamre, c_t1, 0.0101, -1.0, ALU.add,
                              ALU.mult), "v")
        C.selfwait(v, "v")
        C.inc(v.tensor_tensor(c_lnr, c_lamre, c_delta, ALU.mult), "v",
              mark="v_lnr")
        C.selfwait(v, "v")
        C.inc(v.tensor_scalar(c_Tlnr, c_lnr, float(T), None, ALU.mult), "v")
        C.inc(v.tensor_tensor(c_theta, p_lim, c_delta, ALU.mult), "v")
        C.selfwait(v, "v")
        C.inc(v.tensor_scalar(c_t2, c_theta, 1.0 / TWO_PI, None, ALU.mult),
              "v")
        C.selfwait(v, "v")
        C.inc(v.tensor_copy(c_i32, c_t2), "v")
        C.selfwait(v, "v")
        C.inc(v.tensor_copy(c_t2, c_i32), "v")
        C.selfwait(v, "v")
        C.inc(v.scalar_tensor_tensor(c_thred, c_t2, -TWO_PI, c_theta,
                                     ALU.mult, ALU.add), "v")
        C.selfwait(v, "v")
        C.inc(v.tensor_scalar(c_t2, c_thred, PI, None, ALU.is_gt), "v")
        C.selfwait(v, "v")
        C.inc(v.scalar_tensor_tensor(c_thred, c_t2, -TWO_PI, c_thred,
                                     ALU.mult, ALU.add), "v")
        C.selfwait(v, "v")
        C.inc(v.tensor_scalar(c_t2, c_thred, -PI, None, ALU.is_lt), "v")
        C.selfwait(v, "v")
        C.inc(v.scalar_tensor_tensor(c_thred, c_t2, TWO_PI, c_thred,
                                     ALU.mult, ALU.add), "v", mark="v_thred")
        C.wait(v, "s_sincos")
        C.inc(v.tensor_tensor(c_t2, c_sh, c_sh, ALU.mult), "v")
        C.selfwait(v, "v")
        C.inc(v.tensor_scalar(c_costh, c_t2, -2.0, 1.0, ALU.mult, ALU.add),
              "v")
        C.wait(v, "s_r")
        C.selfwait(v, "v")
        C.inc(v.tensor_tensor(c_are, c_r, c_costh, ALU.mult), "v")
        C.inc(v.tensor_tensor(c_aim, c_r, c_sinth, ALU.mult), "v")
        C.selfwait(v, "v")
        C.inc(v.tensor_scalar(c_t1, c_are, -1.0, None, ALU.add), "v")
        C.inc(v.tensor_tensor(c_t2, c_lamre, c_lamre, ALU.mult), "v")
        C.inc(v.tensor_tensor(c_t3, p_lim, p_lim, ALU.mult), "v")
        C.selfwait(v, "v")
        C.inc(v.tensor_tensor(c_t2, c_t2, c_t3, ALU.add), "v")
        C.selfwait(v, "v")
        C.inc(v.reciprocal(c_t2, c_t2), "v")
        C.selfwait(v, "v")
        C.inc(v.tensor_tensor(c_t3, c_t1, c_lamre, ALU.mult), "v")
        C.inc(v.tensor_tensor(c_t4, c_aim, p_lim, ALU.mult), "v")
        C.selfwait(v, "v")
        C.inc(v.tensor_tensor(c_t3, c_t3, c_t4, ALU.add), "v")
        C.selfwait(v, "v")
        C.inc(v.tensor_tensor(c_dre, c_t3, c_t2, ALU.mult), "v")
        C.selfwait(v, "v")
        C.inc(v.tensor_tensor(c_t3, c_aim, c_lamre, ALU.mult), "v")
        C.inc(v.tensor_tensor(c_t4, c_t1, p_lim, ALU.mult), "v")
        C.selfwait(v, "v")
        C.inc(v.tensor_tensor(c_t3, c_t3, c_t4, ALU.subtract), "v")
        C.selfwait(v, "v")
        C.inc(v.tensor_tensor(c_dim, c_t3, c_t2, ALU.mult), "v")
        C.selfwait(v, "v")
        C.inc(v.tensor_scalar(bbar_sA[:, :], p_bre, c_dre, None, ALU.mult),
              "v")
        C.inc(v.tensor_scalar(bbar_sB[:, :], p_bim, c_dim, None, ALU.mult),
              "v")
        C.selfwait(v, "v")
        C.inc(v.tensor_tensor(bbar_re[:, :], bbar_sA[:, :], bbar_sB[:, :],
                              ALU.subtract), "v")
        C.selfwait(v, "v")
        C.inc(v.tensor_scalar(bbar_sA[:, :], p_bim, c_dre, None, ALU.mult),
              "v")
        C.inc(v.tensor_scalar(bbar_sB[:, :], p_bre, c_dim, None, ALU.mult),
              "v")
        C.selfwait(v, "v")
        C.inc(v.tensor_tensor(bbar_im[:, :], bbar_sA[:, :], bbar_sB[:, :],
                              ALU.add), "v", mark="v_bbar")
        C.inc(v.tensor_scalar(ph_p[:, :], it_fwd[:, :], c_theta, None,
                              ALU.mult), "v")
        C.selfwait(v, "v")
        C.inc(v.tensor_scalar(ph_q[:, :], ph_p[:, :], 1.0 / TWO_PI, None,
                              ALU.mult), "v")
        C.selfwait(v, "v")
        C.inc(v.tensor_copy(ph_qi, ph_q[:, :]), "v")
        C.selfwait(v, "v")
        C.inc(v.tensor_copy(ph_r[:, :], ph_qi), "v")
        C.selfwait(v, "v")
        C.inc(v.scalar_tensor_tensor(ph_p[:, :], ph_r[:, :], -TWO_PI,
                                     ph_p[:, :], ALU.mult, ALU.add), "v")
        C.selfwait(v, "v")
        C.inc(v.tensor_scalar(ph_r[:, :], ph_p[:, :], PI, None, ALU.is_gt),
              "v")
        C.selfwait(v, "v")
        C.inc(v.scalar_tensor_tensor(ph_p[:, :], ph_r[:, :], -TWO_PI,
                                     ph_p[:, :], ALU.mult, ALU.add), "v")
        C.selfwait(v, "v")
        C.inc(v.tensor_scalar(ph_r[:, :], ph_p[:, :], -PI, None, ALU.is_lt),
              "v")
        C.selfwait(v, "v")
        C.inc(v.scalar_tensor_tensor(ph_p[:, :], ph_r[:, :], TWO_PI,
                                     ph_p[:, :], ALU.mult, ALU.add), "v",
              mark="v_pred")
        C.wait(v, "s_abs")
        C.selfwait(v, "v")
        C.inc(v.tensor_scalar(ph_r[:, :], ph_q[:, :], -1.0, PI / 2.0,
                              ALU.mult, ALU.add), "v", mark="v_carg")
        C.wait(v, "t_btr")
        C.inc(v.tensor_copy(bret_bf[:, :], pBbf[:, 0:512]), "v")
        C.inc(v.tensor_copy(bimt_bf[:, :], pBbf[:, 512:1024]), "v",
              mark="v_btrev")
        C.wait(v, "t_drep")
        C.inc(v.tensor_copy(d_rep_bf[:, :], pA[:, :]), "v", mark="v_drep")
        C.wait(v, "t_sdrep")
        C.inc(v.tensor_copy(sdrep[:, :], pA[:, :]), "v", mark="v_sdrep")

        NTOT = 2 * NT
        for sl in range(NTOT + 3):
            j = sl
            if j < NTOT:
                C.wait(v, f"du_{j}")
                C.selfwait(v, "v")
                for gq in range(4):
                    C.inc(v.bn_stats(statbuf[:, 6 * gq:6 * (gq + 1)],
                                     ubuf[j % 3][:, 512 * gq:512 * (gq + 1)]),
                          "v")
                C.selfwait(v, "v")
                C.inc(v.bn_aggr(mv_p[j % 2], statbuf[:, :]), "v",
                      mark=f"aggr_{j}")
            k = sl - 1
            if 0 <= k < NTOT:
                C.wait(v, f"sqrt_{k}")
                C.selfwait(v, "v")
                C.inc(v.reciprocal(rstd_p[k % 2], sd_p[k % 2]), "v")
                C.inc(v.tensor_scalar(c_negm, mv_p[k % 2][:, 0:1], -1.0,
                                      None, ALU.mult), "v")
                C.selfwait(v, "v")
                C.inc(v.tensor_tensor(nmr_p[k % 2], c_negm, rstd_p[k % 2],
                                      ALU.mult), "v", mark=f"lnc_{k}")
            k = sl - 2
            if 0 <= k < NTOT:
                C.wait(v, f"tr_{k}")
                for gq in range(2):
                    C.inc(v.tensor_copy(
                        t1nT[k % 2][:, 512 * gq:512 * (gq + 1)],
                        pBbf[:, 512 * gq:512 * (gq + 1)]), "v",
                        mark=(f"t1nTv_{k}" if gq == 1 else None))
            k = sl - 3
            if 0 <= k < NTOT:
                C.wait(v, f"bu_{k}")
                pb = pC if k % 2 == 0 else pD
                bre_dst, bim_dst = bu_dsts(k)
                C.inc(v.tensor_copy(bre_dst, pb[0:NCH, 0:128]), "v")
                C.inc(v.tensor_copy(bim_dst, pb[0:NCH, 128:256]), "v",
                      mark=f"buev_{k}")

        C.inc(v.tensor_scalar(tC[:, :], it_fwd[:, :], c_lnr, None, ALU.mult),
              "v", mark="v_rpe")
        C.wait(v, "s_rpow")
        C.inc(v.tensor_tensor(tA[:, :], cosT[:, :], rp_v[:, :], ALU.mult),
              "v")
        C.inc(v.tensor_tensor(tB[:, :], sinT[:, :], rp_v[:, :], ALU.mult),
              "v")
        C.selfwait(v, "v")
        C.inc(v.tensor_tensor(tC[:, :], tA[:, :], bu_re_p[:, :], ALU.mult),
              "v")
        C.selfwait(v, "v")
        C.inc(v.reduce_sum(c_t1, tC[:, :], axis=AX.X), "v")
        C.selfwait(v, "v")
        C.inc(v.tensor_tensor(tC[:, :], tB[:, :], bu_im_p[:, :], ALU.mult),
              "v")
        C.selfwait(v, "v")
        C.inc(v.reduce_sum(c_t2, tC[:, :], axis=AX.X), "v")
        C.selfwait(v, "v")
        C.inc(v.tensor_tensor(tC[:, :], tA[:, :], bu_im_p[:, :], ALU.mult),
              "v")
        C.selfwait(v, "v")
        C.inc(v.reduce_sum(c_t3, tC[:, :], axis=AX.X), "v")
        C.selfwait(v, "v")
        C.inc(v.tensor_tensor(tC[:, :], tB[:, :], bu_re_p[:, :], ALU.mult),
              "v")
        C.selfwait(v, "v")
        C.inc(v.reduce_sum(c_t4, tC[:, :], axis=AX.X), "v")
        C.selfwait(v, "v")
        C.inc(v.tensor_tensor(c_sr, c_t1, c_t2, ALU.add), "v")
        C.inc(v.tensor_tensor(c_si, c_t3, c_t4, ALU.subtract), "v")
        C.selfwait(v, "v")
        C.inc(v.tensor_tensor(c_t1, cosT[:, T - 1:T], c_sr, ALU.mult), "v")
        C.inc(v.tensor_tensor(c_t2, sinT[:, T - 1:T], c_si, ALU.mult), "v")
        C.inc(v.tensor_tensor(c_t3, sinT[:, T - 1:T], c_sr, ALU.mult), "v")
        C.inc(v.tensor_tensor(c_t4, cosT[:, T - 1:T], c_si, ALU.mult), "v")
        C.selfwait(v, "v")
        C.inc(v.tensor_tensor(c_cre, c_t1, c_t2, ALU.subtract), "v")
        C.inc(v.tensor_tensor(c_cim, c_t3, c_t4, ALU.add), "v")
        C.selfwait(v, "v")
        C.inc(v.tensor_tensor(tA[:, :], cosT[:, :], bu_re_m[:, :], ALU.mult),
              "v")
        C.inc(v.tensor_tensor(tB[:, :], sinT[:, :], bu_im_m[:, :], ALU.mult),
              "v")
        C.selfwait(v, "v")
        C.inc(v.tensor_tensor(bt_re[:, :], tA[:, :], tB[:, :], ALU.add), "v")
        C.selfwait(v, "v")
        C.inc(v.tensor_tensor(tA[:, :], cosT[:, :], bu_im_m[:, :], ALU.mult),
              "v")
        C.inc(v.tensor_tensor(tB[:, :], sinT[:, :], bu_re_m[:, :], ALU.mult),
              "v")
        C.selfwait(v, "v")
        C.inc(v.tensor_tensor(bt_im[:, :], tA[:, :], tB[:, :], ALU.subtract),
              "v")
        C.inc(v.memset(tC[:, :], 1.0), "v")
        C.selfwait(v, "v")
        C.inc(v.tensor_scalar(tC[:, :], tC[:, :], c_r, None, ALU.mult), "v")
        C.selfwait(v, "v")
        C.inc(v.tensor_tensor_scan(tA[:, :], tC[:, :], bt_re[:, :], c_cre,
                                   ALU.mult, ALU.add), "v")
        C.inc(v.tensor_tensor_scan(tB[:, :], tC[:, :], bt_im[:, :], c_cim,
                                   ALU.mult, ALU.add), "v")
        C.selfwait(v, "v")
        C.inc(v.tensor_tensor(tC[:, :], cosT[:, :], tA[:, :], ALU.mult), "v")
        C.inc(v.tensor_tensor(bt_re[:, :], sinT[:, :], tB[:, :], ALU.mult),
              "v")
        C.selfwait(v, "v")
        C.inc(v.tensor_tensor(xs_re[:, :], tC[:, :], bt_re[:, :],
                              ALU.subtract), "v")
        C.selfwait(v, "v")
        C.inc(v.tensor_tensor(tC[:, :], cosT[:, :], tB[:, :], ALU.mult), "v")
        C.inc(v.tensor_tensor(bt_im[:, :], sinT[:, :], tA[:, :], ALU.mult),
              "v")
        C.selfwait(v, "v")
        C.inc(v.tensor_tensor(xs_im[:, :], tC[:, :], bt_im[:, :], ALU.add),
              "v", mark="v_scan_done")

        for sl in range(NT + 5):
            i = sl
            if i < NT:
                C.selfwait(v, "v")
                C.inc(v.tensor_tensor(tmpD_p[i % 2][:, :], d_rep_bf[:, :],
                                      t1n_full[:, i * 2048:(i + 1) * 2048],
                                      ALU.mult), "v", mark=f"td_{i}")
            k = sl - 2
            if 0 <= k < NT:
                C.wait(v, f"ysqrt_{k}")
                C.selfwait(v, "v")
                C.inc(v.reciprocal(rsy_p[k % 2], sdy_p[k % 2]), "v")
                C.inc(v.tensor_scalar(c_negm, mvy_p[k % 2][:, 0:1], -1.0,
                                      None, ALU.mult), "v")
                C.selfwait(v, "v")
                C.inc(v.tensor_tensor(nmy_p[k % 2], c_negm, rsy_p[k % 2],
                                      ALU.mult), "v", mark=f"ylnc_{k}")
            k = sl - 4
            if 0 <= k < NT:
                C.wait(v, f"x1sqrt_{k}")
                C.selfwait(v, "v")
                C.inc(v.reciprocal(rsx_p[k % 2], sdx_p[k % 2]), "v")
                C.inc(v.tensor_scalar(c_negm, mvx_p[k % 2][:, 0:1], -1.0,
                                      None, ALU.mult), "v")
                C.selfwait(v, "v")
                C.inc(v.tensor_tensor(nmx_p[k % 2], c_negm, rsx_p[k % 2],
                                      ALU.mult), "v", mark=f"xlnc_{k}")
            k = sl - 3
            if 0 <= k < NT:
                C.wait(v, f"ln2_{k}")
                C.wait(v, f"dx0_{k}")
                if k >= 2:
                    C.wait(v, f"dx1w_{k-2}")
                    C.wait(v, f"xnapply_{k-2}")
                C.selfwait(v, "v")
                C.inc(v.tensor_tensor(x1buf[k % 2][:, :],
                                      ybuf_p[k % 2][:, :],
                                      x0buf_p[k % 2][:, :],
                                      ALU.add), "v", mark=f"x1_{k}")
                C.selfwait(v, "v")
                for gq in range(4):
                    C.inc(v.bn_stats(
                        statbuf[:, 6 * gq:6 * (gq + 1)],
                        x1buf[k % 2][:, 512 * gq:512 * (gq + 1)]), "v")
                C.selfwait(v, "v")
                C.inc(v.bn_aggr(mvx_p[k % 2], statbuf[:, :]), "v",
                      mark=f"x1aggr_{k}")
            k = sl - 1
            if 0 <= k < NT:
                C.wait(v, f"ymm_{k}")
                if k >= 2:
                    C.wait(v, f"ln2_{k-2}")
                for gq in range(4):
                    C.inc(v.tensor_tensor(
                        ybuf_p[k % 2][:, 512 * gq:512 * (gq + 1)],
                        pA[:, 512 * gq:512 * (gq + 1)],
                        tmpD_p[k % 2][:, 512 * gq:512 * (gq + 1)],
                        ALU.add), "v",
                        mark=(f"yev_{k}" if gq == 3 else None))
                C.selfwait(v, "v")
                for gq in range(4):
                    C.inc(v.bn_stats(
                        statbuf[:, 6 * gq:6 * (gq + 1)],
                        ybuf_p[k % 2][:, 512 * gq:512 * (gq + 1)]), "v")
                C.selfwait(v, "v")
                C.inc(v.bn_aggr(mvy_p[k % 2], statbuf[:, :]), "v",
                      mark=f"yaggr_{k}")
            k = sl - 5
            if 0 <= k < NT:
                C.wait(v, f"xntr_{k}")
                if k >= 9:
                    C.wait(v, f"dxnw_{k-9}")
                for gq in range(2):
                    C.inc(v.tensor_copy(xnev_dst(k, gq), xnev_src(k, gq)),
                          "v", mark=(f"xnevv_{k}" if gq == 1 else None))

        def tail_tile(X, ti):
            C.wait(v, f"dyr_{X}_{ti}")
            C.selfwait(v, "v")
            C.inc(v.tensor_scalar(yrbuf[:, :], yrbuf[:, :], l4nm[ti],
                                  l4rs[ti], ALU.add, ALU.mult), "v",
                  mark=f"ln4apply_{X}_{ti}")
            C.wait(v, f"dx1r_{X}_{ti}")
            gk = X * 8 + ti
            if gk >= 2:
                pgk = gk - 2
                C.wait(v, f"dout_{pgk // 8}_{pgk % 8}")
            C.selfwait(v, "v")
            C.inc(v.tensor_tensor(outbuf[ti % 2][:, :], yrbuf[:, :],
                                  x1rbuf[:, :], ALU.add), "v",
                  mark=f"out_{X}_{ti}")

        for X in range(2):
            for hb in range(HB):
                if X == 0 and hb == 0:
                    C.wait(v, "dxnw_6")
                    C.wait(v, "dxnw_7")
                    C.wait(v, f"dx1w_{NT-2}")
                    C.wait(v, f"dx1w_{NT-1}")
                if X == 1 and hb % 5 == 4 and hb // 5 < 8:
                    tail_tile(0, hb // 5)
                C.wait(v, f"s1mm_{X}_{hb}")
                C.selfwait(v, "v")
                if hb % 2 == 0:
                    ups = [pA[:, 1024:1536], pA[:, 1536:2048]]
                else:
                    ups = [pC[:, :], pD[:, :]]
                for tg in range(2):
                    C.inc(v.tensor_scalar(
                        us_s1[:, tg * 512:(tg + 1) * 512], ups[tg],
                        subuf[:, hb:hb + 1], None, ALU.mult), "v",
                        mark=(f"uev_{X}_{hb}" if tg == 1 else None))
                C.wait(v, f"gs_{X}_{hb}")
                C.selfwait(v, "v")
                for tg in range(2):
                    C.inc(v.tensor_tensor(
                        Hcols(hb, tg), h2buf[:, tg * 512:(tg + 1) * 512],
                        us_s1[:, tg * 512:(tg + 1) * 512], ALU.mult), "v",
                        mark=(f"h_{X}_{hb}" if tg == 1 else None))
            PS2v = [pA[:, 0:512], pA[:, 512:1024], pA[:, 1024:1536],
                    pA[:, 1536:2048], pB[:, 0:512], pB[:, 512:1024],
                    pC[:, :], pD[:, :]]
            for dg in range(4):
                for ti in range(8):
                    par = (dg * 8 + ti) % 2
                    C.wait(v, f"s2f_{X}_{dg}_{ti}")
                    k = dg * 8 + ti
                    if k >= 2:
                        pk = k - 2
                        C.wait(v, f"dyw_{X}_{pk // 8}_{pk % 8}")
                    elif X == 1:
                        C.wait(v, f"dyw_0_3_{6 + ti}")
                    C.inc(v.tensor_tensor(ystage[par][:, :], PS2v[ti],
                                          sdrep[:, dg * 512:(dg + 1) * 512],
                                          ALU.mult), "v")
                    C.selfwait(v, "v")
                    C.inc(v.bn_stats(
                        ln4stats[:, ti * 24 + dg * 6:ti * 24 + dg * 6 + 6],
                        ystage[par][:, :]), "v", mark=f"ys_{X}_{dg}_{ti}")
            C.selfwait(v, "v")
            for ti in range(8):
                C.inc(v.bn_aggr(l4mv[ti], ln4stats[:, ti * 24:(ti + 1) * 24]),
                      "v", mark=f"l4aggr_{X}_{ti}")
            for ti in range(8):
                C.wait(v, f"l4sqrt_{X}_{ti}")
                C.selfwait(v, "v")
                C.inc(v.reciprocal(l4rs[ti], l4nm[ti]), "v")
                C.inc(v.tensor_scalar(c_negm, l4mv[ti][:, 0:1], -1.0, None,
                                      ALU.mult), "v")
                C.selfwait(v, "v")
                C.inc(v.tensor_tensor(l4nm[ti], c_negm, l4rs[ti], ALU.mult),
                      "v", mark=f"l4c_{X}_{ti}")
            if X == 1:
                for ti in range(8):
                    tail_tile(1, ti)

    def run_pass(pn):
        C.begin(pn, sems if pn == 2 else None)
        if pn == 1:
            emit_sync(_FakeEng())
            emit_gpsimd(_FakeEng())
            emit_scalar(_FakeEng())
            emit_vector(_FakeEng())
            emit_tensor(_FakeEng())
            return
        with nc.Block() as block:
            @block.sync
            def _(q):
                emit_sync(q)

            @block.gpsimd
            def _(g):
                emit_gpsimd(g)

            @block.scalar
            def _(s):
                emit_scalar(s)

            @block.vector
            def _(v):
                emit_vector(v)

            @block.tensor
            def _(t):
                emit_tensor(t)

    run_pass(1)
    cnt1 = dict(C.cnt)
    run_pass(2)
    assert C.cnt == cnt1, (C.cnt, cnt1)
    st.close()
    return nc


def _u8view(wq_int32):
    a = np.ascontiguousarray(np.asarray(wq_int32))
    assert a.dtype == np.int32
    return np.ascontiguousarray(a.view(np.uint8)[..., ::4])


def _stage_w_hd(wq):
    a = _u8view(wq).reshape(HB, 128, 16, 128)
    a = np.ascontiguousarray(a.transpose(0, 3, 2, 1))
    return a.reshape(HB, 128, 2048)


def _stage_w_dh(wq):
    a = np.ascontiguousarray(_u8view(wq).T)
    return a.reshape(HB, 128, 2048)


def kernel(L0_attn_u, L0_attn_Lambda_unconstrained_re, L0_attn_Lambda_im,
           L0_attn_B_re, L0_attn_B_im, L0_attn_C_re, L0_attn_C_im,
           L0_attn_D, L0_attn_log_Delta, L0_ffn_wq_gate, L0_ffn_s_gate,
           L0_ffn_wq_up, L0_ffn_s_up, L0_ffn_wq_down, L0_ffn_s_down):
    u = np.ascontiguousarray(np.asarray(L0_attn_u, dtype=np.float32))
    wg = _stage_w_hd(L0_ffn_wq_gate)
    wu = _stage_w_hd(L0_ffn_wq_up)
    wd = _stage_w_dh(L0_ffn_wq_down)
    sg = np.ascontiguousarray(
        np.asarray(L0_ffn_s_gate, np.float32).reshape(HB, 128).T)
    su = np.ascontiguousarray(
        np.asarray(L0_ffn_s_up, np.float32).reshape(HB, 128).T)
    sd = np.ascontiguousarray(np.asarray(L0_ffn_s_down,
                                         np.float32).reshape(1, D))
    bre = np.ascontiguousarray(np.asarray(L0_attn_B_re, np.float32))
    bim = np.ascontiguousarray(np.asarray(L0_attn_B_im, np.float32))
    cret = np.ascontiguousarray(np.asarray(L0_attn_C_re, np.float32).T)
    cimt = np.ascontiguousarray(np.asarray(L0_attn_C_im, np.float32).T)
    drow = np.ascontiguousarray(np.asarray(L0_attn_D,
                                           np.float32).reshape(1, D))
    lunc = np.ascontiguousarray(
        np.asarray(L0_attn_Lambda_unconstrained_re,
                   np.float32).reshape(NCH, 1))
    lim = np.ascontiguousarray(np.asarray(L0_attn_Lambda_im,
                                          np.float32).reshape(NCH, 1))
    logd = np.ascontiguousarray(
        np.asarray(L0_attn_log_Delta, np.float32)[:NCH].reshape(NCH, 1))
    zeros_u = np.zeros((T, D), np.float32)

    nc = build_program()
    in_maps = []
    for c in range(8):
        b, h = c // 2, c % 2
        mine = np.ascontiguousarray(u[b, h * T:(h + 1) * T, :])
        prev = np.ascontiguousarray(u[b, 0:T, :]) if h == 1 else zeros_u
        in_maps.append(dict(
            u=mine, uprev=prev, wg=wg, wu=wu, wd=wd, sg=sg, su=su, sd=sd,
            bre=bre, bim=bim, cret=cret, cimt=cimt, drow=drow, lunc=lunc,
            lim=lim, logd=logd))
    res = run_bass_kernel_spmd(nc, in_maps, core_ids=list(range(8)))
    out = np.empty((B, L, D), np.float32)
    for c in range(8):
        b, h = c // 2, c % 2
        out[b, h * T:(h + 1) * T, :] = np.asarray(res.results[c]["out"])
    return out


# version 10
# speedup vs baseline: 1.2763x; 1.0102x over previous
import contextlib
import numpy as np
import ml_dtypes
import concourse.bass as bass
import concourse.mybir as mybir
from concourse.bass_utils import run_bass_kernel_spmd

F32 = mybir.dt.float32
BF16 = mybir.dt.bfloat16
U8 = mybir.dt.uint8
I32 = mybir.dt.int32
AF = mybir.ActivationFunctionType
ALU = mybir.AluOpType
AX = mybir.AxisListType

B, L, D = 4, 4096, 2048
HID = 5632
NCH = 32
T = 2048
NT = 16
HB = 44
TH = 1024
PI = float(np.pi)
TWO_PI = float(2.0 * np.pi)
EPS_LN = 1e-5
KB = 1024


class Ctx:

    KEYS = ("d_par", "d_u", "d_u2", "d_u3", "d_x0", "d_x02", "d_x1w",
            "d_x1w2", "d_x1r",
            "d_xnw", "d_xnw2", "d_xnr", "d_w", "d_w2", "d_wda", "d_wdl", "d_wdh", "d_yw",
            "d_yw2", "d_yr", "d_yr2", "d_x1r2", "d_out", "d_out2",
            "v", "s", "t", "g")

    def __init__(self):
        self.marks = {}
        self.pass_num = 0
        self.cnt = {}
        self.sems = None

    def begin(self, pass_num, sems=None):
        self.pass_num = pass_num
        self.cnt = {k: 0 for k in self.KEYS}
        self.sems = sems

    def inc(self, inst, key, by=1, mark=None):
        if self.pass_num == 2:
            inst.then_inc(self.sems[key], by)
        self.cnt[key] += by
        if mark is not None:
            if self.pass_num == 1:
                assert mark not in self.marks, f"dup mark {mark}"
                self.marks[mark] = (key, self.cnt[key])
            else:
                assert self.marks[mark] == (key, self.cnt[key]), \
                    f"mark drift {mark}"
        return self.cnt[key]

    def wait(self, eng, mark, delta=0):
        if self.pass_num == 2:
            key, val = self.marks[mark]
            if val + delta > 0:
                eng.wait_ge(self.sems[key], val + delta)

    def wait_cnt(self, eng, key, val):
        if self.pass_num == 2 and val > 0:
            eng.wait_ge(self.sems[key], val)

    def selfwait(self, eng, key):
        if self.pass_num == 2 and self.cnt[key] > 0:
            eng.wait_ge(self.sems[key], self.cnt[key])


class _FakeInst:
    def then_inc(self, *a, **k):
        return self


class _FakeEng:
    def __getattr__(self, name):
        return lambda *a, **k: _FakeInst()


def build_program():
    nc = bass.Bass("TRN2", target_bir_lowering=False, debug=False)

    d_in = {}

    def din(name, shape, dt):
        d_in[name] = nc.dram_tensor(name, shape, dt, kind="ExternalInput").ap()

    din("u", [T, D], F32)
    din("uprev", [T, D], F32)
    din("wg", [HB, 128, D], U8)
    din("wu", [HB, 128, D], U8)
    din("wd", [HB, 128, D], U8)
    din("sg", [128, HB], F32)
    din("su", [128, HB], F32)
    din("sd", [1, D], F32)
    din("bre", [NCH, D], F32)
    din("bim", [NCH, D], F32)
    din("cret", [NCH, D], F32)
    din("cimt", [NCH, D], F32)
    din("drow", [1, D], F32)
    din("lunc", [NCH, 1], F32)
    din("lim", [NCH, 1], F32)
    din("logd", [NCH, 1], F32)
    out_hbm = nc.dram_tensor("out", [T, D], F32, kind="ExternalOutput").ap()
    x1_spill = nc.dram_tensor("x1sp", [T, D], F32, kind="Internal").ap()
    xnt_spill = nc.dram_tensor("xntsp", [128, 16 * TH], BF16,
                               kind="Internal").ap()
    y_spill = nc.dram_tensor("ysp", [TH, D], F32, kind="Internal").ap()

    st = contextlib.ExitStack()
    ARENA = 207 * KB
    arena = st.enter_context(nc.sbuf_tensor("arena", [128, ARENA], U8)).ap()

    def vf32(off_kb, n):
        off = int(off_kb * KB)
        assert off % 4 == 0 and off + 4 * n <= ARENA, (off_kb, n)
        return arena[:, off:off + 4 * n].bitcast(F32)

    def vbf(off_kb, n):
        off = int(off_kb * KB)
        assert off % 2 == 0 and off + 2 * n <= ARENA, (off_kb, n)
        return arena[:, off:off + 2 * n].bitcast(BF16)

    def vi32(off_kb, n):
        off = int(off_kb * KB)
        return arena[:, off:off + 4 * n].bitcast(I32)

    t1nT_full = vbf(0, 32768)
    wd_lo = vbf(0, 11264)
    wd_hi = vbf(22, 11264)
    wgbuf = [vbf(44, 2048), vbf(48, 2048)]
    wubuf = [vbf(52, 2048), vbf(56, 2048)]
    us_s1 = vbf(60, 1024)
    yrbuf_p = [vf32(0, 2048), vf32(8, 2048)]
    x1rbuf_p = [vf32(16, 2048), vf32(24, 2048)]
    outbuf = [vf32(32, 2048), vf32(40, 1024), ]

    xnt = vbf(64, 16384)
    drow_f = vf32(64, 2048)[0:1, :]
    sdrow = vf32(72, 2048)[0:1, :]
    drow_bf = vbf(80, 2048)[0:1, :]

    ubuf = [vf32(96, 2048), vf32(104, 2048), vf32(84, 2048)]
    t1ntmp = [vbf(112, 2048), vbf(116, 2048)]
    bu_re_m = vf32(120, 2048)[0:NCH, :]
    bu_im_m = vf32(128, 2048)[0:NCH, :]
    bu_re_p = vf32(136, 2048)[0:NCH, :]
    bu_im_p = vf32(144, 2048)[0:NCH, :]
    cosT = vf32(152, 2048)[0:NCH, :]
    sinT = vf32(160, 2048)[0:NCH, :]
    it_fwd = vf32(168, 2048)[0:NCH, :]
    rp_v = vf32(168, 2048)[0:NCH, :]
    c2re_v = vbf(176, 2048)[0:NCH, :]
    c2im_v = vbf(180, 2048)[0:NCH, :]
    bbar_sA = vf32(96, 2048)[0:NCH, :]
    bbar_sB = vf32(104, 2048)[0:NCH, :]
    bbar_re = vbf(112, 2048)[0:NCH, :]
    bbar_im = vbf(116, 2048)[0:NCH, :]
    ph_p = vf32(120, 2048)[0:NCH, :]
    ph_q = vf32(128, 2048)[0:NCH, :]
    ph_qi = vi32(136, 2048)[0:NCH, :]
    ph_r = vf32(144, 2048)[0:NCH, :]
    tA = vf32(96, 2048)[0:NCH, :]
    tB = vf32(104, 2048)[0:NCH, :]
    tC = vf32(112, 2048)[0:NCH, :]
    bt_re = vf32(136, 2048)[0:NCH, :]
    bt_im = vf32(144, 2048)[0:NCH, :]
    xs_re = vbf(120, 2048)[0:NCH, :]
    xs_im = vbf(124, 2048)[0:NCH, :]
    ybuf_p = [vf32(96, 2048), vf32(104, 2048)]
    x1buf = [vf32(128, 2048), vf32(136, 2048)]
    x0buf_p = [vf32(144, 2048), vf32(168, 2048)]
    xnbuf = vbf(152, 2048)
    xnt_stage = vbf(156, 2048)
    Hbuf = vbf(96, 45056)

    d_rep_bf = vbf(184, 2048)
    sdrep = vf32(188, 2048)

    id_bf = vbf(196, 128)
    onesrow_bf = vbf(196.25, 128)[0:1, :]
    onesrow_f = vf32(196.5, 128)[0:1, :]
    sgbuf = vf32(197, HB)
    subuf = vf32(197.25, HB)
    colbuf = vf32(197.5, 96)
    statbuf = vf32(197.875, 24)
    bret_bf = vbf(198.25, 512)
    bimt_bf = vbf(199.25, 512)
    ystage = [vf32(200.25, 512), vf32(202.25, 512)]
    ln4stats = vf32(204.25, 192)
    h2buf = vbf(205, 1024)

    pA = st.enter_context(nc.psum_tensor("pA", [128, 2048], F32)).ap()
    pB = st.enter_context(nc.psum_tensor("pB", [128, 1024], F32)).ap()
    pC = st.enter_context(nc.psum_tensor("pC", [128, 512], F32)).ap()
    pD = st.enter_context(nc.psum_tensor("pD", [128, 512], F32)).ap()
    pBbf = pB.bitcast(BF16)

    sems = {k: st.enter_context(nc.semaphore(k)) for k in Ctx.KEYS}
    C = Ctx()

    p_lunc = colbuf[0:NCH, 0:1]
    p_lim = colbuf[0:NCH, 1:2]
    p_logd = colbuf[0:NCH, 2:3]
    c_delta = colbuf[0:NCH, 3:4]
    c_lamre = colbuf[0:NCH, 4:5]
    c_lnr = colbuf[0:NCH, 5:6]
    c_theta = colbuf[0:NCH, 6:7]
    c_r = colbuf[0:NCH, 7:8]
    c_are = colbuf[0:NCH, 8:9]
    c_aim = colbuf[0:NCH, 9:10]
    c_t1 = colbuf[0:NCH, 10:11]
    c_t2 = colbuf[0:NCH, 11:12]
    c_t3 = colbuf[0:NCH, 12:13]
    c_t4 = colbuf[0:NCH, 13:14]
    c_dre = colbuf[0:NCH, 14:15]
    c_dim = colbuf[0:NCH, 15:16]
    c_eps = colbuf[:, 16:17]
    c_cre = colbuf[0:NCH, 17:18]
    c_cim = colbuf[0:NCH, 18:19]
    c_sr = colbuf[0:NCH, 19:20]
    c_si = colbuf[0:NCH, 20:21]
    c_negm = colbuf[:, 21:22]
    c_rstd = colbuf[:, 22:23]
    c_sd = colbuf[:, 23:24]
    c_Tlnr = colbuf[0:NCH, 24:25]
    c_i32 = colbuf.bitcast(I32)[0:NCH, 25:26]
    c_sinth = colbuf[0:NCH, 26:27]
    c_sh = colbuf[0:NCH, 27:28]
    c_costh = colbuf[0:NCH, 28:29]
    c_thred = colbuf[0:NCH, 29:30]
    mvbuf = colbuf[:, 30:32]
    mv_p = [colbuf[:, 32:34], colbuf[:, 34:36]]
    sd_p = [colbuf[:, 36:37], colbuf[:, 37:38]]
    rstd_p = [colbuf[:, 38:39], colbuf[:, 39:40]]
    nmr_p = [colbuf[:, 40:41], colbuf[:, 41:42]]
    mvy_p = [colbuf[:, 42:44], colbuf[:, 44:46]]
    sdy_p = [colbuf[:, 46:47], colbuf[:, 47:48]]
    rsy_p = [colbuf[:, 48:49], colbuf[:, 49:50]]
    nmy_p = [colbuf[:, 50:51], colbuf[:, 51:52]]
    mvx_p = [colbuf[:, 52:54], colbuf[:, 54:56]]
    sdx_p = [colbuf[:, 56:57], colbuf[:, 57:58]]
    rsx_p = [colbuf[:, 58:59], colbuf[:, 59:60]]
    nmx_p = [colbuf[:, 60:61], colbuf[:, 61:62]]
    l4mv = [colbuf[:, 62 + 2 * t_: 64 + 2 * t_] for t_ in range(8)]
    l4rs = [colbuf[:, 78 + t_: 79 + t_] for t_ in range(8)]
    l4nm = [colbuf[:, 86 + t_: 87 + t_] for t_ in range(8)]
    p_bre = bu_re_m
    p_bim = bu_im_m
    p_cret = cosT
    p_cimt = sinT

    def Hcols(hb, tg):
        return Hbuf[:, hb * TH + tg * 512: hb * TH + (tg + 1) * 512]

    def bu_dsts(k):
        i = k % NT
        bre = (bu_re_p if k < NT else bu_re_m)[:, i * 128:(i + 1) * 128]
        bim = (bu_im_p if k < NT else bu_im_m)[:, i * 128:(i + 1) * 128]
        return bre, bim

    def xnev_dst(i, gq):
        if i < 8:
            return xnt.rearrange("p (k tt) -> p k tt", k=16)[
                :, 4 * gq:4 * (gq + 1), i * 128:(i + 1) * 128]
        return xnt_stage[:, 512 * gq:512 * (gq + 1)]

    def xnev_src(i, gq):
        s = pBbf[:, 512 * gq:512 * (gq + 1)]
        if i < 8:
            return s.rearrange("p (k tt) -> p k tt", k=4)
        return s

    def emit_sync(q):
        C.inc(q.dma_start(p_lunc, d_in["lunc"]), "d_par", 16)
        C.inc(q.dma_start(p_lim, d_in["lim"]), "d_par", 16)
        C.inc(q.dma_start(p_logd, d_in["logd"]), "d_par", 16)
        C.inc(q.dma_start(p_bre, d_in["bre"]), "d_par", 16)
        C.inc(q.dma_start(p_bim, d_in["bim"]), "d_par", 16)
        C.inc(q.dma_start(p_cret, d_in["cret"]), "d_par", 16)
        C.inc(q.dma_start(p_cimt, d_in["cimt"]), "d_par", 16)
        C.inc(q.dma_start(drow_f, d_in["drow"]), "d_par", 16)
        C.inc(q.dma_start(sdrow, d_in["sd"]), "d_par", 16)
        C.inc(q.dma_start(sgbuf[:, :], d_in["sg"]), "d_par", 16)
        C.inc(q.dma_start(subuf[:, :], d_in["su"]), "d_par", 16,
              mark="d_params")

        for j in range(2 * NT):
            src = d_in["uprev"] if j < NT else d_in["u"]
            i = j % NT
            if j < 2:
                C.wait(q, "v_bbar")
            elif j >= 3:
                C.wait(q, f"uapply_{j-3}")
            C.inc(q.dma_start(ubuf[j % 3][:, :],
                              src[i * 128:(i + 1) * 128, :]),
                  ("d_u", "d_u2", "d_u3")[j % 3], 16, mark=f"du_{j}")

        for i in range(NT):
            if i < 2:
                C.wait(q, "v_scan_done")
            else:
                C.wait(q, f"x1_{i-2}")
            C.inc(q.dma_start(x0buf_p[i % 2][:, :],
                              d_in["u"][i * 128:(i + 1) * 128, :]),
                  "d_x0" if i % 2 == 0 else "d_x02", 16, mark=f"dx0_{i}")
            if i >= 1:
                C.wait(q, f"x1_{i-1}")
                C.inc(q.dma_start(x1_spill[(i - 1) * 128:i * 128, :],
                                  x1buf[(i - 1) % 2][:, :]),
                      "d_x1w" if (i - 1) % 2 == 0 else "d_x1w2", 16,
                      mark=f"dx1w_{i-1}")
        C.wait(q, f"x1_{NT-1}")
        C.inc(q.dma_start(x1_spill[(NT - 1) * 128:NT * 128, :],
                          x1buf[(NT - 1) % 2][:, :]),
              "d_x1w" if (NT - 1) % 2 == 0 else "d_x1w2", 16,
              mark=f"dx1w_{NT-1}")

        for X in range(2):
            if X == 0:
                C.wait(q, "s1mm_0_43")
                C.inc(q.dma_start(xnt[:, :], xnt_spill[:, :]), "d_xnr", 16,
                      mark="dxnr")
            for dg in range(4):
                for ti in range(8):
                    par = (dg * 8 + ti) % 2
                    C.wait(q, f"ys_{X}_{dg}_{ti}")
                    C.inc(q.dma_start(
                        y_spill[ti * 128:(ti + 1) * 128,
                                dg * 512:(dg + 1) * 512],
                        ystage[par][:, :]),
                        "d_yw" if par == 0 else "d_yw2", 16,
                        mark=f"dyw_{X}_{dg}_{ti}")
            C.wait(q, f"s2f_{X}_3_7")
            C.wait(q, f"dyw_{X}_3_6")
            C.wait(q, f"dyw_{X}_3_7")
            for ti in range(8):
                gt = X * 8 + ti
                if ti >= 2:
                    C.wait(q, f"ln4apply_{X}_{ti-2}")
                C.inc(q.dma_start(yrbuf_p[ti % 2][:, :],
                                  y_spill[ti * 128:(ti + 1) * 128, :]),
                      "d_yr" if ti % 2 == 0 else "d_yr2", 16,
                      mark=f"dyr_{X}_{ti}")
                if ti >= 2:
                    C.wait(q, f"out_{X}_{ti-2}")
                C.inc(q.dma_start(x1rbuf_p[ti % 2][:, :],
                                  x1_spill[gt * 128:(gt + 1) * 128, :]),
                      "d_x1r" if ti % 2 == 0 else "d_x1r2", 16,
                      mark=f"dx1r_{X}_{ti}")
                C.wait(q, f"out_{X}_{ti}")
                C.inc(q.dma_start(out_hbm[gt * 128:(gt + 1) * 128, :],
                                  outbuf[0][:, :]),
                      "d_out" if ti % 2 == 0 else "d_out2", 16,
                      mark=f"dout_{X}_{ti}")
        C.wait_cnt(q, "d_out", 16 * 8)
        C.wait_cnt(q, "d_out2", 16 * 8)

    def emit_gpsimd(g):
        C.inc(g.iota(it_fwd[:, :], pattern=[[1, T]], base=1,
                     channel_multiplier=0,
                     allow_small_or_imprecise_dtypes=True), "g", mark="g_itf")
        C.inc(g.memset(id_bf[:, :], 1.0), "g")
        C.selfwait(g, "g")
        C.inc(g.affine_select(id_bf[:, :], id_bf[:, :], pattern=[[-1, 128]],
                              compare_op=ALU.is_equal, fill=0.0, base=0,
                              channel_multiplier=1), "g", mark="g_id")
        C.inc(g.memset(onesrow_bf[:, :], 1.0), "g")
        C.inc(g.memset(onesrow_f[:, :], 1.0), "g")
        C.inc(g.memset(c_eps, EPS_LN), "g", mark="g_setup")
        C.wait(g, "v_drep")
        for kd in range(16):
            C.inc(g.affine_select(d_rep_bf[:, kd * 128:(kd + 1) * 128],
                                  d_rep_bf[:, kd * 128:(kd + 1) * 128],
                                  pattern=[[-1, 128]],
                                  compare_op=ALU.is_equal, fill=0.0, base=0,
                                  channel_multiplier=1), "g",
                  mark=("g_ddiag" if kd == 15 else None))

        for k in range(8):
            C.wait(g, f"xnevv_{k + 8}")
            C.wait(g, f"xnevs_{k + 8}")
            C.inc(g.dma_start(
                xnt_spill.rearrange("p (kk ti tt) -> p kk ti tt",
                                    k=16, ti=8, kk=16)[:, :, k, :],
                xnt_stage.rearrange("p (kk tt) -> p kk tt", kk=16)),
                "d_xnw" if k % 2 == 0 else "d_xnw2", 16,
                mark=f"dxnw_{k}")

        for X in range(2):
            for hb in range(HB):
                par = hb % 2
                if X == 0 and hb < 2:
                    C.wait(g, "yev_15")
                else:
                    pX, phb = (X, hb - 2) if hb >= 2 else (0, HB - 2 + hb)
                    C.wait(g, f"s1mm_{pX}_{phb}")
                wkey = "d_w" if par == 0 else "d_w2"
                C.inc(g.dma_start(wgbuf[par][:, :], d_in["wg"][hb, :, :]),
                      wkey, 16)
                C.inc(g.dma_start(wubuf[par][:, :], d_in["wu"][hb, :, :]),
                      wkey, 16, mark=f"wconv_{X}_{hb}")
            for dg in range(4):
                if dg == 0:
                    if X == 0:
                        C.wait(g, "yev_15")
                    else:
                        C.wait_cnt(g, "d_out", 16 * 4)
                        C.wait_cnt(g, "d_out2", 16 * 4)
                        C.wait(g, "s2q_0_3")
                else:
                    C.wait(g, f"s2q_{X}_{dg-1}")
                C.inc(g.dma_start(
                    wd_lo.rearrange("p (h dd) -> p h dd", h=22)[:, 0:11, :],
                    d_in["wd"][0:11, :, dg * 512:(dg + 1) * 512].rearrange(
                        "h p dd -> p h dd")), "d_wda", 16,
                    mark=f"dwdla_{X}_{dg}")
                if dg == 0:
                    if X == 1:
                        C.wait(g, "s2lo_0_3")
                else:
                    C.wait(g, f"s2lo_{X}_{dg-1}")
                C.inc(g.dma_start(
                    wd_lo.rearrange("p (h dd) -> p h dd", h=22)[:, 11:22, :],
                    d_in["wd"][11:22, :, dg * 512:(dg + 1) * 512].rearrange(
                        "h p dd -> p h dd")), "d_wdl", 16,
                    mark=f"dwdlo_{X}_{dg}")
                if dg == 0:
                    if X == 1:
                        C.wait(g, "s2f_0_3_7")
                else:
                    C.wait(g, f"s2f_{X}_{dg-1}_7")
                C.inc(g.dma_start(
                    wd_hi.rearrange("p (h dd) -> p h dd", h=22),
                    d_in["wd"][22:44, :, dg * 512:(dg + 1) * 512].rearrange(
                        "h p dd -> p h dd")), "d_wdh", 16,
                    mark=f"dwdhi_{X}_{dg}")

    def emit_scalar(s):
        C.wait(s, "d_params")
        C.inc(s.activation(c_delta, p_logd, AF.Exp), "s")
        C.inc(s.activation(c_t1, p_lunc, AF.Abs), "s")
        C.selfwait(s, "s")
        C.inc(s.activation(c_t2, c_t1, AF.Exp, scale=-1.0), "s")
        C.selfwait(s, "s")
        C.inc(s.activation(c_t3, c_t2, AF.Ln, bias=1.0), "s", mark="s_p1")
        C.wait(s, "v_lnr")
        C.inc(s.activation(c_r, c_lnr, AF.Exp), "s", mark="s_r")
        C.wait(s, "v_thred")
        C.inc(s.activation(c_sinth, c_thred, AF.Sin), "s")
        C.inc(s.activation(c_sh, c_thred, AF.Sin, scale=0.5), "s",
              mark="s_sincos")
        C.wait(s, "v_pred")
        C.inc(s.activation(sinT[:, :], ph_p[:, :], AF.Sin), "s")
        C.inc(s.activation(ph_q[:, :], ph_p[:, :], AF.Abs), "s",
              mark="s_abs")
        C.wait(s, "v_carg")
        C.inc(s.activation(cosT[:, :], ph_r[:, :], AF.Sin), "s",
              mark="s_tables")
        C.inc(s.activation(drow_bf, drow_f, AF.Copy), "s", mark="s_dbf")

        for sl in range(2 * NT + 2):
            j = sl
            if j < 2 * NT:
                C.wait(s, f"aggr_{j}")
                C.inc(s.activation(sd_p[j % 2], mv_p[j % 2][:, 1:2],
                                   AF.Sqrt, bias=c_eps), "s",
                      mark=f"sqrt_{j}")
            k = sl - 1
            if 0 <= k < 2 * NT:
                C.wait(s, f"lnc_{k}")
                i = k % NT
                if k >= 2:
                    C.wait(s, f"tr_{k-2}")
                C.inc(s.activation(t1ntmp[k % 2][:, :],
                                   ubuf[k % 3][:, :], AF.Identity,
                                   scale=rstd_p[k % 2], bias=nmr_p[k % 2]),
                      "s", mark=f"uapply_{k}")
            k = sl - 2
            if 0 <= k < 2 * NT:
                C.wait(s, f"tr_{k}")
                i = k % NT
                for gq in range(2, 4):
                    C.inc(s.activation(
                        t1nT_full[:, i * 2048 + 512 * gq:
                                  i * 2048 + 512 * (gq + 1)],
                        pBbf[:, 512 * gq:512 * (gq + 1)], AF.Copy), "s",
                        mark=(f"t1nTs_{k}" if gq == 3 else None))
        C.wait(s, "v_rpe")
        C.inc(s.activation(rp_v[:, :], tC[:, :], AF.Exp, scale=-1.0,
                           bias=c_Tlnr), "s", mark="s_rpow")
        for sl in range(NT + 5):
            k = sl - 1
            if 0 <= k < NT:
                C.wait(s, f"yaggr_{k}")
                C.inc(s.activation(sdy_p[k % 2], mvy_p[k % 2][:, 1:2],
                                   AF.Sqrt, bias=c_eps), "s",
                      mark=f"ysqrt_{k}")
            k = sl - 2
            if 0 <= k < NT:
                C.wait(s, f"ylnc_{k}")
                C.inc(s.activation(ybuf_p[k % 2][:, :], ybuf_p[k % 2][:, :],
                                   AF.Identity, scale=rsy_p[k % 2],
                                   bias=nmy_p[k % 2]), "s", mark=f"ln2_{k}")
            k = sl - 3
            if 0 <= k < NT:
                C.wait(s, f"x1aggr_{k}")
                C.inc(s.activation(sdx_p[k % 2], mvx_p[k % 2][:, 1:2],
                                   AF.Sqrt, bias=c_eps), "s",
                      mark=f"x1sqrt_{k}")
            k = sl - 4
            if 0 <= k < NT:
                C.wait(s, f"xlnc_{k}")
                if k >= 1:
                    C.wait(s, f"xntr_{k-1}")
                C.inc(s.activation(xnbuf[:, :], x1buf[k % 2][:, :],
                                   AF.Identity, scale=rsx_p[k % 2],
                                   bias=nmx_p[k % 2]), "s",
                      mark=f"xnapply_{k}")
            k = sl - 5
            if 0 <= k < NT:
                C.wait(s, f"xntr_{k}")
                if k >= 9:
                    C.wait(s, f"dxnw_{k-9}")
                for gq in range(2, 4):
                    C.inc(s.activation(xnev_dst(k, gq), xnev_src(k, gq),
                                       AF.Copy), "s",
                          mark=(f"xnevs_{k}" if gq == 3 else None))
        for X in range(2):
            for hb in range(HB):
                C.wait(s, f"gmm_{X}_{hb}")
                if hb >= 1 or X == 1:
                    pX, phb = (X, hb - 1) if hb >= 1 else (0, HB - 1)
                    C.wait(s, f"h_{pX}_{phb}")
                gsrc = pA if hb % 2 == 0 else pB
                for tg in range(2):
                    C.inc(s.activation(h2buf[:, tg * 512:(tg + 1) * 512],
                                       gsrc[:, tg * 512:(tg + 1) * 512],
                                       AF.Silu, scale=sgbuf[:, hb:hb + 1]),
                          "s", mark=(f"gs_{X}_{hb}" if tg == 1 else None))
            for dg in range(4):
                C.wait(s, f"dwdla_{X}_{dg}")
                C.inc(s.activation(wd_lo[:, 0:5632], wd_lo[:, 0:5632],
                                   AF.Copy, bias=-128.0), "s",
                      mark=f"wdla_{X}_{dg}")
                C.wait(s, f"dwdlo_{X}_{dg}")
                C.inc(s.activation(wd_lo[:, 5632:11264], wd_lo[:, 5632:11264],
                                   AF.Copy, bias=-128.0), "s",
                      mark=f"wdlo_{X}_{dg}")
                C.wait(s, f"dwdhi_{X}_{dg}")
                C.inc(s.activation(wd_hi[:, :], wd_hi[:, :], AF.Copy,
                                   bias=-128.0), "s", mark=f"wdhi_{X}_{dg}")
            for ti in range(8):
                C.wait(s, f"l4aggr_{X}_{ti}")
                C.inc(s.activation(l4nm[ti], l4mv[ti][:, 1:2], AF.Sqrt,
                                   bias=c_eps), "s", mark=f"l4sqrt_{X}_{ti}")

    def emit_tensor(t):
        C.wait(t, "g_id")
        C.wait(t, "v_bbar")
        for kt in range(16):
            C.inc(t.transpose(pBbf[:, kt * 32:(kt + 1) * 32],
                              bbar_re[:, kt * 128:(kt + 1) * 128],
                              id_bf[0:NCH, 0:NCH]), "t")
        for kt in range(16):
            C.inc(t.transpose(pBbf[:, 512 + kt * 32:512 + (kt + 1) * 32],
                              bbar_im[:, kt * 128:(kt + 1) * 128],
                              id_bf[0:NCH, 0:NCH]), "t",
                  mark=("t_btr" if kt == 15 else None))
        C.wait(t, "s_dbf")
        for dg in range(4):
            C.inc(t.matmul(pA[:, dg * 512:(dg + 1) * 512], onesrow_bf,
                           drow_bf[:, dg * 512:(dg + 1) * 512],
                           start=True, stop=True), "t",
                  mark=("t_drep" if dg == 3 else None))
        C.wait(t, "v_drep")
        for dg in range(4):
            C.inc(t.matmul(pA[:, dg * 512:(dg + 1) * 512], onesrow_f,
                           sdrow[:, dg * 512:(dg + 1) * 512],
                           start=True, stop=True), "t",
                  mark=("t_sdrep" if dg == 3 else None))

        for j in range(2 * NT):
            C.wait(t, f"uapply_{j}")
            if j >= 1:
                C.wait(t, f"t1nTv_{j-1}")
                C.wait(t, f"t1nTs_{j-1}")
            i = j % NT
            tsrc = t1ntmp[j % 2]
            for kt in range(16):
                C.inc(t.transpose(pBbf[:, kt * 128:(kt + 1) * 128],
                                  tsrc[:, kt * 128:(kt + 1) * 128],
                                  id_bf[:, :]), "t",
                      mark=(f"tr_{j}" if kt == 15 else None))
            C.wait(t, f"t1nTv_{j}")
            C.wait(t, f"t1nTs_{j}")
            if j >= 2:
                C.wait(t, f"buev_{j-2}")
            pb = pC if j % 2 == 0 else pD
            for kt in range(16):
                C.inc(t.matmul(pb[0:NCH, 0:128],
                               bret_bf[:, kt * 32:(kt + 1) * 32],
                               t1nT_full[:, i * 2048 + kt * 128:
                                         i * 2048 + (kt + 1) * 128],
                               start=(kt == 0), stop=(kt == 15)), "t")
            for kt in range(16):
                C.inc(t.matmul(pb[0:NCH, 128:256],
                               bimt_bf[:, kt * 32:(kt + 1) * 32],
                               t1nT_full[:, i * 2048 + kt * 128:
                                         i * 2048 + (kt + 1) * 128],
                               start=(kt == 0), stop=(kt == 15)), "t",
                      mark=(f"bu_{j}" if kt == 15 else None))

        def xntr_block(k):
            C.wait(t, f"xnapply_{k}")
            if k == 0:
                C.wait(t, f"buev_{2*NT-1}")
            else:
                C.wait(t, f"xnevv_{k-1}")
                C.wait(t, f"xnevs_{k-1}")
            for kt in range(16):
                C.inc(t.transpose(pBbf[:, kt * 128:(kt + 1) * 128],
                                  xnbuf[:, kt * 128:(kt + 1) * 128],
                                  id_bf[:, :]), "t",
                      mark=(f"xntr_{k}" if kt == 15 else None))

        C.wait(t, "v_scan_done")
        C.wait(t, "v_sdrep")
        for i in range(NT):
            if i >= 1:
                C.wait(t, f"yev_{i-1}")
            if i == 0:
                C.wait(t, "g_ddiag")
            for dg in range(4):
                C.inc(t.matmul(pA[:, dg * 512:(dg + 1) * 512],
                               xs_re[:, i * 128:(i + 1) * 128],
                               c2re_v[:, dg * 512:(dg + 1) * 512],
                               start=True, stop=False), "t")
                C.inc(t.matmul(pA[:, dg * 512:(dg + 1) * 512],
                               xs_im[:, i * 128:(i + 1) * 128],
                               c2im_v[:, dg * 512:(dg + 1) * 512],
                               start=False, stop=False), "t")
            for kd in range(16):
                C.inc(t.matmul(pA[:, kd * 128:(kd + 1) * 128],
                               t1nT_full[:, i * 2048 + kd * 128:
                                         i * 2048 + (kd + 1) * 128],
                               d_rep_bf[:, kd * 128:(kd + 1) * 128],
                               start=False, stop=(kd % 4 == 3)), "t",
                      mark=(f"ymm_{i}" if kd == 15 else None))
            if i >= 5:
                xntr_block(i - 5)
        for k in range(NT - 5, NT):
            xntr_block(k)

        PS2 = [pA[:, 0:512], pA[:, 512:1024], pA[:, 1024:1536],
               pA[:, 1536:2048], pB[:, 0:512], pB[:, 512:1024], pC, pD]
        for X in range(2):
            for hb in range(HB):
                par = hb % 2
                C.wait(t, f"wconv_{X}_{hb}")
                if X == 0 and hb == 0:
                    C.wait(t, "yev_15")
                if X == 0 and hb == 1:
                    C.wait(t, "xnevv_15")
                    C.wait(t, "xnevs_15")
                if X == 1 and hb == 0:
                    C.wait(t, "dxnr")
                    C.wait(t, "gs_0_42")
                    C.wait(t, "uev_0_42")
                    C.wait(t, "ys_0_3_3")
                if X == 1 and hb == 1:
                    C.wait(t, "gs_0_43")
                    C.wait(t, "uev_0_43")
                    C.wait(t, "ys_0_3_7")
                if hb >= 2:
                    C.wait(t, f"gs_{X}_{hb-2}")
                    C.wait(t, f"uev_{X}_{hb-2}")
                if par == 0:
                    gps = [pA[:, 0:512], pA[:, 512:1024]]
                    ups = [pA[:, 1024:1536], pA[:, 1536:2048]]
                else:
                    gps = [pB[:, 0:512], pB[:, 512:1024]]
                    ups = [pC[:, :], pD[:, :]]
                for tg in range(2):
                    for kt in range(16):
                        C.inc(t.matmul(
                            gps[tg],
                            wgbuf[par][:, kt * 128:(kt + 1) * 128],
                            xnt[:, kt * TH + tg * 512:
                                kt * TH + (tg + 1) * 512],
                            start=(kt == 0), stop=(kt == 15)), "t",
                            mark=(f"gmm_{X}_{hb}"
                                  if tg == 1 and kt == 15 else None))
                for tg in range(2):
                    for kt in range(16):
                        C.inc(t.matmul(
                            ups[tg],
                            wubuf[par][:, kt * 128:(kt + 1) * 128],
                            xnt[:, kt * TH + tg * 512:
                                kt * TH + (tg + 1) * 512],
                            start=(kt == 0), stop=(kt == 15)), "t",
                            mark=(f"s1mm_{X}_{hb}"
                                  if tg == 1 and kt == 15 else None))
            for dg in range(4):
                for hb in range(HB):
                    if hb == 0:
                        C.wait(t, f"wdla_{X}_{dg}")
                    if hb == 11:
                        C.wait(t, f"wdlo_{X}_{dg}")
                    if hb == 22:
                        C.wait(t, f"wdhi_{X}_{dg}")
                    if dg == 0:
                        C.wait(t, f"h_{X}_{hb}")
                    wsrc = wd_lo if hb < 22 else wd_hi
                    wof = hb if hb < 22 else hb - 22
                    for ti in range(8):
                        if hb == 0:
                            if dg == 0:
                                C.wait(t, f"gs_{X}_{HB-1}")
                                C.wait(t, f"uev_{X}_{HB-1}")
                            else:
                                C.wait(t, f"ys_{X}_{dg-1}_{ti}")
                        C.inc(t.matmul(
                            PS2[ti],
                            Hbuf[:, hb * TH + ti * 128:
                                 hb * TH + (ti + 1) * 128],
                            wsrc[:, wof * 512:(wof + 1) * 512],
                            start=(hb == 0), stop=(hb == HB - 1)), "t",
                            mark=(f"s2f_{X}_{dg}_{ti}" if hb == HB - 1
                                  else (f"s2lo_{X}_{dg}"
                                        if hb == 21 and ti == 7 else
                                        (f"s2q_{X}_{dg}"
                                         if hb == 10 and ti == 7
                                         else None))))

    def emit_vector(v):
        C.wait(v, "d_params")
        C.wait(v, "g_setup")
        C.inc(v.tensor_scalar(c2re_v[:, :], p_cret, 2.0, None, ALU.mult),
              "v")
        C.inc(v.tensor_scalar(c2im_v[:, :], p_cimt, -2.0, None, ALU.mult),
              "v", mark="v_c2")
        C.wait(v, "s_p1")
        C.inc(v.tensor_scalar(c_t1, p_lunc, 0.0, None, ALU.max), "v")
        C.selfwait(v, "v")
        C.inc(v.tensor_tensor(c_t1, c_t1, c_t3, ALU.add), "v")
        C.selfwait(v, "v")
        C.inc(v.tensor_scalar(c_lamre, c_t1, 0.0101, -1.0, ALU.add,
                              ALU.mult), "v")
        C.selfwait(v, "v")
        C.inc(v.tensor_tensor(c_lnr, c_lamre, c_delta, ALU.mult), "v",
              mark="v_lnr")
        C.selfwait(v, "v")
        C.inc(v.tensor_scalar(c_Tlnr, c_lnr, float(T), None, ALU.mult), "v")
        C.inc(v.tensor_tensor(c_theta, p_lim, c_delta, ALU.mult), "v")
        C.selfwait(v, "v")
        C.inc(v.tensor_scalar(c_t2, c_theta, 1.0 / TWO_PI, None, ALU.mult),
              "v")
        C.selfwait(v, "v")
        C.inc(v.tensor_copy(c_i32, c_t2), "v")
        C.selfwait(v, "v")
        C.inc(v.tensor_copy(c_t2, c_i32), "v")
        C.selfwait(v, "v")
        C.inc(v.scalar_tensor_tensor(c_thred, c_t2, -TWO_PI, c_theta,
                                     ALU.mult, ALU.add), "v")
        C.selfwait(v, "v")
        C.inc(v.tensor_scalar(c_t2, c_thred, PI, None, ALU.is_gt), "v")
        C.selfwait(v, "v")
        C.inc(v.scalar_tensor_tensor(c_thred, c_t2, -TWO_PI, c_thred,
                                     ALU.mult, ALU.add), "v")
        C.selfwait(v, "v")
        C.inc(v.tensor_scalar(c_t2, c_thred, -PI, None, ALU.is_lt), "v")
        C.selfwait(v, "v")
        C.inc(v.scalar_tensor_tensor(c_thred, c_t2, TWO_PI, c_thred,
                                     ALU.mult, ALU.add), "v", mark="v_thred")
        C.wait(v, "s_sincos")
        C.inc(v.tensor_tensor(c_t2, c_sh, c_sh, ALU.mult), "v")
        C.selfwait(v, "v")
        C.inc(v.tensor_scalar(c_costh, c_t2, -2.0, 1.0, ALU.mult, ALU.add),
              "v")
        C.wait(v, "s_r")
        C.selfwait(v, "v")
        C.inc(v.tensor_tensor(c_are, c_r, c_costh, ALU.mult), "v")
        C.inc(v.tensor_tensor(c_aim, c_r, c_sinth, ALU.mult), "v")
        C.selfwait(v, "v")
        C.inc(v.tensor_scalar(c_t1, c_are, -1.0, None, ALU.add), "v")
        C.inc(v.tensor_tensor(c_t2, c_lamre, c_lamre, ALU.mult), "v")
        C.inc(v.tensor_tensor(c_t3, p_lim, p_lim, ALU.mult), "v")
        C.selfwait(v, "v")
        C.inc(v.tensor_tensor(c_t2, c_t2, c_t3, ALU.add), "v")
        C.selfwait(v, "v")
        C.inc(v.reciprocal(c_t2, c_t2), "v")
        C.selfwait(v, "v")
        C.inc(v.tensor_tensor(c_t3, c_t1, c_lamre, ALU.mult), "v")
        C.inc(v.tensor_tensor(c_t4, c_aim, p_lim, ALU.mult), "v")
        C.selfwait(v, "v")
        C.inc(v.tensor_tensor(c_t3, c_t3, c_t4, ALU.add), "v")
        C.selfwait(v, "v")
        C.inc(v.tensor_tensor(c_dre, c_t3, c_t2, ALU.mult), "v")
        C.selfwait(v, "v")
        C.inc(v.tensor_tensor(c_t3, c_aim, c_lamre, ALU.mult), "v")
        C.inc(v.tensor_tensor(c_t4, c_t1, p_lim, ALU.mult), "v")
        C.selfwait(v, "v")
        C.inc(v.tensor_tensor(c_t3, c_t3, c_t4, ALU.subtract), "v")
        C.selfwait(v, "v")
        C.inc(v.tensor_tensor(c_dim, c_t3, c_t2, ALU.mult), "v")
        C.selfwait(v, "v")
        C.inc(v.tensor_scalar(bbar_sA[:, :], p_bre, c_dre, None, ALU.mult),
              "v")
        C.inc(v.tensor_scalar(bbar_sB[:, :], p_bim, c_dim, None, ALU.mult),
              "v")
        C.selfwait(v, "v")
        C.inc(v.tensor_tensor(bbar_re[:, :], bbar_sA[:, :], bbar_sB[:, :],
                              ALU.subtract), "v")
        C.selfwait(v, "v")
        C.inc(v.tensor_scalar(bbar_sA[:, :], p_bim, c_dre, None, ALU.mult),
              "v")
        C.inc(v.tensor_scalar(bbar_sB[:, :], p_bre, c_dim, None, ALU.mult),
              "v")
        C.selfwait(v, "v")
        C.inc(v.tensor_tensor(bbar_im[:, :], bbar_sA[:, :], bbar_sB[:, :],
                              ALU.add), "v", mark="v_bbar")
        C.inc(v.tensor_scalar(ph_p[:, :], it_fwd[:, :], c_theta, None,
                              ALU.mult), "v")
        C.selfwait(v, "v")
        C.inc(v.tensor_scalar(ph_q[:, :], ph_p[:, :], 1.0 / TWO_PI, None,
                              ALU.mult), "v")
        C.selfwait(v, "v")
        C.inc(v.tensor_copy(ph_qi, ph_q[:, :]), "v")
        C.selfwait(v, "v")
        C.inc(v.tensor_copy(ph_r[:, :], ph_qi), "v")
        C.selfwait(v, "v")
        C.inc(v.scalar_tensor_tensor(ph_p[:, :], ph_r[:, :], -TWO_PI,
                                     ph_p[:, :], ALU.mult, ALU.add), "v")
        C.selfwait(v, "v")
        C.inc(v.tensor_scalar(ph_r[:, :], ph_p[:, :], PI, None, ALU.is_gt),
              "v")
        C.selfwait(v, "v")
        C.inc(v.scalar_tensor_tensor(ph_p[:, :], ph_r[:, :], -TWO_PI,
                                     ph_p[:, :], ALU.mult, ALU.add), "v")
        C.selfwait(v, "v")
        C.inc(v.tensor_scalar(ph_r[:, :], ph_p[:, :], -PI, None, ALU.is_lt),
              "v")
        C.selfwait(v, "v")
        C.inc(v.scalar_tensor_tensor(ph_p[:, :], ph_r[:, :], TWO_PI,
                                     ph_p[:, :], ALU.mult, ALU.add), "v",
              mark="v_pred")
        C.wait(v, "s_abs")
        C.selfwait(v, "v")
        C.inc(v.tensor_scalar(ph_r[:, :], ph_q[:, :], -1.0, PI / 2.0,
                              ALU.mult, ALU.add), "v", mark="v_carg")
        C.wait(v, "t_btr")
        C.inc(v.tensor_copy(bret_bf[:, :], pBbf[:, 0:512]), "v")
        C.inc(v.tensor_copy(bimt_bf[:, :], pBbf[:, 512:1024]), "v",
              mark="v_btrev")
        C.wait(v, "t_drep")
        C.inc(v.tensor_copy(d_rep_bf[:, :], pA[:, :]), "v", mark="v_drep")
        C.wait(v, "t_sdrep")
        C.inc(v.tensor_copy(sdrep[:, :], pA[:, :]), "v", mark="v_sdrep")

        NTOT = 2 * NT
        for sl in range(NTOT + 3):
            j = sl
            if j < NTOT:
                C.wait(v, f"du_{j}")
                C.selfwait(v, "v")
                for gq in range(4):
                    C.inc(v.bn_stats(statbuf[:, 6 * gq:6 * (gq + 1)],
                                     ubuf[j % 3][:, 512 * gq:512 * (gq + 1)]),
                          "v")
                C.selfwait(v, "v")
                C.inc(v.bn_aggr(mv_p[j % 2], statbuf[:, :]), "v",
                      mark=f"aggr_{j}")
            k = sl - 1
            if 0 <= k < NTOT:
                C.wait(v, f"sqrt_{k}")
                C.selfwait(v, "v")
                C.inc(v.reciprocal(rstd_p[k % 2], sd_p[k % 2]), "v")
                C.inc(v.tensor_scalar(c_negm, mv_p[k % 2][:, 0:1], -1.0,
                                      None, ALU.mult), "v")
                C.selfwait(v, "v")
                C.inc(v.tensor_tensor(nmr_p[k % 2], c_negm, rstd_p[k % 2],
                                      ALU.mult), "v", mark=f"lnc_{k}")
            k = sl - 2
            if 0 <= k < NTOT:
                C.wait(v, f"tr_{k}")
                i = k % NT
                for gq in range(2):
                    C.inc(v.tensor_copy(
                        t1nT_full[:, i * 2048 + 512 * gq:
                                  i * 2048 + 512 * (gq + 1)],
                        pBbf[:, 512 * gq:512 * (gq + 1)]), "v",
                        mark=(f"t1nTv_{k}" if gq == 1 else None))
            k = sl - 3
            if 0 <= k < NTOT:
                C.wait(v, f"bu_{k}")
                pb = pC if k % 2 == 0 else pD
                bre_dst, bim_dst = bu_dsts(k)
                C.inc(v.tensor_copy(bre_dst, pb[0:NCH, 0:128]), "v")
                C.inc(v.tensor_copy(bim_dst, pb[0:NCH, 128:256]), "v",
                      mark=f"buev_{k}")

        C.inc(v.tensor_scalar(tC[:, :], it_fwd[:, :], c_lnr, None, ALU.mult),
              "v", mark="v_rpe")
        C.wait(v, "s_rpow")
        C.inc(v.tensor_tensor(tA[:, :], cosT[:, :], rp_v[:, :], ALU.mult),
              "v")
        C.inc(v.tensor_tensor(tB[:, :], sinT[:, :], rp_v[:, :], ALU.mult),
              "v")
        C.selfwait(v, "v")
        C.inc(v.tensor_tensor(tC[:, :], tA[:, :], bu_re_p[:, :], ALU.mult),
              "v")
        C.selfwait(v, "v")
        C.inc(v.reduce_sum(c_t1, tC[:, :], axis=AX.X), "v")
        C.selfwait(v, "v")
        C.inc(v.tensor_tensor(tC[:, :], tB[:, :], bu_im_p[:, :], ALU.mult),
              "v")
        C.selfwait(v, "v")
        C.inc(v.reduce_sum(c_t2, tC[:, :], axis=AX.X), "v")
        C.selfwait(v, "v")
        C.inc(v.tensor_tensor(tC[:, :], tA[:, :], bu_im_p[:, :], ALU.mult),
              "v")
        C.selfwait(v, "v")
        C.inc(v.reduce_sum(c_t3, tC[:, :], axis=AX.X), "v")
        C.selfwait(v, "v")
        C.inc(v.tensor_tensor(tC[:, :], tB[:, :], bu_re_p[:, :], ALU.mult),
              "v")
        C.selfwait(v, "v")
        C.inc(v.reduce_sum(c_t4, tC[:, :], axis=AX.X), "v")
        C.selfwait(v, "v")
        C.inc(v.tensor_tensor(c_sr, c_t1, c_t2, ALU.add), "v")
        C.inc(v.tensor_tensor(c_si, c_t3, c_t4, ALU.subtract), "v")
        C.selfwait(v, "v")
        C.inc(v.tensor_tensor(c_t1, cosT[:, T - 1:T], c_sr, ALU.mult), "v")
        C.inc(v.tensor_tensor(c_t2, sinT[:, T - 1:T], c_si, ALU.mult), "v")
        C.inc(v.tensor_tensor(c_t3, sinT[:, T - 1:T], c_sr, ALU.mult), "v")
        C.inc(v.tensor_tensor(c_t4, cosT[:, T - 1:T], c_si, ALU.mult), "v")
        C.selfwait(v, "v")
        C.inc(v.tensor_tensor(c_cre, c_t1, c_t2, ALU.subtract), "v")
        C.inc(v.tensor_tensor(c_cim, c_t3, c_t4, ALU.add), "v")
        C.selfwait(v, "v")
        C.inc(v.tensor_tensor(tA[:, :], cosT[:, :], bu_re_m[:, :], ALU.mult),
              "v")
        C.inc(v.tensor_tensor(tB[:, :], sinT[:, :], bu_im_m[:, :], ALU.mult),
              "v")
        C.selfwait(v, "v")
        C.inc(v.tensor_tensor(bt_re[:, :], tA[:, :], tB[:, :], ALU.add), "v")
        C.selfwait(v, "v")
        C.inc(v.tensor_tensor(tA[:, :], cosT[:, :], bu_im_m[:, :], ALU.mult),
              "v")
        C.inc(v.tensor_tensor(tB[:, :], sinT[:, :], bu_re_m[:, :], ALU.mult),
              "v")
        C.selfwait(v, "v")
        C.inc(v.tensor_tensor(bt_im[:, :], tA[:, :], tB[:, :], ALU.subtract),
              "v")
        C.inc(v.memset(tC[:, :], 1.0), "v")
        C.selfwait(v, "v")
        C.inc(v.tensor_scalar(tC[:, :], tC[:, :], c_r, None, ALU.mult), "v")
        C.selfwait(v, "v")
        C.inc(v.tensor_tensor_scan(tA[:, :], tC[:, :], bt_re[:, :], c_cre,
                                   ALU.mult, ALU.add), "v")
        C.inc(v.tensor_tensor_scan(tB[:, :], tC[:, :], bt_im[:, :], c_cim,
                                   ALU.mult, ALU.add), "v")
        C.selfwait(v, "v")
        C.inc(v.tensor_tensor(tC[:, :], cosT[:, :], tA[:, :], ALU.mult), "v")
        C.inc(v.tensor_tensor(bt_re[:, :], sinT[:, :], tB[:, :], ALU.mult),
              "v")
        C.selfwait(v, "v")
        C.inc(v.tensor_tensor(xs_re[:, :], tC[:, :], bt_re[:, :],
                              ALU.subtract), "v")
        C.selfwait(v, "v")
        C.inc(v.tensor_tensor(tC[:, :], cosT[:, :], tB[:, :], ALU.mult), "v")
        C.inc(v.tensor_tensor(bt_im[:, :], sinT[:, :], tA[:, :], ALU.mult),
              "v")
        C.selfwait(v, "v")
        C.inc(v.tensor_tensor(xs_im[:, :], tC[:, :], bt_im[:, :], ALU.add),
              "v", mark="v_scan_done")

        for sl in range(NT + 5):
            k = sl - 2
            if 0 <= k < NT:
                C.wait(v, f"ysqrt_{k}")
                C.selfwait(v, "v")
                C.inc(v.reciprocal(rsy_p[k % 2], sdy_p[k % 2]), "v")
                C.inc(v.tensor_scalar(c_negm, mvy_p[k % 2][:, 0:1], -1.0,
                                      None, ALU.mult), "v")
                C.selfwait(v, "v")
                C.inc(v.tensor_tensor(nmy_p[k % 2], c_negm, rsy_p[k % 2],
                                      ALU.mult), "v", mark=f"ylnc_{k}")
            k = sl - 4
            if 0 <= k < NT:
                C.wait(v, f"x1sqrt_{k}")
                C.selfwait(v, "v")
                C.inc(v.reciprocal(rsx_p[k % 2], sdx_p[k % 2]), "v")
                C.inc(v.tensor_scalar(c_negm, mvx_p[k % 2][:, 0:1], -1.0,
                                      None, ALU.mult), "v")
                C.selfwait(v, "v")
                C.inc(v.tensor_tensor(nmx_p[k % 2], c_negm, rsx_p[k % 2],
                                      ALU.mult), "v", mark=f"xlnc_{k}")
            k = sl - 3
            if 0 <= k < NT:
                C.wait(v, f"ln2_{k}")
                C.wait(v, f"dx0_{k}")
                if k >= 2:
                    C.wait(v, f"dx1w_{k-2}")
                    C.wait(v, f"xnapply_{k-2}")
                C.selfwait(v, "v")
                C.inc(v.tensor_tensor(x1buf[k % 2][:, :],
                                      ybuf_p[k % 2][:, :],
                                      x0buf_p[k % 2][:, :],
                                      ALU.add), "v", mark=f"x1_{k}")
                C.selfwait(v, "v")
                for gq in range(4):
                    C.inc(v.bn_stats(
                        statbuf[:, 6 * gq:6 * (gq + 1)],
                        x1buf[k % 2][:, 512 * gq:512 * (gq + 1)]), "v")
                C.selfwait(v, "v")
                C.inc(v.bn_aggr(mvx_p[k % 2], statbuf[:, :]), "v",
                      mark=f"x1aggr_{k}")
            k = sl - 1
            if 0 <= k < NT:
                C.wait(v, f"ymm_{k}")
                if k >= 2:
                    C.wait(v, f"ln2_{k-2}")
                C.selfwait(v, "v")
                for gq in range(4):
                    C.inc(v.tensor_copy(
                        ybuf_p[k % 2][:, 512 * gq:512 * (gq + 1)],
                        pA[:, 512 * gq:512 * (gq + 1)]), "v",
                        mark=(f"yev_{k}" if gq == 3 else None))
                C.selfwait(v, "v")
                for gq in range(4):
                    C.inc(v.bn_stats(
                        statbuf[:, 6 * gq:6 * (gq + 1)],
                        ybuf_p[k % 2][:, 512 * gq:512 * (gq + 1)]), "v")
                C.selfwait(v, "v")
                C.inc(v.bn_aggr(mvy_p[k % 2], statbuf[:, :]), "v",
                      mark=f"yaggr_{k}")
            k = sl - 5
            if 0 <= k < NT:
                C.wait(v, f"xntr_{k}")
                if k >= 9:
                    C.wait(v, f"dxnw_{k-9}")
                for gq in range(2):
                    C.inc(v.tensor_copy(xnev_dst(k, gq), xnev_src(k, gq)),
                          "v", mark=(f"xnevv_{k}" if gq == 1 else None))

        def tail_tile(X, ti):
            C.wait(v, f"dyr_{X}_{ti}")
            C.selfwait(v, "v")
            C.inc(v.tensor_scalar(yrbuf_p[ti % 2][:, :],
                                  yrbuf_p[ti % 2][:, :], l4nm[ti],
                                  l4rs[ti], ALU.add, ALU.mult), "v",
                  mark=f"ln4apply_{X}_{ti}")
            C.wait(v, f"dx1r_{X}_{ti}")
            gk = X * 8 + ti
            if gk >= 1:
                pgk = gk - 1
                C.wait(v, f"dout_{pgk // 8}_{pgk % 8}")
            C.selfwait(v, "v")
            C.inc(v.tensor_tensor(outbuf[0][:, :], yrbuf_p[ti % 2][:, :],
                                  x1rbuf_p[ti % 2][:, :], ALU.add), "v",
                  mark=f"out_{X}_{ti}")

        for X in range(2):
            for hb in range(HB):
                if X == 0 and hb == 0:
                    C.wait(v, "dxnw_6")
                    C.wait(v, "dxnw_7")
                    C.wait(v, f"dx1w_{NT-2}")
                    C.wait(v, f"dx1w_{NT-1}")
                if X == 1 and hb % 5 == 4 and hb // 5 < 8:
                    tail_tile(0, hb // 5)
                C.wait(v, f"s1mm_{X}_{hb}")
                C.selfwait(v, "v")
                if hb % 2 == 0:
                    ups = [pA[:, 1024:1536], pA[:, 1536:2048]]
                else:
                    ups = [pC[:, :], pD[:, :]]
                for tg in range(2):
                    C.inc(v.tensor_scalar(
                        us_s1[:, tg * 512:(tg + 1) * 512], ups[tg],
                        subuf[:, hb:hb + 1], None, ALU.mult), "v",
                        mark=(f"uev_{X}_{hb}" if tg == 1 else None))
                C.wait(v, f"gs_{X}_{hb}")
                C.selfwait(v, "v")
                for tg in range(2):
                    C.inc(v.tensor_tensor(
                        Hcols(hb, tg), h2buf[:, tg * 512:(tg + 1) * 512],
                        us_s1[:, tg * 512:(tg + 1) * 512], ALU.mult), "v",
                        mark=(f"h_{X}_{hb}" if tg == 1 else None))
            PS2v = [pA[:, 0:512], pA[:, 512:1024], pA[:, 1024:1536],
                    pA[:, 1536:2048], pB[:, 0:512], pB[:, 512:1024],
                    pC[:, :], pD[:, :]]
            for dg in range(4):
                for ti in range(8):
                    par = (dg * 8 + ti) % 2
                    C.wait(v, f"s2f_{X}_{dg}_{ti}")
                    k = dg * 8 + ti
                    if k >= 2:
                        pk = k - 2
                        C.wait(v, f"dyw_{X}_{pk // 8}_{pk % 8}")
                    elif X == 1:
                        C.wait(v, f"dyw_0_3_{6 + ti}")
                    C.inc(v.tensor_tensor(ystage[par][:, :], PS2v[ti],
                                          sdrep[:, dg * 512:(dg + 1) * 512],
                                          ALU.mult), "v")
                    C.selfwait(v, "v")
                    C.inc(v.bn_stats(
                        ln4stats[:, ti * 24 + dg * 6:ti * 24 + dg * 6 + 6],
                        ystage[par][:, :]), "v", mark=f"ys_{X}_{dg}_{ti}")
            C.selfwait(v, "v")
            for ti in range(8):
                C.inc(v.bn_aggr(l4mv[ti], ln4stats[:, ti * 24:(ti + 1) * 24]),
                      "v", mark=f"l4aggr_{X}_{ti}")
            for ti in range(8):
                C.wait(v, f"l4sqrt_{X}_{ti}")
                C.selfwait(v, "v")
                C.inc(v.reciprocal(l4rs[ti], l4nm[ti]), "v")
                C.inc(v.tensor_scalar(c_negm, l4mv[ti][:, 0:1], -1.0, None,
                                      ALU.mult), "v")
                C.selfwait(v, "v")
                C.inc(v.tensor_tensor(l4nm[ti], c_negm, l4rs[ti], ALU.mult),
                      "v", mark=f"l4c_{X}_{ti}")
            if X == 1:
                for ti in range(8):
                    tail_tile(1, ti)

    def run_pass(pn):
        C.begin(pn, sems if pn == 2 else None)
        if pn == 1:
            emit_sync(_FakeEng())
            emit_gpsimd(_FakeEng())
            emit_scalar(_FakeEng())
            emit_vector(_FakeEng())
            emit_tensor(_FakeEng())
            return
        with nc.Block() as block:
            @block.sync
            def _(q):
                emit_sync(q)

            @block.gpsimd
            def _(g):
                emit_gpsimd(g)

            @block.scalar
            def _(s):
                emit_scalar(s)

            @block.vector
            def _(v):
                emit_vector(v)

            @block.tensor
            def _(t):
                emit_tensor(t)

    run_pass(1)
    cnt1 = dict(C.cnt)
    run_pass(2)
    assert C.cnt == cnt1, (C.cnt, cnt1)
    st.close()
    return nc


def _u8view(wq_int32):
    a = np.ascontiguousarray(np.asarray(wq_int32))
    assert a.dtype == np.int32
    return np.ascontiguousarray(a.view(np.uint8)[..., ::4])


def _stage_w_hd(wq):
    a = _u8view(wq).reshape(HB, 128, 16, 128)
    a = np.ascontiguousarray(a.transpose(0, 3, 2, 1))
    return a.reshape(HB, 128, 2048)


def _stage_w_dh(wq):
    a = np.ascontiguousarray(_u8view(wq).T)
    return a.reshape(HB, 128, 2048)


def kernel(L0_attn_u, L0_attn_Lambda_unconstrained_re, L0_attn_Lambda_im,
           L0_attn_B_re, L0_attn_B_im, L0_attn_C_re, L0_attn_C_im,
           L0_attn_D, L0_attn_log_Delta, L0_ffn_wq_gate, L0_ffn_s_gate,
           L0_ffn_wq_up, L0_ffn_s_up, L0_ffn_wq_down, L0_ffn_s_down):
    u = np.ascontiguousarray(np.asarray(L0_attn_u, dtype=np.float32))
    wg = _stage_w_hd(L0_ffn_wq_gate)
    wu = _stage_w_hd(L0_ffn_wq_up)
    wd = _stage_w_dh(L0_ffn_wq_down)
    sg = np.ascontiguousarray(
        np.asarray(L0_ffn_s_gate, np.float32).reshape(HB, 128).T)
    su = np.ascontiguousarray(
        np.asarray(L0_ffn_s_up, np.float32).reshape(HB, 128).T)
    sd = np.ascontiguousarray(np.asarray(L0_ffn_s_down,
                                         np.float32).reshape(1, D))
    bre = np.ascontiguousarray(np.asarray(L0_attn_B_re, np.float32))
    bim = np.ascontiguousarray(np.asarray(L0_attn_B_im, np.float32))
    cret = np.ascontiguousarray(np.asarray(L0_attn_C_re, np.float32).T)
    cimt = np.ascontiguousarray(np.asarray(L0_attn_C_im, np.float32).T)
    drow = np.ascontiguousarray(np.asarray(L0_attn_D,
                                           np.float32).reshape(1, D))
    lunc = np.ascontiguousarray(
        np.asarray(L0_attn_Lambda_unconstrained_re,
                   np.float32).reshape(NCH, 1))
    lim = np.ascontiguousarray(np.asarray(L0_attn_Lambda_im,
                                          np.float32).reshape(NCH, 1))
    logd = np.ascontiguousarray(
        np.asarray(L0_attn_log_Delta, np.float32)[:NCH].reshape(NCH, 1))
    zeros_u = np.zeros((T, D), np.float32)

    nc = build_program()
    in_maps = []
    for c in range(8):
        b, h = c // 2, c % 2
        mine = np.ascontiguousarray(u[b, h * T:(h + 1) * T, :])
        prev = np.ascontiguousarray(u[b, 0:T, :]) if h == 1 else zeros_u
        in_maps.append(dict(
            u=mine, uprev=prev, wg=wg, wu=wu, wd=wd, sg=sg, su=su, sd=sd,
            bre=bre, bim=bim, cret=cret, cimt=cimt, drow=drow, lunc=lunc,
            lim=lim, logd=logd))
    res = run_bass_kernel_spmd(nc, in_maps, core_ids=list(range(8)))
    out = np.empty((B, L, D), np.float32)
    for c in range(8):
        b, h = c // 2, c % 2
        out[b, h * T:(h + 1) * T, :] = np.asarray(res.results[c]["out"])
    return out


# version 11
# speedup vs baseline: 1.3140x; 1.0295x over previous
import contextlib
import numpy as np
import ml_dtypes
import concourse.bass as bass
import concourse.mybir as mybir
from concourse.bass_utils import run_bass_kernel_spmd

F32 = mybir.dt.float32
BF16 = mybir.dt.bfloat16
U8 = mybir.dt.uint8
I32 = mybir.dt.int32
AF = mybir.ActivationFunctionType
ALU = mybir.AluOpType
AX = mybir.AxisListType

B, L, D = 4, 4096, 2048
HID = 5632
NCH = 32
T = 2048
NT = 16
HB = 44
TH = 1024
PI = float(np.pi)
TWO_PI = float(2.0 * np.pi)
EPS_LN = 1e-5
KB = 1024


class Ctx:

    KEYS = ("d_par", "d_u", "d_u2", "d_u3", "d_x0", "d_x02", "d_x1w",
            "d_x1w2", "d_x1r",
            "d_xnw", "d_xnw2", "d_xnr", "d_w", "d_w2", "d_wda", "d_wdl", "d_wdh", "d_yw",
            "d_yw2", "d_yr", "d_yr2", "d_x1r2", "d_out", "d_out2",
            "v", "s", "t", "g")

    def __init__(self):
        self.marks = {}
        self.pass_num = 0
        self.cnt = {}
        self.sems = None

    def begin(self, pass_num, sems=None):
        self.pass_num = pass_num
        self.cnt = {k: 0 for k in self.KEYS}
        self.sems = sems

    def inc(self, inst, key, by=1, mark=None):
        if self.pass_num == 2:
            inst.then_inc(self.sems[key], by)
        self.cnt[key] += by
        if mark is not None:
            if self.pass_num == 1:
                assert mark not in self.marks, f"dup mark {mark}"
                self.marks[mark] = (key, self.cnt[key])
            else:
                assert self.marks[mark] == (key, self.cnt[key]), \
                    f"mark drift {mark}"
        return self.cnt[key]

    def wait(self, eng, mark, delta=0):
        if self.pass_num == 2:
            key, val = self.marks[mark]
            if val + delta > 0:
                eng.wait_ge(self.sems[key], val + delta)

    def wait_cnt(self, eng, key, val):
        if self.pass_num == 2 and val > 0:
            eng.wait_ge(self.sems[key], val)

    def selfwait(self, eng, key):
        if self.pass_num == 2 and self.cnt[key] > 0:
            eng.wait_ge(self.sems[key], self.cnt[key])


class _FakeInst:
    def then_inc(self, *a, **k):
        return self


class _FakeEng:
    def __getattr__(self, name):
        return lambda *a, **k: _FakeInst()


def build_program():
    nc = bass.Bass("TRN2", target_bir_lowering=False, debug=False)

    d_in = {}

    def din(name, shape, dt):
        d_in[name] = nc.dram_tensor(name, shape, dt, kind="ExternalInput").ap()

    din("u", [T, D], F32)
    din("uprev", [T, D], F32)
    din("wg", [HB, 128, D], U8)
    din("wu", [HB, 128, D], U8)
    din("wd", [4, 128, HB * 512], U8)
    din("sg", [128, HB], F32)
    din("su", [128, HB], F32)
    din("sd", [1, D], F32)
    din("bre", [NCH, D], F32)
    din("bim", [NCH, D], F32)
    din("cret", [NCH, D], F32)
    din("cimt", [NCH, D], F32)
    din("drow", [1, D], F32)
    din("lunc", [NCH, 1], F32)
    din("lim", [NCH, 1], F32)
    din("logd", [NCH, 1], F32)
    out_hbm = nc.dram_tensor("out", [T, D], F32, kind="ExternalOutput").ap()
    x1_spill = nc.dram_tensor("x1sp", [T, D], F32, kind="Internal").ap()
    xnt_spill = nc.dram_tensor("xntsp", [128, 16 * TH], BF16,
                               kind="Internal").ap()
    y_spill = nc.dram_tensor("ysp", [TH, D], F32, kind="Internal").ap()

    st = contextlib.ExitStack()
    ARENA = 207 * KB
    arena = st.enter_context(nc.sbuf_tensor("arena", [128, ARENA], U8)).ap()

    def vf32(off_kb, n):
        off = int(off_kb * KB)
        assert off % 4 == 0 and off + 4 * n <= ARENA, (off_kb, n)
        return arena[:, off:off + 4 * n].bitcast(F32)

    def vbf(off_kb, n):
        off = int(off_kb * KB)
        assert off % 2 == 0 and off + 2 * n <= ARENA, (off_kb, n)
        return arena[:, off:off + 2 * n].bitcast(BF16)

    def vi32(off_kb, n):
        off = int(off_kb * KB)
        return arena[:, off:off + 4 * n].bitcast(I32)

    t1nT_full = vbf(0, 32768)
    wd_lo = vbf(0, 11264)
    wd_hi = vbf(22, 11264)
    wgbuf = [vbf(44, 2048), vbf(48, 2048)]
    wubuf = [vbf(52, 2048), vbf(56, 2048)]
    us_s1 = vbf(60, 1024)
    yrbuf_p = [vf32(0, 2048), vf32(8, 2048)]
    x1rbuf_p = [vf32(16, 2048), vf32(24, 2048)]
    outbuf = [vf32(32, 2048), vf32(40, 1024), ]
    ybig = [vf32(64, 2048), vf32(72, 2048), vf32(80, 2048), vf32(88, 2048),
            vf32(44, 2048), vf32(52, 2048), vf32(96, 2048), vf32(104, 2048)]
    x1rb2 = [vf32(112, 2048), vf32(120, 2048)]
    outb2 = [vf32(128, 2048), vf32(136, 2048)]

    xnt = vbf(64, 16384)
    drow_f = vf32(64, 2048)[0:1, :]
    sdrow = vf32(72, 2048)[0:1, :]
    drow_bf = vbf(80, 2048)[0:1, :]

    ubuf = [vf32(96, 2048), vf32(104, 2048), vf32(84, 2048)]
    t1ntmp = [vbf(112, 2048), vbf(116, 2048)]
    bu_re_m = vf32(120, 2048)[0:NCH, :]
    bu_im_m = vf32(128, 2048)[0:NCH, :]
    bu_re_p = vf32(136, 2048)[0:NCH, :]
    bu_im_p = vf32(144, 2048)[0:NCH, :]
    cosT = vf32(152, 2048)[0:NCH, :]
    sinT = vf32(160, 2048)[0:NCH, :]
    it_fwd = vf32(168, 2048)[0:NCH, :]
    rp_v = vf32(168, 2048)[0:NCH, :]
    c2re_v = vbf(176, 2048)[0:NCH, :]
    c2im_v = vbf(180, 2048)[0:NCH, :]
    bbar_sA = vf32(96, 2048)[0:NCH, :]
    bbar_sB = vf32(104, 2048)[0:NCH, :]
    bbar_re = vbf(112, 2048)[0:NCH, :]
    bbar_im = vbf(116, 2048)[0:NCH, :]
    ph_p = vf32(120, 2048)[0:NCH, :]
    ph_q = vf32(128, 2048)[0:NCH, :]
    ph_qi = vi32(136, 2048)[0:NCH, :]
    ph_r = vf32(144, 2048)[0:NCH, :]
    tA = vf32(96, 2048)[0:NCH, :]
    tB = vf32(104, 2048)[0:NCH, :]
    tC = vf32(112, 2048)[0:NCH, :]
    bt_re = vf32(136, 2048)[0:NCH, :]
    bt_im = vf32(144, 2048)[0:NCH, :]
    xs_re = vbf(120, 2048)[0:NCH, :]
    xs_im = vbf(124, 2048)[0:NCH, :]
    ybuf_p = [vf32(96, 2048), vf32(104, 2048)]
    x1buf = [vf32(128, 2048), vf32(136, 2048)]
    x0buf_p = [vf32(144, 2048), vf32(168, 2048)]
    xnbuf = vbf(152, 2048)
    xnt_stage = vbf(156, 2048)
    Hbuf = vbf(96, 45056)

    d_rep_bf = vbf(184, 2048)
    sdrep = vf32(188, 2048)

    id_bf = vbf(196, 128)
    onesrow_bf = vbf(196.25, 128)[0:1, :]
    onesrow_f = vf32(196.5, 128)[0:1, :]
    sgbuf = vf32(197, HB)
    subuf = vf32(197.25, HB)
    colbuf = vf32(197.5, 96)
    statbuf = vf32(197.875, 24)
    bret_bf = vbf(198.25, 512)
    bimt_bf = vbf(199.25, 512)
    ystage = [vf32(200.25, 512), vf32(202.25, 512)]
    ln4stats = vf32(204.25, 192)
    h2buf = vbf(205, 1024)

    pA = st.enter_context(nc.psum_tensor("pA", [128, 2048], F32)).ap()
    pB = st.enter_context(nc.psum_tensor("pB", [128, 1024], F32)).ap()
    pC = st.enter_context(nc.psum_tensor("pC", [128, 512], F32)).ap()
    pD = st.enter_context(nc.psum_tensor("pD", [128, 512], F32)).ap()
    pBbf = pB.bitcast(BF16)

    sems = {k: st.enter_context(nc.semaphore(k)) for k in Ctx.KEYS}
    C = Ctx()

    p_lunc = colbuf[0:NCH, 0:1]
    p_lim = colbuf[0:NCH, 1:2]
    p_logd = colbuf[0:NCH, 2:3]
    c_delta = colbuf[0:NCH, 3:4]
    c_lamre = colbuf[0:NCH, 4:5]
    c_lnr = colbuf[0:NCH, 5:6]
    c_theta = colbuf[0:NCH, 6:7]
    c_r = colbuf[0:NCH, 7:8]
    c_are = colbuf[0:NCH, 8:9]
    c_aim = colbuf[0:NCH, 9:10]
    c_t1 = colbuf[0:NCH, 10:11]
    c_t2 = colbuf[0:NCH, 11:12]
    c_t3 = colbuf[0:NCH, 12:13]
    c_t4 = colbuf[0:NCH, 13:14]
    c_dre = colbuf[0:NCH, 14:15]
    c_dim = colbuf[0:NCH, 15:16]
    c_eps = colbuf[:, 16:17]
    c_cre = colbuf[0:NCH, 17:18]
    c_cim = colbuf[0:NCH, 18:19]
    c_sr = colbuf[0:NCH, 19:20]
    c_si = colbuf[0:NCH, 20:21]
    c_negm = colbuf[:, 21:22]
    c_rstd = colbuf[:, 22:23]
    c_sd = colbuf[:, 23:24]
    c_Tlnr = colbuf[0:NCH, 24:25]
    c_i32 = colbuf.bitcast(I32)[0:NCH, 25:26]
    c_sinth = colbuf[0:NCH, 26:27]
    c_sh = colbuf[0:NCH, 27:28]
    c_costh = colbuf[0:NCH, 28:29]
    c_thred = colbuf[0:NCH, 29:30]
    mvbuf = colbuf[:, 30:32]
    mv_p = [colbuf[:, 32:34], colbuf[:, 34:36]]
    sd_p = [colbuf[:, 36:37], colbuf[:, 37:38]]
    rstd_p = [colbuf[:, 38:39], colbuf[:, 39:40]]
    nmr_p = [colbuf[:, 40:41], colbuf[:, 41:42]]
    mvy_p = [colbuf[:, 42:44], colbuf[:, 44:46]]
    sdy_p = [colbuf[:, 46:47], colbuf[:, 47:48]]
    rsy_p = [colbuf[:, 48:49], colbuf[:, 49:50]]
    nmy_p = [colbuf[:, 50:51], colbuf[:, 51:52]]
    mvx_p = [colbuf[:, 52:54], colbuf[:, 54:56]]
    sdx_p = [colbuf[:, 56:57], colbuf[:, 57:58]]
    rsx_p = [colbuf[:, 58:59], colbuf[:, 59:60]]
    nmx_p = [colbuf[:, 60:61], colbuf[:, 61:62]]
    l4mv = [colbuf[:, 62 + 2 * t_: 64 + 2 * t_] for t_ in range(8)]
    l4rs = [colbuf[:, 78 + t_: 79 + t_] for t_ in range(8)]
    l4nm = [colbuf[:, 86 + t_: 87 + t_] for t_ in range(8)]
    p_bre = bu_re_m
    p_bim = bu_im_m
    p_cret = cosT
    p_cimt = sinT

    def Hcols(hb, tg):
        return Hbuf[:, hb * TH + tg * 512: hb * TH + (tg + 1) * 512]

    def bu_dsts(k):
        i = k % NT
        bre = (bu_re_p if k < NT else bu_re_m)[:, i * 128:(i + 1) * 128]
        bim = (bu_im_p if k < NT else bu_im_m)[:, i * 128:(i + 1) * 128]
        return bre, bim

    def xnev_dst(i, gq):
        if i < 8:
            return xnt.rearrange("p (k tt) -> p k tt", k=16)[
                :, 4 * gq:4 * (gq + 1), i * 128:(i + 1) * 128]
        return xnt_stage[:, 512 * gq:512 * (gq + 1)]

    def xnev_src(i, gq):
        s = pBbf[:, 512 * gq:512 * (gq + 1)]
        if i < 8:
            return s.rearrange("p (k tt) -> p k tt", k=4)
        return s

    def emit_sync(q):
        C.inc(q.dma_start(p_lunc, d_in["lunc"]), "d_par", 16)
        C.inc(q.dma_start(p_lim, d_in["lim"]), "d_par", 16)
        C.inc(q.dma_start(p_logd, d_in["logd"]), "d_par", 16)
        C.inc(q.dma_start(p_bre, d_in["bre"]), "d_par", 16)
        C.inc(q.dma_start(p_bim, d_in["bim"]), "d_par", 16)
        C.inc(q.dma_start(p_cret, d_in["cret"]), "d_par", 16)
        C.inc(q.dma_start(p_cimt, d_in["cimt"]), "d_par", 16)
        C.inc(q.dma_start(drow_f, d_in["drow"]), "d_par", 16)
        C.inc(q.dma_start(sdrow, d_in["sd"]), "d_par", 16)
        C.inc(q.dma_start(sgbuf[:, :], d_in["sg"]), "d_par", 16)
        C.inc(q.dma_start(subuf[:, :], d_in["su"]), "d_par", 16,
              mark="d_params")

        for j in range(2 * NT):
            src = d_in["uprev"] if j < NT else d_in["u"]
            i = j % NT
            if j < 2:
                C.wait(q, "v_bbar")
            elif j >= 3:
                C.wait(q, f"uapply_{j-3}")
            C.inc(q.dma_start(ubuf[j % 3][:, :],
                              src[i * 128:(i + 1) * 128, :]),
                  ("d_u", "d_u2", "d_u3")[j % 3], 16, mark=f"du_{j}")

        for i in range(NT):
            if i < 2:
                C.wait(q, "v_scan_done")
            else:
                C.wait(q, f"x1_{i-2}")
            C.inc(q.dma_start(x0buf_p[i % 2][:, :],
                              d_in["u"][i * 128:(i + 1) * 128, :]),
                  "d_x0" if i % 2 == 0 else "d_x02", 16, mark=f"dx0_{i}")
            if i >= 1:
                C.wait(q, f"x1_{i-1}")
                C.inc(q.dma_start(x1_spill[(i - 1) * 128:i * 128, :],
                                  x1buf[(i - 1) % 2][:, :]),
                      "d_x1w" if (i - 1) % 2 == 0 else "d_x1w2", 16,
                      mark=f"dx1w_{i-1}")
        C.wait(q, f"x1_{NT-1}")
        C.inc(q.dma_start(x1_spill[(NT - 1) * 128:NT * 128, :],
                          x1buf[(NT - 1) % 2][:, :]),
              "d_x1w" if (NT - 1) % 2 == 0 else "d_x1w2", 16,
              mark=f"dx1w_{NT-1}")

        for X in range(2):
            if X == 0:
                C.wait(q, "s1mm_0_43")
                C.inc(q.dma_start(xnt[:, :], xnt_spill[:, :]), "d_xnr", 16,
                      mark="dxnr")
            for dg in range(4):
                for ti in range(8):
                    if X == 1 and (ti < 6 or dg == 3):
                        continue
                    par = (dg * 8 + ti) % 2
                    C.wait(q, f"ys_{X}_{dg}_{ti}")
                    C.inc(q.dma_start(
                        y_spill[ti * 128:(ti + 1) * 128,
                                dg * 512:(dg + 1) * 512],
                        ystage[par][:, :]),
                        "d_yw" if par == 0 else "d_yw2", 16,
                        mark=f"dyw_{X}_{dg}_{ti}")
            if X == 0:
                C.wait(q, "s2f_0_3_7")
                C.wait(q, "dyw_0_3_6")
                C.wait(q, "dyw_0_3_7")
                for ti in range(8):
                    if ti >= 2:
                        C.wait(q, f"ln4apply_0_{ti-2}")
                    C.inc(q.dma_start(yrbuf_p[ti % 2][:, :],
                                      y_spill[ti * 128:(ti + 1) * 128, :]),
                          "d_yr" if ti % 2 == 0 else "d_yr2", 16,
                          mark=f"dyr_0_{ti}")
                    if ti >= 2:
                        C.wait(q, f"out_0_{ti-2}")
                    C.inc(q.dma_start(x1rbuf_p[ti % 2][:, :],
                                      x1_spill[ti * 128:(ti + 1) * 128, :]),
                          "d_x1r" if ti % 2 == 0 else "d_x1r2", 16,
                          mark=f"dx1r_0_{ti}")
                    C.wait(q, f"out_0_{ti}")
                    C.inc(q.dma_start(out_hbm[ti * 128:(ti + 1) * 128, :],
                                      outbuf[0][:, :]),
                          "d_out" if ti % 2 == 0 else "d_out2", 16,
                          mark=f"dout_0_{ti}")
            else:
                C.wait(q, "s2e_7")
                C.wait(q, "dyw_1_2_6")
                C.inc(q.dma_start(ybig[6][:, 0:1536],
                                  y_spill[6 * 128:7 * 128, 0:1536]),
                      "d_yr", 16, mark="dyr_1_6")
                C.wait(q, "dyw_1_2_7")
                C.inc(q.dma_start(ybig[7][:, 0:1536],
                                  y_spill[7 * 128:8 * 128, 0:1536]),
                      "d_yr2", 16, mark="dyr_1_7")
                for ti in range(8):
                    if ti == 0:
                        C.wait(q, "s2e_11")
                    elif ti == 1:
                        C.wait(q, "s2e_15")
                    else:
                        C.wait(q, f"out_1_{ti-2}")
                    C.inc(q.dma_start(
                        x1rb2[ti % 2][:, :],
                        x1_spill[(8 + ti) * 128:(9 + ti) * 128, :]),
                        "d_x1r" if ti % 2 == 0 else "d_x1r2", 16,
                        mark=f"dx1r_1_{ti}")
        C.wait(q, "dout_0_6")
        C.wait(q, "dout_1_6")
        C.wait(q, "dout_1_7")

    def emit_gpsimd(g):
        C.inc(g.iota(it_fwd[:, :], pattern=[[1, T]], base=1,
                     channel_multiplier=0,
                     allow_small_or_imprecise_dtypes=True), "g", mark="g_itf")
        C.inc(g.memset(id_bf[:, :], 1.0), "g")
        C.selfwait(g, "g")
        C.inc(g.affine_select(id_bf[:, :], id_bf[:, :], pattern=[[-1, 128]],
                              compare_op=ALU.is_equal, fill=0.0, base=0,
                              channel_multiplier=1), "g", mark="g_id")
        C.inc(g.memset(onesrow_bf[:, :], 1.0), "g")
        C.inc(g.memset(onesrow_f[:, :], 1.0), "g")
        C.inc(g.memset(c_eps, EPS_LN), "g", mark="g_setup")
        C.wait(g, "v_drep")
        for kd in range(16):
            C.inc(g.affine_select(d_rep_bf[:, kd * 128:(kd + 1) * 128],
                                  d_rep_bf[:, kd * 128:(kd + 1) * 128],
                                  pattern=[[-1, 128]],
                                  compare_op=ALU.is_equal, fill=0.0, base=0,
                                  channel_multiplier=1), "g",
                  mark=("g_ddiag" if kd == 15 else None))

        for k in range(8):
            C.wait(g, f"xnevv_{k + 8}")
            C.wait(g, f"xnevs_{k + 8}")
            C.inc(g.dma_start(
                xnt_spill.rearrange("p (kk ti tt) -> p kk ti tt",
                                    k=16, ti=8, kk=16)[:, :, k, :],
                xnt_stage.rearrange("p (kk tt) -> p kk tt", kk=16)),
                "d_xnw" if k % 2 == 0 else "d_xnw2", 16,
                mark=f"dxnw_{k}")

        for X in range(2):
            for hb in range(HB):
                par = hb % 2
                if X == 0 and hb < 2:
                    C.wait(g, "yev_15")
                else:
                    pX, phb = (X, hb - 2) if hb >= 2 else (0, HB - 2 + hb)
                    C.wait(g, f"s1mm_{pX}_{phb}")
                wkey = "d_w" if par == 0 else "d_w2"
                C.inc(g.dma_start(wgbuf[par][:, :], d_in["wg"][hb, :, :]),
                      wkey, 16)
                C.inc(g.dma_start(wubuf[par][:, :], d_in["wu"][hb, :, :]),
                      wkey, 16, mark=f"wconv_{X}_{hb}")
            for dg in range(4):
                if dg == 0:
                    if X == 0:
                        C.wait(g, "yev_15")
                    else:
                        C.wait_cnt(g, "d_out", 16 * 4)
                        C.wait_cnt(g, "d_out2", 16 * 4)
                        C.wait(g, "s2q_0_3")
                else:
                    C.wait(g, f"s2q_{X}_{dg-1}")
                C.inc(g.dma_start(
                    wd_lo[:, 0:5632],
                    d_in["wd"][dg, :, 0:5632]), "d_wda", 16,
                    mark=f"dwdla_{X}_{dg}")
                if dg == 0:
                    if X == 1:
                        C.wait(g, "s2lo_0_3")
                else:
                    C.wait(g, f"s2lo_{X}_{dg-1}")
                C.inc(g.dma_start(
                    wd_lo[:, 5632:11264],
                    d_in["wd"][dg, :, 5632:11264]), "d_wdl", 16,
                    mark=f"dwdlo_{X}_{dg}")
                if dg == 0:
                    if X == 1:
                        C.wait(g, "s2f_0_3_7")
                else:
                    C.wait(g, f"s2f_{X}_{dg-1}_7")
                C.inc(g.dma_start(
                    wd_hi[:, :],
                    d_in["wd"][dg, :, 11264:22528]), "d_wdh", 16,
                    mark=f"dwdhi_{X}_{dg}")
        for ti in (0, 2, 4, 6):
            C.wait(g, f"out_1_{ti}")
            C.inc(g.dma_start(out_hbm[(8 + ti) * 128:(9 + ti) * 128, :],
                              outb2[0][:, :]), "d_w", 16,
                  mark=f"dout_1_{ti}")

    def emit_scalar(s):
        C.wait(s, "d_params")
        C.inc(s.activation(c_delta, p_logd, AF.Exp), "s")
        C.inc(s.activation(c_t1, p_lunc, AF.Abs), "s")
        C.selfwait(s, "s")
        C.inc(s.activation(c_t2, c_t1, AF.Exp, scale=-1.0), "s")
        C.selfwait(s, "s")
        C.inc(s.activation(c_t3, c_t2, AF.Ln, bias=1.0), "s", mark="s_p1")
        C.wait(s, "v_lnr")
        C.inc(s.activation(c_r, c_lnr, AF.Exp), "s", mark="s_r")
        C.wait(s, "v_thred")
        C.inc(s.activation(c_sinth, c_thred, AF.Sin), "s")
        C.inc(s.activation(c_sh, c_thred, AF.Sin, scale=0.5), "s",
              mark="s_sincos")
        C.wait(s, "v_pred")
        C.inc(s.activation(sinT[:, :], ph_p[:, :], AF.Sin), "s")
        C.inc(s.activation(ph_q[:, :], ph_p[:, :], AF.Abs), "s",
              mark="s_abs")
        C.wait(s, "v_carg")
        C.inc(s.activation(cosT[:, :], ph_r[:, :], AF.Sin), "s",
              mark="s_tables")
        C.inc(s.activation(drow_bf, drow_f, AF.Copy), "s", mark="s_dbf")

        for sl in range(2 * NT + 2):
            j = sl
            if j < 2 * NT:
                C.wait(s, f"aggr_{j}")
                C.inc(s.activation(sd_p[j % 2], mv_p[j % 2][:, 1:2],
                                   AF.Sqrt, bias=c_eps), "s",
                      mark=f"sqrt_{j}")
            k = sl - 1
            if 0 <= k < 2 * NT:
                C.wait(s, f"lnc_{k}")
                i = k % NT
                if k >= 2:
                    C.wait(s, f"tr_{k-2}")
                C.inc(s.activation(t1ntmp[k % 2][:, :],
                                   ubuf[k % 3][:, :], AF.Identity,
                                   scale=rstd_p[k % 2], bias=nmr_p[k % 2]),
                      "s", mark=f"uapply_{k}")
            k = sl - 2
            if 0 <= k < 2 * NT:
                C.wait(s, f"tr_{k}")
                i = k % NT
                for gq in range(2, 4):
                    C.inc(s.activation(
                        t1nT_full[:, i * 2048 + 512 * gq:
                                  i * 2048 + 512 * (gq + 1)],
                        pBbf[:, 512 * gq:512 * (gq + 1)], AF.Copy), "s",
                        mark=(f"t1nTs_{k}" if gq == 3 else None))
        C.wait(s, "v_rpe")
        C.inc(s.activation(rp_v[:, :], tC[:, :], AF.Exp, scale=-1.0,
                           bias=c_Tlnr), "s", mark="s_rpow")
        for sl in range(NT + 5):
            k = sl - 1
            if 0 <= k < NT:
                C.wait(s, f"yaggr_{k}")
                C.inc(s.activation(sdy_p[k % 2], mvy_p[k % 2][:, 1:2],
                                   AF.Sqrt, bias=c_eps), "s",
                      mark=f"ysqrt_{k}")
            k = sl - 2
            if 0 <= k < NT:
                C.wait(s, f"ylnc_{k}")
                C.inc(s.activation(ybuf_p[k % 2][:, :], ybuf_p[k % 2][:, :],
                                   AF.Identity, scale=rsy_p[k % 2],
                                   bias=nmy_p[k % 2]), "s", mark=f"ln2_{k}")
            k = sl - 3
            if 0 <= k < NT:
                C.wait(s, f"x1aggr_{k}")
                C.inc(s.activation(sdx_p[k % 2], mvx_p[k % 2][:, 1:2],
                                   AF.Sqrt, bias=c_eps), "s",
                      mark=f"x1sqrt_{k}")
            k = sl - 4
            if 0 <= k < NT:
                C.wait(s, f"xlnc_{k}")
                if k >= 1:
                    C.wait(s, f"xntr_{k-1}")
                C.inc(s.activation(xnbuf[:, :], x1buf[k % 2][:, :],
                                   AF.Identity, scale=rsx_p[k % 2],
                                   bias=nmx_p[k % 2]), "s",
                      mark=f"xnapply_{k}")
            k = sl - 5
            if 0 <= k < NT:
                C.wait(s, f"xntr_{k}")
                if k >= 9:
                    C.wait(s, f"dxnw_{k-9}")
                for gq in range(2, 4):
                    C.inc(s.activation(xnev_dst(k, gq), xnev_src(k, gq),
                                       AF.Copy), "s",
                          mark=(f"xnevs_{k}" if gq == 3 else None))
        for X in range(2):
            for hb in range(HB):
                C.wait(s, f"gmm_{X}_{hb}")
                if hb >= 1 or X == 1:
                    pX, phb = (X, hb - 1) if hb >= 1 else (0, HB - 1)
                    C.wait(s, f"h_{pX}_{phb}")
                gsrc = pA if hb % 2 == 0 else pB
                for tg in range(2):
                    C.inc(s.activation(h2buf[:, tg * 512:(tg + 1) * 512],
                                       gsrc[:, tg * 512:(tg + 1) * 512],
                                       AF.Silu, scale=sgbuf[:, hb:hb + 1]),
                          "s", mark=(f"gs_{X}_{hb}" if tg == 1 else None))
            for dg in range(4):
                C.wait(s, f"dwdla_{X}_{dg}")
                C.inc(s.activation(wd_lo[:, 0:5632], wd_lo[:, 0:5632],
                                   AF.Copy, bias=-128.0), "s",
                      mark=f"wdla_{X}_{dg}")
                C.wait(s, f"dwdlo_{X}_{dg}")
                C.inc(s.activation(wd_lo[:, 5632:11264], wd_lo[:, 5632:11264],
                                   AF.Copy, bias=-128.0), "s",
                      mark=f"wdlo_{X}_{dg}")
                C.wait(s, f"dwdhi_{X}_{dg}")
                C.inc(s.activation(wd_hi[:, :], wd_hi[:, :], AF.Copy,
                                   bias=-128.0), "s", mark=f"wdhi_{X}_{dg}")
            for ti in range(8):
                C.wait(s, f"l4aggr_{X}_{ti}")
                C.inc(s.activation(l4nm[ti], l4mv[ti][:, 1:2], AF.Sqrt,
                                   bias=c_eps), "s", mark=f"l4sqrt_{X}_{ti}")
            if X == 1:
                for ti in (0, 2, 4, 6):
                    C.wait(s, f"l4c_1_{ti}")
                    C.wait(s, f"ys_1_3_{ti}")
                    if ti == 6:
                        C.wait(s, "dyr_1_6")
                    C.inc(s.activation(ybig[ti][:, :], ybig[ti][:, :],
                                       AF.Identity, scale=l4rs[ti],
                                       bias=l4nm[ti]), "s",
                          mark=f"l4app_{ti}")
                    if ti >= 2:
                        to = ti - 1
                        C.wait(s, f"out_1_{to}")
                        C.inc(s.dma_start(
                            out_hbm[(8 + to) * 128:(9 + to) * 128, :],
                            outb2[1][:, :]), "d_out2", 16,
                            mark=f"dout_1_{to}")
                for to in (7,):
                    C.wait(s, f"out_1_{to}")
                    C.inc(s.dma_start(
                        out_hbm[(8 + to) * 128:(9 + to) * 128, :],
                        outb2[1][:, :]), "d_out2", 16,
                        mark=f"dout_1_{to}")

    def emit_tensor(t):
        C.wait(t, "g_id")
        C.wait(t, "v_bbar")
        for kt in range(16):
            C.inc(t.transpose(pBbf[:, kt * 32:(kt + 1) * 32],
                              bbar_re[:, kt * 128:(kt + 1) * 128],
                              id_bf[0:NCH, 0:NCH]), "t")
        for kt in range(16):
            C.inc(t.transpose(pBbf[:, 512 + kt * 32:512 + (kt + 1) * 32],
                              bbar_im[:, kt * 128:(kt + 1) * 128],
                              id_bf[0:NCH, 0:NCH]), "t",
                  mark=("t_btr" if kt == 15 else None))
        C.wait(t, "s_dbf")
        for dg in range(4):
            C.inc(t.matmul(pA[:, dg * 512:(dg + 1) * 512], onesrow_bf,
                           drow_bf[:, dg * 512:(dg + 1) * 512],
                           start=True, stop=True), "t",
                  mark=("t_drep" if dg == 3 else None))
        C.wait(t, "v_drep")
        for dg in range(4):
            C.inc(t.matmul(pA[:, dg * 512:(dg + 1) * 512], onesrow_f,
                           sdrow[:, dg * 512:(dg + 1) * 512],
                           start=True, stop=True), "t",
                  mark=("t_sdrep" if dg == 3 else None))

        for j in range(2 * NT):
            C.wait(t, f"uapply_{j}")
            if j >= 1:
                C.wait(t, f"t1nTv_{j-1}")
                C.wait(t, f"t1nTs_{j-1}")
            i = j % NT
            tsrc = t1ntmp[j % 2]
            for kt in range(16):
                C.inc(t.transpose(pBbf[:, kt * 128:(kt + 1) * 128],
                                  tsrc[:, kt * 128:(kt + 1) * 128],
                                  id_bf[:, :]), "t",
                      mark=(f"tr_{j}" if kt == 15 else None))
            C.wait(t, f"t1nTv_{j}")
            C.wait(t, f"t1nTs_{j}")
            if j >= 2:
                C.wait(t, f"buev_{j-2}")
            pb = pC if j % 2 == 0 else pD
            for kt in range(16):
                C.inc(t.matmul(pb[0:NCH, 0:128],
                               bret_bf[:, kt * 32:(kt + 1) * 32],
                               t1nT_full[:, i * 2048 + kt * 128:
                                         i * 2048 + (kt + 1) * 128],
                               start=(kt == 0), stop=(kt == 15)), "t")
            for kt in range(16):
                C.inc(t.matmul(pb[0:NCH, 128:256],
                               bimt_bf[:, kt * 32:(kt + 1) * 32],
                               t1nT_full[:, i * 2048 + kt * 128:
                                         i * 2048 + (kt + 1) * 128],
                               start=(kt == 0), stop=(kt == 15)), "t",
                      mark=(f"bu_{j}" if kt == 15 else None))

        def xntr_block(k):
            C.wait(t, f"xnapply_{k}")
            if k == 0:
                C.wait(t, f"buev_{2*NT-1}")
            else:
                C.wait(t, f"xnevv_{k-1}")
                C.wait(t, f"xnevs_{k-1}")
            for kt in range(16):
                C.inc(t.transpose(pBbf[:, kt * 128:(kt + 1) * 128],
                                  xnbuf[:, kt * 128:(kt + 1) * 128],
                                  id_bf[:, :]), "t",
                      mark=(f"xntr_{k}" if kt == 15 else None))

        C.wait(t, "v_scan_done")
        C.wait(t, "v_sdrep")
        for i in range(NT):
            if i >= 1:
                C.wait(t, f"yev_{i-1}")
            if i == 0:
                C.wait(t, "g_ddiag")
            for dg in range(4):
                C.inc(t.matmul(pA[:, dg * 512:(dg + 1) * 512],
                               xs_re[:, i * 128:(i + 1) * 128],
                               c2re_v[:, dg * 512:(dg + 1) * 512],
                               start=True, stop=False), "t")
                C.inc(t.matmul(pA[:, dg * 512:(dg + 1) * 512],
                               xs_im[:, i * 128:(i + 1) * 128],
                               c2im_v[:, dg * 512:(dg + 1) * 512],
                               start=False, stop=False), "t")
            for kd in range(16):
                C.inc(t.matmul(pA[:, kd * 128:(kd + 1) * 128],
                               t1nT_full[:, i * 2048 + kd * 128:
                                         i * 2048 + (kd + 1) * 128],
                               d_rep_bf[:, kd * 128:(kd + 1) * 128],
                               start=False, stop=(kd % 4 == 3)), "t",
                      mark=(f"ymm_{i}" if kd == 15 else None))
            if i >= 5:
                xntr_block(i - 5)
        for k in range(NT - 5, NT):
            xntr_block(k)

        PS2 = [pA[:, 0:512], pA[:, 512:1024], pA[:, 1024:1536],
               pA[:, 1536:2048], pB[:, 0:512], pB[:, 512:1024], pC, pD]
        for X in range(2):
            for hb in range(HB):
                par = hb % 2
                C.wait(t, f"wconv_{X}_{hb}")
                if X == 0 and hb == 0:
                    C.wait(t, "yev_15")
                if X == 0 and hb == 1:
                    C.wait(t, "xnevv_15")
                    C.wait(t, "xnevs_15")
                if X == 1 and hb == 0:
                    C.wait(t, "dxnr")
                    C.wait(t, "gs_0_42")
                    C.wait(t, "uev_0_42")
                    C.wait(t, "ys_0_3_3")
                if X == 1 and hb == 1:
                    C.wait(t, "gs_0_43")
                    C.wait(t, "uev_0_43")
                    C.wait(t, "ys_0_3_7")
                if hb >= 2:
                    C.wait(t, f"gs_{X}_{hb-2}")
                    C.wait(t, f"uev_{X}_{hb-2}")
                if par == 0:
                    gps = [pA[:, 0:512], pA[:, 512:1024]]
                    ups = [pA[:, 1024:1536], pA[:, 1536:2048]]
                else:
                    gps = [pB[:, 0:512], pB[:, 512:1024]]
                    ups = [pC[:, :], pD[:, :]]
                for tg in range(2):
                    for kt in range(16):
                        C.inc(t.matmul(
                            gps[tg],
                            wgbuf[par][:, kt * 128:(kt + 1) * 128],
                            xnt[:, kt * TH + tg * 512:
                                kt * TH + (tg + 1) * 512],
                            start=(kt == 0), stop=(kt == 15)), "t",
                            mark=(f"gmm_{X}_{hb}"
                                  if tg == 1 and kt == 15 else None))
                for tg in range(2):
                    for kt in range(16):
                        C.inc(t.matmul(
                            ups[tg],
                            wubuf[par][:, kt * 128:(kt + 1) * 128],
                            xnt[:, kt * TH + tg * 512:
                                kt * TH + (tg + 1) * 512],
                            start=(kt == 0), stop=(kt == 15)), "t",
                            mark=(f"s1mm_{X}_{hb}"
                                  if tg == 1 and kt == 15 else None))
            for dg in range(4):
                for hb in range(HB):
                    if hb == 0:
                        C.wait(t, f"wdla_{X}_{dg}")
                    if hb == 11:
                        C.wait(t, f"wdlo_{X}_{dg}")
                    if hb == 22:
                        C.wait(t, f"wdhi_{X}_{dg}")
                    if dg == 0:
                        C.wait(t, f"h_{X}_{hb}")
                    wsrc = wd_lo if hb < 22 else wd_hi
                    wof = hb if hb < 22 else hb - 22
                    for ti in range(8):
                        if hb == 0:
                            if dg == 0:
                                C.wait(t, f"gs_{X}_{HB-1}")
                                C.wait(t, f"uev_{X}_{HB-1}")
                            else:
                                C.wait(t, f"ys_{X}_{dg-1}_{ti}")
                        C.inc(t.matmul(
                            PS2[ti],
                            Hbuf[:, hb * TH + ti * 128:
                                 hb * TH + (ti + 1) * 128],
                            wsrc[:, wof * 512:(wof + 1) * 512],
                            start=(hb == 0), stop=(hb == HB - 1)), "t",
                            mark=(f"s2f_{X}_{dg}_{ti}" if hb == HB - 1
                                  else (f"s2lo_{X}_{dg}"
                                        if hb == 21 and ti == 7 else
                                        (f"s2q_{X}_{dg}"
                                         if hb == 10 and ti == 7 else
                                         (f"s2e_{hb}"
                                          if X == 1 and dg == 3 and ti == 7
                                          and hb in (7, 11, 15, 19, 23)
                                          else None)))))

    def emit_vector(v):
        C.wait(v, "d_params")
        C.wait(v, "g_setup")
        C.inc(v.tensor_scalar(c2re_v[:, :], p_cret, 2.0, None, ALU.mult),
              "v")
        C.inc(v.tensor_scalar(c2im_v[:, :], p_cimt, -2.0, None, ALU.mult),
              "v", mark="v_c2")
        C.wait(v, "s_p1")
        C.inc(v.tensor_scalar(c_t1, p_lunc, 0.0, None, ALU.max), "v")
        C.selfwait(v, "v")
        C.inc(v.tensor_tensor(c_t1, c_t1, c_t3, ALU.add), "v")
        C.selfwait(v, "v")
        C.inc(v.tensor_scalar(c_lamre, c_t1, 0.0101, -1.0, ALU.add,
                              ALU.mult), "v")
        C.selfwait(v, "v")
        C.inc(v.tensor_tensor(c_lnr, c_lamre, c_delta, ALU.mult), "v",
              mark="v_lnr")
        C.selfwait(v, "v")
        C.inc(v.tensor_scalar(c_Tlnr, c_lnr, float(T), None, ALU.mult), "v")
        C.inc(v.tensor_tensor(c_theta, p_lim, c_delta, ALU.mult), "v")
        C.selfwait(v, "v")
        C.inc(v.tensor_scalar(c_t2, c_theta, 1.0 / TWO_PI, None, ALU.mult),
              "v")
        C.selfwait(v, "v")
        C.inc(v.tensor_copy(c_i32, c_t2), "v")
        C.selfwait(v, "v")
        C.inc(v.tensor_copy(c_t2, c_i32), "v")
        C.selfwait(v, "v")
        C.inc(v.scalar_tensor_tensor(c_thred, c_t2, -TWO_PI, c_theta,
                                     ALU.mult, ALU.add), "v")
        C.selfwait(v, "v")
        C.inc(v.tensor_scalar(c_t2, c_thred, PI, None, ALU.is_gt), "v")
        C.selfwait(v, "v")
        C.inc(v.scalar_tensor_tensor(c_thred, c_t2, -TWO_PI, c_thred,
                                     ALU.mult, ALU.add), "v")
        C.selfwait(v, "v")
        C.inc(v.tensor_scalar(c_t2, c_thred, -PI, None, ALU.is_lt), "v")
        C.selfwait(v, "v")
        C.inc(v.scalar_tensor_tensor(c_thred, c_t2, TWO_PI, c_thred,
                                     ALU.mult, ALU.add), "v", mark="v_thred")
        C.wait(v, "s_sincos")
        C.inc(v.tensor_tensor(c_t2, c_sh, c_sh, ALU.mult), "v")
        C.selfwait(v, "v")
        C.inc(v.tensor_scalar(c_costh, c_t2, -2.0, 1.0, ALU.mult, ALU.add),
              "v")
        C.wait(v, "s_r")
        C.selfwait(v, "v")
        C.inc(v.tensor_tensor(c_are, c_r, c_costh, ALU.mult), "v")
        C.inc(v.tensor_tensor(c_aim, c_r, c_sinth, ALU.mult), "v")
        C.selfwait(v, "v")
        C.inc(v.tensor_scalar(c_t1, c_are, -1.0, None, ALU.add), "v")
        C.inc(v.tensor_tensor(c_t2, c_lamre, c_lamre, ALU.mult), "v")
        C.inc(v.tensor_tensor(c_t3, p_lim, p_lim, ALU.mult), "v")
        C.selfwait(v, "v")
        C.inc(v.tensor_tensor(c_t2, c_t2, c_t3, ALU.add), "v")
        C.selfwait(v, "v")
        C.inc(v.reciprocal(c_t2, c_t2), "v")
        C.selfwait(v, "v")
        C.inc(v.tensor_tensor(c_t3, c_t1, c_lamre, ALU.mult), "v")
        C.inc(v.tensor_tensor(c_t4, c_aim, p_lim, ALU.mult), "v")
        C.selfwait(v, "v")
        C.inc(v.tensor_tensor(c_t3, c_t3, c_t4, ALU.add), "v")
        C.selfwait(v, "v")
        C.inc(v.tensor_tensor(c_dre, c_t3, c_t2, ALU.mult), "v")
        C.selfwait(v, "v")
        C.inc(v.tensor_tensor(c_t3, c_aim, c_lamre, ALU.mult), "v")
        C.inc(v.tensor_tensor(c_t4, c_t1, p_lim, ALU.mult), "v")
        C.selfwait(v, "v")
        C.inc(v.tensor_tensor(c_t3, c_t3, c_t4, ALU.subtract), "v")
        C.selfwait(v, "v")
        C.inc(v.tensor_tensor(c_dim, c_t3, c_t2, ALU.mult), "v")
        C.selfwait(v, "v")
        C.inc(v.tensor_scalar(bbar_sA[:, :], p_bre, c_dre, None, ALU.mult),
              "v")
        C.inc(v.tensor_scalar(bbar_sB[:, :], p_bim, c_dim, None, ALU.mult),
              "v")
        C.selfwait(v, "v")
        C.inc(v.tensor_tensor(bbar_re[:, :], bbar_sA[:, :], bbar_sB[:, :],
                              ALU.subtract), "v")
        C.selfwait(v, "v")
        C.inc(v.tensor_scalar(bbar_sA[:, :], p_bim, c_dre, None, ALU.mult),
              "v")
        C.inc(v.tensor_scalar(bbar_sB[:, :], p_bre, c_dim, None, ALU.mult),
              "v")
        C.selfwait(v, "v")
        C.inc(v.tensor_tensor(bbar_im[:, :], bbar_sA[:, :], bbar_sB[:, :],
                              ALU.add), "v", mark="v_bbar")
        C.inc(v.tensor_scalar(ph_p[:, :], it_fwd[:, :], c_theta, None,
                              ALU.mult), "v")
        C.selfwait(v, "v")
        C.inc(v.tensor_scalar(ph_q[:, :], ph_p[:, :], 1.0 / TWO_PI, None,
                              ALU.mult), "v")
        C.selfwait(v, "v")
        C.inc(v.tensor_copy(ph_qi, ph_q[:, :]), "v")
        C.selfwait(v, "v")
        C.inc(v.tensor_copy(ph_r[:, :], ph_qi), "v")
        C.selfwait(v, "v")
        C.inc(v.scalar_tensor_tensor(ph_p[:, :], ph_r[:, :], -TWO_PI,
                                     ph_p[:, :], ALU.mult, ALU.add), "v")
        C.selfwait(v, "v")
        C.inc(v.tensor_scalar(ph_r[:, :], ph_p[:, :], PI, None, ALU.is_gt),
              "v")
        C.selfwait(v, "v")
        C.inc(v.scalar_tensor_tensor(ph_p[:, :], ph_r[:, :], -TWO_PI,
                                     ph_p[:, :], ALU.mult, ALU.add), "v")
        C.selfwait(v, "v")
        C.inc(v.tensor_scalar(ph_r[:, :], ph_p[:, :], -PI, None, ALU.is_lt),
              "v")
        C.selfwait(v, "v")
        C.inc(v.scalar_tensor_tensor(ph_p[:, :], ph_r[:, :], TWO_PI,
                                     ph_p[:, :], ALU.mult, ALU.add), "v",
              mark="v_pred")
        C.wait(v, "s_abs")
        C.selfwait(v, "v")
        C.inc(v.tensor_scalar(ph_r[:, :], ph_q[:, :], -1.0, PI / 2.0,
                              ALU.mult, ALU.add), "v", mark="v_carg")
        C.wait(v, "t_btr")
        C.inc(v.tensor_copy(bret_bf[:, :], pBbf[:, 0:512]), "v")
        C.inc(v.tensor_copy(bimt_bf[:, :], pBbf[:, 512:1024]), "v",
              mark="v_btrev")
        C.wait(v, "t_drep")
        C.inc(v.tensor_copy(d_rep_bf[:, :], pA[:, :]), "v", mark="v_drep")
        C.wait(v, "t_sdrep")
        C.inc(v.tensor_copy(sdrep[:, :], pA[:, :]), "v", mark="v_sdrep")

        NTOT = 2 * NT
        for sl in range(NTOT + 3):
            j = sl
            if j < NTOT:
                C.wait(v, f"du_{j}")
                C.selfwait(v, "v")
                for gq in range(4):
                    C.inc(v.bn_stats(statbuf[:, 6 * gq:6 * (gq + 1)],
                                     ubuf[j % 3][:, 512 * gq:512 * (gq + 1)]),
                          "v")
                C.selfwait(v, "v")
                C.inc(v.bn_aggr(mv_p[j % 2], statbuf[:, :]), "v",
                      mark=f"aggr_{j}")
            k = sl - 1
            if 0 <= k < NTOT:
                C.wait(v, f"sqrt_{k}")
                C.selfwait(v, "v")
                C.inc(v.reciprocal(rstd_p[k % 2], sd_p[k % 2]), "v")
                C.inc(v.tensor_scalar(c_negm, mv_p[k % 2][:, 0:1], -1.0,
                                      None, ALU.mult), "v")
                C.selfwait(v, "v")
                C.inc(v.tensor_tensor(nmr_p[k % 2], c_negm, rstd_p[k % 2],
                                      ALU.mult), "v", mark=f"lnc_{k}")
            k = sl - 2
            if 0 <= k < NTOT:
                C.wait(v, f"tr_{k}")
                i = k % NT
                for gq in range(2):
                    C.inc(v.tensor_copy(
                        t1nT_full[:, i * 2048 + 512 * gq:
                                  i * 2048 + 512 * (gq + 1)],
                        pBbf[:, 512 * gq:512 * (gq + 1)]), "v",
                        mark=(f"t1nTv_{k}" if gq == 1 else None))
            k = sl - 3
            if 0 <= k < NTOT:
                C.wait(v, f"bu_{k}")
                pb = pC if k % 2 == 0 else pD
                bre_dst, bim_dst = bu_dsts(k)
                C.inc(v.tensor_copy(bre_dst, pb[0:NCH, 0:128]), "v")
                C.inc(v.tensor_copy(bim_dst, pb[0:NCH, 128:256]), "v",
                      mark=f"buev_{k}")

        C.inc(v.tensor_scalar(tC[:, :], it_fwd[:, :], c_lnr, None, ALU.mult),
              "v", mark="v_rpe")
        C.wait(v, "s_rpow")
        C.inc(v.tensor_tensor(tA[:, :], cosT[:, :], rp_v[:, :], ALU.mult),
              "v")
        C.inc(v.tensor_tensor(tB[:, :], sinT[:, :], rp_v[:, :], ALU.mult),
              "v")
        C.selfwait(v, "v")
        C.inc(v.tensor_tensor(tC[:, :], tA[:, :], bu_re_p[:, :], ALU.mult),
              "v")
        C.selfwait(v, "v")
        C.inc(v.reduce_sum(c_t1, tC[:, :], axis=AX.X), "v")
        C.selfwait(v, "v")
        C.inc(v.tensor_tensor(tC[:, :], tB[:, :], bu_im_p[:, :], ALU.mult),
              "v")
        C.selfwait(v, "v")
        C.inc(v.reduce_sum(c_t2, tC[:, :], axis=AX.X), "v")
        C.selfwait(v, "v")
        C.inc(v.tensor_tensor(tC[:, :], tA[:, :], bu_im_p[:, :], ALU.mult),
              "v")
        C.selfwait(v, "v")
        C.inc(v.reduce_sum(c_t3, tC[:, :], axis=AX.X), "v")
        C.selfwait(v, "v")
        C.inc(v.tensor_tensor(tC[:, :], tB[:, :], bu_re_p[:, :], ALU.mult),
              "v")
        C.selfwait(v, "v")
        C.inc(v.reduce_sum(c_t4, tC[:, :], axis=AX.X), "v")
        C.selfwait(v, "v")
        C.inc(v.tensor_tensor(c_sr, c_t1, c_t2, ALU.add), "v")
        C.inc(v.tensor_tensor(c_si, c_t3, c_t4, ALU.subtract), "v")
        C.selfwait(v, "v")
        C.inc(v.tensor_tensor(c_t1, cosT[:, T - 1:T], c_sr, ALU.mult), "v")
        C.inc(v.tensor_tensor(c_t2, sinT[:, T - 1:T], c_si, ALU.mult), "v")
        C.inc(v.tensor_tensor(c_t3, sinT[:, T - 1:T], c_sr, ALU.mult), "v")
        C.inc(v.tensor_tensor(c_t4, cosT[:, T - 1:T], c_si, ALU.mult), "v")
        C.selfwait(v, "v")
        C.inc(v.tensor_tensor(c_cre, c_t1, c_t2, ALU.subtract), "v")
        C.inc(v.tensor_tensor(c_cim, c_t3, c_t4, ALU.add), "v")
        C.selfwait(v, "v")
        C.inc(v.tensor_tensor(tA[:, :], cosT[:, :], bu_re_m[:, :], ALU.mult),
              "v")
        C.inc(v.tensor_tensor(tB[:, :], sinT[:, :], bu_im_m[:, :], ALU.mult),
              "v")
        C.selfwait(v, "v")
        C.inc(v.tensor_tensor(bt_re[:, :], tA[:, :], tB[:, :], ALU.add), "v")
        C.selfwait(v, "v")
        C.inc(v.tensor_tensor(tA[:, :], cosT[:, :], bu_im_m[:, :], ALU.mult),
              "v")
        C.inc(v.tensor_tensor(tB[:, :], sinT[:, :], bu_re_m[:, :], ALU.mult),
              "v")
        C.selfwait(v, "v")
        C.inc(v.tensor_tensor(bt_im[:, :], tA[:, :], tB[:, :], ALU.subtract),
              "v")
        C.inc(v.memset(tC[:, :], 1.0), "v")
        C.selfwait(v, "v")
        C.inc(v.tensor_scalar(tC[:, :], tC[:, :], c_r, None, ALU.mult), "v")
        C.selfwait(v, "v")
        C.inc(v.tensor_tensor_scan(tA[:, :], tC[:, :], bt_re[:, :], c_cre,
                                   ALU.mult, ALU.add), "v")
        C.inc(v.tensor_tensor_scan(tB[:, :], tC[:, :], bt_im[:, :], c_cim,
                                   ALU.mult, ALU.add), "v")
        C.selfwait(v, "v")
        C.inc(v.tensor_tensor(tC[:, :], cosT[:, :], tA[:, :], ALU.mult), "v")
        C.inc(v.tensor_tensor(bt_re[:, :], sinT[:, :], tB[:, :], ALU.mult),
              "v")
        C.selfwait(v, "v")
        C.inc(v.tensor_tensor(xs_re[:, :], tC[:, :], bt_re[:, :],
                              ALU.subtract), "v")
        C.selfwait(v, "v")
        C.inc(v.tensor_tensor(tC[:, :], cosT[:, :], tB[:, :], ALU.mult), "v")
        C.inc(v.tensor_tensor(bt_im[:, :], sinT[:, :], tA[:, :], ALU.mult),
              "v")
        C.selfwait(v, "v")
        C.inc(v.tensor_tensor(xs_im[:, :], tC[:, :], bt_im[:, :], ALU.add),
              "v", mark="v_scan_done")

        for sl in range(NT + 5):
            k = sl - 2
            if 0 <= k < NT:
                C.wait(v, f"ysqrt_{k}")
                C.selfwait(v, "v")
                C.inc(v.reciprocal(rsy_p[k % 2], sdy_p[k % 2]), "v")
                C.inc(v.tensor_scalar(c_negm, mvy_p[k % 2][:, 0:1], -1.0,
                                      None, ALU.mult), "v")
                C.selfwait(v, "v")
                C.inc(v.tensor_tensor(nmy_p[k % 2], c_negm, rsy_p[k % 2],
                                      ALU.mult), "v", mark=f"ylnc_{k}")
            k = sl - 4
            if 0 <= k < NT:
                C.wait(v, f"x1sqrt_{k}")
                C.selfwait(v, "v")
                C.inc(v.reciprocal(rsx_p[k % 2], sdx_p[k % 2]), "v")
                C.inc(v.tensor_scalar(c_negm, mvx_p[k % 2][:, 0:1], -1.0,
                                      None, ALU.mult), "v")
                C.selfwait(v, "v")
                C.inc(v.tensor_tensor(nmx_p[k % 2], c_negm, rsx_p[k % 2],
                                      ALU.mult), "v", mark=f"xlnc_{k}")
            k = sl - 3
            if 0 <= k < NT:
                C.wait(v, f"ln2_{k}")
                C.wait(v, f"dx0_{k}")
                if k >= 2:
                    C.wait(v, f"dx1w_{k-2}")
                    C.wait(v, f"xnapply_{k-2}")
                C.selfwait(v, "v")
                C.inc(v.tensor_tensor(x1buf[k % 2][:, :],
                                      ybuf_p[k % 2][:, :],
                                      x0buf_p[k % 2][:, :],
                                      ALU.add), "v", mark=f"x1_{k}")
                C.selfwait(v, "v")
                for gq in range(4):
                    C.inc(v.bn_stats(
                        statbuf[:, 6 * gq:6 * (gq + 1)],
                        x1buf[k % 2][:, 512 * gq:512 * (gq + 1)]), "v")
                C.selfwait(v, "v")
                C.inc(v.bn_aggr(mvx_p[k % 2], statbuf[:, :]), "v",
                      mark=f"x1aggr_{k}")
            k = sl - 1
            if 0 <= k < NT:
                C.wait(v, f"ymm_{k}")
                if k >= 2:
                    C.wait(v, f"ln2_{k-2}")
                C.selfwait(v, "v")
                for gq in range(4):
                    C.inc(v.tensor_copy(
                        ybuf_p[k % 2][:, 512 * gq:512 * (gq + 1)],
                        pA[:, 512 * gq:512 * (gq + 1)]), "v",
                        mark=(f"yev_{k}" if gq == 3 else None))
                C.selfwait(v, "v")
                for gq in range(4):
                    C.inc(v.bn_stats(
                        statbuf[:, 6 * gq:6 * (gq + 1)],
                        ybuf_p[k % 2][:, 512 * gq:512 * (gq + 1)]), "v")
                C.selfwait(v, "v")
                C.inc(v.bn_aggr(mvy_p[k % 2], statbuf[:, :]), "v",
                      mark=f"yaggr_{k}")
            k = sl - 5
            if 0 <= k < NT:
                C.wait(v, f"xntr_{k}")
                if k >= 9:
                    C.wait(v, f"dxnw_{k-9}")
                for gq in range(2):
                    C.inc(v.tensor_copy(xnev_dst(k, gq), xnev_src(k, gq)),
                          "v", mark=(f"xnevv_{k}" if gq == 1 else None))

        def tail_tile(X, ti):
            C.wait(v, f"dyr_{X}_{ti}")
            C.selfwait(v, "v")
            C.inc(v.tensor_scalar(yrbuf_p[ti % 2][:, :],
                                  yrbuf_p[ti % 2][:, :], l4nm[ti],
                                  l4rs[ti], ALU.add, ALU.mult), "v",
                  mark=f"ln4apply_{X}_{ti}")
            C.wait(v, f"dx1r_{X}_{ti}")
            gk = X * 8 + ti
            if gk >= 1:
                pgk = gk - 1
                C.wait(v, f"dout_{pgk // 8}_{pgk % 8}")
            C.selfwait(v, "v")
            C.inc(v.tensor_tensor(outbuf[0][:, :], yrbuf_p[ti % 2][:, :],
                                  x1rbuf_p[ti % 2][:, :], ALU.add), "v",
                  mark=f"out_{X}_{ti}")

        for X in range(2):
            for hb in range(HB):
                if X == 0 and hb == 0:
                    C.wait(v, "dxnw_6")
                    C.wait(v, "dxnw_7")
                    C.wait(v, f"dx1w_{NT-2}")
                    C.wait(v, f"dx1w_{NT-1}")
                if X == 1 and hb % 5 == 4 and hb // 5 < 8:
                    tail_tile(0, hb // 5)
                C.wait(v, f"s1mm_{X}_{hb}")
                C.selfwait(v, "v")
                if hb % 2 == 0:
                    ups = [pA[:, 1024:1536], pA[:, 1536:2048]]
                else:
                    ups = [pC[:, :], pD[:, :]]
                for tg in range(2):
                    C.inc(v.tensor_scalar(
                        us_s1[:, tg * 512:(tg + 1) * 512], ups[tg],
                        subuf[:, hb:hb + 1], None, ALU.mult), "v",
                        mark=(f"uev_{X}_{hb}" if tg == 1 else None))
                C.wait(v, f"gs_{X}_{hb}")
                C.selfwait(v, "v")
                for tg in range(2):
                    C.inc(v.tensor_tensor(
                        Hcols(hb, tg), h2buf[:, tg * 512:(tg + 1) * 512],
                        us_s1[:, tg * 512:(tg + 1) * 512], ALU.mult), "v",
                        mark=(f"h_{X}_{hb}" if tg == 1 else None))
            PS2v = [pA[:, 0:512], pA[:, 512:1024], pA[:, 1024:1536],
                    pA[:, 1536:2048], pB[:, 0:512], pB[:, 512:1024],
                    pC[:, :], pD[:, :]]
            for dg in range(4):
                for ti in range(8):
                    par = (dg * 8 + ti) % 2
                    C.wait(v, f"s2f_{X}_{dg}_{ti}")
                    if X == 0:
                        k = dg * 8 + ti
                        if k >= 2:
                            pk = k - 2
                            C.wait(v, f"dyw_{X}_{pk // 8}_{pk % 8}")
                        dst = ystage[par][:, :]
                    elif ti < 6 or dg == 3:
                        dst = ybig[ti][:, dg * 512:(dg + 1) * 512]
                    else:
                        if dg == 0:
                            C.wait(v, f"dyw_0_3_{ti}")
                        else:
                            C.wait(v, f"dyw_1_{dg-1}_{ti}")
                        dst = ystage[ti - 6][:, :]
                    C.inc(v.tensor_tensor(dst, PS2v[ti],
                                          sdrep[:, dg * 512:(dg + 1) * 512],
                                          ALU.mult), "v")
                    C.selfwait(v, "v")
                    C.inc(v.bn_stats(
                        ln4stats[:, ti * 24 + dg * 6:ti * 24 + dg * 6 + 6],
                        dst), "v", mark=f"ys_{X}_{dg}_{ti}")
            C.selfwait(v, "v")
            for ti in range(8):
                C.inc(v.bn_aggr(l4mv[ti], ln4stats[:, ti * 24:(ti + 1) * 24]),
                      "v", mark=f"l4aggr_{X}_{ti}")
            for ti in range(8):
                C.wait(v, f"l4sqrt_{X}_{ti}")
                C.selfwait(v, "v")
                C.inc(v.reciprocal(l4rs[ti], l4nm[ti]), "v")
                C.inc(v.tensor_scalar(c_negm, l4mv[ti][:, 0:1], -1.0, None,
                                      ALU.mult), "v")
                C.selfwait(v, "v")
                C.inc(v.tensor_tensor(l4nm[ti], c_negm, l4rs[ti], ALU.mult),
                      "v", mark=f"l4c_{X}_{ti}")
            if X == 1:
                for ti in range(8):
                    if ti == 0:
                        C.wait(v, "s2e_19")
                    if ti == 1:
                        C.wait(v, "s2e_23")
                    if ti % 2 == 1:
                        if ti == 7:
                            C.wait(v, "dyr_1_7")
                        C.selfwait(v, "v")
                        C.inc(v.tensor_scalar(ybig[ti][:, :], ybig[ti][:, :],
                                              l4nm[ti], l4rs[ti], ALU.add,
                                              ALU.mult), "v")
                    else:
                        C.wait(v, f"l4app_{ti}")
                    C.wait(v, f"dx1r_1_{ti}")
                    if ti >= 2:
                        C.wait(v, f"dout_1_{ti-2}")
                    C.selfwait(v, "v")
                    C.inc(v.tensor_tensor(outb2[ti % 2][:, :],
                                          ybig[ti][:, :],
                                          x1rb2[ti % 2][:, :], ALU.add),
                          "v", mark=f"out_1_{ti}")

    def run_pass(pn):
        C.begin(pn, sems if pn == 2 else None)
        if pn == 1:
            emit_sync(_FakeEng())
            emit_gpsimd(_FakeEng())
            emit_scalar(_FakeEng())
            emit_vector(_FakeEng())
            emit_tensor(_FakeEng())
            return
        with nc.Block() as block:
            @block.sync
            def _(q):
                emit_sync(q)

            @block.gpsimd
            def _(g):
                emit_gpsimd(g)

            @block.scalar
            def _(s):
                emit_scalar(s)

            @block.vector
            def _(v):
                emit_vector(v)

            @block.tensor
            def _(t):
                emit_tensor(t)

    run_pass(1)
    cnt1 = dict(C.cnt)
    run_pass(2)
    assert C.cnt == cnt1, (C.cnt, cnt1)
    st.close()
    return nc


def _u8view(wq_int32):
    a = np.ascontiguousarray(np.asarray(wq_int32))
    assert a.dtype == np.int32
    return np.ascontiguousarray(a.view(np.uint8)[..., ::4])


def _stage_w_hd(wq):
    a = _u8view(wq).reshape(HB, 128, 16, 128)
    a = np.ascontiguousarray(a.transpose(0, 3, 2, 1))
    return a.reshape(HB, 128, 2048)


def _stage_w_dh(wq):
    a = np.ascontiguousarray(_u8view(wq).T)
    a = a.reshape(HB, 128, 4, 512).transpose(2, 1, 0, 3)
    return np.ascontiguousarray(a).reshape(4, 128, HB * 512)


def kernel(L0_attn_u, L0_attn_Lambda_unconstrained_re, L0_attn_Lambda_im,
           L0_attn_B_re, L0_attn_B_im, L0_attn_C_re, L0_attn_C_im,
           L0_attn_D, L0_attn_log_Delta, L0_ffn_wq_gate, L0_ffn_s_gate,
           L0_ffn_wq_up, L0_ffn_s_up, L0_ffn_wq_down, L0_ffn_s_down):
    u = np.ascontiguousarray(np.asarray(L0_attn_u, dtype=np.float32))
    wg = _stage_w_hd(L0_ffn_wq_gate)
    wu = _stage_w_hd(L0_ffn_wq_up)
    wd = _stage_w_dh(L0_ffn_wq_down)
    sg = np.ascontiguousarray(
        np.asarray(L0_ffn_s_gate, np.float32).reshape(HB, 128).T)
    su = np.ascontiguousarray(
        np.asarray(L0_ffn_s_up, np.float32).reshape(HB, 128).T)
    sd = np.ascontiguousarray(np.asarray(L0_ffn_s_down,
                                         np.float32).reshape(1, D))
    bre = np.ascontiguousarray(np.asarray(L0_attn_B_re, np.float32))
    bim = np.ascontiguousarray(np.asarray(L0_attn_B_im, np.float32))
    cret = np.ascontiguousarray(np.asarray(L0_attn_C_re, np.float32).T)
    cimt = np.ascontiguousarray(np.asarray(L0_attn_C_im, np.float32).T)
    drow = np.ascontiguousarray(np.asarray(L0_attn_D,
                                           np.float32).reshape(1, D))
    lunc = np.ascontiguousarray(
        np.asarray(L0_attn_Lambda_unconstrained_re,
                   np.float32).reshape(NCH, 1))
    lim = np.ascontiguousarray(np.asarray(L0_attn_Lambda_im,
                                          np.float32).reshape(NCH, 1))
    logd = np.ascontiguousarray(
        np.asarray(L0_attn_log_Delta, np.float32)[:NCH].reshape(NCH, 1))
    zeros_u = np.zeros((T, D), np.float32)

    nc = build_program()
    in_maps = []
    for c in range(8):
        b, h = c // 2, c % 2
        mine = np.ascontiguousarray(u[b, h * T:(h + 1) * T, :])
        prev = np.ascontiguousarray(u[b, 0:T, :]) if h == 1 else zeros_u
        in_maps.append(dict(
            u=mine, uprev=prev, wg=wg, wu=wu, wd=wd, sg=sg, su=su, sd=sd,
            bre=bre, bim=bim, cret=cret, cimt=cimt, drow=drow, lunc=lunc,
            lim=lim, logd=logd))
    res = run_bass_kernel_spmd(nc, in_maps, core_ids=list(range(8)))
    out = np.empty((B, L, D), np.float32)
    for c in range(8):
        b, h = c // 2, c % 2
        out[b, h * T:(h + 1) * T, :] = np.asarray(res.results[c]["out"])
    return out


# version 12
# speedup vs baseline: 1.3253x; 1.0085x over previous
import contextlib
import numpy as np
import ml_dtypes
import concourse.bass as bass
import concourse.mybir as mybir
from concourse.bass_utils import run_bass_kernel_spmd

F32 = mybir.dt.float32
BF16 = mybir.dt.bfloat16
U8 = mybir.dt.uint8
I32 = mybir.dt.int32
AF = mybir.ActivationFunctionType
ALU = mybir.AluOpType
AX = mybir.AxisListType

B, L, D = 4, 4096, 2048
HID = 5632
NCH = 32
T = 2048
NT = 16
HB = 44
TH = 1024
PI = float(np.pi)
TWO_PI = float(2.0 * np.pi)
EPS_LN = 1e-5
KB = 1024


class Ctx:

    KEYS = ("d_par", "d_u", "d_u2", "d_u3", "d_x0", "d_x02", "d_x1w",
            "d_x1w2", "d_x1r",
            "d_xnw", "d_xnw2", "d_xnr", "d_w", "d_w2", "d_wda", "d_wdl", "d_wdh", "d_yw",
            "d_yw2", "d_yr", "d_yr2", "d_x1r2", "d_out", "d_out2",
            "v", "s", "t", "g")

    def __init__(self):
        self.marks = {}
        self.pass_num = 0
        self.cnt = {}
        self.sems = None

    def begin(self, pass_num, sems=None):
        self.pass_num = pass_num
        self.cnt = {k: 0 for k in self.KEYS}
        self.sems = sems

    def inc(self, inst, key, by=1, mark=None):
        if self.pass_num == 2:
            inst.then_inc(self.sems[key], by)
        self.cnt[key] += by
        if mark is not None:
            if self.pass_num == 1:
                assert mark not in self.marks, f"dup mark {mark}"
                self.marks[mark] = (key, self.cnt[key])
            else:
                assert self.marks[mark] == (key, self.cnt[key]), \
                    f"mark drift {mark}"
        return self.cnt[key]

    def wait(self, eng, mark, delta=0):
        if self.pass_num == 2:
            key, val = self.marks[mark]
            if val + delta > 0:
                eng.wait_ge(self.sems[key], val + delta)

    def wait_cnt(self, eng, key, val):
        if self.pass_num == 2 and val > 0:
            eng.wait_ge(self.sems[key], val)

    def selfwait(self, eng, key):
        if self.pass_num == 2 and self.cnt[key] > 0:
            eng.wait_ge(self.sems[key], self.cnt[key])


class _FakeInst:
    def then_inc(self, *a, **k):
        return self


class _FakeEng:
    def __getattr__(self, name):
        return lambda *a, **k: _FakeInst()


def build_program():
    nc = bass.Bass("TRN2", target_bir_lowering=False, debug=False)

    d_in = {}

    def din(name, shape, dt):
        d_in[name] = nc.dram_tensor(name, shape, dt, kind="ExternalInput").ap()

    din("u", [T, D], F32)
    din("uprev", [T, D], F32)
    din("wg", [HB, 128, D], U8)
    din("wu", [HB, 128, D], U8)
    din("wd", [4, 128, HB * 512], U8)
    din("sg", [128, HB], F32)
    din("su", [128, HB], F32)
    din("sd", [1, D], F32)
    din("bre", [NCH, D], F32)
    din("bim", [NCH, D], F32)
    din("cret", [NCH, D], F32)
    din("cimt", [NCH, D], F32)
    din("drow", [1, D], F32)
    din("lunc", [NCH, 1], F32)
    din("lim", [NCH, 1], F32)
    din("logd", [NCH, 1], F32)
    out_hbm = nc.dram_tensor("out", [T, D], F32, kind="ExternalOutput").ap()
    x1_spill = nc.dram_tensor("x1sp", [T, D], F32, kind="Internal").ap()
    xnt_spill = nc.dram_tensor("xntsp", [128, 16 * TH], BF16,
                               kind="Internal").ap()
    y_spill = nc.dram_tensor("ysp", [TH, D], F32, kind="Internal").ap()

    st = contextlib.ExitStack()
    ARENA = 207 * KB
    arena = st.enter_context(nc.sbuf_tensor("arena", [128, ARENA], U8)).ap()

    def vf32(off_kb, n):
        off = int(off_kb * KB)
        assert off % 4 == 0 and off + 4 * n <= ARENA, (off_kb, n)
        return arena[:, off:off + 4 * n].bitcast(F32)

    def vbf(off_kb, n):
        off = int(off_kb * KB)
        assert off % 2 == 0 and off + 2 * n <= ARENA, (off_kb, n)
        return arena[:, off:off + 2 * n].bitcast(BF16)

    def vi32(off_kb, n):
        off = int(off_kb * KB)
        return arena[:, off:off + 4 * n].bitcast(I32)

    t1nT_full = vbf(0, 32768)
    wd_lo = vbf(0, 11264)
    wd_hi = vbf(22, 11264)
    wgbuf = [vbf(44, 2048), vbf(48, 2048)]
    wubuf = [vbf(52, 2048), vbf(56, 2048)]
    us_s1 = vbf(60, 1024)
    yrbuf_p = [vf32(0, 2048), vf32(8, 2048)]
    x1rbuf_p = [vf32(16, 2048), vf32(24, 2048)]
    outbuf = [vf32(32, 2048), vf32(40, 1024), ]
    ybig = [vf32(64, 2048), vf32(72, 2048), vf32(80, 2048), vf32(88, 2048),
            vf32(44, 2048), vf32(52, 2048), vf32(96, 2048), vf32(104, 2048)]
    x1rb2 = [vf32(112, 2048), vf32(120, 2048)]
    outb2 = [vf32(128, 2048), vf32(136, 2048)]

    xnt = vbf(64, 16384)
    drow_f = vf32(64, 2048)[0:1, :]
    sdrow = vf32(72, 2048)[0:1, :]
    drow_bf = vbf(80, 2048)[0:1, :]

    ubuf = [vf32(96, 2048), vf32(104, 2048), vf32(84, 2048)]
    t1ntmp = [vbf(112, 2048), vbf(116, 2048)]
    bu_re_m = vf32(120, 2048)[0:NCH, :]
    bu_im_m = vf32(128, 2048)[0:NCH, :]
    bu_re_p = vf32(136, 2048)[0:NCH, :]
    bu_im_p = vf32(144, 2048)[0:NCH, :]
    cosT = vf32(152, 2048)[0:NCH, :]
    sinT = vf32(160, 2048)[0:NCH, :]
    it_fwd = vf32(168, 2048)[0:NCH, :]
    rp_v = vf32(168, 2048)[0:NCH, :]
    c2re_v = vbf(176, 2048)[0:NCH, :]
    c2im_v = vbf(180, 2048)[0:NCH, :]
    bbar_sA = vf32(96, 2048)[0:NCH, :]
    bbar_sB = vf32(104, 2048)[0:NCH, :]
    bbar_re = vbf(112, 2048)[0:NCH, :]
    bbar_im = vbf(116, 2048)[0:NCH, :]
    ph_p = vf32(120, 2048)[0:NCH, :]
    ph_q = vf32(128, 2048)[0:NCH, :]
    ph_qi = vi32(136, 2048)[0:NCH, :]
    ph_r = vf32(144, 2048)[0:NCH, :]
    tA = vf32(96, 2048)[0:NCH, :]
    tB = vf32(104, 2048)[0:NCH, :]
    tC = vf32(112, 2048)[0:NCH, :]
    bt_re = vf32(136, 2048)[0:NCH, :]
    bt_im = vf32(144, 2048)[0:NCH, :]
    xs_re = vbf(120, 2048)[0:NCH, :]
    xs_im = vbf(124, 2048)[0:NCH, :]
    ybuf_p = [vf32(96, 2048), vf32(104, 2048)]
    x1buf = [vf32(128, 2048), vf32(136, 2048)]
    x0buf_p = [vf32(144, 2048), vf32(168, 2048)]
    xnbuf = vbf(152, 2048)
    xnt_stage = vbf(156, 2048)
    Hbuf = vbf(96, 45056)

    d_rep_bf = vbf(184, 2048)
    sdrep = vf32(188, 2048)

    id_bf = vbf(196, 128)
    onesrow_bf = vbf(196.25, 128)[0:1, :]
    onesrow_f = vf32(196.5, 128)[0:1, :]
    sgbuf = vf32(197, HB)
    subuf = vf32(197.25, HB)
    colbuf = vf32(197.5, 96)
    statbuf = vf32(197.875, 24)
    bret_bf = vbf(198.25, 512)
    bimt_bf = vbf(199.25, 512)
    ystage = [vf32(200.25, 512), vf32(202.25, 512)]
    ystage4 = ystage + [vf32(60, 512), vf32(62, 512)]
    ln4stats = vf32(204.25, 192)
    h2buf = vbf(205, 1024)

    pA = st.enter_context(nc.psum_tensor("pA", [128, 2048], F32)).ap()
    pB = st.enter_context(nc.psum_tensor("pB", [128, 1024], F32)).ap()
    pC = st.enter_context(nc.psum_tensor("pC", [128, 512], F32)).ap()
    pD = st.enter_context(nc.psum_tensor("pD", [128, 512], F32)).ap()
    pBbf = pB.bitcast(BF16)

    sems = {k: st.enter_context(nc.semaphore(k)) for k in Ctx.KEYS}
    C = Ctx()

    p_lunc = colbuf[0:NCH, 0:1]
    p_lim = colbuf[0:NCH, 1:2]
    p_logd = colbuf[0:NCH, 2:3]
    c_delta = colbuf[0:NCH, 3:4]
    c_lamre = colbuf[0:NCH, 4:5]
    c_lnr = colbuf[0:NCH, 5:6]
    c_theta = colbuf[0:NCH, 6:7]
    c_r = colbuf[0:NCH, 7:8]
    c_are = colbuf[0:NCH, 8:9]
    c_aim = colbuf[0:NCH, 9:10]
    c_t1 = colbuf[0:NCH, 10:11]
    c_t2 = colbuf[0:NCH, 11:12]
    c_t3 = colbuf[0:NCH, 12:13]
    c_t4 = colbuf[0:NCH, 13:14]
    c_dre = colbuf[0:NCH, 14:15]
    c_dim = colbuf[0:NCH, 15:16]
    c_eps = colbuf[:, 16:17]
    c_cre = colbuf[0:NCH, 17:18]
    c_cim = colbuf[0:NCH, 18:19]
    c_sr = colbuf[0:NCH, 19:20]
    c_si = colbuf[0:NCH, 20:21]
    c_negm = colbuf[:, 21:22]
    c_rstd = colbuf[:, 22:23]
    c_sd = colbuf[:, 23:24]
    c_Tlnr = colbuf[0:NCH, 24:25]
    c_i32 = colbuf.bitcast(I32)[0:NCH, 25:26]
    c_sinth = colbuf[0:NCH, 26:27]
    c_sh = colbuf[0:NCH, 27:28]
    c_costh = colbuf[0:NCH, 28:29]
    c_thred = colbuf[0:NCH, 29:30]
    mvbuf = colbuf[:, 30:32]
    mv_p = [colbuf[:, 32:34], colbuf[:, 34:36]]
    sd_p = [colbuf[:, 36:37], colbuf[:, 37:38]]
    rstd_p = [colbuf[:, 38:39], colbuf[:, 39:40]]
    nmr_p = [colbuf[:, 40:41], colbuf[:, 41:42]]
    mvy_p = [colbuf[:, 42:44], colbuf[:, 44:46]]
    sdy_p = [colbuf[:, 46:47], colbuf[:, 47:48]]
    rsy_p = [colbuf[:, 48:49], colbuf[:, 49:50]]
    nmy_p = [colbuf[:, 50:51], colbuf[:, 51:52]]
    mvx_p = [colbuf[:, 52:54], colbuf[:, 54:56]]
    sdx_p = [colbuf[:, 56:57], colbuf[:, 57:58]]
    rsx_p = [colbuf[:, 58:59], colbuf[:, 59:60]]
    nmx_p = [colbuf[:, 60:61], colbuf[:, 61:62]]
    l4mv = [colbuf[:, 62 + 2 * t_: 64 + 2 * t_] for t_ in range(8)]
    l4rs = [colbuf[:, 78 + t_: 79 + t_] for t_ in range(8)]
    l4nm = [colbuf[:, 86 + t_: 87 + t_] for t_ in range(8)]
    p_bre = bu_re_m
    p_bim = bu_im_m
    p_cret = cosT
    p_cimt = sinT

    def Hcols(hb, tg):
        return Hbuf[:, hb * TH + tg * 512: hb * TH + (tg + 1) * 512]

    def bu_dsts(k):
        i = k % NT
        bre = (bu_re_p if k < NT else bu_re_m)[:, i * 128:(i + 1) * 128]
        bim = (bu_im_p if k < NT else bu_im_m)[:, i * 128:(i + 1) * 128]
        return bre, bim

    def xnev_dst(i, gq):
        if i < 8:
            return xnt.rearrange("p (k tt) -> p k tt", k=16)[
                :, 4 * gq:4 * (gq + 1), i * 128:(i + 1) * 128]
        return xnt_stage[:, 512 * gq:512 * (gq + 1)]

    def xnev_src(i, gq):
        s = pBbf[:, 512 * gq:512 * (gq + 1)]
        if i < 8:
            return s.rearrange("p (k tt) -> p k tt", k=4)
        return s

    def emit_sync(q):
        C.inc(q.dma_start(p_lunc, d_in["lunc"]), "d_par", 16)
        C.inc(q.dma_start(p_lim, d_in["lim"]), "d_par", 16)
        C.inc(q.dma_start(p_logd, d_in["logd"]), "d_par", 16)
        C.inc(q.dma_start(p_bre, d_in["bre"]), "d_par", 16)
        C.inc(q.dma_start(p_bim, d_in["bim"]), "d_par", 16)
        C.inc(q.dma_start(p_cret, d_in["cret"]), "d_par", 16)
        C.inc(q.dma_start(p_cimt, d_in["cimt"]), "d_par", 16)
        C.inc(q.dma_start(drow_f, d_in["drow"]), "d_par", 16)
        C.inc(q.dma_start(sdrow, d_in["sd"]), "d_par", 16)
        C.inc(q.dma_start(sgbuf[:, :], d_in["sg"]), "d_par", 16)
        C.inc(q.dma_start(subuf[:, :], d_in["su"]), "d_par", 16,
              mark="d_params")

        for j in range(2 * NT):
            src = d_in["uprev"] if j < NT else d_in["u"]
            i = j % NT
            if j < 2:
                C.wait(q, "v_bbar")
            elif j >= 3:
                C.wait(q, f"uapply_{j-3}")
            C.inc(q.dma_start(ubuf[j % 3][:, :],
                              src[i * 128:(i + 1) * 128, :]),
                  ("d_u", "d_u2", "d_u3")[j % 3], 16, mark=f"du_{j}")

        for i in range(NT):
            if i < 2:
                C.wait(q, "v_scan_done")
            else:
                C.wait(q, f"x1_{i-2}")
            C.inc(q.dma_start(x0buf_p[i % 2][:, :],
                              d_in["u"][i * 128:(i + 1) * 128, :]),
                  "d_x0" if i % 2 == 0 else "d_x02", 16, mark=f"dx0_{i}")
            if i >= 1:
                C.wait(q, f"x1_{i-1}")
                C.inc(q.dma_start(x1_spill[(i - 1) * 128:i * 128, :],
                                  x1buf[(i - 1) % 2][:, :]),
                      "d_x1w" if (i - 1) % 2 == 0 else "d_x1w2", 16,
                      mark=f"dx1w_{i-1}")
        C.wait(q, f"x1_{NT-1}")
        C.inc(q.dma_start(x1_spill[(NT - 1) * 128:NT * 128, :],
                          x1buf[(NT - 1) % 2][:, :]),
              "d_x1w" if (NT - 1) % 2 == 0 else "d_x1w2", 16,
              mark=f"dx1w_{NT-1}")

        for X in range(2):
            if X == 0:
                C.wait(q, "s1mm_0_43")
                C.inc(q.dma_start(xnt[:, :], xnt_spill[:, :]), "d_xnr", 16,
                      mark="dxnr")
            for dg in range(4):
                for ti in range(8):
                    if X == 1 and (ti < 6 or dg == 3):
                        continue
                    k = dg * 8 + ti
                    C.wait(q, f"ys_{X}_{dg}_{ti}")
                    if X == 0:
                        src = ystage4[k % 4][:, :]
                        skey = ("d_yw", "d_yw2", "d_x1w", "d_x1w2")[k % 4]
                    else:
                        src = ystage[ti - 6][:, :]
                        skey = "d_yw" if ti == 6 else "d_yw2"
                    C.inc(q.dma_start(
                        y_spill[ti * 128:(ti + 1) * 128,
                                dg * 512:(dg + 1) * 512], src),
                        skey, 16, mark=f"dyw_{X}_{dg}_{ti}")
            if X == 0:
                C.wait(q, "s2f_0_3_7")
                for lt in (4, 5, 6, 7):
                    C.wait(q, f"dyw_0_3_{lt}")
                for ti in range(8):
                    if ti >= 2:
                        C.wait(q, f"ln4apply_0_{ti-2}")
                    C.inc(q.dma_start(yrbuf_p[ti % 2][:, :],
                                      y_spill[ti * 128:(ti + 1) * 128, :]),
                          "d_yr" if ti % 2 == 0 else "d_yr2", 16,
                          mark=f"dyr_0_{ti}")
                    if ti >= 2:
                        C.wait(q, f"out_0_{ti-2}")
                    C.inc(q.dma_start(x1rbuf_p[ti % 2][:, :],
                                      x1_spill[ti * 128:(ti + 1) * 128, :]),
                          "d_x1r" if ti % 2 == 0 else "d_x1r2", 16,
                          mark=f"dx1r_0_{ti}")
                    C.wait(q, f"out_0_{ti}")
                    C.inc(q.dma_start(out_hbm[ti * 128:(ti + 1) * 128, :],
                                      outbuf[0][:, :]),
                          "d_out" if ti % 2 == 0 else "d_out2", 16,
                          mark=f"dout_0_{ti}")
            else:
                C.wait(q, "s2e_7")
                C.wait(q, "dyw_1_2_6")
                C.inc(q.dma_start(ybig[6][:, 0:1536],
                                  y_spill[6 * 128:7 * 128, 0:1536]),
                      "d_yr", 16, mark="dyr_1_6")
                C.wait(q, "dyw_1_2_7")
                C.inc(q.dma_start(ybig[7][:, 0:1536],
                                  y_spill[7 * 128:8 * 128, 0:1536]),
                      "d_yr2", 16, mark="dyr_1_7")
                for ti in range(8):
                    if ti == 0:
                        C.wait(q, "s2e_11")
                    elif ti == 1:
                        C.wait(q, "s2e_15")
                    else:
                        C.wait(q, f"out_1_{ti-2}")
                    C.inc(q.dma_start(
                        x1rb2[ti % 2][:, :],
                        x1_spill[(8 + ti) * 128:(9 + ti) * 128, :]),
                        "d_x1r" if ti % 2 == 0 else "d_x1r2", 16,
                        mark=f"dx1r_1_{ti}")
        C.wait(q, "dout_0_6")
        C.wait(q, "dout_1_6")
        C.wait(q, "dout_1_7")

    def emit_gpsimd(g):
        C.inc(g.iota(it_fwd[:, :], pattern=[[1, T]], base=1,
                     channel_multiplier=0,
                     allow_small_or_imprecise_dtypes=True), "g", mark="g_itf")
        C.inc(g.memset(id_bf[:, :], 1.0), "g")
        C.selfwait(g, "g")
        C.inc(g.affine_select(id_bf[:, :], id_bf[:, :], pattern=[[-1, 128]],
                              compare_op=ALU.is_equal, fill=0.0, base=0,
                              channel_multiplier=1), "g", mark="g_id")
        C.inc(g.memset(onesrow_bf[:, :], 1.0), "g")
        C.inc(g.memset(onesrow_f[:, :], 1.0), "g")
        C.inc(g.memset(c_eps, EPS_LN), "g", mark="g_setup")
        C.wait(g, "v_drep")
        for kd in range(16):
            C.inc(g.affine_select(d_rep_bf[:, kd * 128:(kd + 1) * 128],
                                  d_rep_bf[:, kd * 128:(kd + 1) * 128],
                                  pattern=[[-1, 128]],
                                  compare_op=ALU.is_equal, fill=0.0, base=0,
                                  channel_multiplier=1), "g",
                  mark=("g_ddiag" if kd == 15 else None))

        for k in range(8):
            C.wait(g, f"xnevv_{k + 8}")
            C.wait(g, f"xnevs_{k + 8}")
            C.inc(g.dma_start(
                xnt_spill.rearrange("p (kk ti tt) -> p kk ti tt",
                                    k=16, ti=8, kk=16)[:, :, k, :],
                xnt_stage.rearrange("p (kk tt) -> p kk tt", kk=16)),
                "d_xnw" if k % 2 == 0 else "d_xnw2", 16,
                mark=f"dxnw_{k}")

        for X in range(2):
            for hb in range(HB):
                par = hb % 2
                if X == 0 and hb < 2:
                    C.wait(g, "yev_15")
                else:
                    pX, phb = (X, hb - 2) if hb >= 2 else (0, HB - 2 + hb)
                    C.wait(g, f"s1mm_{pX}_{phb}")
                wkey = "d_w" if par == 0 else "d_w2"
                C.inc(g.dma_start(wgbuf[par][:, :], d_in["wg"][hb, :, :]),
                      wkey, 16)
                C.inc(g.dma_start(wubuf[par][:, :], d_in["wu"][hb, :, :]),
                      wkey, 16, mark=f"wconv_{X}_{hb}")
            for dg in range(4):
                if dg == 0:
                    if X == 0:
                        C.wait(g, "yev_15")
                    else:
                        C.wait_cnt(g, "d_out", 16 * 4)
                        C.wait_cnt(g, "d_out2", 16 * 4)
                        C.wait(g, "s2q_0_3")
                else:
                    C.wait(g, f"s2q_{X}_{dg-1}")
                C.inc(g.dma_start(
                    wd_lo[:, 0:5632],
                    d_in["wd"][dg, :, 0:5632]), "d_wda", 16,
                    mark=f"dwdla_{X}_{dg}")
                if dg == 0:
                    if X == 1:
                        C.wait(g, "s2lo_0_3")
                else:
                    C.wait(g, f"s2lo_{X}_{dg-1}")
                C.inc(g.dma_start(
                    wd_lo[:, 5632:11264],
                    d_in["wd"][dg, :, 5632:11264]), "d_wdl", 16,
                    mark=f"dwdlo_{X}_{dg}")
                if dg == 0:
                    if X == 1:
                        C.wait(g, "s2f_0_3_7")
                else:
                    C.wait(g, f"s2f_{X}_{dg-1}_7")
                C.inc(g.dma_start(
                    wd_hi[:, :],
                    d_in["wd"][dg, :, 11264:22528]), "d_wdh", 16,
                    mark=f"dwdhi_{X}_{dg}")
        for ti in (0, 2, 4, 6):
            C.wait(g, f"out_1_{ti}")
            C.inc(g.dma_start(out_hbm[(8 + ti) * 128:(9 + ti) * 128, :],
                              outb2[0][:, :]), "d_w", 16,
                  mark=f"dout_1_{ti}")

    def emit_scalar(s):
        C.wait(s, "d_params")
        C.inc(s.activation(c_delta, p_logd, AF.Exp), "s")
        C.inc(s.activation(c_t1, p_lunc, AF.Abs), "s")
        C.selfwait(s, "s")
        C.inc(s.activation(c_t2, c_t1, AF.Exp, scale=-1.0), "s")
        C.selfwait(s, "s")
        C.inc(s.activation(c_t3, c_t2, AF.Ln, bias=1.0), "s", mark="s_p1")
        C.wait(s, "v_lnr")
        C.inc(s.activation(c_r, c_lnr, AF.Exp), "s", mark="s_r")
        C.wait(s, "v_thred")
        C.inc(s.activation(c_sinth, c_thred, AF.Sin), "s")
        C.inc(s.activation(c_sh, c_thred, AF.Sin, scale=0.5), "s",
              mark="s_sincos")
        C.wait(s, "v_pred")
        C.inc(s.activation(sinT[:, :], ph_p[:, :], AF.Sin), "s")
        C.inc(s.activation(ph_q[:, :], ph_p[:, :], AF.Abs), "s",
              mark="s_abs")
        C.wait(s, "v_carg")
        C.inc(s.activation(cosT[:, :], ph_r[:, :], AF.Sin), "s",
              mark="s_tables")
        C.inc(s.activation(drow_bf, drow_f, AF.Copy), "s", mark="s_dbf")

        for sl in range(2 * NT + 2):
            j = sl
            if j < 2 * NT:
                C.wait(s, f"aggr_{j}")
                C.inc(s.activation(sd_p[j % 2], mv_p[j % 2][:, 1:2],
                                   AF.Sqrt, bias=c_eps), "s",
                      mark=f"sqrt_{j}")
            k = sl - 1
            if 0 <= k < 2 * NT:
                C.wait(s, f"lnc_{k}")
                i = k % NT
                if k >= 2:
                    C.wait(s, f"tr_{k-2}")
                C.inc(s.activation(t1ntmp[k % 2][:, :],
                                   ubuf[k % 3][:, :], AF.Identity,
                                   scale=rstd_p[k % 2], bias=nmr_p[k % 2]),
                      "s", mark=f"uapply_{k}")
            k = sl - 2
            if 0 <= k < 2 * NT:
                C.wait(s, f"tr_{k}")
                i = k % NT
                for gq in range(2, 4):
                    C.inc(s.activation(
                        t1nT_full[:, i * 2048 + 512 * gq:
                                  i * 2048 + 512 * (gq + 1)],
                        pBbf[:, 512 * gq:512 * (gq + 1)], AF.Copy), "s",
                        mark=(f"t1nTs_{k}" if gq == 3 else None))
        C.wait(s, "v_rpe")
        C.inc(s.activation(rp_v[:, :], tC[:, :], AF.Exp, scale=-1.0,
                           bias=c_Tlnr), "s", mark="s_rpow")
        for sl in range(NT + 5):
            k = sl - 1
            if 0 <= k < NT:
                C.wait(s, f"yaggr_{k}")
                C.inc(s.activation(sdy_p[k % 2], mvy_p[k % 2][:, 1:2],
                                   AF.Sqrt, bias=c_eps), "s",
                      mark=f"ysqrt_{k}")
            k = sl - 2
            if 0 <= k < NT:
                C.wait(s, f"ylnc_{k}")
                C.inc(s.activation(ybuf_p[k % 2][:, :], ybuf_p[k % 2][:, :],
                                   AF.Identity, scale=rsy_p[k % 2],
                                   bias=nmy_p[k % 2]), "s", mark=f"ln2_{k}")
            k = sl - 3
            if 0 <= k < NT:
                C.wait(s, f"x1aggr_{k}")
                C.inc(s.activation(sdx_p[k % 2], mvx_p[k % 2][:, 1:2],
                                   AF.Sqrt, bias=c_eps), "s",
                      mark=f"x1sqrt_{k}")
            k = sl - 4
            if 0 <= k < NT:
                C.wait(s, f"xlnc_{k}")
                if k >= 1:
                    C.wait(s, f"xntr_{k-1}")
                C.inc(s.activation(xnbuf[:, :], x1buf[k % 2][:, :],
                                   AF.Identity, scale=rsx_p[k % 2],
                                   bias=nmx_p[k % 2]), "s",
                      mark=f"xnapply_{k}")
            k = sl - 5
            if 0 <= k < NT:
                C.wait(s, f"xntr_{k}")
                if k >= 9:
                    C.wait(s, f"dxnw_{k-9}")
                for gq in range(2, 4):
                    C.inc(s.activation(xnev_dst(k, gq), xnev_src(k, gq),
                                       AF.Copy), "s",
                          mark=(f"xnevs_{k}" if gq == 3 else None))
        for X in range(2):
            for hb in range(HB):
                C.wait(s, f"gmm_{X}_{hb}")
                if hb >= 1 or X == 1:
                    pX, phb = (X, hb - 1) if hb >= 1 else (0, HB - 1)
                    C.wait(s, f"h_{pX}_{phb}")
                gsrc = pA if hb % 2 == 0 else pB
                for tg in range(2):
                    C.inc(s.activation(h2buf[:, tg * 512:(tg + 1) * 512],
                                       gsrc[:, tg * 512:(tg + 1) * 512],
                                       AF.Silu, scale=sgbuf[:, hb:hb + 1]),
                          "s", mark=(f"gs_{X}_{hb}" if tg == 1 else None))
            for dg in range(4):
                C.wait(s, f"dwdla_{X}_{dg}")
                C.inc(s.activation(wd_lo[:, 0:5632], wd_lo[:, 0:5632],
                                   AF.Copy, bias=-128.0), "s",
                      mark=f"wdla_{X}_{dg}")
                C.wait(s, f"dwdlo_{X}_{dg}")
                C.inc(s.activation(wd_lo[:, 5632:11264], wd_lo[:, 5632:11264],
                                   AF.Copy, bias=-128.0), "s",
                      mark=f"wdlo_{X}_{dg}")
                C.wait(s, f"dwdhi_{X}_{dg}")
                C.inc(s.activation(wd_hi[:, :], wd_hi[:, :], AF.Copy,
                                   bias=-128.0), "s", mark=f"wdhi_{X}_{dg}")
            for ti in range(8):
                C.wait(s, f"l4aggr_{X}_{ti}")
                C.inc(s.activation(l4nm[ti], l4mv[ti][:, 1:2], AF.Sqrt,
                                   bias=c_eps), "s", mark=f"l4sqrt_{X}_{ti}")
            if X == 1:
                for ti in (0, 2, 4, 6):
                    C.wait(s, f"l4c_1_{ti}")
                    C.wait(s, f"ys_1_3_{ti}")
                    if ti == 6:
                        C.wait(s, "dyr_1_6")
                    C.inc(s.activation(ybig[ti][:, :], ybig[ti][:, :],
                                       AF.Identity, scale=l4rs[ti],
                                       bias=l4nm[ti]), "s",
                          mark=f"l4app_{ti}")
                    if ti >= 2:
                        to = ti - 1
                        C.wait(s, f"out_1_{to}")
                        C.inc(s.dma_start(
                            out_hbm[(8 + to) * 128:(9 + to) * 128, :],
                            outb2[1][:, :]), "d_out2", 16,
                            mark=f"dout_1_{to}")
                for to in (7,):
                    C.wait(s, f"out_1_{to}")
                    C.inc(s.dma_start(
                        out_hbm[(8 + to) * 128:(9 + to) * 128, :],
                        outb2[1][:, :]), "d_out2", 16,
                        mark=f"dout_1_{to}")

    def emit_tensor(t):
        C.wait(t, "g_id")
        C.wait(t, "v_bbar")
        for kt in range(16):
            C.inc(t.transpose(pBbf[:, kt * 32:(kt + 1) * 32],
                              bbar_re[:, kt * 128:(kt + 1) * 128],
                              id_bf[0:NCH, 0:NCH]), "t")
        for kt in range(16):
            C.inc(t.transpose(pBbf[:, 512 + kt * 32:512 + (kt + 1) * 32],
                              bbar_im[:, kt * 128:(kt + 1) * 128],
                              id_bf[0:NCH, 0:NCH]), "t",
                  mark=("t_btr" if kt == 15 else None))
        C.wait(t, "s_dbf")
        for dg in range(4):
            C.inc(t.matmul(pA[:, dg * 512:(dg + 1) * 512], onesrow_bf,
                           drow_bf[:, dg * 512:(dg + 1) * 512],
                           start=True, stop=True), "t",
                  mark=("t_drep" if dg == 3 else None))
        C.wait(t, "v_drep")
        for dg in range(4):
            C.inc(t.matmul(pA[:, dg * 512:(dg + 1) * 512], onesrow_f,
                           sdrow[:, dg * 512:(dg + 1) * 512],
                           start=True, stop=True), "t",
                  mark=("t_sdrep" if dg == 3 else None))

        for j in range(2 * NT):
            C.wait(t, f"uapply_{j}")
            if j >= 1:
                C.wait(t, f"t1nTv_{j-1}")
                C.wait(t, f"t1nTs_{j-1}")
            i = j % NT
            tsrc = t1ntmp[j % 2]
            for kt in range(16):
                C.inc(t.transpose(pBbf[:, kt * 128:(kt + 1) * 128],
                                  tsrc[:, kt * 128:(kt + 1) * 128],
                                  id_bf[:, :]), "t",
                      mark=(f"tr_{j}" if kt == 15 else None))
            C.wait(t, f"t1nTv_{j}")
            C.wait(t, f"t1nTs_{j}")
            if j >= 2:
                C.wait(t, f"buev_{j-2}")
            pb = pC if j % 2 == 0 else pD
            for kt in range(16):
                C.inc(t.matmul(pb[0:NCH, 0:128],
                               bret_bf[:, kt * 32:(kt + 1) * 32],
                               t1nT_full[:, i * 2048 + kt * 128:
                                         i * 2048 + (kt + 1) * 128],
                               start=(kt == 0), stop=(kt == 15)), "t")
            for kt in range(16):
                C.inc(t.matmul(pb[0:NCH, 128:256],
                               bimt_bf[:, kt * 32:(kt + 1) * 32],
                               t1nT_full[:, i * 2048 + kt * 128:
                                         i * 2048 + (kt + 1) * 128],
                               start=(kt == 0), stop=(kt == 15)), "t",
                      mark=(f"bu_{j}" if kt == 15 else None))

        def xntr_block(k):
            C.wait(t, f"xnapply_{k}")
            if k == 0:
                C.wait(t, f"buev_{2*NT-1}")
            else:
                C.wait(t, f"xnevv_{k-1}")
                C.wait(t, f"xnevs_{k-1}")
            for kt in range(16):
                C.inc(t.transpose(pBbf[:, kt * 128:(kt + 1) * 128],
                                  xnbuf[:, kt * 128:(kt + 1) * 128],
                                  id_bf[:, :]), "t",
                      mark=(f"xntr_{k}" if kt == 15 else None))

        C.wait(t, "v_scan_done")
        C.wait(t, "v_sdrep")
        for i in range(NT):
            if i >= 1:
                C.wait(t, f"yev_{i-1}")
            if i == 0:
                C.wait(t, "g_ddiag")
            for dg in range(4):
                C.inc(t.matmul(pA[:, dg * 512:(dg + 1) * 512],
                               xs_re[:, i * 128:(i + 1) * 128],
                               c2re_v[:, dg * 512:(dg + 1) * 512],
                               start=True, stop=False), "t")
                C.inc(t.matmul(pA[:, dg * 512:(dg + 1) * 512],
                               xs_im[:, i * 128:(i + 1) * 128],
                               c2im_v[:, dg * 512:(dg + 1) * 512],
                               start=False, stop=False), "t")
            for kd in range(16):
                C.inc(t.matmul(pA[:, kd * 128:(kd + 1) * 128],
                               t1nT_full[:, i * 2048 + kd * 128:
                                         i * 2048 + (kd + 1) * 128],
                               d_rep_bf[:, kd * 128:(kd + 1) * 128],
                               start=False, stop=(kd % 4 == 3)), "t",
                      mark=(f"ymm_{i}" if kd == 15 else None))
            if i >= 5:
                xntr_block(i - 5)
        for k in range(NT - 5, NT):
            xntr_block(k)

        PS2 = [pA[:, 0:512], pA[:, 512:1024], pA[:, 1024:1536],
               pA[:, 1536:2048], pB[:, 0:512], pB[:, 512:1024], pC, pD]
        for X in range(2):
            for hb in range(HB):
                par = hb % 2
                C.wait(t, f"wconv_{X}_{hb}")
                if X == 0 and hb == 0:
                    C.wait(t, "yev_15")
                if X == 0 and hb == 1:
                    C.wait(t, "xnevv_15")
                    C.wait(t, "xnevs_15")
                if X == 1 and hb == 0:
                    C.wait(t, "dxnr")
                    C.wait(t, "gs_0_42")
                    C.wait(t, "uev_0_42")
                    C.wait(t, "ys_0_3_3")
                if X == 1 and hb == 1:
                    C.wait(t, "gs_0_43")
                    C.wait(t, "uev_0_43")
                    C.wait(t, "ys_0_3_7")
                if hb >= 2:
                    C.wait(t, f"gs_{X}_{hb-2}")
                    C.wait(t, f"uev_{X}_{hb-2}")
                if par == 0:
                    gps = [pA[:, 0:512], pA[:, 512:1024]]
                    ups = [pA[:, 1024:1536], pA[:, 1536:2048]]
                else:
                    gps = [pB[:, 0:512], pB[:, 512:1024]]
                    ups = [pC[:, :], pD[:, :]]
                for tg in range(2):
                    for kt in range(16):
                        C.inc(t.matmul(
                            gps[tg],
                            wgbuf[par][:, kt * 128:(kt + 1) * 128],
                            xnt[:, kt * TH + tg * 512:
                                kt * TH + (tg + 1) * 512],
                            start=(kt == 0), stop=(kt == 15)), "t",
                            mark=(f"gmm_{X}_{hb}"
                                  if tg == 1 and kt == 15 else None))
                for tg in range(2):
                    for kt in range(16):
                        C.inc(t.matmul(
                            ups[tg],
                            wubuf[par][:, kt * 128:(kt + 1) * 128],
                            xnt[:, kt * TH + tg * 512:
                                kt * TH + (tg + 1) * 512],
                            start=(kt == 0), stop=(kt == 15)), "t",
                            mark=(f"s1mm_{X}_{hb}"
                                  if tg == 1 and kt == 15 else None))
            for dg in range(4):
                for hb in range(HB):
                    if hb == 0:
                        C.wait(t, f"wdla_{X}_{dg}")
                    if hb == 11:
                        C.wait(t, f"wdlo_{X}_{dg}")
                    if hb == 22:
                        C.wait(t, f"wdhi_{X}_{dg}")
                    if dg == 0:
                        C.wait(t, f"h_{X}_{hb}")
                    wsrc = wd_lo if hb < 22 else wd_hi
                    wof = hb if hb < 22 else hb - 22
                    for ti in range(8):
                        if hb == 0:
                            if dg == 0:
                                C.wait(t, f"gs_{X}_{HB-1}")
                                C.wait(t, f"uev_{X}_{HB-1}")
                            else:
                                C.wait(t, f"ys_{X}_{dg-1}_{ti}")
                        C.inc(t.matmul(
                            PS2[ti],
                            Hbuf[:, hb * TH + ti * 128:
                                 hb * TH + (ti + 1) * 128],
                            wsrc[:, wof * 512:(wof + 1) * 512],
                            start=(hb == 0), stop=(hb == HB - 1)), "t",
                            mark=(f"s2f_{X}_{dg}_{ti}" if hb == HB - 1
                                  else (f"s2lo_{X}_{dg}"
                                        if hb == 21 and ti == 7 else
                                        (f"s2q_{X}_{dg}"
                                         if hb == 10 and ti == 7 else
                                         (f"s2e_{hb}"
                                          if X == 1 and dg == 3 and ti == 7
                                          and hb in (7, 11, 15, 19, 23)
                                          else None)))))

    def emit_vector(v):
        C.wait(v, "d_params")
        C.wait(v, "g_setup")
        C.inc(v.tensor_scalar(c2re_v[:, :], p_cret, 2.0, None, ALU.mult),
              "v")
        C.inc(v.tensor_scalar(c2im_v[:, :], p_cimt, -2.0, None, ALU.mult),
              "v", mark="v_c2")
        C.wait(v, "s_p1")
        C.inc(v.tensor_scalar(c_t1, p_lunc, 0.0, None, ALU.max), "v")
        C.selfwait(v, "v")
        C.inc(v.tensor_tensor(c_t1, c_t1, c_t3, ALU.add), "v")
        C.selfwait(v, "v")
        C.inc(v.tensor_scalar(c_lamre, c_t1, 0.0101, -1.0, ALU.add,
                              ALU.mult), "v")
        C.selfwait(v, "v")
        C.inc(v.tensor_tensor(c_lnr, c_lamre, c_delta, ALU.mult), "v",
              mark="v_lnr")
        C.selfwait(v, "v")
        C.inc(v.tensor_scalar(c_Tlnr, c_lnr, float(T), None, ALU.mult), "v")
        C.inc(v.tensor_tensor(c_theta, p_lim, c_delta, ALU.mult), "v")
        C.selfwait(v, "v")
        C.inc(v.tensor_scalar(c_t2, c_theta, 1.0 / TWO_PI, None, ALU.mult),
              "v")
        C.selfwait(v, "v")
        C.inc(v.tensor_copy(c_i32, c_t2), "v")
        C.selfwait(v, "v")
        C.inc(v.tensor_copy(c_t2, c_i32), "v")
        C.selfwait(v, "v")
        C.inc(v.scalar_tensor_tensor(c_thred, c_t2, -TWO_PI, c_theta,
                                     ALU.mult, ALU.add), "v")
        C.selfwait(v, "v")
        C.inc(v.tensor_scalar(c_t2, c_thred, PI, None, ALU.is_gt), "v")
        C.selfwait(v, "v")
        C.inc(v.scalar_tensor_tensor(c_thred, c_t2, -TWO_PI, c_thred,
                                     ALU.mult, ALU.add), "v")
        C.selfwait(v, "v")
        C.inc(v.tensor_scalar(c_t2, c_thred, -PI, None, ALU.is_lt), "v")
        C.selfwait(v, "v")
        C.inc(v.scalar_tensor_tensor(c_thred, c_t2, TWO_PI, c_thred,
                                     ALU.mult, ALU.add), "v", mark="v_thred")
        C.wait(v, "s_sincos")
        C.inc(v.tensor_tensor(c_t2, c_sh, c_sh, ALU.mult), "v")
        C.selfwait(v, "v")
        C.inc(v.tensor_scalar(c_costh, c_t2, -2.0, 1.0, ALU.mult, ALU.add),
              "v")
        C.wait(v, "s_r")
        C.selfwait(v, "v")
        C.inc(v.tensor_tensor(c_are, c_r, c_costh, ALU.mult), "v")
        C.inc(v.tensor_tensor(c_aim, c_r, c_sinth, ALU.mult), "v")
        C.selfwait(v, "v")
        C.inc(v.tensor_scalar(c_t1, c_are, -1.0, None, ALU.add), "v")
        C.inc(v.tensor_tensor(c_t2, c_lamre, c_lamre, ALU.mult), "v")
        C.inc(v.tensor_tensor(c_t3, p_lim, p_lim, ALU.mult), "v")
        C.selfwait(v, "v")
        C.inc(v.tensor_tensor(c_t2, c_t2, c_t3, ALU.add), "v")
        C.selfwait(v, "v")
        C.inc(v.reciprocal(c_t2, c_t2), "v")
        C.selfwait(v, "v")
        C.inc(v.tensor_tensor(c_t3, c_t1, c_lamre, ALU.mult), "v")
        C.inc(v.tensor_tensor(c_t4, c_aim, p_lim, ALU.mult), "v")
        C.selfwait(v, "v")
        C.inc(v.tensor_tensor(c_t3, c_t3, c_t4, ALU.add), "v")
        C.selfwait(v, "v")
        C.inc(v.tensor_tensor(c_dre, c_t3, c_t2, ALU.mult), "v")
        C.selfwait(v, "v")
        C.inc(v.tensor_tensor(c_t3, c_aim, c_lamre, ALU.mult), "v")
        C.inc(v.tensor_tensor(c_t4, c_t1, p_lim, ALU.mult), "v")
        C.selfwait(v, "v")
        C.inc(v.tensor_tensor(c_t3, c_t3, c_t4, ALU.subtract), "v")
        C.selfwait(v, "v")
        C.inc(v.tensor_tensor(c_dim, c_t3, c_t2, ALU.mult), "v")
        C.selfwait(v, "v")
        C.inc(v.tensor_scalar(bbar_sA[:, :], p_bre, c_dre, None, ALU.mult),
              "v")
        C.inc(v.tensor_scalar(bbar_sB[:, :], p_bim, c_dim, None, ALU.mult),
              "v")
        C.selfwait(v, "v")
        C.inc(v.tensor_tensor(bbar_re[:, :], bbar_sA[:, :], bbar_sB[:, :],
                              ALU.subtract), "v")
        C.selfwait(v, "v")
        C.inc(v.tensor_scalar(bbar_sA[:, :], p_bim, c_dre, None, ALU.mult),
              "v")
        C.inc(v.tensor_scalar(bbar_sB[:, :], p_bre, c_dim, None, ALU.mult),
              "v")
        C.selfwait(v, "v")
        C.inc(v.tensor_tensor(bbar_im[:, :], bbar_sA[:, :], bbar_sB[:, :],
                              ALU.add), "v", mark="v_bbar")
        C.inc(v.tensor_scalar(ph_p[:, :], it_fwd[:, :], c_theta, None,
                              ALU.mult), "v")
        C.selfwait(v, "v")
        C.inc(v.tensor_scalar(ph_q[:, :], ph_p[:, :], 1.0 / TWO_PI, None,
                              ALU.mult), "v")
        C.selfwait(v, "v")
        C.inc(v.tensor_copy(ph_qi, ph_q[:, :]), "v")
        C.selfwait(v, "v")
        C.inc(v.tensor_copy(ph_r[:, :], ph_qi), "v")
        C.selfwait(v, "v")
        C.inc(v.scalar_tensor_tensor(ph_p[:, :], ph_r[:, :], -TWO_PI,
                                     ph_p[:, :], ALU.mult, ALU.add), "v")
        C.selfwait(v, "v")
        C.inc(v.tensor_scalar(ph_r[:, :], ph_p[:, :], PI, None, ALU.is_gt),
              "v")
        C.selfwait(v, "v")
        C.inc(v.scalar_tensor_tensor(ph_p[:, :], ph_r[:, :], -TWO_PI,
                                     ph_p[:, :], ALU.mult, ALU.add), "v")
        C.selfwait(v, "v")
        C.inc(v.tensor_scalar(ph_r[:, :], ph_p[:, :], -PI, None, ALU.is_lt),
              "v")
        C.selfwait(v, "v")
        C.inc(v.scalar_tensor_tensor(ph_p[:, :], ph_r[:, :], TWO_PI,
                                     ph_p[:, :], ALU.mult, ALU.add), "v",
              mark="v_pred")
        C.wait(v, "s_abs")
        C.selfwait(v, "v")
        C.inc(v.tensor_scalar(ph_r[:, :], ph_q[:, :], -1.0, PI / 2.0,
                              ALU.mult, ALU.add), "v", mark="v_carg")
        C.wait(v, "t_btr")
        C.inc(v.tensor_copy(bret_bf[:, :], pBbf[:, 0:512]), "v")
        C.inc(v.tensor_copy(bimt_bf[:, :], pBbf[:, 512:1024]), "v",
              mark="v_btrev")
        C.wait(v, "t_drep")
        C.inc(v.tensor_copy(d_rep_bf[:, :], pA[:, :]), "v", mark="v_drep")
        C.wait(v, "t_sdrep")
        C.inc(v.tensor_copy(sdrep[:, :], pA[:, :]), "v", mark="v_sdrep")

        NTOT = 2 * NT
        for sl in range(NTOT + 3):
            j = sl
            if j < NTOT:
                C.wait(v, f"du_{j}")
                C.selfwait(v, "v")
                for gq in range(4):
                    C.inc(v.bn_stats(statbuf[:, 6 * gq:6 * (gq + 1)],
                                     ubuf[j % 3][:, 512 * gq:512 * (gq + 1)]),
                          "v")
                C.selfwait(v, "v")
                C.inc(v.bn_aggr(mv_p[j % 2], statbuf[:, :]), "v",
                      mark=f"aggr_{j}")
            k = sl - 1
            if 0 <= k < NTOT:
                C.wait(v, f"sqrt_{k}")
                C.selfwait(v, "v")
                C.inc(v.reciprocal(rstd_p[k % 2], sd_p[k % 2]), "v")
                C.inc(v.tensor_scalar(c_negm, mv_p[k % 2][:, 0:1], -1.0,
                                      None, ALU.mult), "v")
                C.selfwait(v, "v")
                C.inc(v.tensor_tensor(nmr_p[k % 2], c_negm, rstd_p[k % 2],
                                      ALU.mult), "v", mark=f"lnc_{k}")
            k = sl - 2
            if 0 <= k < NTOT:
                C.wait(v, f"tr_{k}")
                i = k % NT
                for gq in range(2):
                    C.inc(v.tensor_copy(
                        t1nT_full[:, i * 2048 + 512 * gq:
                                  i * 2048 + 512 * (gq + 1)],
                        pBbf[:, 512 * gq:512 * (gq + 1)]), "v",
                        mark=(f"t1nTv_{k}" if gq == 1 else None))
            k = sl - 3
            if 0 <= k < NTOT:
                C.wait(v, f"bu_{k}")
                pb = pC if k % 2 == 0 else pD
                bre_dst, bim_dst = bu_dsts(k)
                C.inc(v.tensor_copy(bre_dst, pb[0:NCH, 0:128]), "v")
                C.inc(v.tensor_copy(bim_dst, pb[0:NCH, 128:256]), "v",
                      mark=f"buev_{k}")

        C.inc(v.tensor_scalar(tC[:, :], it_fwd[:, :], c_lnr, None, ALU.mult),
              "v", mark="v_rpe")
        C.wait(v, "s_rpow")
        C.inc(v.tensor_tensor(tA[:, :], cosT[:, :], rp_v[:, :], ALU.mult),
              "v")
        C.inc(v.tensor_tensor(tB[:, :], sinT[:, :], rp_v[:, :], ALU.mult),
              "v")
        C.selfwait(v, "v")
        C.inc(v.tensor_tensor(tC[:, :], tA[:, :], bu_re_p[:, :], ALU.mult),
              "v")
        C.selfwait(v, "v")
        C.inc(v.reduce_sum(c_t1, tC[:, :], axis=AX.X), "v")
        C.selfwait(v, "v")
        C.inc(v.tensor_tensor(tC[:, :], tB[:, :], bu_im_p[:, :], ALU.mult),
              "v")
        C.selfwait(v, "v")
        C.inc(v.reduce_sum(c_t2, tC[:, :], axis=AX.X), "v")
        C.selfwait(v, "v")
        C.inc(v.tensor_tensor(tC[:, :], tA[:, :], bu_im_p[:, :], ALU.mult),
              "v")
        C.selfwait(v, "v")
        C.inc(v.reduce_sum(c_t3, tC[:, :], axis=AX.X), "v")
        C.selfwait(v, "v")
        C.inc(v.tensor_tensor(tC[:, :], tB[:, :], bu_re_p[:, :], ALU.mult),
              "v")
        C.selfwait(v, "v")
        C.inc(v.reduce_sum(c_t4, tC[:, :], axis=AX.X), "v")
        C.selfwait(v, "v")
        C.inc(v.tensor_tensor(c_sr, c_t1, c_t2, ALU.add), "v")
        C.inc(v.tensor_tensor(c_si, c_t3, c_t4, ALU.subtract), "v")
        C.selfwait(v, "v")
        C.inc(v.tensor_tensor(c_t1, cosT[:, T - 1:T], c_sr, ALU.mult), "v")
        C.inc(v.tensor_tensor(c_t2, sinT[:, T - 1:T], c_si, ALU.mult), "v")
        C.inc(v.tensor_tensor(c_t3, sinT[:, T - 1:T], c_sr, ALU.mult), "v")
        C.inc(v.tensor_tensor(c_t4, cosT[:, T - 1:T], c_si, ALU.mult), "v")
        C.selfwait(v, "v")
        C.inc(v.tensor_tensor(c_cre, c_t1, c_t2, ALU.subtract), "v")
        C.inc(v.tensor_tensor(c_cim, c_t3, c_t4, ALU.add), "v")
        C.selfwait(v, "v")
        C.inc(v.tensor_tensor(tA[:, :], cosT[:, :], bu_re_m[:, :], ALU.mult),
              "v")
        C.inc(v.tensor_tensor(tB[:, :], sinT[:, :], bu_im_m[:, :], ALU.mult),
              "v")
        C.selfwait(v, "v")
        C.inc(v.tensor_tensor(bt_re[:, :], tA[:, :], tB[:, :], ALU.add), "v")
        C.selfwait(v, "v")
        C.inc(v.tensor_tensor(tA[:, :], cosT[:, :], bu_im_m[:, :], ALU.mult),
              "v")
        C.inc(v.tensor_tensor(tB[:, :], sinT[:, :], bu_re_m[:, :], ALU.mult),
              "v")
        C.selfwait(v, "v")
        C.inc(v.tensor_tensor(bt_im[:, :], tA[:, :], tB[:, :], ALU.subtract),
              "v")
        C.inc(v.memset(tC[:, :], 1.0), "v")
        C.selfwait(v, "v")
        C.inc(v.tensor_scalar(tC[:, :], tC[:, :], c_r, None, ALU.mult), "v")
        C.selfwait(v, "v")
        C.inc(v.tensor_tensor_scan(tA[:, :], tC[:, :], bt_re[:, :], c_cre,
                                   ALU.mult, ALU.add), "v")
        C.inc(v.tensor_tensor_scan(tB[:, :], tC[:, :], bt_im[:, :], c_cim,
                                   ALU.mult, ALU.add), "v")
        C.selfwait(v, "v")
        C.inc(v.tensor_tensor(tC[:, :], cosT[:, :], tA[:, :], ALU.mult), "v")
        C.inc(v.tensor_tensor(bt_re[:, :], sinT[:, :], tB[:, :], ALU.mult),
              "v")
        C.selfwait(v, "v")
        C.inc(v.tensor_tensor(xs_re[:, :], tC[:, :], bt_re[:, :],
                              ALU.subtract), "v")
        C.selfwait(v, "v")
        C.inc(v.tensor_tensor(tC[:, :], cosT[:, :], tB[:, :], ALU.mult), "v")
        C.inc(v.tensor_tensor(bt_im[:, :], sinT[:, :], tA[:, :], ALU.mult),
              "v")
        C.selfwait(v, "v")
        C.inc(v.tensor_tensor(xs_im[:, :], tC[:, :], bt_im[:, :], ALU.add),
              "v", mark="v_scan_done")

        for sl in range(NT + 5):
            k = sl - 2
            if 0 <= k < NT:
                C.wait(v, f"ysqrt_{k}")
                C.selfwait(v, "v")
                C.inc(v.reciprocal(rsy_p[k % 2], sdy_p[k % 2]), "v")
                C.inc(v.tensor_scalar(c_negm, mvy_p[k % 2][:, 0:1], -1.0,
                                      None, ALU.mult), "v")
                C.selfwait(v, "v")
                C.inc(v.tensor_tensor(nmy_p[k % 2], c_negm, rsy_p[k % 2],
                                      ALU.mult), "v", mark=f"ylnc_{k}")
            k = sl - 4
            if 0 <= k < NT:
                C.wait(v, f"x1sqrt_{k}")
                C.selfwait(v, "v")
                C.inc(v.reciprocal(rsx_p[k % 2], sdx_p[k % 2]), "v")
                C.inc(v.tensor_scalar(c_negm, mvx_p[k % 2][:, 0:1], -1.0,
                                      None, ALU.mult), "v")
                C.selfwait(v, "v")
                C.inc(v.tensor_tensor(nmx_p[k % 2], c_negm, rsx_p[k % 2],
                                      ALU.mult), "v", mark=f"xlnc_{k}")
            k = sl - 3
            if 0 <= k < NT:
                C.wait(v, f"ln2_{k}")
                C.wait(v, f"dx0_{k}")
                if k >= 2:
                    C.wait(v, f"dx1w_{k-2}")
                    C.wait(v, f"xnapply_{k-2}")
                C.selfwait(v, "v")
                C.inc(v.tensor_tensor(x1buf[k % 2][:, :],
                                      ybuf_p[k % 2][:, :],
                                      x0buf_p[k % 2][:, :],
                                      ALU.add), "v", mark=f"x1_{k}")
                C.selfwait(v, "v")
                for gq in range(4):
                    C.inc(v.bn_stats(
                        statbuf[:, 6 * gq:6 * (gq + 1)],
                        x1buf[k % 2][:, 512 * gq:512 * (gq + 1)]), "v")
                C.selfwait(v, "v")
                C.inc(v.bn_aggr(mvx_p[k % 2], statbuf[:, :]), "v",
                      mark=f"x1aggr_{k}")
            k = sl - 1
            if 0 <= k < NT:
                C.wait(v, f"ymm_{k}")
                if k >= 2:
                    C.wait(v, f"ln2_{k-2}")
                C.selfwait(v, "v")
                for gq in range(4):
                    C.inc(v.tensor_copy(
                        ybuf_p[k % 2][:, 512 * gq:512 * (gq + 1)],
                        pA[:, 512 * gq:512 * (gq + 1)]), "v",
                        mark=(f"yev_{k}" if gq == 3 else None))
                C.selfwait(v, "v")
                for gq in range(4):
                    C.inc(v.bn_stats(
                        statbuf[:, 6 * gq:6 * (gq + 1)],
                        ybuf_p[k % 2][:, 512 * gq:512 * (gq + 1)]), "v")
                C.selfwait(v, "v")
                C.inc(v.bn_aggr(mvy_p[k % 2], statbuf[:, :]), "v",
                      mark=f"yaggr_{k}")
            k = sl - 5
            if 0 <= k < NT:
                C.wait(v, f"xntr_{k}")
                if k >= 9:
                    C.wait(v, f"dxnw_{k-9}")
                for gq in range(2):
                    C.inc(v.tensor_copy(xnev_dst(k, gq), xnev_src(k, gq)),
                          "v", mark=(f"xnevv_{k}" if gq == 1 else None))

        def tail_tile(X, ti):
            C.wait(v, f"dyr_{X}_{ti}")
            C.selfwait(v, "v")
            C.inc(v.tensor_scalar(yrbuf_p[ti % 2][:, :],
                                  yrbuf_p[ti % 2][:, :], l4nm[ti],
                                  l4rs[ti], ALU.add, ALU.mult), "v",
                  mark=f"ln4apply_{X}_{ti}")
            C.wait(v, f"dx1r_{X}_{ti}")
            gk = X * 8 + ti
            if gk >= 1:
                pgk = gk - 1
                C.wait(v, f"dout_{pgk // 8}_{pgk % 8}")
            C.selfwait(v, "v")
            C.inc(v.tensor_tensor(outbuf[0][:, :], yrbuf_p[ti % 2][:, :],
                                  x1rbuf_p[ti % 2][:, :], ALU.add), "v",
                  mark=f"out_{X}_{ti}")

        for X in range(2):
            for hb in range(HB):
                if X == 0 and hb == 0:
                    C.wait(v, "dxnw_6")
                    C.wait(v, "dxnw_7")
                    C.wait(v, f"dx1w_{NT-2}")
                    C.wait(v, f"dx1w_{NT-1}")
                if X == 1 and hb == 0:
                    C.wait(v, "dyw_0_3_6")
                if X == 1 and hb % 5 == 4 and hb // 5 < 8:
                    tail_tile(0, hb // 5)
                C.wait(v, f"s1mm_{X}_{hb}")
                C.selfwait(v, "v")
                if hb % 2 == 0:
                    ups = [pA[:, 1024:1536], pA[:, 1536:2048]]
                else:
                    ups = [pC[:, :], pD[:, :]]
                for tg in range(2):
                    C.inc(v.tensor_scalar(
                        us_s1[:, tg * 512:(tg + 1) * 512], ups[tg],
                        subuf[:, hb:hb + 1], None, ALU.mult), "v",
                        mark=(f"uev_{X}_{hb}" if tg == 1 else None))
                C.wait(v, f"gs_{X}_{hb}")
                C.selfwait(v, "v")
                for tg in range(2):
                    C.inc(v.tensor_tensor(
                        Hcols(hb, tg), h2buf[:, tg * 512:(tg + 1) * 512],
                        us_s1[:, tg * 512:(tg + 1) * 512], ALU.mult), "v",
                        mark=(f"h_{X}_{hb}" if tg == 1 else None))
            PS2v = [pA[:, 0:512], pA[:, 512:1024], pA[:, 1024:1536],
                    pA[:, 1536:2048], pB[:, 0:512], pB[:, 512:1024],
                    pC[:, :], pD[:, :]]
            for dg in range(4):
                for ti in range(8):
                    par = (dg * 8 + ti) % 2
                    C.wait(v, f"s2f_{X}_{dg}_{ti}")
                    if X == 0:
                        k = dg * 8 + ti
                        if k >= 4:
                            pk = k - 4
                            C.wait(v, f"dyw_{X}_{pk // 8}_{pk % 8}")
                        dst = ystage4[k % 4][:, :]
                    elif ti < 6 or dg == 3:
                        dst = ybig[ti][:, dg * 512:(dg + 1) * 512]
                    else:
                        if dg == 0:
                            C.wait(v, f"dyw_0_3_{ti - 2}")
                        else:
                            C.wait(v, f"dyw_1_{dg-1}_{ti}")
                        dst = ystage[ti - 6][:, :]
                    C.inc(v.tensor_tensor(dst, PS2v[ti],
                                          sdrep[:, dg * 512:(dg + 1) * 512],
                                          ALU.mult), "v")
                    C.selfwait(v, "v")
                    C.inc(v.bn_stats(
                        ln4stats[:, ti * 24 + dg * 6:ti * 24 + dg * 6 + 6],
                        dst), "v", mark=f"ys_{X}_{dg}_{ti}")
            C.selfwait(v, "v")
            for ti in range(8):
                C.inc(v.bn_aggr(l4mv[ti], ln4stats[:, ti * 24:(ti + 1) * 24]),
                      "v", mark=f"l4aggr_{X}_{ti}")
            for ti in range(8):
                C.wait(v, f"l4sqrt_{X}_{ti}")
                C.selfwait(v, "v")
                C.inc(v.reciprocal(l4rs[ti], l4nm[ti]), "v")
                C.inc(v.tensor_scalar(c_negm, l4mv[ti][:, 0:1], -1.0, None,
                                      ALU.mult), "v")
                C.selfwait(v, "v")
                C.inc(v.tensor_tensor(l4nm[ti], c_negm, l4rs[ti], ALU.mult),
                      "v", mark=f"l4c_{X}_{ti}")
            if X == 1:
                for ti in range(8):
                    if ti == 0:
                        C.wait(v, "s2e_19")
                    if ti == 1:
                        C.wait(v, "s2e_23")
                    if ti % 2 == 1:
                        if ti == 7:
                            C.wait(v, "dyr_1_7")
                        C.selfwait(v, "v")
                        C.inc(v.tensor_scalar(ybig[ti][:, :], ybig[ti][:, :],
                                              l4nm[ti], l4rs[ti], ALU.add,
                                              ALU.mult), "v")
                    else:
                        C.wait(v, f"l4app_{ti}")
                    C.wait(v, f"dx1r_1_{ti}")
                    if ti >= 2:
                        C.wait(v, f"dout_1_{ti-2}")
                    C.selfwait(v, "v")
                    C.inc(v.tensor_tensor(outb2[ti % 2][:, :],
                                          ybig[ti][:, :],
                                          x1rb2[ti % 2][:, :], ALU.add),
                          "v", mark=f"out_1_{ti}")

    def run_pass(pn):
        C.begin(pn, sems if pn == 2 else None)
        if pn == 1:
            emit_sync(_FakeEng())
            emit_gpsimd(_FakeEng())
            emit_scalar(_FakeEng())
            emit_vector(_FakeEng())
            emit_tensor(_FakeEng())
            return
        with nc.Block() as block:
            @block.sync
            def _(q):
                emit_sync(q)

            @block.gpsimd
            def _(g):
                emit_gpsimd(g)

            @block.scalar
            def _(s):
                emit_scalar(s)

            @block.vector
            def _(v):
                emit_vector(v)

            @block.tensor
            def _(t):
                emit_tensor(t)

    run_pass(1)
    cnt1 = dict(C.cnt)
    run_pass(2)
    assert C.cnt == cnt1, (C.cnt, cnt1)
    st.close()
    return nc


def _u8view(wq_int32):
    a = np.ascontiguousarray(np.asarray(wq_int32))
    assert a.dtype == np.int32
    return np.ascontiguousarray(a.view(np.uint8)[..., ::4])


def _stage_w_hd(wq):
    a = _u8view(wq).reshape(HB, 128, 16, 128)
    a = np.ascontiguousarray(a.transpose(0, 3, 2, 1))
    return a.reshape(HB, 128, 2048)


def _stage_w_dh(wq):
    a = np.ascontiguousarray(_u8view(wq).T)
    a = a.reshape(HB, 128, 4, 512).transpose(2, 1, 0, 3)
    return np.ascontiguousarray(a).reshape(4, 128, HB * 512)


def kernel(L0_attn_u, L0_attn_Lambda_unconstrained_re, L0_attn_Lambda_im,
           L0_attn_B_re, L0_attn_B_im, L0_attn_C_re, L0_attn_C_im,
           L0_attn_D, L0_attn_log_Delta, L0_ffn_wq_gate, L0_ffn_s_gate,
           L0_ffn_wq_up, L0_ffn_s_up, L0_ffn_wq_down, L0_ffn_s_down):
    u = np.ascontiguousarray(np.asarray(L0_attn_u, dtype=np.float32))
    wg = _stage_w_hd(L0_ffn_wq_gate)
    wu = _stage_w_hd(L0_ffn_wq_up)
    wd = _stage_w_dh(L0_ffn_wq_down)
    sg = np.ascontiguousarray(
        np.asarray(L0_ffn_s_gate, np.float32).reshape(HB, 128).T)
    su = np.ascontiguousarray(
        np.asarray(L0_ffn_s_up, np.float32).reshape(HB, 128).T)
    sd = np.ascontiguousarray(np.asarray(L0_ffn_s_down,
                                         np.float32).reshape(1, D))
    bre = np.ascontiguousarray(np.asarray(L0_attn_B_re, np.float32))
    bim = np.ascontiguousarray(np.asarray(L0_attn_B_im, np.float32))
    cret = np.ascontiguousarray(np.asarray(L0_attn_C_re, np.float32).T)
    cimt = np.ascontiguousarray(np.asarray(L0_attn_C_im, np.float32).T)
    drow = np.ascontiguousarray(np.asarray(L0_attn_D,
                                           np.float32).reshape(1, D))
    lunc = np.ascontiguousarray(
        np.asarray(L0_attn_Lambda_unconstrained_re,
                   np.float32).reshape(NCH, 1))
    lim = np.ascontiguousarray(np.asarray(L0_attn_Lambda_im,
                                          np.float32).reshape(NCH, 1))
    logd = np.ascontiguousarray(
        np.asarray(L0_attn_log_Delta, np.float32)[:NCH].reshape(NCH, 1))
    zeros_u = np.zeros((T, D), np.float32)

    nc = build_program()
    in_maps = []
    for c in range(8):
        b, h = c // 2, c % 2
        mine = np.ascontiguousarray(u[b, h * T:(h + 1) * T, :])
        prev = np.ascontiguousarray(u[b, 0:T, :]) if h == 1 else zeros_u
        in_maps.append(dict(
            u=mine, uprev=prev, wg=wg, wu=wu, wd=wd, sg=sg, su=su, sd=sd,
            bre=bre, bim=bim, cret=cret, cimt=cimt, drow=drow, lunc=lunc,
            lim=lim, logd=logd))
    res = run_bass_kernel_spmd(nc, in_maps, core_ids=list(range(8)))
    out = np.empty((B, L, D), np.float32)
    for c in range(8):
        b, h = c // 2, c % 2
        out[b, h * T:(h + 1) * T, :] = np.asarray(res.results[c]["out"])
    return out


# version 13
# speedup vs baseline: 1.3538x; 1.0215x over previous
import contextlib
import numpy as np
import ml_dtypes
import concourse.bass as bass
import concourse.mybir as mybir
from concourse.bass_utils import run_bass_kernel_spmd

F32 = mybir.dt.float32
BF16 = mybir.dt.bfloat16
U8 = mybir.dt.uint8
I32 = mybir.dt.int32
AF = mybir.ActivationFunctionType
ALU = mybir.AluOpType
AX = mybir.AxisListType

B, L, D = 4, 4096, 2048
HID = 5632
NCH = 32
T = 2048
NT = 16
HB = 44
TH = 1024
PI = float(np.pi)
TWO_PI = float(2.0 * np.pi)
EPS_LN = 1e-5
KB = 1024


class Ctx:

    KEYS = ("d_par", "d_u", "d_u2", "d_u3", "d_x0", "d_x02", "d_x1w",
            "d_x1w2", "d_x1r",
            "d_xnw", "d_xnw2", "d_xnr", "d_w", "d_w2", "d_wda", "d_wdl", "d_wdh", "d_yw",
            "d_yw2", "d_yr", "d_yr2", "d_x1r2", "d_out", "d_out2",
            "v", "s", "t", "g")

    def __init__(self):
        self.marks = {}
        self.pass_num = 0
        self.cnt = {}
        self.sems = None

    def begin(self, pass_num, sems=None):
        self.pass_num = pass_num
        self.cnt = {k: 0 for k in self.KEYS}
        self.sems = sems

    def inc(self, inst, key, by=1, mark=None):
        if self.pass_num == 2:
            inst.then_inc(self.sems[key], by)
        self.cnt[key] += by
        if mark is not None:
            if self.pass_num == 1:
                assert mark not in self.marks, f"dup mark {mark}"
                self.marks[mark] = (key, self.cnt[key])
            else:
                assert self.marks[mark] == (key, self.cnt[key]), \
                    f"mark drift {mark}"
        return self.cnt[key]

    def wait(self, eng, mark, delta=0):
        if self.pass_num == 2:
            key, val = self.marks[mark]
            if val + delta > 0:
                eng.wait_ge(self.sems[key], val + delta)

    def wait_cnt(self, eng, key, val):
        if self.pass_num == 2 and val > 0:
            eng.wait_ge(self.sems[key], val)

    def selfwait(self, eng, key):
        if self.pass_num == 2 and self.cnt[key] > 0:
            eng.wait_ge(self.sems[key], self.cnt[key])


class _FakeInst:
    def then_inc(self, *a, **k):
        return self


class _FakeEng:
    def __getattr__(self, name):
        return lambda *a, **k: _FakeInst()


def build_program():
    nc = bass.Bass("TRN2", target_bir_lowering=False, debug=False)

    d_in = {}

    def din(name, shape, dt):
        d_in[name] = nc.dram_tensor(name, shape, dt, kind="ExternalInput").ap()

    din("u", [T, D], F32)
    din("uprev", [T, D], F32)
    din("wg", [HB, 128, D], U8)
    din("wu", [HB, 128, D], U8)
    din("wd", [4, 128, HB * 512], U8)
    din("sg", [128, HB], F32)
    din("su", [128, HB], F32)
    din("sd", [1, D], F32)
    din("bre", [NCH, D], F32)
    din("bim", [NCH, D], F32)
    din("cret", [NCH, D], F32)
    din("cimt", [NCH, D], F32)
    din("drow", [1, D], F32)
    din("lunc", [NCH, 1], F32)
    din("lim", [NCH, 1], F32)
    din("logd", [NCH, 1], F32)
    out_hbm = nc.dram_tensor("out", [T, D], F32, kind="ExternalOutput").ap()
    x1_spill = nc.dram_tensor("x1sp", [T, D], F32, kind="Internal").ap()
    xnt_spill = nc.dram_tensor("xntsp", [128, 16 * TH], BF16,
                               kind="Internal").ap()
    y_spill = nc.dram_tensor("ysp", [TH, D], F32, kind="Internal").ap()

    st = contextlib.ExitStack()
    ARENA = 207 * KB
    arena = st.enter_context(nc.sbuf_tensor("arena", [128, ARENA], U8)).ap()

    def vf32(off_kb, n):
        off = int(off_kb * KB)
        assert off % 4 == 0 and off + 4 * n <= ARENA, (off_kb, n)
        return arena[:, off:off + 4 * n].bitcast(F32)

    def vbf(off_kb, n):
        off = int(off_kb * KB)
        assert off % 2 == 0 and off + 2 * n <= ARENA, (off_kb, n)
        return arena[:, off:off + 2 * n].bitcast(BF16)

    def vi32(off_kb, n):
        off = int(off_kb * KB)
        return arena[:, off:off + 4 * n].bitcast(I32)

    t1nT_full = vbf(0, 32768)
    wd_lo = vbf(0, 11264)
    wd_hi = vbf(22, 11264)
    wgbuf = [vbf(44, 2048), vbf(48, 2048)]
    wubuf = [vbf(52, 2048), vbf(56, 2048)]
    us_s1 = vbf(60, 1024)
    yrbuf_p = [vf32(0, 2048), vf32(8, 2048)]
    x1rbuf_p = [vf32(16, 2048), vf32(24, 2048)]
    outbuf = [vf32(32, 2048), vf32(40, 1024), ]
    ybig = [vf32(64, 2048), vf32(72, 2048), vf32(80, 2048), vf32(88, 2048),
            vf32(44, 2048), vf32(52, 2048), vf32(96, 2048), vf32(104, 2048)]
    x1rb2 = [vf32(112, 2048), vf32(120, 2048)]
    outb2 = [vf32(128, 2048), vf32(136, 2048)]

    xnt = vbf(64, 16384)
    drow_f = vf32(64, 2048)[0:1, :]
    sdrow = vf32(72, 2048)[0:1, :]
    drow_bf = vbf(80, 2048)[0:1, :]

    ubuf = [vf32(96, 2048), vf32(104, 2048), vf32(84, 2048)]
    t1ntmp = [vbf(112, 2048), vbf(116, 2048)]
    bu_re_m = vf32(120, 2048)[0:NCH, :]
    bu_im_m = vf32(128, 2048)[0:NCH, :]
    bu_re_p = vf32(136, 2048)[0:NCH, :]
    bu_im_p = vf32(144, 2048)[0:NCH, :]
    cosT = vf32(152, 2048)[0:NCH, :]
    sinT = vf32(160, 2048)[0:NCH, :]
    it_fwd = vf32(168, 2048)[0:NCH, :]
    rp_v = vf32(168, 2048)[0:NCH, :]
    c2re_v = vbf(176, 2048)[0:NCH, :]
    c2im_v = vbf(180, 2048)[0:NCH, :]
    bbar_sA = vf32(96, 2048)[0:NCH, :]
    bbar_sB = vf32(104, 2048)[0:NCH, :]
    bbar_re = vbf(112, 2048)[0:NCH, :]
    bbar_im = vbf(116, 2048)[0:NCH, :]
    ph_p = vf32(120, 2048)[0:NCH, :]
    ph_q = vf32(128, 2048)[0:NCH, :]
    ph_qi = vi32(136, 2048)[0:NCH, :]
    ph_r = vf32(144, 2048)[0:NCH, :]
    tA = vf32(96, 2048)[0:NCH, :]
    tB = vf32(104, 2048)[0:NCH, :]
    tC = vf32(112, 2048)[0:NCH, :]
    bt_re = vf32(136, 2048)[0:NCH, :]
    bt_im = vf32(144, 2048)[0:NCH, :]
    xs_re = vbf(120, 2048)[0:NCH, :]
    xs_im = vbf(124, 2048)[0:NCH, :]
    ybuf_p = [vf32(96, 2048), vf32(104, 2048)]
    x1buf = [vf32(128, 2048), vf32(136, 2048)]
    x0buf_p = [vf32(144, 2048), vf32(168, 2048)]
    xnbuf = vbf(152, 2048)
    xnt_stage = vbf(156, 2048)
    Hbuf = vbf(96, 45056)

    d_rep_bf = vbf(184, 2048)
    sdrep = vf32(188, 2048)

    id_bf = vbf(196, 128)
    onesrow_bf = vbf(196.25, 128)[0:1, :]
    onesrow_f = vf32(196.5, 128)[0:1, :]
    sgbuf = vf32(197, HB)
    subuf = vf32(197.25, HB)
    colbuf = vf32(197.5, 96)
    statbuf = vf32(197.875, 24)
    bret_bf = vbf(198.25, 512)
    bimt_bf = vbf(199.25, 512)
    ystage = [vf32(200.25, 512), vf32(202.25, 512)]
    ystage4 = ystage + [vf32(60, 512), vf32(62, 512)]
    ln4stats = vf32(204.25, 192)
    h2buf = vbf(205, 1024)

    pA = st.enter_context(nc.psum_tensor("pA", [128, 2048], F32)).ap()
    pB = st.enter_context(nc.psum_tensor("pB", [128, 1024], F32)).ap()
    pC = st.enter_context(nc.psum_tensor("pC", [128, 512], F32)).ap()
    pD = st.enter_context(nc.psum_tensor("pD", [128, 512], F32)).ap()
    pBbf = pB.bitcast(BF16)

    sems = {k: st.enter_context(nc.semaphore(k)) for k in Ctx.KEYS}
    C = Ctx()

    p_lunc = colbuf[0:NCH, 0:1]
    p_lim = colbuf[0:NCH, 1:2]
    p_logd = colbuf[0:NCH, 2:3]
    c_delta = colbuf[0:NCH, 3:4]
    c_lamre = colbuf[0:NCH, 4:5]
    c_lnr = colbuf[0:NCH, 5:6]
    c_theta = colbuf[0:NCH, 6:7]
    c_r = colbuf[0:NCH, 7:8]
    c_are = colbuf[0:NCH, 8:9]
    c_aim = colbuf[0:NCH, 9:10]
    c_t1 = colbuf[0:NCH, 10:11]
    c_t2 = colbuf[0:NCH, 11:12]
    c_t3 = colbuf[0:NCH, 12:13]
    c_t4 = colbuf[0:NCH, 13:14]
    c_dre = colbuf[0:NCH, 14:15]
    c_dim = colbuf[0:NCH, 15:16]
    c_eps = colbuf[:, 16:17]
    c_cre = colbuf[0:NCH, 17:18]
    c_cim = colbuf[0:NCH, 18:19]
    c_sr = colbuf[0:NCH, 19:20]
    c_si = colbuf[0:NCH, 20:21]
    c_negm = colbuf[:, 21:22]
    c_rstd = colbuf[:, 22:23]
    c_sd = colbuf[:, 23:24]
    c_Tlnr = colbuf[0:NCH, 24:25]
    c_i32 = colbuf.bitcast(I32)[0:NCH, 25:26]
    c_sinth = colbuf[0:NCH, 26:27]
    c_sh = colbuf[0:NCH, 27:28]
    c_costh = colbuf[0:NCH, 28:29]
    c_thred = colbuf[0:NCH, 29:30]
    mvbuf = colbuf[:, 30:32]
    mv_p = [colbuf[:, 32:34], colbuf[:, 34:36]]
    sd_p = [colbuf[:, 36:37], colbuf[:, 37:38]]
    rstd_p = [colbuf[:, 38:39], colbuf[:, 39:40]]
    nmr_p = [colbuf[:, 40:41], colbuf[:, 41:42]]
    mvy_p = [colbuf[:, 42:44], colbuf[:, 44:46]]
    sdy_p = [colbuf[:, 46:47], colbuf[:, 47:48]]
    rsy_p = [colbuf[:, 48:49], colbuf[:, 49:50]]
    nmy_p = [colbuf[:, 50:51], colbuf[:, 51:52]]
    mvx_p = [colbuf[:, 52:54], colbuf[:, 54:56]]
    sdx_p = [colbuf[:, 56:57], colbuf[:, 57:58]]
    rsx_p = [colbuf[:, 58:59], colbuf[:, 59:60]]
    nmx_p = [colbuf[:, 60:61], colbuf[:, 61:62]]
    l4mv = [colbuf[:, 62 + 2 * t_: 64 + 2 * t_] for t_ in range(8)]
    l4rs = [colbuf[:, 78 + t_: 79 + t_] for t_ in range(8)]
    l4nm = [colbuf[:, 86 + t_: 87 + t_] for t_ in range(8)]
    p_bre = bu_re_m
    p_bim = bu_im_m
    p_cret = cosT
    p_cimt = sinT

    def Hcols(hb, tg):
        return Hbuf[:, hb * TH + tg * 512: hb * TH + (tg + 1) * 512]

    def bu_dsts(k):
        i = k % NT
        bre = (bu_re_p if k < NT else bu_re_m)[:, i * 128:(i + 1) * 128]
        bim = (bu_im_p if k < NT else bu_im_m)[:, i * 128:(i + 1) * 128]
        return bre, bim

    def xnev_dst(i, gq):
        if i < 8:
            return xnt.rearrange("p (k tt) -> p k tt", k=16)[
                :, 4 * gq:4 * (gq + 1), i * 128:(i + 1) * 128]
        return xnt_stage[:, 512 * gq:512 * (gq + 1)]

    def xnev_src(i, gq):
        s = pBbf[:, 512 * gq:512 * (gq + 1)]
        if i < 8:
            return s.rearrange("p (k tt) -> p k tt", k=4)
        return s

    def emit_sync(q):
        C.inc(q.dma_start(p_lunc, d_in["lunc"]), "d_par", 16)
        C.inc(q.dma_start(p_lim, d_in["lim"]), "d_par", 16)
        C.inc(q.dma_start(p_logd, d_in["logd"]), "d_par", 16)
        C.inc(q.dma_start(p_bre, d_in["bre"]), "d_par", 16)
        C.inc(q.dma_start(p_bim, d_in["bim"]), "d_par", 16)
        C.inc(q.dma_start(p_cret, d_in["cret"]), "d_par", 16)
        C.inc(q.dma_start(p_cimt, d_in["cimt"]), "d_par", 16)
        C.inc(q.dma_start(drow_f, d_in["drow"]), "d_par", 16)
        C.inc(q.dma_start(sdrow, d_in["sd"]), "d_par", 16)
        C.inc(q.dma_start(sgbuf[:, :], d_in["sg"]), "d_par", 16)
        C.inc(q.dma_start(subuf[:, :], d_in["su"]), "d_par", 16,
              mark="d_params")

        for j in range(2 * NT):
            src = d_in["uprev"] if j < NT else d_in["u"]
            i = j % NT
            if j < 2:
                C.wait(q, "v_bbar")
            elif j >= 3:
                C.wait(q, f"uapply_{j-3}")
            C.inc(q.dma_start(ubuf[j % 3][:, :],
                              src[i * 128:(i + 1) * 128, :]),
                  ("d_u", "d_u2", "d_u3")[j % 3], 16, mark=f"du_{j}")

        for i in range(NT):
            if i < 2:
                C.wait(q, "v_scan_done")
            else:
                C.wait(q, f"x1_{i-2}")
            C.inc(q.dma_start(x0buf_p[i % 2][:, :],
                              d_in["u"][i * 128:(i + 1) * 128, :]),
                  "d_x0" if i % 2 == 0 else "d_x02", 16, mark=f"dx0_{i}")
            if i >= 1:
                C.wait(q, f"x1_{i-1}")
                C.inc(q.dma_start(x1_spill[(i - 1) * 128:i * 128, :],
                                  x1buf[(i - 1) % 2][:, :]),
                      "d_x1w" if (i - 1) % 2 == 0 else "d_x1w2", 16,
                      mark=f"dx1w_{i-1}")
        C.wait(q, f"x1_{NT-1}")
        C.inc(q.dma_start(x1_spill[(NT - 1) * 128:NT * 128, :],
                          x1buf[(NT - 1) % 2][:, :]),
              "d_x1w" if (NT - 1) % 2 == 0 else "d_x1w2", 16,
              mark=f"dx1w_{NT-1}")

        for X in range(2):
            if X == 0:
                C.wait(q, "s1mm_0_43")
                C.inc(q.dma_start(xnt[:, :], xnt_spill[:, :]), "d_xnr", 16,
                      mark="dxnr")
            for dg in range(4):
                for ti in range(8):
                    if X == 1 and (ti < 6 or dg == 3):
                        continue
                    k = dg * 8 + ti
                    C.wait(q, f"ys_{X}_{dg}_{ti}")
                    if X == 0:
                        src = ystage4[k % 4][:, :]
                        skey = ("d_yw", "d_yw2", "d_x1w", "d_x1w2")[k % 4]
                    else:
                        src = ystage[ti - 6][:, :]
                        skey = "d_yw" if ti == 6 else "d_yw2"
                    C.inc(q.dma_start(
                        y_spill[ti * 128:(ti + 1) * 128,
                                dg * 512:(dg + 1) * 512], src),
                        skey, 16, mark=f"dyw_{X}_{dg}_{ti}")
            if X == 0:
                C.wait(q, "s2f_0_3_7")
                for lt in (4, 5, 6, 7):
                    C.wait(q, f"dyw_0_3_{lt}")
                for ti in range(8):
                    if ti >= 2:
                        C.wait(q, f"ln4apply_0_{ti-2}")
                    C.inc(q.dma_start(yrbuf_p[ti % 2][:, :],
                                      y_spill[ti * 128:(ti + 1) * 128, :]),
                          "d_yr" if ti % 2 == 0 else "d_yr2", 16,
                          mark=f"dyr_0_{ti}")
                    if ti >= 2:
                        C.wait(q, f"out_0_{ti-2}")
                    C.inc(q.dma_start(x1rbuf_p[ti % 2][:, :],
                                      x1_spill[ti * 128:(ti + 1) * 128, :]),
                          "d_x1r" if ti % 2 == 0 else "d_x1r2", 16,
                          mark=f"dx1r_0_{ti}")
                    C.wait(q, f"out_0_{ti}")
                    C.inc(q.dma_start(out_hbm[ti * 128:(ti + 1) * 128, :],
                                      outbuf[0][:, :]),
                          "d_out" if ti % 2 == 0 else "d_out2", 16,
                          mark=f"dout_0_{ti}")
            else:
                C.wait(q, "s2e_7")
                C.wait(q, "dyw_1_2_6")
                C.inc(q.dma_start(ybig[6][:, 0:1536],
                                  y_spill[6 * 128:7 * 128, 0:1536]),
                      "d_yr", 16, mark="dyr_1_6")
                C.wait(q, "dyw_1_2_7")
                C.inc(q.dma_start(ybig[7][:, 0:1536],
                                  y_spill[7 * 128:8 * 128, 0:1536]),
                      "d_yr2", 16, mark="dyr_1_7")
                for ti in range(8):
                    if ti == 0:
                        C.wait(q, "s2e_11")
                    elif ti == 1:
                        C.wait(q, "s2e_15")
                    else:
                        C.wait(q, f"out_1_{ti-2}")
                    C.inc(q.dma_start(
                        x1rb2[ti % 2][:, :],
                        x1_spill[(8 + ti) * 128:(9 + ti) * 128, :]),
                        "d_x1r" if ti % 2 == 0 else "d_x1r2", 16,
                        mark=f"dx1r_1_{ti}")
        C.wait(q, "dout_0_6")
        C.wait(q, "dout_1_6")
        C.wait(q, "dout_1_7")

    def emit_gpsimd(g):
        C.inc(g.iota(it_fwd[:, :], pattern=[[1, T]], base=1,
                     channel_multiplier=0,
                     allow_small_or_imprecise_dtypes=True), "g", mark="g_itf")
        C.inc(g.memset(id_bf[:, :], 1.0), "g")
        C.selfwait(g, "g")
        C.inc(g.affine_select(id_bf[:, :], id_bf[:, :], pattern=[[-1, 128]],
                              compare_op=ALU.is_equal, fill=0.0, base=0,
                              channel_multiplier=1), "g", mark="g_id")
        C.inc(g.memset(onesrow_bf[:, :], 1.0), "g")
        C.inc(g.memset(onesrow_f[:, :], 1.0), "g")
        C.inc(g.memset(c_eps, EPS_LN), "g", mark="g_setup")
        C.wait(g, "v_drep")
        for kd in range(16):
            C.inc(g.affine_select(d_rep_bf[:, kd * 128:(kd + 1) * 128],
                                  d_rep_bf[:, kd * 128:(kd + 1) * 128],
                                  pattern=[[-1, 128]],
                                  compare_op=ALU.is_equal, fill=0.0, base=0,
                                  channel_multiplier=1), "g",
                  mark=("g_ddiag" if kd == 15 else None))

        for hb in (0, 1):
            wkey = "d_w" if hb % 2 == 0 else "d_w2"
            C.wait(g, f"ymm_{11 + hb}")
            C.inc(g.dma_start(wgbuf[hb][:, :], d_in["wg"][hb, :, :]),
                  wkey, 16)
            C.wait(g, f"ymm_{13 + hb}")
            C.inc(g.dma_start(wubuf[hb][:, :], d_in["wu"][hb, :, :]),
                  wkey, 16, mark=f"wconv_0_{hb}")

        for k in range(8):
            C.wait(g, f"xnevv_{k + 8}")
            C.wait(g, f"xnevs_{k + 8}")
            C.inc(g.dma_start(
                xnt_spill.rearrange("p (kk ti tt) -> p kk ti tt",
                                    k=16, ti=8, kk=16)[:, :, k, :],
                xnt_stage.rearrange("p (kk tt) -> p kk tt", kk=16)),
                "d_xnw" if k % 2 == 0 else "d_xnw2", 16,
                mark=f"dxnw_{k}")

        for X in range(2):
            for hb in range(HB):
                if X == 0 and hb < 2:
                    continue
                par = hb % 2
                pX, phb = (X, hb - 2) if hb >= 2 else (0, HB - 2 + hb)
                C.wait(g, f"s1mm_{pX}_{phb}")
                wkey = "d_w" if par == 0 else "d_w2"
                C.inc(g.dma_start(wgbuf[par][:, :], d_in["wg"][hb, :, :]),
                      wkey, 16)
                C.inc(g.dma_start(wubuf[par][:, :], d_in["wu"][hb, :, :]),
                      wkey, 16, mark=f"wconv_{X}_{hb}")
            for dg in range(4):
                if dg == 0:
                    if X == 0:
                        C.wait(g, "yev_15")
                    else:
                        C.wait_cnt(g, "d_out", 16 * 4)
                        C.wait_cnt(g, "d_out2", 16 * 4)
                        C.wait(g, "s2q_0_3")
                else:
                    C.wait(g, f"s2q_{X}_{dg-1}")
                C.inc(g.dma_start(
                    wd_lo[:, 0:5632],
                    d_in["wd"][dg, :, 0:5632]), "d_wda", 16,
                    mark=f"dwdla_{X}_{dg}")
                if dg == 0:
                    if X == 1:
                        C.wait(g, "s2lo_0_3")
                else:
                    C.wait(g, f"s2lo_{X}_{dg-1}")
                C.inc(g.dma_start(
                    wd_lo[:, 5632:11264],
                    d_in["wd"][dg, :, 5632:11264]), "d_wdl", 16,
                    mark=f"dwdlo_{X}_{dg}")
                if dg == 0:
                    if X == 1:
                        C.wait(g, "s2f_0_3_7")
                else:
                    C.wait(g, f"s2f_{X}_{dg-1}_7")
                C.inc(g.dma_start(
                    wd_hi[:, :],
                    d_in["wd"][dg, :, 11264:22528]), "d_wdh", 16,
                    mark=f"dwdhi_{X}_{dg}")
        for ti in (0, 2, 4, 6):
            C.wait(g, f"out_1_{ti}")
            C.inc(g.dma_start(out_hbm[(8 + ti) * 128:(9 + ti) * 128, :],
                              outb2[0][:, :]), "d_w", 16,
                  mark=f"dout_1_{ti}")

    def emit_scalar(s):
        C.wait(s, "d_params")
        C.inc(s.activation(c_delta, p_logd, AF.Exp), "s")
        C.inc(s.activation(c_t1, p_lunc, AF.Abs), "s")
        C.selfwait(s, "s")
        C.inc(s.activation(c_t2, c_t1, AF.Exp, scale=-1.0), "s")
        C.selfwait(s, "s")
        C.inc(s.activation(c_t3, c_t2, AF.Ln, bias=1.0), "s", mark="s_p1")
        C.wait(s, "v_lnr")
        C.inc(s.activation(c_r, c_lnr, AF.Exp), "s", mark="s_r")
        C.wait(s, "v_thred")
        C.inc(s.activation(c_sinth, c_thred, AF.Sin), "s")
        C.inc(s.activation(c_sh, c_thred, AF.Sin, scale=0.5), "s",
              mark="s_sincos")
        C.wait(s, "v_pred")
        C.inc(s.activation(sinT[:, :], ph_p[:, :], AF.Sin), "s")
        C.inc(s.activation(ph_q[:, :], ph_p[:, :], AF.Abs), "s",
              mark="s_abs")
        C.wait(s, "v_carg")
        C.inc(s.activation(cosT[:, :], ph_r[:, :], AF.Sin), "s",
              mark="s_tables")
        C.inc(s.activation(drow_bf, drow_f, AF.Copy), "s", mark="s_dbf")

        for sl in range(2 * NT + 2):
            j = sl
            if j < 2 * NT:
                C.wait(s, f"aggr_{j}")
                C.inc(s.activation(sd_p[j % 2], mv_p[j % 2][:, 1:2],
                                   AF.Sqrt, bias=c_eps), "s",
                      mark=f"sqrt_{j}")
            k = sl - 1
            if 0 <= k < 2 * NT:
                C.wait(s, f"lnc_{k}")
                i = k % NT
                if k >= 2:
                    C.wait(s, f"tr_{k-2}")
                C.inc(s.activation(t1ntmp[k % 2][:, :],
                                   ubuf[k % 3][:, :], AF.Identity,
                                   scale=rstd_p[k % 2], bias=nmr_p[k % 2]),
                      "s", mark=f"uapply_{k}")
            k = sl - 2
            if 0 <= k < 2 * NT:
                C.wait(s, f"tr_{k}")
                i = k % NT
                for gq in range(2, 4):
                    C.inc(s.activation(
                        t1nT_full[:, i * 2048 + 512 * gq:
                                  i * 2048 + 512 * (gq + 1)],
                        pBbf[:, 512 * gq:512 * (gq + 1)], AF.Copy), "s",
                        mark=(f"t1nTs_{k}" if gq == 3 else None))
        C.wait(s, "v_rpe")
        C.inc(s.activation(rp_v[:, :], tC[:, :], AF.Exp, scale=-1.0,
                           bias=c_Tlnr), "s", mark="s_rpow")
        for sl in range(NT + 5):
            k = sl - 1
            if 0 <= k < NT:
                C.wait(s, f"ymm_{k}")
                if k >= 2:
                    C.wait(s, f"x1_{k-2}")
                for gq in range(4):
                    C.inc(s.activation(
                        ybuf_p[k % 2][:, 512 * gq:512 * (gq + 1)],
                        pA[:, 512 * gq:512 * (gq + 1)], AF.Copy), "s",
                        mark=(f"yev_{k}" if gq == 3 else None))
            if 0 <= k < NT:
                C.wait(s, f"yaggr_{k}")
                C.inc(s.activation(sdy_p[k % 2], mvy_p[k % 2][:, 1:2],
                                   AF.Sqrt, bias=c_eps), "s",
                      mark=f"ysqrt_{k}")
            k = sl - 2
            if 0 <= k < NT:
                C.wait(s, f"ylnc_{k}")
                C.inc(s.activation(ybuf_p[k % 2][:, :], ybuf_p[k % 2][:, :],
                                   AF.Identity, scale=rsy_p[k % 2],
                                   bias=nmy_p[k % 2]), "s", mark=f"ln2_{k}")
            k = sl - 3
            if 0 <= k < NT:
                C.wait(s, f"x1aggr_{k}")
                C.inc(s.activation(sdx_p[k % 2], mvx_p[k % 2][:, 1:2],
                                   AF.Sqrt, bias=c_eps), "s",
                      mark=f"x1sqrt_{k}")
            k = sl - 4
            if 0 <= k < NT:
                C.wait(s, f"xlnc_{k}")
                if k >= 1:
                    C.wait(s, f"xntr_{k-1}")
                C.inc(s.activation(xnbuf[:, :], x1buf[k % 2][:, :],
                                   AF.Identity, scale=rsx_p[k % 2],
                                   bias=nmx_p[k % 2]), "s",
                      mark=f"xnapply_{k}")
            k = sl - 5
            if 0 <= k < NT:
                C.wait(s, f"xntr_{k}")
                if k >= 9:
                    C.wait(s, f"dxnw_{k-9}")
                for gq in range(2, 4):
                    C.inc(s.activation(xnev_dst(k, gq), xnev_src(k, gq),
                                       AF.Copy), "s",
                          mark=(f"xnevs_{k}" if gq == 3 else None))
        for X in range(2):
            for hb in range(HB):
                C.wait(s, f"gmm_{X}_{hb}")
                if hb >= 1 or X == 1:
                    pX, phb = (X, hb - 1) if hb >= 1 else (0, HB - 1)
                    C.wait(s, f"h_{pX}_{phb}")
                gsrc = pA if hb % 2 == 0 else pB
                for tg in range(2):
                    C.inc(s.activation(h2buf[:, tg * 512:(tg + 1) * 512],
                                       gsrc[:, tg * 512:(tg + 1) * 512],
                                       AF.Silu, scale=sgbuf[:, hb:hb + 1]),
                          "s", mark=(f"gs_{X}_{hb}" if tg == 1 else None))
            for dg in range(4):
                C.wait(s, f"dwdla_{X}_{dg}")
                C.inc(s.activation(wd_lo[:, 0:5632], wd_lo[:, 0:5632],
                                   AF.Copy, bias=-128.0), "s",
                      mark=f"wdla_{X}_{dg}")
                C.wait(s, f"dwdlo_{X}_{dg}")
                C.inc(s.activation(wd_lo[:, 5632:11264], wd_lo[:, 5632:11264],
                                   AF.Copy, bias=-128.0), "s",
                      mark=f"wdlo_{X}_{dg}")
                C.wait(s, f"dwdhi_{X}_{dg}")
                C.inc(s.activation(wd_hi[:, :], wd_hi[:, :], AF.Copy,
                                   bias=-128.0), "s", mark=f"wdhi_{X}_{dg}")
            for ti in range(8):
                C.wait(s, f"l4aggr_{X}_{ti}")
                C.inc(s.activation(l4nm[ti], l4mv[ti][:, 1:2], AF.Sqrt,
                                   bias=c_eps), "s", mark=f"l4sqrt_{X}_{ti}")
            if X == 1:
                for ti in (0, 2, 4, 6):
                    C.wait(s, f"l4c_1_{ti}")
                    C.wait(s, f"ys_1_3_{ti}")
                    if ti == 6:
                        C.wait(s, "dyr_1_6")
                    C.inc(s.activation(ybig[ti][:, :], ybig[ti][:, :],
                                       AF.Identity, scale=l4rs[ti],
                                       bias=l4nm[ti]), "s",
                          mark=f"l4app_{ti}")
                    if ti >= 2:
                        to = ti - 1
                        C.wait(s, f"out_1_{to}")
                        C.inc(s.dma_start(
                            out_hbm[(8 + to) * 128:(9 + to) * 128, :],
                            outb2[1][:, :]), "d_out2", 16,
                            mark=f"dout_1_{to}")
                for to in (7,):
                    C.wait(s, f"out_1_{to}")
                    C.inc(s.dma_start(
                        out_hbm[(8 + to) * 128:(9 + to) * 128, :],
                        outb2[1][:, :]), "d_out2", 16,
                        mark=f"dout_1_{to}")

    def emit_tensor(t):
        C.wait(t, "g_id")
        C.wait(t, "v_bbar")
        for kt in range(16):
            C.inc(t.transpose(pBbf[:, kt * 32:(kt + 1) * 32],
                              bbar_re[:, kt * 128:(kt + 1) * 128],
                              id_bf[0:NCH, 0:NCH]), "t")
        for kt in range(16):
            C.inc(t.transpose(pBbf[:, 512 + kt * 32:512 + (kt + 1) * 32],
                              bbar_im[:, kt * 128:(kt + 1) * 128],
                              id_bf[0:NCH, 0:NCH]), "t",
                  mark=("t_btr" if kt == 15 else None))
        C.wait(t, "s_dbf")
        for dg in range(4):
            C.inc(t.matmul(pA[:, dg * 512:(dg + 1) * 512], onesrow_bf,
                           drow_bf[:, dg * 512:(dg + 1) * 512],
                           start=True, stop=True), "t",
                  mark=("t_drep" if dg == 3 else None))
        C.wait(t, "v_drep")
        for dg in range(4):
            C.inc(t.matmul(pA[:, dg * 512:(dg + 1) * 512], onesrow_f,
                           sdrow[:, dg * 512:(dg + 1) * 512],
                           start=True, stop=True), "t",
                  mark=("t_sdrep" if dg == 3 else None))

        for j in range(2 * NT):
            C.wait(t, f"uapply_{j}")
            if j >= 1:
                C.wait(t, f"t1nTv_{j-1}")
                C.wait(t, f"t1nTs_{j-1}")
            i = j % NT
            tsrc = t1ntmp[j % 2]
            for kt in range(16):
                C.inc(t.transpose(pBbf[:, kt * 128:(kt + 1) * 128],
                                  tsrc[:, kt * 128:(kt + 1) * 128],
                                  id_bf[:, :]), "t",
                      mark=(f"tr_{j}" if kt == 15 else None))
            C.wait(t, f"t1nTv_{j}")
            C.wait(t, f"t1nTs_{j}")
            if j >= 2:
                C.wait(t, f"buev_{j-2}")
            pb = pC if j % 2 == 0 else pD
            for kt in range(16):
                C.inc(t.matmul(pb[0:NCH, 0:128],
                               bret_bf[:, kt * 32:(kt + 1) * 32],
                               t1nT_full[:, i * 2048 + kt * 128:
                                         i * 2048 + (kt + 1) * 128],
                               start=(kt == 0), stop=(kt == 15)), "t")
            for kt in range(16):
                C.inc(t.matmul(pb[0:NCH, 128:256],
                               bimt_bf[:, kt * 32:(kt + 1) * 32],
                               t1nT_full[:, i * 2048 + kt * 128:
                                         i * 2048 + (kt + 1) * 128],
                               start=(kt == 0), stop=(kt == 15)), "t",
                      mark=(f"bu_{j}" if kt == 15 else None))

        def xntr_block(k):
            C.wait(t, f"xnapply_{k}")
            if k == 0:
                C.wait(t, f"buev_{2*NT-1}")
            else:
                C.wait(t, f"xnevv_{k-1}")
                C.wait(t, f"xnevs_{k-1}")
            for kt in range(16):
                C.inc(t.transpose(pBbf[:, kt * 128:(kt + 1) * 128],
                                  xnbuf[:, kt * 128:(kt + 1) * 128],
                                  id_bf[:, :]), "t",
                      mark=(f"xntr_{k}" if kt == 15 else None))

        C.wait(t, "v_scan_done")
        C.wait(t, "v_sdrep")
        for i in range(NT):
            if i >= 1:
                C.wait(t, f"yev_{i-1}")
            if i == 0:
                C.wait(t, "g_ddiag")
            for dg in range(4):
                C.inc(t.matmul(pA[:, dg * 512:(dg + 1) * 512],
                               xs_re[:, i * 128:(i + 1) * 128],
                               c2re_v[:, dg * 512:(dg + 1) * 512],
                               start=True, stop=False), "t")
                C.inc(t.matmul(pA[:, dg * 512:(dg + 1) * 512],
                               xs_im[:, i * 128:(i + 1) * 128],
                               c2im_v[:, dg * 512:(dg + 1) * 512],
                               start=False, stop=False), "t")
            for kd in range(16):
                C.inc(t.matmul(pA[:, kd * 128:(kd + 1) * 128],
                               t1nT_full[:, i * 2048 + kd * 128:
                                         i * 2048 + (kd + 1) * 128],
                               d_rep_bf[:, kd * 128:(kd + 1) * 128],
                               start=False, stop=(kd % 4 == 3)), "t",
                      mark=(f"ymm_{i}" if kd == 15 else None))
            if i >= 5:
                xntr_block(i - 5)
        def s1q0(buf, ps, tg, mk):
            for kt in range(16):
                C.inc(t.matmul(ps,
                               buf[:, kt * 128:(kt + 1) * 128],
                               xnt[:, kt * TH + tg * 512:
                                   kt * TH + (tg + 1) * 512],
                               start=(kt == 0), stop=(kt == 15)), "t",
                      mark=(mk if kt == 15 else None))

        xntr_block(11)
        C.wait(t, "wconv_0_0")
        C.wait(t, "yev_15")
        C.wait(t, "xnevv_3")
        C.wait(t, "xnevs_3")
        s1q0(wgbuf[0], pA[:, 0:512], 0, None)
        xntr_block(12)
        C.wait(t, "xnevv_7")
        C.wait(t, "xnevs_7")
        s1q0(wgbuf[0], pA[:, 512:1024], 1, "gmm_0_0")
        xntr_block(13)
        s1q0(wubuf[0], pA[:, 1024:1536], 0, None)
        xntr_block(14)
        s1q0(wubuf[0], pA[:, 1536:2048], 1, "s1mm_0_0")
        xntr_block(15)

        PS2 = [pA[:, 0:512], pA[:, 512:1024], pA[:, 1024:1536],
               pA[:, 1536:2048], pB[:, 0:512], pB[:, 512:1024], pC, pD]
        for X in range(2):
            for hb in range(HB):
                if X == 0 and hb == 0:
                    continue
                par = hb % 2
                C.wait(t, f"wconv_{X}_{hb}")
                if X == 0 and hb == 1:
                    C.wait(t, "xnevv_15")
                    C.wait(t, "xnevs_15")
                if X == 1 and hb == 0:
                    C.wait(t, "dxnr")
                    C.wait(t, "gs_0_42")
                    C.wait(t, "uev_0_42")
                    C.wait(t, "ys_0_3_3")
                if X == 1 and hb == 1:
                    C.wait(t, "gs_0_43")
                    C.wait(t, "uev_0_43")
                    C.wait(t, "ys_0_3_7")
                if hb >= 2:
                    C.wait(t, f"gs_{X}_{hb-2}")
                    C.wait(t, f"uev_{X}_{hb-2}")
                if par == 0:
                    gps = [pA[:, 0:512], pA[:, 512:1024]]
                    ups = [pA[:, 1024:1536], pA[:, 1536:2048]]
                else:
                    gps = [pB[:, 0:512], pB[:, 512:1024]]
                    ups = [pC[:, :], pD[:, :]]
                for tg in range(2):
                    for kt in range(16):
                        C.inc(t.matmul(
                            gps[tg],
                            wgbuf[par][:, kt * 128:(kt + 1) * 128],
                            xnt[:, kt * TH + tg * 512:
                                kt * TH + (tg + 1) * 512],
                            start=(kt == 0), stop=(kt == 15)), "t",
                            mark=(f"gmm_{X}_{hb}"
                                  if tg == 1 and kt == 15 else None))
                for tg in range(2):
                    for kt in range(16):
                        C.inc(t.matmul(
                            ups[tg],
                            wubuf[par][:, kt * 128:(kt + 1) * 128],
                            xnt[:, kt * TH + tg * 512:
                                kt * TH + (tg + 1) * 512],
                            start=(kt == 0), stop=(kt == 15)), "t",
                            mark=(f"s1mm_{X}_{hb}"
                                  if tg == 1 and kt == 15 else None))
            for dg in range(4):
                for hb in range(HB):
                    if hb == 0:
                        C.wait(t, f"wdla_{X}_{dg}")
                    if hb == 11:
                        C.wait(t, f"wdlo_{X}_{dg}")
                    if hb == 22:
                        C.wait(t, f"wdhi_{X}_{dg}")
                    if dg == 0:
                        C.wait(t, f"h_{X}_{hb}")
                    wsrc = wd_lo if hb < 22 else wd_hi
                    wof = hb if hb < 22 else hb - 22
                    for ti in range(8):
                        if hb == 0:
                            if dg == 0:
                                C.wait(t, f"gs_{X}_{HB-1}")
                                C.wait(t, f"uev_{X}_{HB-1}")
                            else:
                                C.wait(t, f"ys_{X}_{dg-1}_{ti}")
                        C.inc(t.matmul(
                            PS2[ti],
                            Hbuf[:, hb * TH + ti * 128:
                                 hb * TH + (ti + 1) * 128],
                            wsrc[:, wof * 512:(wof + 1) * 512],
                            start=(hb == 0), stop=(hb == HB - 1)), "t",
                            mark=(f"s2f_{X}_{dg}_{ti}" if hb == HB - 1
                                  else (f"s2lo_{X}_{dg}"
                                        if hb == 21 and ti == 7 else
                                        (f"s2q_{X}_{dg}"
                                         if hb == 10 and ti == 7 else
                                         (f"s2e_{hb}"
                                          if X == 1 and dg == 3 and ti == 7
                                          and hb in (7, 11, 15, 19, 23)
                                          else None)))))

    def emit_vector(v):
        C.wait(v, "d_params")
        C.wait(v, "g_setup")
        C.inc(v.tensor_scalar(c2re_v[:, :], p_cret, 2.0, None, ALU.mult),
              "v")
        C.inc(v.tensor_scalar(c2im_v[:, :], p_cimt, -2.0, None, ALU.mult),
              "v", mark="v_c2")
        C.wait(v, "s_p1")
        C.inc(v.tensor_scalar(c_t1, p_lunc, 0.0, None, ALU.max), "v")
        C.selfwait(v, "v")
        C.inc(v.tensor_tensor(c_t1, c_t1, c_t3, ALU.add), "v")
        C.selfwait(v, "v")
        C.inc(v.tensor_scalar(c_lamre, c_t1, 0.0101, -1.0, ALU.add,
                              ALU.mult), "v")
        C.selfwait(v, "v")
        C.inc(v.tensor_tensor(c_lnr, c_lamre, c_delta, ALU.mult), "v",
              mark="v_lnr")
        C.selfwait(v, "v")
        C.inc(v.tensor_scalar(c_Tlnr, c_lnr, float(T), None, ALU.mult), "v")
        C.inc(v.tensor_tensor(c_theta, p_lim, c_delta, ALU.mult), "v")
        C.selfwait(v, "v")
        C.inc(v.tensor_scalar(c_t2, c_theta, 1.0 / TWO_PI, None, ALU.mult),
              "v")
        C.selfwait(v, "v")
        C.inc(v.tensor_copy(c_i32, c_t2), "v")
        C.selfwait(v, "v")
        C.inc(v.tensor_copy(c_t2, c_i32), "v")
        C.selfwait(v, "v")
        C.inc(v.scalar_tensor_tensor(c_thred, c_t2, -TWO_PI, c_theta,
                                     ALU.mult, ALU.add), "v")
        C.selfwait(v, "v")
        C.inc(v.tensor_scalar(c_t2, c_thred, PI, None, ALU.is_gt), "v")
        C.selfwait(v, "v")
        C.inc(v.scalar_tensor_tensor(c_thred, c_t2, -TWO_PI, c_thred,
                                     ALU.mult, ALU.add), "v")
        C.selfwait(v, "v")
        C.inc(v.tensor_scalar(c_t2, c_thred, -PI, None, ALU.is_lt), "v")
        C.selfwait(v, "v")
        C.inc(v.scalar_tensor_tensor(c_thred, c_t2, TWO_PI, c_thred,
                                     ALU.mult, ALU.add), "v", mark="v_thred")
        C.wait(v, "s_sincos")
        C.inc(v.tensor_tensor(c_t2, c_sh, c_sh, ALU.mult), "v")
        C.selfwait(v, "v")
        C.inc(v.tensor_scalar(c_costh, c_t2, -2.0, 1.0, ALU.mult, ALU.add),
              "v")
        C.wait(v, "s_r")
        C.selfwait(v, "v")
        C.inc(v.tensor_tensor(c_are, c_r, c_costh, ALU.mult), "v")
        C.inc(v.tensor_tensor(c_aim, c_r, c_sinth, ALU.mult), "v")
        C.selfwait(v, "v")
        C.inc(v.tensor_scalar(c_t1, c_are, -1.0, None, ALU.add), "v")
        C.inc(v.tensor_tensor(c_t2, c_lamre, c_lamre, ALU.mult), "v")
        C.inc(v.tensor_tensor(c_t3, p_lim, p_lim, ALU.mult), "v")
        C.selfwait(v, "v")
        C.inc(v.tensor_tensor(c_t2, c_t2, c_t3, ALU.add), "v")
        C.selfwait(v, "v")
        C.inc(v.reciprocal(c_t2, c_t2), "v")
        C.selfwait(v, "v")
        C.inc(v.tensor_tensor(c_t3, c_t1, c_lamre, ALU.mult), "v")
        C.inc(v.tensor_tensor(c_t4, c_aim, p_lim, ALU.mult), "v")
        C.selfwait(v, "v")
        C.inc(v.tensor_tensor(c_t3, c_t3, c_t4, ALU.add), "v")
        C.selfwait(v, "v")
        C.inc(v.tensor_tensor(c_dre, c_t3, c_t2, ALU.mult), "v")
        C.selfwait(v, "v")
        C.inc(v.tensor_tensor(c_t3, c_aim, c_lamre, ALU.mult), "v")
        C.inc(v.tensor_tensor(c_t4, c_t1, p_lim, ALU.mult), "v")
        C.selfwait(v, "v")
        C.inc(v.tensor_tensor(c_t3, c_t3, c_t4, ALU.subtract), "v")
        C.selfwait(v, "v")
        C.inc(v.tensor_tensor(c_dim, c_t3, c_t2, ALU.mult), "v")
        C.selfwait(v, "v")
        C.inc(v.tensor_scalar(bbar_sA[:, :], p_bre, c_dre, None, ALU.mult),
              "v")
        C.inc(v.tensor_scalar(bbar_sB[:, :], p_bim, c_dim, None, ALU.mult),
              "v")
        C.selfwait(v, "v")
        C.inc(v.tensor_tensor(bbar_re[:, :], bbar_sA[:, :], bbar_sB[:, :],
                              ALU.subtract), "v")
        C.selfwait(v, "v")
        C.inc(v.tensor_scalar(bbar_sA[:, :], p_bim, c_dre, None, ALU.mult),
              "v")
        C.inc(v.tensor_scalar(bbar_sB[:, :], p_bre, c_dim, None, ALU.mult),
              "v")
        C.selfwait(v, "v")
        C.inc(v.tensor_tensor(bbar_im[:, :], bbar_sA[:, :], bbar_sB[:, :],
                              ALU.add), "v", mark="v_bbar")
        C.inc(v.tensor_scalar(ph_p[:, :], it_fwd[:, :], c_theta, None,
                              ALU.mult), "v")
        C.selfwait(v, "v")
        C.inc(v.tensor_scalar(ph_q[:, :], ph_p[:, :], 1.0 / TWO_PI, None,
                              ALU.mult), "v")
        C.selfwait(v, "v")
        C.inc(v.tensor_copy(ph_qi, ph_q[:, :]), "v")
        C.selfwait(v, "v")
        C.inc(v.tensor_copy(ph_r[:, :], ph_qi), "v")
        C.selfwait(v, "v")
        C.inc(v.scalar_tensor_tensor(ph_p[:, :], ph_r[:, :], -TWO_PI,
                                     ph_p[:, :], ALU.mult, ALU.add), "v")
        C.selfwait(v, "v")
        C.inc(v.tensor_scalar(ph_r[:, :], ph_p[:, :], PI, None, ALU.is_gt),
              "v")
        C.selfwait(v, "v")
        C.inc(v.scalar_tensor_tensor(ph_p[:, :], ph_r[:, :], -TWO_PI,
                                     ph_p[:, :], ALU.mult, ALU.add), "v")
        C.selfwait(v, "v")
        C.inc(v.tensor_scalar(ph_r[:, :], ph_p[:, :], -PI, None, ALU.is_lt),
              "v")
        C.selfwait(v, "v")
        C.inc(v.scalar_tensor_tensor(ph_p[:, :], ph_r[:, :], TWO_PI,
                                     ph_p[:, :], ALU.mult, ALU.add), "v",
              mark="v_pred")
        C.wait(v, "s_abs")
        C.selfwait(v, "v")
        C.inc(v.tensor_scalar(ph_r[:, :], ph_q[:, :], -1.0, PI / 2.0,
                              ALU.mult, ALU.add), "v", mark="v_carg")
        C.wait(v, "t_btr")
        C.inc(v.tensor_copy(bret_bf[:, :], pBbf[:, 0:512]), "v")
        C.inc(v.tensor_copy(bimt_bf[:, :], pBbf[:, 512:1024]), "v",
              mark="v_btrev")
        C.wait(v, "t_drep")
        C.inc(v.tensor_copy(d_rep_bf[:, :], pA[:, :]), "v", mark="v_drep")
        C.wait(v, "t_sdrep")
        C.inc(v.tensor_copy(sdrep[:, :], pA[:, :]), "v", mark="v_sdrep")

        NTOT = 2 * NT
        for sl in range(NTOT + 3):
            j = sl
            if j < NTOT:
                C.wait(v, f"du_{j}")
                C.selfwait(v, "v")
                for gq in range(4):
                    C.inc(v.bn_stats(statbuf[:, 6 * gq:6 * (gq + 1)],
                                     ubuf[j % 3][:, 512 * gq:512 * (gq + 1)]),
                          "v")
                C.selfwait(v, "v")
                C.inc(v.bn_aggr(mv_p[j % 2], statbuf[:, :]), "v",
                      mark=f"aggr_{j}")
            k = sl - 1
            if 0 <= k < NTOT:
                C.wait(v, f"sqrt_{k}")
                C.selfwait(v, "v")
                C.inc(v.reciprocal(rstd_p[k % 2], sd_p[k % 2]), "v")
                C.inc(v.tensor_scalar(c_negm, mv_p[k % 2][:, 0:1], -1.0,
                                      None, ALU.mult), "v")
                C.selfwait(v, "v")
                C.inc(v.tensor_tensor(nmr_p[k % 2], c_negm, rstd_p[k % 2],
                                      ALU.mult), "v", mark=f"lnc_{k}")
            k = sl - 2
            if 0 <= k < NTOT:
                C.wait(v, f"tr_{k}")
                i = k % NT
                for gq in range(2):
                    C.inc(v.tensor_copy(
                        t1nT_full[:, i * 2048 + 512 * gq:
                                  i * 2048 + 512 * (gq + 1)],
                        pBbf[:, 512 * gq:512 * (gq + 1)]), "v",
                        mark=(f"t1nTv_{k}" if gq == 1 else None))
            k = sl - 3
            if 0 <= k < NTOT:
                C.wait(v, f"bu_{k}")
                pb = pC if k % 2 == 0 else pD
                bre_dst, bim_dst = bu_dsts(k)
                C.inc(v.tensor_copy(bre_dst, pb[0:NCH, 0:128]), "v")
                C.inc(v.tensor_copy(bim_dst, pb[0:NCH, 128:256]), "v",
                      mark=f"buev_{k}")

        C.inc(v.tensor_scalar(tC[:, :], it_fwd[:, :], c_lnr, None, ALU.mult),
              "v", mark="v_rpe")
        C.wait(v, "s_rpow")
        C.inc(v.tensor_tensor(tA[:, :], cosT[:, :], rp_v[:, :], ALU.mult),
              "v")
        C.inc(v.tensor_tensor(tB[:, :], sinT[:, :], rp_v[:, :], ALU.mult),
              "v")
        C.selfwait(v, "v")
        C.inc(v.tensor_tensor(tC[:, :], tA[:, :], bu_re_p[:, :], ALU.mult),
              "v")
        C.selfwait(v, "v")
        C.inc(v.reduce_sum(c_t1, tC[:, :], axis=AX.X), "v")
        C.selfwait(v, "v")
        C.inc(v.tensor_tensor(tC[:, :], tB[:, :], bu_im_p[:, :], ALU.mult),
              "v")
        C.selfwait(v, "v")
        C.inc(v.reduce_sum(c_t2, tC[:, :], axis=AX.X), "v")
        C.selfwait(v, "v")
        C.inc(v.tensor_tensor(tC[:, :], tA[:, :], bu_im_p[:, :], ALU.mult),
              "v")
        C.selfwait(v, "v")
        C.inc(v.reduce_sum(c_t3, tC[:, :], axis=AX.X), "v")
        C.selfwait(v, "v")
        C.inc(v.tensor_tensor(tC[:, :], tB[:, :], bu_re_p[:, :], ALU.mult),
              "v")
        C.selfwait(v, "v")
        C.inc(v.reduce_sum(c_t4, tC[:, :], axis=AX.X), "v")
        C.selfwait(v, "v")
        C.inc(v.tensor_tensor(c_sr, c_t1, c_t2, ALU.add), "v")
        C.inc(v.tensor_tensor(c_si, c_t3, c_t4, ALU.subtract), "v")
        C.selfwait(v, "v")
        C.inc(v.tensor_tensor(c_t1, cosT[:, T - 1:T], c_sr, ALU.mult), "v")
        C.inc(v.tensor_tensor(c_t2, sinT[:, T - 1:T], c_si, ALU.mult), "v")
        C.inc(v.tensor_tensor(c_t3, sinT[:, T - 1:T], c_sr, ALU.mult), "v")
        C.inc(v.tensor_tensor(c_t4, cosT[:, T - 1:T], c_si, ALU.mult), "v")
        C.selfwait(v, "v")
        C.inc(v.tensor_tensor(c_cre, c_t1, c_t2, ALU.subtract), "v")
        C.inc(v.tensor_tensor(c_cim, c_t3, c_t4, ALU.add), "v")
        C.selfwait(v, "v")
        C.inc(v.tensor_tensor(tA[:, :], cosT[:, :], bu_re_m[:, :], ALU.mult),
              "v")
        C.inc(v.tensor_tensor(tB[:, :], sinT[:, :], bu_im_m[:, :], ALU.mult),
              "v")
        C.selfwait(v, "v")
        C.inc(v.tensor_tensor(bt_re[:, :], tA[:, :], tB[:, :], ALU.add), "v")
        C.selfwait(v, "v")
        C.inc(v.tensor_tensor(tA[:, :], cosT[:, :], bu_im_m[:, :], ALU.mult),
              "v")
        C.inc(v.tensor_tensor(tB[:, :], sinT[:, :], bu_re_m[:, :], ALU.mult),
              "v")
        C.selfwait(v, "v")
        C.inc(v.tensor_tensor(bt_im[:, :], tA[:, :], tB[:, :], ALU.subtract),
              "v")
        C.inc(v.memset(tC[:, :], 1.0), "v")
        C.selfwait(v, "v")
        C.inc(v.tensor_scalar(tC[:, :], tC[:, :], c_r, None, ALU.mult), "v")
        C.selfwait(v, "v")
        C.inc(v.tensor_tensor_scan(tA[:, :], tC[:, :], bt_re[:, :], c_cre,
                                   ALU.mult, ALU.add), "v")
        C.inc(v.tensor_tensor_scan(tB[:, :], tC[:, :], bt_im[:, :], c_cim,
                                   ALU.mult, ALU.add), "v")
        C.selfwait(v, "v")
        C.inc(v.tensor_tensor(tC[:, :], cosT[:, :], tA[:, :], ALU.mult), "v")
        C.inc(v.tensor_tensor(bt_re[:, :], sinT[:, :], tB[:, :], ALU.mult),
              "v")
        C.selfwait(v, "v")
        C.inc(v.tensor_tensor(xs_re[:, :], tC[:, :], bt_re[:, :],
                              ALU.subtract), "v")
        C.selfwait(v, "v")
        C.inc(v.tensor_tensor(tC[:, :], cosT[:, :], tB[:, :], ALU.mult), "v")
        C.inc(v.tensor_tensor(bt_im[:, :], sinT[:, :], tA[:, :], ALU.mult),
              "v")
        C.selfwait(v, "v")
        C.inc(v.tensor_tensor(xs_im[:, :], tC[:, :], bt_im[:, :], ALU.add),
              "v", mark="v_scan_done")

        for sl in range(NT + 5):
            k = sl - 2
            if 0 <= k < NT:
                C.wait(v, f"ysqrt_{k}")
                C.selfwait(v, "v")
                C.inc(v.reciprocal(rsy_p[k % 2], sdy_p[k % 2]), "v")
                C.inc(v.tensor_scalar(c_negm, mvy_p[k % 2][:, 0:1], -1.0,
                                      None, ALU.mult), "v")
                C.selfwait(v, "v")
                C.inc(v.tensor_tensor(nmy_p[k % 2], c_negm, rsy_p[k % 2],
                                      ALU.mult), "v", mark=f"ylnc_{k}")
            k = sl - 4
            if 0 <= k < NT:
                C.wait(v, f"x1sqrt_{k}")
                C.selfwait(v, "v")
                C.inc(v.reciprocal(rsx_p[k % 2], sdx_p[k % 2]), "v")
                C.inc(v.tensor_scalar(c_negm, mvx_p[k % 2][:, 0:1], -1.0,
                                      None, ALU.mult), "v")
                C.selfwait(v, "v")
                C.inc(v.tensor_tensor(nmx_p[k % 2], c_negm, rsx_p[k % 2],
                                      ALU.mult), "v", mark=f"xlnc_{k}")
            k = sl - 3
            if 0 <= k < NT:
                C.wait(v, f"ln2_{k}")
                C.wait(v, f"dx0_{k}")
                if k >= 2:
                    C.wait(v, f"dx1w_{k-2}")
                    C.wait(v, f"xnapply_{k-2}")
                C.selfwait(v, "v")
                C.inc(v.tensor_tensor(x1buf[k % 2][:, :],
                                      ybuf_p[k % 2][:, :],
                                      x0buf_p[k % 2][:, :],
                                      ALU.add), "v", mark=f"x1_{k}")
                C.selfwait(v, "v")
                for gq in range(4):
                    C.inc(v.bn_stats(
                        statbuf[:, 6 * gq:6 * (gq + 1)],
                        x1buf[k % 2][:, 512 * gq:512 * (gq + 1)]), "v")
                C.selfwait(v, "v")
                C.inc(v.bn_aggr(mvx_p[k % 2], statbuf[:, :]), "v",
                      mark=f"x1aggr_{k}")
            k = sl - 1
            if 0 <= k < NT:
                C.wait(v, f"yev_{k}")
                C.selfwait(v, "v")
                for gq in range(4):
                    C.inc(v.bn_stats(
                        statbuf[:, 6 * gq:6 * (gq + 1)],
                        ybuf_p[k % 2][:, 512 * gq:512 * (gq + 1)]), "v")
                C.selfwait(v, "v")
                C.inc(v.bn_aggr(mvy_p[k % 2], statbuf[:, :]), "v",
                      mark=f"yaggr_{k}")
            k = sl - 5
            if 0 <= k < NT:
                C.wait(v, f"xntr_{k}")
                if k >= 9:
                    C.wait(v, f"dxnw_{k-9}")
                for gq in range(2):
                    C.inc(v.tensor_copy(xnev_dst(k, gq), xnev_src(k, gq)),
                          "v", mark=(f"xnevv_{k}" if gq == 1 else None))

        def tail_tile(X, ti):
            C.wait(v, f"dyr_{X}_{ti}")
            C.selfwait(v, "v")
            C.inc(v.tensor_scalar(yrbuf_p[ti % 2][:, :],
                                  yrbuf_p[ti % 2][:, :], l4nm[ti],
                                  l4rs[ti], ALU.add, ALU.mult), "v",
                  mark=f"ln4apply_{X}_{ti}")
            C.wait(v, f"dx1r_{X}_{ti}")
            gk = X * 8 + ti
            if gk >= 1:
                pgk = gk - 1
                C.wait(v, f"dout_{pgk // 8}_{pgk % 8}")
            C.selfwait(v, "v")
            C.inc(v.tensor_tensor(outbuf[0][:, :], yrbuf_p[ti % 2][:, :],
                                  x1rbuf_p[ti % 2][:, :], ALU.add), "v",
                  mark=f"out_{X}_{ti}")

        for X in range(2):
            for hb in range(HB):
                if X == 0 and hb == 0:
                    C.wait(v, "dxnw_6")
                    C.wait(v, "dxnw_7")
                    C.wait(v, f"dx1w_{NT-2}")
                    C.wait(v, f"dx1w_{NT-1}")
                if X == 1 and hb == 0:
                    C.wait(v, "dyw_0_3_6")
                if X == 1 and hb % 5 == 4 and hb // 5 < 8:
                    tail_tile(0, hb // 5)
                C.wait(v, f"s1mm_{X}_{hb}")
                C.selfwait(v, "v")
                if hb % 2 == 0:
                    ups = [pA[:, 1024:1536], pA[:, 1536:2048]]
                else:
                    ups = [pC[:, :], pD[:, :]]
                for tg in range(2):
                    C.inc(v.tensor_scalar(
                        us_s1[:, tg * 512:(tg + 1) * 512], ups[tg],
                        subuf[:, hb:hb + 1], None, ALU.mult), "v",
                        mark=(f"uev_{X}_{hb}" if tg == 1 else None))
                C.wait(v, f"gs_{X}_{hb}")
                C.selfwait(v, "v")
                for tg in range(2):
                    C.inc(v.tensor_tensor(
                        Hcols(hb, tg), h2buf[:, tg * 512:(tg + 1) * 512],
                        us_s1[:, tg * 512:(tg + 1) * 512], ALU.mult), "v",
                        mark=(f"h_{X}_{hb}" if tg == 1 else None))
            PS2v = [pA[:, 0:512], pA[:, 512:1024], pA[:, 1024:1536],
                    pA[:, 1536:2048], pB[:, 0:512], pB[:, 512:1024],
                    pC[:, :], pD[:, :]]
            for dg in range(4):
                for ti in range(8):
                    par = (dg * 8 + ti) % 2
                    C.wait(v, f"s2f_{X}_{dg}_{ti}")
                    if X == 0:
                        k = dg * 8 + ti
                        if k >= 4:
                            pk = k - 4
                            C.wait(v, f"dyw_{X}_{pk // 8}_{pk % 8}")
                        dst = ystage4[k % 4][:, :]
                    elif ti < 6 or dg == 3:
                        dst = ybig[ti][:, dg * 512:(dg + 1) * 512]
                    else:
                        if dg == 0:
                            C.wait(v, f"dyw_0_3_{ti - 2}")
                        else:
                            C.wait(v, f"dyw_1_{dg-1}_{ti}")
                        dst = ystage[ti - 6][:, :]
                    C.inc(v.tensor_tensor(dst, PS2v[ti],
                                          sdrep[:, dg * 512:(dg + 1) * 512],
                                          ALU.mult), "v")
                    C.selfwait(v, "v")
                    C.inc(v.bn_stats(
                        ln4stats[:, ti * 24 + dg * 6:ti * 24 + dg * 6 + 6],
                        dst), "v", mark=f"ys_{X}_{dg}_{ti}")
            C.selfwait(v, "v")
            for ti in range(8):
                C.inc(v.bn_aggr(l4mv[ti], ln4stats[:, ti * 24:(ti + 1) * 24]),
                      "v", mark=f"l4aggr_{X}_{ti}")
            for ti in range(8):
                C.wait(v, f"l4sqrt_{X}_{ti}")
                C.selfwait(v, "v")
                C.inc(v.reciprocal(l4rs[ti], l4nm[ti]), "v")
                C.inc(v.tensor_scalar(c_negm, l4mv[ti][:, 0:1], -1.0, None,
                                      ALU.mult), "v")
                C.selfwait(v, "v")
                C.inc(v.tensor_tensor(l4nm[ti], c_negm, l4rs[ti], ALU.mult),
                      "v", mark=f"l4c_{X}_{ti}")
            if X == 1:
                for ti in range(8):
                    if ti == 0:
                        C.wait(v, "s2e_19")
                    if ti == 1:
                        C.wait(v, "s2e_23")
                    if ti % 2 == 1:
                        if ti == 7:
                            C.wait(v, "dyr_1_7")
                        C.selfwait(v, "v")
                        C.inc(v.tensor_scalar(ybig[ti][:, :], ybig[ti][:, :],
                                              l4nm[ti], l4rs[ti], ALU.add,
                                              ALU.mult), "v")
                    else:
                        C.wait(v, f"l4app_{ti}")
                    C.wait(v, f"dx1r_1_{ti}")
                    if ti >= 2:
                        C.wait(v, f"dout_1_{ti-2}")
                    C.selfwait(v, "v")
                    C.inc(v.tensor_tensor(outb2[ti % 2][:, :],
                                          ybig[ti][:, :],
                                          x1rb2[ti % 2][:, :], ALU.add),
                          "v", mark=f"out_1_{ti}")

    def run_pass(pn):
        C.begin(pn, sems if pn == 2 else None)
        if pn == 1:
            emit_sync(_FakeEng())
            emit_gpsimd(_FakeEng())
            emit_scalar(_FakeEng())
            emit_vector(_FakeEng())
            emit_tensor(_FakeEng())
            return
        with nc.Block() as block:
            @block.sync
            def _(q):
                emit_sync(q)

            @block.gpsimd
            def _(g):
                emit_gpsimd(g)

            @block.scalar
            def _(s):
                emit_scalar(s)

            @block.vector
            def _(v):
                emit_vector(v)

            @block.tensor
            def _(t):
                emit_tensor(t)

    run_pass(1)
    cnt1 = dict(C.cnt)
    run_pass(2)
    assert C.cnt == cnt1, (C.cnt, cnt1)
    st.close()
    return nc


def _u8view(wq_int32):
    a = np.ascontiguousarray(np.asarray(wq_int32))
    assert a.dtype == np.int32
    return np.ascontiguousarray(a.view(np.uint8)[..., ::4])


def _stage_w_hd(wq):
    a = _u8view(wq).reshape(HB, 128, 16, 128)
    a = np.ascontiguousarray(a.transpose(0, 3, 2, 1))
    return a.reshape(HB, 128, 2048)


def _stage_w_dh(wq):
    a = np.ascontiguousarray(_u8view(wq).T)
    a = a.reshape(HB, 128, 4, 512).transpose(2, 1, 0, 3)
    return np.ascontiguousarray(a).reshape(4, 128, HB * 512)


def kernel(L0_attn_u, L0_attn_Lambda_unconstrained_re, L0_attn_Lambda_im,
           L0_attn_B_re, L0_attn_B_im, L0_attn_C_re, L0_attn_C_im,
           L0_attn_D, L0_attn_log_Delta, L0_ffn_wq_gate, L0_ffn_s_gate,
           L0_ffn_wq_up, L0_ffn_s_up, L0_ffn_wq_down, L0_ffn_s_down):
    u = np.ascontiguousarray(np.asarray(L0_attn_u, dtype=np.float32))
    wg = _stage_w_hd(L0_ffn_wq_gate)
    wu = _stage_w_hd(L0_ffn_wq_up)
    wd = _stage_w_dh(L0_ffn_wq_down)
    sg = np.ascontiguousarray(
        np.asarray(L0_ffn_s_gate, np.float32).reshape(HB, 128).T)
    su = np.ascontiguousarray(
        np.asarray(L0_ffn_s_up, np.float32).reshape(HB, 128).T)
    sd = np.ascontiguousarray(np.asarray(L0_ffn_s_down,
                                         np.float32).reshape(1, D))
    bre = np.ascontiguousarray(np.asarray(L0_attn_B_re, np.float32))
    bim = np.ascontiguousarray(np.asarray(L0_attn_B_im, np.float32))
    cret = np.ascontiguousarray(np.asarray(L0_attn_C_re, np.float32).T)
    cimt = np.ascontiguousarray(np.asarray(L0_attn_C_im, np.float32).T)
    drow = np.ascontiguousarray(np.asarray(L0_attn_D,
                                           np.float32).reshape(1, D))
    lunc = np.ascontiguousarray(
        np.asarray(L0_attn_Lambda_unconstrained_re,
                   np.float32).reshape(NCH, 1))
    lim = np.ascontiguousarray(np.asarray(L0_attn_Lambda_im,
                                          np.float32).reshape(NCH, 1))
    logd = np.ascontiguousarray(
        np.asarray(L0_attn_log_Delta, np.float32)[:NCH].reshape(NCH, 1))
    zeros_u = np.zeros((T, D), np.float32)

    nc = build_program()
    in_maps = []
    for c in range(8):
        b, h = c // 2, c % 2
        mine = np.ascontiguousarray(u[b, h * T:(h + 1) * T, :])
        prev = np.ascontiguousarray(u[b, 0:T, :]) if h == 1 else zeros_u
        in_maps.append(dict(
            u=mine, uprev=prev, wg=wg, wu=wu, wd=wd, sg=sg, su=su, sd=sd,
            bre=bre, bim=bim, cret=cret, cimt=cimt, drow=drow, lunc=lunc,
            lim=lim, logd=logd))
    res = run_bass_kernel_spmd(nc, in_maps, core_ids=list(range(8)))
    out = np.empty((B, L, D), np.float32)
    for c in range(8):
        b, h = c // 2, c % 2
        out[b, h * T:(h + 1) * T, :] = np.asarray(res.results[c]["out"])
    return out


# version 14
# speedup vs baseline: 1.3641x; 1.0077x over previous
import contextlib
import numpy as np
import ml_dtypes
import concourse.bass as bass
import concourse.mybir as mybir
from concourse.bass_utils import run_bass_kernel_spmd

F32 = mybir.dt.float32
BF16 = mybir.dt.bfloat16
U8 = mybir.dt.uint8
I32 = mybir.dt.int32
AF = mybir.ActivationFunctionType
ALU = mybir.AluOpType
AX = mybir.AxisListType

B, L, D = 4, 4096, 2048
HID = 5632
NCH = 32
T = 2048
NT = 16
HB = 44
TH = 1024
PI = float(np.pi)
TWO_PI = float(2.0 * np.pi)
EPS_LN = 1e-5
KB = 1024


class Ctx:

    KEYS = ("d_par", "d_u", "d_u2", "d_u3", "d_x0", "d_x02", "d_x1w",
            "d_x1w2", "d_x1r",
            "d_xnw", "d_xnw2", "d_xnr", "d_w", "d_w2", "d_wda", "d_wdl", "d_wdh", "d_yw",
            "d_yw2", "d_yr", "d_yr2", "d_x1r2", "d_out", "d_out2",
            "v", "s", "t", "g")

    def __init__(self):
        self.marks = {}
        self.pass_num = 0
        self.cnt = {}
        self.sems = None

    def begin(self, pass_num, sems=None):
        self.pass_num = pass_num
        self.cnt = {k: 0 for k in self.KEYS}
        self.sems = sems

    def inc(self, inst, key, by=1, mark=None):
        if self.pass_num == 2:
            inst.then_inc(self.sems[key], by)
        self.cnt[key] += by
        if mark is not None:
            if self.pass_num == 1:
                assert mark not in self.marks, f"dup mark {mark}"
                self.marks[mark] = (key, self.cnt[key])
            else:
                assert self.marks[mark] == (key, self.cnt[key]), \
                    f"mark drift {mark}"
        return self.cnt[key]

    def wait(self, eng, mark, delta=0):
        if self.pass_num == 2:
            key, val = self.marks[mark]
            if val + delta > 0:
                eng.wait_ge(self.sems[key], val + delta)

    def wait_cnt(self, eng, key, val):
        if self.pass_num == 2 and val > 0:
            eng.wait_ge(self.sems[key], val)

    def selfwait(self, eng, key):
        if self.pass_num == 2 and self.cnt[key] > 0:
            eng.wait_ge(self.sems[key], self.cnt[key])


class _FakeInst:
    def then_inc(self, *a, **k):
        return self


class _FakeEng:
    def __getattr__(self, name):
        return lambda *a, **k: _FakeInst()


def build_program():
    nc = bass.Bass("TRN2", target_bir_lowering=False, debug=False)

    d_in = {}

    def din(name, shape, dt):
        d_in[name] = nc.dram_tensor(name, shape, dt, kind="ExternalInput").ap()

    din("u", [T, D], F32)
    din("uprev", [T, D], F32)
    din("wg", [HB, 128, D], U8)
    din("wu", [HB, 128, D], U8)
    din("wd", [4, 128, HB * 512], U8)
    din("sg", [128, HB], F32)
    din("su", [128, HB], F32)
    din("sd", [1, D], F32)
    din("bre", [NCH, D], F32)
    din("bim", [NCH, D], F32)
    din("cret", [NCH, D], F32)
    din("cimt", [NCH, D], F32)
    din("drow", [1, D], F32)
    din("lunc", [NCH, 1], F32)
    din("lim", [NCH, 1], F32)
    din("logd", [NCH, 1], F32)
    out_hbm = nc.dram_tensor("out", [T, D], F32, kind="ExternalOutput").ap()
    x1_spill = nc.dram_tensor("x1sp", [T, D], F32, kind="Internal").ap()
    xnt_spill = nc.dram_tensor("xntsp", [128, 16 * TH], BF16,
                               kind="Internal").ap()
    y_spill = nc.dram_tensor("ysp", [TH, D], F32, kind="Internal").ap()

    st = contextlib.ExitStack()
    ARENA = 207 * KB
    arena = st.enter_context(nc.sbuf_tensor("arena", [128, ARENA], U8)).ap()

    def vf32(off_kb, n):
        off = int(off_kb * KB)
        assert off % 4 == 0 and off + 4 * n <= ARENA, (off_kb, n)
        return arena[:, off:off + 4 * n].bitcast(F32)

    def vbf(off_kb, n):
        off = int(off_kb * KB)
        assert off % 2 == 0 and off + 2 * n <= ARENA, (off_kb, n)
        return arena[:, off:off + 2 * n].bitcast(BF16)

    def vi32(off_kb, n):
        off = int(off_kb * KB)
        return arena[:, off:off + 4 * n].bitcast(I32)

    t1nT_full = vbf(0, 32768)
    wd_lo = vbf(0, 11264)
    wd_hi = vbf(22, 11264)
    wgbuf = [vbf(44, 2048), vbf(48, 2048)]
    wubuf = [vbf(52, 2048), vbf(56, 2048)]
    us_s1 = vbf(60, 1024)
    yrbuf_p = [vf32(0, 2048), vf32(8, 2048)]
    x1rbuf_p = [vf32(16, 2048), vf32(24, 2048)]
    outbuf = [vf32(32, 2048), vf32(40, 1024), ]
    ybig = [vf32(64, 2048), vf32(72, 2048), vf32(80, 2048), vf32(88, 2048),
            vf32(44, 2048), vf32(52, 2048), vf32(96, 2048), vf32(104, 2048)]
    x1rb2 = [vf32(112, 2048), vf32(120, 2048)]
    outb2 = [vf32(128, 2048), vf32(136, 2048)]

    xnt = vbf(64, 16384)
    drow_f = vf32(64, 2048)[0:1, :]
    sdrow = vf32(72, 2048)[0:1, :]
    drow_bf = vbf(80, 2048)[0:1, :]

    ubuf = [vf32(96, 2048), vf32(104, 2048), vf32(84, 2048)]
    t1ntmp = [vbf(112, 2048), vbf(116, 2048)]
    bu_re_m = vf32(120, 2048)[0:NCH, :]
    bu_im_m = vf32(128, 2048)[0:NCH, :]
    bu_re_p = vf32(136, 2048)[0:NCH, :]
    bu_im_p = vf32(144, 2048)[0:NCH, :]
    cosT = vf32(152, 2048)[0:NCH, :]
    sinT = vf32(160, 2048)[0:NCH, :]
    it_fwd = vf32(168, 2048)[0:NCH, :]
    rp_v = vf32(168, 2048)[0:NCH, :]
    c2re_v = vbf(176, 2048)[0:NCH, :]
    c2im_v = vbf(180, 2048)[0:NCH, :]
    bbar_sA = vf32(96, 2048)[0:NCH, :]
    bbar_sB = vf32(104, 2048)[0:NCH, :]
    bbar_re = vbf(112, 2048)[0:NCH, :]
    bbar_im = vbf(116, 2048)[0:NCH, :]
    ph_p = vf32(120, 2048)[0:NCH, :]
    ph_q = vf32(128, 2048)[0:NCH, :]
    ph_qi = vi32(136, 2048)[0:NCH, :]
    ph_r = vf32(144, 2048)[0:NCH, :]
    tA = vf32(96, 2048)[0:NCH, :]
    tB = vf32(104, 2048)[0:NCH, :]
    tC = vf32(112, 2048)[0:NCH, :]
    bt_re = vf32(136, 2048)[0:NCH, :]
    bt_im = vf32(144, 2048)[0:NCH, :]
    xs_re = vbf(120, 2048)[0:NCH, :]
    xs_im = vbf(124, 2048)[0:NCH, :]
    btre_s = vf32(168, 2048)[0:NCH, :]
    qtmp = vf32(205, 256)[0:NCH, :]
    ybuf_p = [vf32(96, 2048), vf32(104, 2048)]
    x1buf = [vf32(128, 2048), vf32(136, 2048)]
    x0buf_p = [vf32(144, 2048), vf32(168, 2048)]
    xnbuf = vbf(152, 2048)
    xnt_stage = vbf(156, 2048)
    Hbuf = vbf(96, 45056)

    d_rep_bf = vbf(184, 2048)
    sdrep = vf32(188, 2048)

    id_bf = vbf(196, 128)
    onesrow_bf = vbf(196.25, 128)[0:1, :]
    onesrow_f = vf32(196.5, 128)[0:1, :]
    sgbuf = vf32(197, HB)
    subuf = vf32(197.25, HB)
    colbuf = vf32(197.5, 96)
    statbuf = vf32(197.875, 24)
    bret_bf = vbf(198.25, 512)
    bimt_bf = vbf(199.25, 512)
    ystage = [vf32(200.25, 512), vf32(202.25, 512)]
    ystage4 = ystage + [vf32(60, 512), vf32(62, 512)]
    ln4stats = vf32(204.25, 192)
    h2buf = vbf(205, 1024)

    pA = st.enter_context(nc.psum_tensor("pA", [128, 2048], F32)).ap()
    pB = st.enter_context(nc.psum_tensor("pB", [128, 1024], F32)).ap()
    pC = st.enter_context(nc.psum_tensor("pC", [128, 512], F32)).ap()
    pD = st.enter_context(nc.psum_tensor("pD", [128, 512], F32)).ap()
    pBbf = pB.bitcast(BF16)
    cosp_ps = pA[0:NCH, :]
    sinp_ps = pA[32:32 + NCH, :]
    prod_ps = pA[64:64 + NCH, :]
    btim_ps = pA[96:96 + NCH, :]

    sems = {k: st.enter_context(nc.semaphore(k)) for k in Ctx.KEYS}
    C = Ctx()

    p_lunc = colbuf[0:NCH, 0:1]
    p_lim = colbuf[0:NCH, 1:2]
    p_logd = colbuf[0:NCH, 2:3]
    c_delta = colbuf[0:NCH, 3:4]
    c_lamre = colbuf[0:NCH, 4:5]
    c_lnr = colbuf[0:NCH, 5:6]
    c_theta = colbuf[0:NCH, 6:7]
    c_r = colbuf[0:NCH, 7:8]
    c_are = colbuf[0:NCH, 8:9]
    c_aim = colbuf[0:NCH, 9:10]
    c_t1 = colbuf[0:NCH, 10:11]
    c_t2 = colbuf[0:NCH, 11:12]
    c_t3 = colbuf[0:NCH, 12:13]
    c_t4 = colbuf[0:NCH, 13:14]
    c_dre = colbuf[0:NCH, 14:15]
    c_dim = colbuf[0:NCH, 15:16]
    c_eps = colbuf[:, 16:17]
    c_cre = colbuf[0:NCH, 17:18]
    c_cim = colbuf[0:NCH, 18:19]
    c_sr = colbuf[0:NCH, 19:20]
    c_si = colbuf[0:NCH, 20:21]
    c_negm = colbuf[:, 21:22]
    c_rstd = colbuf[:, 22:23]
    c_sd = colbuf[:, 23:24]
    c_Tlnr = colbuf[0:NCH, 24:25]
    c_i32 = colbuf.bitcast(I32)[0:NCH, 25:26]
    c_sinth = colbuf[0:NCH, 26:27]
    c_sh = colbuf[0:NCH, 27:28]
    c_costh = colbuf[0:NCH, 28:29]
    c_thred = colbuf[0:NCH, 29:30]
    mvbuf = colbuf[:, 30:32]
    mv_p = [colbuf[:, 32:34], colbuf[:, 34:36]]
    sd_p = [colbuf[:, 36:37], colbuf[:, 37:38]]
    rstd_p = [colbuf[:, 38:39], colbuf[:, 39:40]]
    nmr_p = [colbuf[:, 40:41], colbuf[:, 41:42]]
    mvy_p = [colbuf[:, 42:44], colbuf[:, 44:46]]
    sdy_p = [colbuf[:, 46:47], colbuf[:, 47:48]]
    rsy_p = [colbuf[:, 48:49], colbuf[:, 49:50]]
    nmy_p = [colbuf[:, 50:51], colbuf[:, 51:52]]
    mvx_p = [colbuf[:, 52:54], colbuf[:, 54:56]]
    sdx_p = [colbuf[:, 56:57], colbuf[:, 57:58]]
    rsx_p = [colbuf[:, 58:59], colbuf[:, 59:60]]
    nmx_p = [colbuf[:, 60:61], colbuf[:, 61:62]]
    l4mv = [colbuf[:, 62 + 2 * t_: 64 + 2 * t_] for t_ in range(8)]
    l4rs = [colbuf[:, 78 + t_: 79 + t_] for t_ in range(8)]
    l4nm = [colbuf[:, 86 + t_: 87 + t_] for t_ in range(8)]
    p_bre = bu_re_m
    p_bim = bu_im_m
    p_cret = cosT
    p_cimt = sinT

    def Hcols(hb, tg):
        return Hbuf[:, hb * TH + tg * 512: hb * TH + (tg + 1) * 512]

    def bu_dsts(k):
        i = k % NT
        bre = (bu_re_p if k < NT else bu_re_m)[:, i * 128:(i + 1) * 128]
        bim = (bu_im_p if k < NT else bu_im_m)[:, i * 128:(i + 1) * 128]
        return bre, bim

    def xnev_dst(i, gq):
        if i < 8:
            return xnt.rearrange("p (k tt) -> p k tt", k=16)[
                :, 4 * gq:4 * (gq + 1), i * 128:(i + 1) * 128]
        return xnt_stage[:, 512 * gq:512 * (gq + 1)]

    def xnev_src(i, gq):
        s = pBbf[:, 512 * gq:512 * (gq + 1)]
        if i < 8:
            return s.rearrange("p (k tt) -> p k tt", k=4)
        return s

    def emit_sync(q):
        C.inc(q.dma_start(p_lunc, d_in["lunc"]), "d_par", 16)
        C.inc(q.dma_start(p_lim, d_in["lim"]), "d_par", 16)
        C.inc(q.dma_start(p_logd, d_in["logd"]), "d_par", 16)
        C.inc(q.dma_start(p_bre, d_in["bre"]), "d_par", 16)
        C.inc(q.dma_start(p_bim, d_in["bim"]), "d_par", 16)
        C.inc(q.dma_start(p_cret, d_in["cret"]), "d_par", 16)
        C.inc(q.dma_start(p_cimt, d_in["cimt"]), "d_par", 16)
        C.inc(q.dma_start(drow_f, d_in["drow"]), "d_par", 16)
        C.inc(q.dma_start(sdrow, d_in["sd"]), "d_par", 16)
        C.inc(q.dma_start(sgbuf[:, :], d_in["sg"]), "d_par", 16)
        C.inc(q.dma_start(subuf[:, :], d_in["su"]), "d_par", 16,
              mark="d_params")

        for j in range(2 * NT):
            src = d_in["uprev"] if j < NT else d_in["u"]
            i = j % NT
            if j < 2:
                C.wait(q, "v_bbar")
            elif j >= 3:
                C.wait(q, f"uapply_{j-3}")
            C.inc(q.dma_start(ubuf[j % 3][:, :],
                              src[i * 128:(i + 1) * 128, :]),
                  ("d_u", "d_u2", "d_u3")[j % 3], 16, mark=f"du_{j}")

        for i in range(NT):
            if i < 2:
                C.wait(q, "v_scan_done")
            else:
                C.wait(q, f"x1_{i-2}")
            C.inc(q.dma_start(x0buf_p[i % 2][:, :],
                              d_in["u"][i * 128:(i + 1) * 128, :]),
                  "d_x0" if i % 2 == 0 else "d_x02", 16, mark=f"dx0_{i}")
            if i >= 1:
                C.wait(q, f"x1_{i-1}")
                C.inc(q.dma_start(x1_spill[(i - 1) * 128:i * 128, :],
                                  x1buf[(i - 1) % 2][:, :]),
                      "d_x1w" if (i - 1) % 2 == 0 else "d_x1w2", 16,
                      mark=f"dx1w_{i-1}")
        C.wait(q, f"x1_{NT-1}")
        C.inc(q.dma_start(x1_spill[(NT - 1) * 128:NT * 128, :],
                          x1buf[(NT - 1) % 2][:, :]),
              "d_x1w" if (NT - 1) % 2 == 0 else "d_x1w2", 16,
              mark=f"dx1w_{NT-1}")

        for X in range(2):
            if X == 0:
                C.wait(q, "s1mm_0_43")
                C.inc(q.dma_start(xnt[:, :], xnt_spill[:, :]), "d_xnr", 16,
                      mark="dxnr")
            for dg in range(4):
                for ti in range(8):
                    if X == 1 and (ti < 6 or dg == 3):
                        continue
                    k = dg * 8 + ti
                    C.wait(q, f"ys_{X}_{dg}_{ti}")
                    if X == 0:
                        src = ystage4[k % 4][:, :]
                        skey = ("d_yw", "d_yw2", "d_x1w", "d_x1w2")[k % 4]
                    else:
                        src = ystage[ti - 6][:, :]
                        skey = "d_yw" if ti == 6 else "d_yw2"
                    C.inc(q.dma_start(
                        y_spill[ti * 128:(ti + 1) * 128,
                                dg * 512:(dg + 1) * 512], src),
                        skey, 16, mark=f"dyw_{X}_{dg}_{ti}")
            if X == 0:
                C.wait(q, "s2f_0_3_7")
                for lt in (4, 5, 6, 7):
                    C.wait(q, f"dyw_0_3_{lt}")
                for ti in range(8):
                    if ti >= 2:
                        C.wait(q, f"ln4apply_0_{ti-2}")
                    C.inc(q.dma_start(yrbuf_p[ti % 2][:, :],
                                      y_spill[ti * 128:(ti + 1) * 128, :]),
                          "d_yr" if ti % 2 == 0 else "d_yr2", 16,
                          mark=f"dyr_0_{ti}")
                    if ti >= 2:
                        C.wait(q, f"out_0_{ti-2}")
                    C.inc(q.dma_start(x1rbuf_p[ti % 2][:, :],
                                      x1_spill[ti * 128:(ti + 1) * 128, :]),
                          "d_x1r" if ti % 2 == 0 else "d_x1r2", 16,
                          mark=f"dx1r_0_{ti}")
                    C.wait(q, f"out_0_{ti}")
                    C.inc(q.dma_start(out_hbm[ti * 128:(ti + 1) * 128, :],
                                      outbuf[0][:, :]),
                          "d_out" if ti % 2 == 0 else "d_out2", 16,
                          mark=f"dout_0_{ti}")
            else:
                C.wait(q, "s2e_7")
                C.wait(q, "dyw_1_2_6")
                C.inc(q.dma_start(ybig[6][:, 0:1536],
                                  y_spill[6 * 128:7 * 128, 0:1536]),
                      "d_yr", 16, mark="dyr_1_6")
                C.wait(q, "dyw_1_2_7")
                C.inc(q.dma_start(ybig[7][:, 0:1536],
                                  y_spill[7 * 128:8 * 128, 0:1536]),
                      "d_yr2", 16, mark="dyr_1_7")
                for ti in range(8):
                    if ti == 0:
                        C.wait(q, "s2e_11")
                    elif ti == 1:
                        C.wait(q, "s2e_15")
                    else:
                        C.wait(q, f"out_1_{ti-2}")
                    C.inc(q.dma_start(
                        x1rb2[ti % 2][:, :],
                        x1_spill[(8 + ti) * 128:(9 + ti) * 128, :]),
                        "d_x1r" if ti % 2 == 0 else "d_x1r2", 16,
                        mark=f"dx1r_1_{ti}")
        C.wait(q, "dout_0_6")
        C.wait(q, "dout_1_6")
        C.wait(q, "dout_1_7")

    def emit_gpsimd(g):
        C.inc(g.iota(it_fwd[:, :], pattern=[[1, T]], base=1,
                     channel_multiplier=0,
                     allow_small_or_imprecise_dtypes=True), "g", mark="g_itf")
        C.inc(g.memset(id_bf[:, :], 1.0), "g")
        C.selfwait(g, "g")
        C.inc(g.affine_select(id_bf[:, :], id_bf[:, :], pattern=[[-1, 128]],
                              compare_op=ALU.is_equal, fill=0.0, base=0,
                              channel_multiplier=1), "g", mark="g_id")
        C.inc(g.memset(onesrow_bf[:, :], 1.0), "g")
        C.inc(g.memset(onesrow_f[:, :], 1.0), "g")
        C.inc(g.memset(c_eps, EPS_LN), "g", mark="g_setup")
        C.wait(g, "v_drep")
        for kd in range(16):
            C.inc(g.affine_select(d_rep_bf[:, kd * 128:(kd + 1) * 128],
                                  d_rep_bf[:, kd * 128:(kd + 1) * 128],
                                  pattern=[[-1, 128]],
                                  compare_op=ALU.is_equal, fill=0.0, base=0,
                                  channel_multiplier=1), "g",
                  mark=("g_ddiag" if kd == 15 else None))

        for hb in (0, 1):
            wkey = "d_w" if hb % 2 == 0 else "d_w2"
            C.wait(g, f"ymm_{11 + hb}")
            C.inc(g.dma_start(wgbuf[hb][:, :], d_in["wg"][hb, :, :]),
                  wkey, 16)
            C.wait(g, f"ymm_{13 + hb}")
            C.inc(g.dma_start(wubuf[hb][:, :], d_in["wu"][hb, :, :]),
                  wkey, 16, mark=f"wconv_0_{hb}")

        for k in range(8):
            C.wait(g, f"xnevv_{k + 8}")
            C.wait(g, f"xnevs_{k + 8}")
            C.inc(g.dma_start(
                xnt_spill.rearrange("p (kk ti tt) -> p kk ti tt",
                                    k=16, ti=8, kk=16)[:, :, k, :],
                xnt_stage.rearrange("p (kk tt) -> p kk tt", kk=16)),
                "d_xnw" if k % 2 == 0 else "d_xnw2", 16,
                mark=f"dxnw_{k}")

        for X in range(2):
            for hb in range(HB):
                if X == 0 and hb < 2:
                    continue
                par = hb % 2
                pX, phb = (X, hb - 2) if hb >= 2 else (0, HB - 2 + hb)
                C.wait(g, f"s1mm_{pX}_{phb}")
                wkey = "d_w" if par == 0 else "d_w2"
                C.inc(g.dma_start(wgbuf[par][:, :], d_in["wg"][hb, :, :]),
                      wkey, 16)
                C.inc(g.dma_start(wubuf[par][:, :], d_in["wu"][hb, :, :]),
                      wkey, 16, mark=f"wconv_{X}_{hb}")
            for dg in range(4):
                if dg == 0:
                    if X == 0:
                        C.wait(g, "yev_15")
                    else:
                        C.wait_cnt(g, "d_out", 16 * 4)
                        C.wait_cnt(g, "d_out2", 16 * 4)
                        C.wait(g, "s2q_0_3")
                else:
                    C.wait(g, f"s2q_{X}_{dg-1}")
                C.inc(g.dma_start(
                    wd_lo[:, 0:5632],
                    d_in["wd"][dg, :, 0:5632]), "d_wda", 16,
                    mark=f"dwdla_{X}_{dg}")
                if dg == 0:
                    if X == 1:
                        C.wait(g, "s2lo_0_3")
                else:
                    C.wait(g, f"s2lo_{X}_{dg-1}")
                C.inc(g.dma_start(
                    wd_lo[:, 5632:11264],
                    d_in["wd"][dg, :, 5632:11264]), "d_wdl", 16,
                    mark=f"dwdlo_{X}_{dg}")
                if dg == 0:
                    if X == 1:
                        C.wait(g, "s2f_0_3_7")
                else:
                    C.wait(g, f"s2f_{X}_{dg-1}_7")
                C.inc(g.dma_start(
                    wd_hi[:, :],
                    d_in["wd"][dg, :, 11264:22528]), "d_wdh", 16,
                    mark=f"dwdhi_{X}_{dg}")
        for ti in (0, 2, 4, 6):
            C.wait(g, f"out_1_{ti}")
            C.inc(g.dma_start(out_hbm[(8 + ti) * 128:(9 + ti) * 128, :],
                              outb2[0][:, :]), "d_w", 16,
                  mark=f"dout_1_{ti}")

    def emit_scalar(s):
        C.wait(s, "d_params")
        C.inc(s.activation(c_delta, p_logd, AF.Exp), "s")
        C.inc(s.activation(c_t1, p_lunc, AF.Abs), "s")
        C.selfwait(s, "s")
        C.inc(s.activation(c_t2, c_t1, AF.Exp, scale=-1.0), "s")
        C.selfwait(s, "s")
        C.inc(s.activation(c_t3, c_t2, AF.Ln, bias=1.0), "s", mark="s_p1")
        C.wait(s, "v_lnr")
        C.inc(s.activation(c_r, c_lnr, AF.Exp), "s", mark="s_r")
        C.wait(s, "v_thred")
        C.inc(s.activation(c_sinth, c_thred, AF.Sin), "s")
        C.inc(s.activation(c_sh, c_thred, AF.Sin, scale=0.5), "s",
              mark="s_sincos")
        C.wait(s, "v_pred")
        C.inc(s.activation(sinT[:, :], ph_p[:, :], AF.Sin), "s")
        C.inc(s.activation(ph_q[:, :], ph_p[:, :], AF.Abs), "s",
              mark="s_abs")
        C.wait(s, "v_carg")
        C.inc(s.activation(cosT[:, :], ph_r[:, :], AF.Sin), "s",
              mark="s_tables")
        C.inc(s.activation(drow_bf, drow_f, AF.Copy), "s", mark="s_dbf")
        C.wait(s, "v_rpe")
        C.inc(s.activation(rp_v[:, :], rp_v[:, :], AF.Exp, scale=-1.0,
                           bias=c_Tlnr), "s", mark="s_rpow")

        for sl in range(2 * NT + 2):
            j = sl
            if j < 2 * NT:
                C.wait(s, f"aggr_{j}")
                C.inc(s.activation(sd_p[j % 2], mv_p[j % 2][:, 1:2],
                                   AF.Sqrt, bias=c_eps), "s",
                      mark=f"sqrt_{j}")
            k = sl - 1
            if 0 <= k < 2 * NT:
                C.wait(s, f"lnc_{k}")
                i = k % NT
                if k >= 2:
                    C.wait(s, f"tr_{k-2}")
                C.inc(s.activation(t1ntmp[k % 2][:, :],
                                   ubuf[k % 3][:, :], AF.Identity,
                                   scale=rstd_p[k % 2], bias=nmr_p[k % 2]),
                      "s", mark=f"uapply_{k}")
            k = sl - 2
            if 0 <= k < 2 * NT:
                C.wait(s, f"tr_{k}")
                i = k % NT
                for gq in range(2, 4):
                    C.inc(s.activation(
                        t1nT_full[:, i * 2048 + 512 * gq:
                                  i * 2048 + 512 * (gq + 1)],
                        pBbf[:, 512 * gq:512 * (gq + 1)], AF.Copy), "s",
                        mark=(f"t1nTs_{k}" if gq == 3 else None))
        for sl in range(NT + 5):
            k = sl - 1
            if 0 <= k < NT:
                C.wait(s, f"ymm_{k}")
                if k >= 2:
                    C.wait(s, f"x1_{k-2}")
                for gq in range(4):
                    C.inc(s.activation(
                        ybuf_p[k % 2][:, 512 * gq:512 * (gq + 1)],
                        pA[:, 512 * gq:512 * (gq + 1)], AF.Copy), "s",
                        mark=(f"yev_{k}" if gq == 3 else None))
            if 0 <= k < NT:
                C.wait(s, f"yaggr_{k}")
                C.inc(s.activation(sdy_p[k % 2], mvy_p[k % 2][:, 1:2],
                                   AF.Sqrt, bias=c_eps), "s",
                      mark=f"ysqrt_{k}")
            k = sl - 2
            if 0 <= k < NT:
                C.wait(s, f"ylnc_{k}")
                C.inc(s.activation(ybuf_p[k % 2][:, :], ybuf_p[k % 2][:, :],
                                   AF.Identity, scale=rsy_p[k % 2],
                                   bias=nmy_p[k % 2]), "s", mark=f"ln2_{k}")
            k = sl - 3
            if 0 <= k < NT:
                C.wait(s, f"x1aggr_{k}")
                C.inc(s.activation(sdx_p[k % 2], mvx_p[k % 2][:, 1:2],
                                   AF.Sqrt, bias=c_eps), "s",
                      mark=f"x1sqrt_{k}")
            k = sl - 4
            if 0 <= k < NT:
                C.wait(s, f"xlnc_{k}")
                if k >= 1:
                    C.wait(s, f"xntr_{k-1}")
                C.inc(s.activation(xnbuf[:, :], x1buf[k % 2][:, :],
                                   AF.Identity, scale=rsx_p[k % 2],
                                   bias=nmx_p[k % 2]), "s",
                      mark=f"xnapply_{k}")
            k = sl - 5
            if 0 <= k < NT:
                C.wait(s, f"xntr_{k}")
                if k >= 9:
                    C.wait(s, f"dxnw_{k-9}")
                for gq in range(2, 4):
                    C.inc(s.activation(xnev_dst(k, gq), xnev_src(k, gq),
                                       AF.Copy), "s",
                          mark=(f"xnevs_{k}" if gq == 3 else None))
        for X in range(2):
            for hb in range(HB):
                C.wait(s, f"gmm_{X}_{hb}")
                if hb >= 1 or X == 1:
                    pX, phb = (X, hb - 1) if hb >= 1 else (0, HB - 1)
                    C.wait(s, f"h_{pX}_{phb}")
                gsrc = pA if hb % 2 == 0 else pB
                for tg in range(2):
                    C.inc(s.activation(h2buf[:, tg * 512:(tg + 1) * 512],
                                       gsrc[:, tg * 512:(tg + 1) * 512],
                                       AF.Silu, scale=sgbuf[:, hb:hb + 1]),
                          "s", mark=(f"gs_{X}_{hb}" if tg == 1 else None))
            for dg in range(4):
                C.wait(s, f"dwdla_{X}_{dg}")
                C.inc(s.activation(wd_lo[:, 0:5632], wd_lo[:, 0:5632],
                                   AF.Copy, bias=-128.0), "s",
                      mark=f"wdla_{X}_{dg}")
                C.wait(s, f"dwdlo_{X}_{dg}")
                C.inc(s.activation(wd_lo[:, 5632:11264], wd_lo[:, 5632:11264],
                                   AF.Copy, bias=-128.0), "s",
                      mark=f"wdlo_{X}_{dg}")
                C.wait(s, f"dwdhi_{X}_{dg}")
                C.inc(s.activation(wd_hi[:, :], wd_hi[:, :], AF.Copy,
                                   bias=-128.0), "s", mark=f"wdhi_{X}_{dg}")
            for ti in range(8):
                C.wait(s, f"l4aggr_{X}_{ti}")
                C.inc(s.activation(l4nm[ti], l4mv[ti][:, 1:2], AF.Sqrt,
                                   bias=c_eps), "s", mark=f"l4sqrt_{X}_{ti}")
            if X == 1:
                for ti in (0, 2, 4, 6):
                    C.wait(s, f"l4c_1_{ti}")
                    C.wait(s, f"ys_1_3_{ti}")
                    if ti == 6:
                        C.wait(s, "dyr_1_6")
                    C.inc(s.activation(ybig[ti][:, :], ybig[ti][:, :],
                                       AF.Identity, scale=l4rs[ti],
                                       bias=l4nm[ti]), "s",
                          mark=f"l4app_{ti}")
                    if ti >= 2:
                        to = ti - 1
                        C.wait(s, f"out_1_{to}")
                        C.inc(s.dma_start(
                            out_hbm[(8 + to) * 128:(9 + to) * 128, :],
                            outb2[1][:, :]), "d_out2", 16,
                            mark=f"dout_1_{to}")
                for to in (7,):
                    C.wait(s, f"out_1_{to}")
                    C.inc(s.dma_start(
                        out_hbm[(8 + to) * 128:(9 + to) * 128, :],
                        outb2[1][:, :]), "d_out2", 16,
                        mark=f"dout_1_{to}")

    def emit_tensor(t):
        C.wait(t, "g_id")
        C.wait(t, "v_bbar")
        for kt in range(16):
            C.inc(t.transpose(pBbf[:, kt * 32:(kt + 1) * 32],
                              bbar_re[:, kt * 128:(kt + 1) * 128],
                              id_bf[0:NCH, 0:NCH]), "t")
        for kt in range(16):
            C.inc(t.transpose(pBbf[:, 512 + kt * 32:512 + (kt + 1) * 32],
                              bbar_im[:, kt * 128:(kt + 1) * 128],
                              id_bf[0:NCH, 0:NCH]), "t",
                  mark=("t_btr" if kt == 15 else None))
        C.wait(t, "s_dbf")
        for dg in range(4):
            C.inc(t.matmul(pA[:, dg * 512:(dg + 1) * 512], onesrow_bf,
                           drow_bf[:, dg * 512:(dg + 1) * 512],
                           start=True, stop=True), "t",
                  mark=("t_drep" if dg == 3 else None))
        C.wait(t, "v_drep")
        for dg in range(4):
            C.inc(t.matmul(pA[:, dg * 512:(dg + 1) * 512], onesrow_f,
                           sdrow[:, dg * 512:(dg + 1) * 512],
                           start=True, stop=True), "t",
                  mark=("t_sdrep" if dg == 3 else None))

        for j in range(2 * NT):
            C.wait(t, f"uapply_{j}")
            if j >= 1:
                C.wait(t, f"t1nTv_{j-1}")
                C.wait(t, f"t1nTs_{j-1}")
            i = j % NT
            tsrc = t1ntmp[j % 2]
            for kt in range(16):
                C.inc(t.transpose(pBbf[:, kt * 128:(kt + 1) * 128],
                                  tsrc[:, kt * 128:(kt + 1) * 128],
                                  id_bf[:, :]), "t",
                      mark=(f"tr_{j}" if kt == 15 else None))
            C.wait(t, f"t1nTv_{j}")
            C.wait(t, f"t1nTs_{j}")
            if j >= 2:
                C.wait(t, f"buev_{j-2}")
            pb = pC if j % 2 == 0 else pD
            for kt in range(16):
                C.inc(t.matmul(pb[0:NCH, 0:128],
                               bret_bf[:, kt * 32:(kt + 1) * 32],
                               t1nT_full[:, i * 2048 + kt * 128:
                                         i * 2048 + (kt + 1) * 128],
                               start=(kt == 0), stop=(kt == 15)), "t")
            for kt in range(16):
                C.inc(t.matmul(pb[0:NCH, 128:256],
                               bimt_bf[:, kt * 32:(kt + 1) * 32],
                               t1nT_full[:, i * 2048 + kt * 128:
                                         i * 2048 + (kt + 1) * 128],
                               start=(kt == 0), stop=(kt == 15)), "t",
                      mark=(f"bu_{j}" if kt == 15 else None))

        def xntr_block(k):
            C.wait(t, f"xnapply_{k}")
            if k == 0:
                C.wait(t, f"buev_{2*NT-1}")
            else:
                C.wait(t, f"xnevv_{k-1}")
                C.wait(t, f"xnevs_{k-1}")
            for kt in range(16):
                C.inc(t.transpose(pBbf[:, kt * 128:(kt + 1) * 128],
                                  xnbuf[:, kt * 128:(kt + 1) * 128],
                                  id_bf[:, :]), "t",
                      mark=(f"xntr_{k}" if kt == 15 else None))

        C.wait(t, "v_scan_done")
        C.wait(t, "v_sdrep")
        for i in range(NT):
            if i >= 1:
                C.wait(t, f"yev_{i-1}")
            if i == 0:
                C.wait(t, "g_ddiag")
            for dg in range(4):
                C.inc(t.matmul(pA[:, dg * 512:(dg + 1) * 512],
                               xs_re[:, i * 128:(i + 1) * 128],
                               c2re_v[:, dg * 512:(dg + 1) * 512],
                               start=True, stop=False), "t")
                C.inc(t.matmul(pA[:, dg * 512:(dg + 1) * 512],
                               xs_im[:, i * 128:(i + 1) * 128],
                               c2im_v[:, dg * 512:(dg + 1) * 512],
                               start=False, stop=False), "t")
            for kd in range(16):
                C.inc(t.matmul(pA[:, kd * 128:(kd + 1) * 128],
                               t1nT_full[:, i * 2048 + kd * 128:
                                         i * 2048 + (kd + 1) * 128],
                               d_rep_bf[:, kd * 128:(kd + 1) * 128],
                               start=False, stop=(kd % 4 == 3)), "t",
                      mark=(f"ymm_{i}" if kd == 15 else None))
            if i >= 5:
                xntr_block(i - 5)
        def s1q0(buf, ps, tg, mk):
            for kt in range(16):
                C.inc(t.matmul(ps,
                               buf[:, kt * 128:(kt + 1) * 128],
                               xnt[:, kt * TH + tg * 512:
                                   kt * TH + (tg + 1) * 512],
                               start=(kt == 0), stop=(kt == 15)), "t",
                      mark=(mk if kt == 15 else None))

        xntr_block(11)
        C.wait(t, "wconv_0_0")
        C.wait(t, "yev_15")
        C.wait(t, "xnevv_3")
        C.wait(t, "xnevs_3")
        s1q0(wgbuf[0], pA[:, 0:512], 0, None)
        xntr_block(12)
        C.wait(t, "xnevv_7")
        C.wait(t, "xnevs_7")
        s1q0(wgbuf[0], pA[:, 512:1024], 1, "gmm_0_0")
        xntr_block(13)
        s1q0(wubuf[0], pA[:, 1024:1536], 0, None)
        xntr_block(14)
        s1q0(wubuf[0], pA[:, 1536:2048], 1, "s1mm_0_0")
        xntr_block(15)

        PS2 = [pA[:, 0:512], pA[:, 512:1024], pA[:, 1024:1536],
               pA[:, 1536:2048], pB[:, 0:512], pB[:, 512:1024], pC, pD]
        for X in range(2):
            for hb in range(HB):
                if X == 0 and hb == 0:
                    continue
                par = hb % 2
                C.wait(t, f"wconv_{X}_{hb}")
                if X == 0 and hb == 1:
                    C.wait(t, "xnevv_15")
                    C.wait(t, "xnevs_15")
                if X == 1 and hb == 0:
                    C.wait(t, "dxnr")
                    C.wait(t, "gs_0_42")
                    C.wait(t, "uev_0_42")
                    C.wait(t, "ys_0_3_3")
                if X == 1 and hb == 1:
                    C.wait(t, "gs_0_43")
                    C.wait(t, "uev_0_43")
                    C.wait(t, "ys_0_3_7")
                if hb >= 2:
                    C.wait(t, f"gs_{X}_{hb-2}")
                    C.wait(t, f"uev_{X}_{hb-2}")
                if par == 0:
                    gps = [pA[:, 0:512], pA[:, 512:1024]]
                    ups = [pA[:, 1024:1536], pA[:, 1536:2048]]
                else:
                    gps = [pB[:, 0:512], pB[:, 512:1024]]
                    ups = [pC[:, :], pD[:, :]]
                for tg in range(2):
                    for kt in range(16):
                        C.inc(t.matmul(
                            gps[tg],
                            wgbuf[par][:, kt * 128:(kt + 1) * 128],
                            xnt[:, kt * TH + tg * 512:
                                kt * TH + (tg + 1) * 512],
                            start=(kt == 0), stop=(kt == 15)), "t",
                            mark=(f"gmm_{X}_{hb}"
                                  if tg == 1 and kt == 15 else None))
                for tg in range(2):
                    for kt in range(16):
                        C.inc(t.matmul(
                            ups[tg],
                            wubuf[par][:, kt * 128:(kt + 1) * 128],
                            xnt[:, kt * TH + tg * 512:
                                kt * TH + (tg + 1) * 512],
                            start=(kt == 0), stop=(kt == 15)), "t",
                            mark=(f"s1mm_{X}_{hb}"
                                  if tg == 1 and kt == 15 else None))
            for dg in range(4):
                for hb in range(HB):
                    if hb == 0:
                        C.wait(t, f"wdla_{X}_{dg}")
                    if hb == 11:
                        C.wait(t, f"wdlo_{X}_{dg}")
                    if hb == 22:
                        C.wait(t, f"wdhi_{X}_{dg}")
                    if dg == 0:
                        C.wait(t, f"h_{X}_{hb}")
                    wsrc = wd_lo if hb < 22 else wd_hi
                    wof = hb if hb < 22 else hb - 22
                    for ti in range(8):
                        if hb == 0:
                            if dg == 0:
                                C.wait(t, f"gs_{X}_{HB-1}")
                                C.wait(t, f"uev_{X}_{HB-1}")
                            else:
                                C.wait(t, f"ys_{X}_{dg-1}_{ti}")
                        C.inc(t.matmul(
                            PS2[ti],
                            Hbuf[:, hb * TH + ti * 128:
                                 hb * TH + (ti + 1) * 128],
                            wsrc[:, wof * 512:(wof + 1) * 512],
                            start=(hb == 0), stop=(hb == HB - 1)), "t",
                            mark=(f"s2f_{X}_{dg}_{ti}" if hb == HB - 1
                                  else (f"s2lo_{X}_{dg}"
                                        if hb == 21 and ti == 7 else
                                        (f"s2q_{X}_{dg}"
                                         if hb == 10 and ti == 7 else
                                         (f"s2e_{hb}"
                                          if X == 1 and dg == 3 and ti == 7
                                          and hb in (7, 11, 15, 19, 23)
                                          else None)))))

    def emit_vector(v):
        C.wait(v, "d_params")
        C.wait(v, "g_setup")
        C.inc(v.tensor_scalar(c2re_v[:, :], p_cret, 2.0, None, ALU.mult),
              "v")
        C.inc(v.tensor_scalar(c2im_v[:, :], p_cimt, -2.0, None, ALU.mult),
              "v", mark="v_c2")
        C.wait(v, "s_p1")
        C.inc(v.tensor_scalar(c_t1, p_lunc, 0.0, None, ALU.max), "v")
        C.selfwait(v, "v")
        C.inc(v.tensor_tensor(c_t1, c_t1, c_t3, ALU.add), "v")
        C.selfwait(v, "v")
        C.inc(v.tensor_scalar(c_lamre, c_t1, 0.0101, -1.0, ALU.add,
                              ALU.mult), "v")
        C.selfwait(v, "v")
        C.inc(v.tensor_tensor(c_lnr, c_lamre, c_delta, ALU.mult), "v",
              mark="v_lnr")
        C.selfwait(v, "v")
        C.inc(v.tensor_scalar(c_Tlnr, c_lnr, float(T), None, ALU.mult), "v")
        C.inc(v.tensor_tensor(c_theta, p_lim, c_delta, ALU.mult), "v")
        C.selfwait(v, "v")
        C.inc(v.tensor_scalar(c_t2, c_theta, 1.0 / TWO_PI, None, ALU.mult),
              "v")
        C.selfwait(v, "v")
        C.inc(v.tensor_copy(c_i32, c_t2), "v")
        C.selfwait(v, "v")
        C.inc(v.tensor_copy(c_t2, c_i32), "v")
        C.selfwait(v, "v")
        C.inc(v.scalar_tensor_tensor(c_thred, c_t2, -TWO_PI, c_theta,
                                     ALU.mult, ALU.add), "v")
        C.selfwait(v, "v")
        C.inc(v.tensor_scalar(c_t2, c_thred, PI, None, ALU.is_gt), "v")
        C.selfwait(v, "v")
        C.inc(v.scalar_tensor_tensor(c_thred, c_t2, -TWO_PI, c_thred,
                                     ALU.mult, ALU.add), "v")
        C.selfwait(v, "v")
        C.inc(v.tensor_scalar(c_t2, c_thred, -PI, None, ALU.is_lt), "v")
        C.selfwait(v, "v")
        C.inc(v.scalar_tensor_tensor(c_thred, c_t2, TWO_PI, c_thred,
                                     ALU.mult, ALU.add), "v", mark="v_thred")
        C.wait(v, "s_sincos")
        C.inc(v.tensor_tensor(c_t2, c_sh, c_sh, ALU.mult), "v")
        C.selfwait(v, "v")
        C.inc(v.tensor_scalar(c_costh, c_t2, -2.0, 1.0, ALU.mult, ALU.add),
              "v")
        C.wait(v, "s_r")
        C.selfwait(v, "v")
        C.inc(v.tensor_tensor(c_are, c_r, c_costh, ALU.mult), "v")
        C.inc(v.tensor_tensor(c_aim, c_r, c_sinth, ALU.mult), "v")
        C.selfwait(v, "v")
        C.inc(v.tensor_scalar(c_t1, c_are, -1.0, None, ALU.add), "v")
        C.inc(v.tensor_tensor(c_t2, c_lamre, c_lamre, ALU.mult), "v")
        C.inc(v.tensor_tensor(c_t3, p_lim, p_lim, ALU.mult), "v")
        C.selfwait(v, "v")
        C.inc(v.tensor_tensor(c_t2, c_t2, c_t3, ALU.add), "v")
        C.selfwait(v, "v")
        C.inc(v.reciprocal(c_t2, c_t2), "v")
        C.selfwait(v, "v")
        C.inc(v.tensor_tensor(c_t3, c_t1, c_lamre, ALU.mult), "v")
        C.inc(v.tensor_tensor(c_t4, c_aim, p_lim, ALU.mult), "v")
        C.selfwait(v, "v")
        C.inc(v.tensor_tensor(c_t3, c_t3, c_t4, ALU.add), "v")
        C.selfwait(v, "v")
        C.inc(v.tensor_tensor(c_dre, c_t3, c_t2, ALU.mult), "v")
        C.selfwait(v, "v")
        C.inc(v.tensor_tensor(c_t3, c_aim, c_lamre, ALU.mult), "v")
        C.inc(v.tensor_tensor(c_t4, c_t1, p_lim, ALU.mult), "v")
        C.selfwait(v, "v")
        C.inc(v.tensor_tensor(c_t3, c_t3, c_t4, ALU.subtract), "v")
        C.selfwait(v, "v")
        C.inc(v.tensor_tensor(c_dim, c_t3, c_t2, ALU.mult), "v")
        C.selfwait(v, "v")
        C.inc(v.tensor_scalar(bbar_sA[:, :], p_bre, c_dre, None, ALU.mult),
              "v")
        C.inc(v.tensor_scalar(bbar_sB[:, :], p_bim, c_dim, None, ALU.mult),
              "v")
        C.selfwait(v, "v")
        C.inc(v.tensor_tensor(bbar_re[:, :], bbar_sA[:, :], bbar_sB[:, :],
                              ALU.subtract), "v")
        C.selfwait(v, "v")
        C.inc(v.tensor_scalar(bbar_sA[:, :], p_bim, c_dre, None, ALU.mult),
              "v")
        C.inc(v.tensor_scalar(bbar_sB[:, :], p_bre, c_dim, None, ALU.mult),
              "v")
        C.selfwait(v, "v")
        C.inc(v.tensor_tensor(bbar_im[:, :], bbar_sA[:, :], bbar_sB[:, :],
                              ALU.add), "v", mark="v_bbar")
        C.inc(v.tensor_scalar(ph_p[:, :], it_fwd[:, :], c_theta, None,
                              ALU.mult), "v")
        C.selfwait(v, "v")
        C.inc(v.tensor_scalar(it_fwd[:, :], it_fwd[:, :], c_lnr, None,
                              ALU.mult), "v", mark="v_rpe")
        C.inc(v.tensor_scalar(ph_q[:, :], ph_p[:, :], 1.0 / TWO_PI, None,
                              ALU.mult), "v")
        C.selfwait(v, "v")
        C.inc(v.tensor_copy(ph_qi, ph_q[:, :]), "v")
        C.selfwait(v, "v")
        C.inc(v.tensor_copy(ph_r[:, :], ph_qi), "v")
        C.selfwait(v, "v")
        C.inc(v.scalar_tensor_tensor(ph_p[:, :], ph_r[:, :], -TWO_PI,
                                     ph_p[:, :], ALU.mult, ALU.add), "v")
        C.selfwait(v, "v")
        C.inc(v.tensor_scalar(ph_r[:, :], ph_p[:, :], PI, None, ALU.is_gt),
              "v")
        C.selfwait(v, "v")
        C.inc(v.scalar_tensor_tensor(ph_p[:, :], ph_r[:, :], -TWO_PI,
                                     ph_p[:, :], ALU.mult, ALU.add), "v")
        C.selfwait(v, "v")
        C.inc(v.tensor_scalar(ph_r[:, :], ph_p[:, :], -PI, None, ALU.is_lt),
              "v")
        C.selfwait(v, "v")
        C.inc(v.scalar_tensor_tensor(ph_p[:, :], ph_r[:, :], TWO_PI,
                                     ph_p[:, :], ALU.mult, ALU.add), "v",
              mark="v_pred")
        C.wait(v, "s_abs")
        C.selfwait(v, "v")
        C.inc(v.tensor_scalar(ph_r[:, :], ph_q[:, :], -1.0, PI / 2.0,
                              ALU.mult, ALU.add), "v", mark="v_carg")
        C.wait(v, "t_btr")
        C.inc(v.tensor_copy(bret_bf[:, :], pBbf[:, 0:512]), "v")
        C.inc(v.tensor_copy(bimt_bf[:, :], pBbf[:, 512:1024]), "v",
              mark="v_btrev")
        C.wait(v, "t_drep")
        C.inc(v.tensor_copy(d_rep_bf[:, :], pA[:, :]), "v", mark="v_drep")
        C.wait(v, "t_sdrep")
        C.inc(v.tensor_copy(sdrep[:, :], pA[:, :]), "v", mark="v_sdrep")

        NTOT = 2 * NT
        for sl in range(NTOT + 3):
            j = sl
            if j < NTOT:
                C.wait(v, f"du_{j}")
                C.selfwait(v, "v")
                for gq in range(4):
                    C.inc(v.bn_stats(statbuf[:, 6 * gq:6 * (gq + 1)],
                                     ubuf[j % 3][:, 512 * gq:512 * (gq + 1)]),
                          "v")
                C.selfwait(v, "v")
                C.inc(v.bn_aggr(mv_p[j % 2], statbuf[:, :]), "v",
                      mark=f"aggr_{j}")
            k = sl - 1
            if 0 <= k < NTOT:
                C.wait(v, f"sqrt_{k}")
                C.selfwait(v, "v")
                C.inc(v.reciprocal(rstd_p[k % 2], sd_p[k % 2]), "v")
                C.inc(v.tensor_scalar(c_negm, mv_p[k % 2][:, 0:1], -1.0,
                                      None, ALU.mult), "v")
                C.selfwait(v, "v")
                C.inc(v.tensor_tensor(nmr_p[k % 2], c_negm, rstd_p[k % 2],
                                      ALU.mult), "v", mark=f"lnc_{k}")
            k = sl - 2
            if 0 <= k < NTOT:
                C.wait(v, f"tr_{k}")
                i = k % NT
                for gq in range(2):
                    C.inc(v.tensor_copy(
                        t1nT_full[:, i * 2048 + 512 * gq:
                                  i * 2048 + 512 * (gq + 1)],
                        pBbf[:, 512 * gq:512 * (gq + 1)]), "v",
                        mark=(f"t1nTv_{k}" if gq == 1 else None))
            k = sl - 3
            if 0 <= k < NTOT:
                C.wait(v, f"bu_{k}")
                pb = pC if k % 2 == 0 else pD
                bre_dst, bim_dst = bu_dsts(k)
                C.inc(v.tensor_copy(bre_dst, pb[0:NCH, 0:128]), "v")
                C.inc(v.tensor_copy(bim_dst, pb[0:NCH, 128:256]), "v",
                      mark=f"buev_{k}")
            c = sl - 17
            if c == 0:
                C.wait(v, "s_rpow")
                C.inc(v.tensor_tensor(cosp_ps, cosT[:, :], rp_v[:, :],
                                      ALU.mult), "v")
            elif c == 1:
                C.inc(v.tensor_tensor(sinp_ps, sinT[:, :], rp_v[:, :],
                                      ALU.mult), "v")
            elif c in (2, 4, 6, 8):
                a, bb = ((cosp_ps, bu_re_p), (sinp_ps, bu_im_p),
                         (cosp_ps, bu_im_p), (sinp_ps, bu_re_p))[(c - 2) // 2]
                C.selfwait(v, "v")
                C.inc(v.tensor_tensor(prod_ps, a, bb[:, :], ALU.mult), "v")
            elif c in (3, 5, 7, 9):
                dst = (c_t1, c_t2, c_t3, c_t4)[(c - 3) // 2]
                C.selfwait(v, "v")
                C.inc(v.reduce_sum(dst, prod_ps, axis=AX.X), "v")
            elif c == 10:
                C.selfwait(v, "v")
                C.inc(v.tensor_tensor(c_sr, c_t1, c_t2, ALU.add), "v")
                C.inc(v.tensor_tensor(c_si, c_t3, c_t4, ALU.subtract), "v")
            elif c == 11:
                C.selfwait(v, "v")
                C.inc(v.tensor_tensor(c_t1, cosT[:, T - 1:T], c_sr,
                                      ALU.mult), "v")
                C.inc(v.tensor_tensor(c_t2, sinT[:, T - 1:T], c_si,
                                      ALU.mult), "v")
                C.inc(v.tensor_tensor(c_t3, sinT[:, T - 1:T], c_sr,
                                      ALU.mult), "v")
                C.inc(v.tensor_tensor(c_t4, cosT[:, T - 1:T], c_si,
                                      ALU.mult), "v")
            elif c == 12:
                C.selfwait(v, "v")
                C.inc(v.tensor_tensor(c_cre, c_t1, c_t2, ALU.subtract), "v")
                C.inc(v.tensor_tensor(c_cim, c_t3, c_t4, ALU.add), "v")
            elif c == 13:
                C.selfwait(v, "v")
                C.inc(v.memset(bt_re[:, :], 1.0), "v")
            elif c == 14:
                C.selfwait(v, "v")
                C.inc(v.tensor_scalar(bt_re[:, :], bt_re[:, :], c_r, None,
                                      ALU.mult), "v")
            if 16 <= k < NTOT:
                i = k - 16
                ci = slice(i * 128, (i + 1) * 128)
                C.selfwait(v, "v")
                C.inc(v.tensor_tensor(qtmp[:, 0:128], cosT[:, ci],
                                      bu_re_m[:, ci], ALU.mult), "v")
                C.inc(v.tensor_tensor(qtmp[:, 128:256], sinT[:, ci],
                                      bu_im_m[:, ci], ALU.mult), "v")
                C.selfwait(v, "v")
                C.inc(v.tensor_tensor(btre_s[:, ci], qtmp[:, 0:128],
                                      qtmp[:, 128:256], ALU.add), "v")
                C.selfwait(v, "v")
                C.inc(v.tensor_tensor(qtmp[:, 0:128], cosT[:, ci],
                                      bu_im_m[:, ci], ALU.mult), "v")
                C.inc(v.tensor_tensor(qtmp[:, 128:256], sinT[:, ci],
                                      bu_re_m[:, ci], ALU.mult), "v")
                C.selfwait(v, "v")
                C.inc(v.tensor_tensor(btim_ps[:, ci], qtmp[:, 0:128],
                                      qtmp[:, 128:256], ALU.subtract), "v")

        C.selfwait(v, "v")
        C.inc(v.tensor_tensor_scan(tA[:, :], bt_re[:, :], btre_s[:, :],
                                   c_cre, ALU.mult, ALU.add), "v")
        C.inc(v.tensor_tensor_scan(tB[:, :], bt_re[:, :], btim_ps, c_cim,
                                   ALU.mult, ALU.add), "v")
        C.selfwait(v, "v")
        C.inc(v.tensor_tensor(tC[:, :], cosT[:, :], tA[:, :], ALU.mult), "v")
        C.inc(v.tensor_tensor(bt_re[:, :], sinT[:, :], tB[:, :], ALU.mult),
              "v")
        C.selfwait(v, "v")
        C.inc(v.tensor_tensor(xs_re[:, :], tC[:, :], bt_re[:, :],
                              ALU.subtract), "v")
        C.selfwait(v, "v")
        C.inc(v.tensor_tensor(tC[:, :], cosT[:, :], tB[:, :], ALU.mult), "v")
        C.inc(v.tensor_tensor(bt_im[:, :], sinT[:, :], tA[:, :], ALU.mult),
              "v")
        C.selfwait(v, "v")
        C.inc(v.tensor_tensor(xs_im[:, :], tC[:, :], bt_im[:, :], ALU.add),
              "v", mark="v_scan_done")

        for sl in range(NT + 5):
            k = sl - 2
            if 0 <= k < NT:
                C.wait(v, f"ysqrt_{k}")
                C.selfwait(v, "v")
                C.inc(v.reciprocal(rsy_p[k % 2], sdy_p[k % 2]), "v")
                C.inc(v.tensor_scalar(c_negm, mvy_p[k % 2][:, 0:1], -1.0,
                                      None, ALU.mult), "v")
                C.selfwait(v, "v")
                C.inc(v.tensor_tensor(nmy_p[k % 2], c_negm, rsy_p[k % 2],
                                      ALU.mult), "v", mark=f"ylnc_{k}")
            k = sl - 4
            if 0 <= k < NT:
                C.wait(v, f"x1sqrt_{k}")
                C.selfwait(v, "v")
                C.inc(v.reciprocal(rsx_p[k % 2], sdx_p[k % 2]), "v")
                C.inc(v.tensor_scalar(c_negm, mvx_p[k % 2][:, 0:1], -1.0,
                                      None, ALU.mult), "v")
                C.selfwait(v, "v")
                C.inc(v.tensor_tensor(nmx_p[k % 2], c_negm, rsx_p[k % 2],
                                      ALU.mult), "v", mark=f"xlnc_{k}")
            k = sl - 3
            if 0 <= k < NT:
                C.wait(v, f"ln2_{k}")
                C.wait(v, f"dx0_{k}")
                if k >= 2:
                    C.wait(v, f"dx1w_{k-2}")
                    C.wait(v, f"xnapply_{k-2}")
                C.selfwait(v, "v")
                C.inc(v.tensor_tensor(x1buf[k % 2][:, :],
                                      ybuf_p[k % 2][:, :],
                                      x0buf_p[k % 2][:, :],
                                      ALU.add), "v", mark=f"x1_{k}")
                C.selfwait(v, "v")
                for gq in range(4):
                    C.inc(v.bn_stats(
                        statbuf[:, 6 * gq:6 * (gq + 1)],
                        x1buf[k % 2][:, 512 * gq:512 * (gq + 1)]), "v")
                C.selfwait(v, "v")
                C.inc(v.bn_aggr(mvx_p[k % 2], statbuf[:, :]), "v",
                      mark=f"x1aggr_{k}")
            k = sl - 1
            if 0 <= k < NT:
                C.wait(v, f"yev_{k}")
                C.selfwait(v, "v")
                for gq in range(4):
                    C.inc(v.bn_stats(
                        statbuf[:, 6 * gq:6 * (gq + 1)],
                        ybuf_p[k % 2][:, 512 * gq:512 * (gq + 1)]), "v")
                C.selfwait(v, "v")
                C.inc(v.bn_aggr(mvy_p[k % 2], statbuf[:, :]), "v",
                      mark=f"yaggr_{k}")
            k = sl - 5
            if 0 <= k < NT:
                C.wait(v, f"xntr_{k}")
                if k >= 9:
                    C.wait(v, f"dxnw_{k-9}")
                for gq in range(2):
                    C.inc(v.tensor_copy(xnev_dst(k, gq), xnev_src(k, gq)),
                          "v", mark=(f"xnevv_{k}" if gq == 1 else None))

        def tail_tile(X, ti):
            C.wait(v, f"dyr_{X}_{ti}")
            C.selfwait(v, "v")
            C.inc(v.tensor_scalar(yrbuf_p[ti % 2][:, :],
                                  yrbuf_p[ti % 2][:, :], l4nm[ti],
                                  l4rs[ti], ALU.add, ALU.mult), "v",
                  mark=f"ln4apply_{X}_{ti}")
            C.wait(v, f"dx1r_{X}_{ti}")
            gk = X * 8 + ti
            if gk >= 1:
                pgk = gk - 1
                C.wait(v, f"dout_{pgk // 8}_{pgk % 8}")
            C.selfwait(v, "v")
            C.inc(v.tensor_tensor(outbuf[0][:, :], yrbuf_p[ti % 2][:, :],
                                  x1rbuf_p[ti % 2][:, :], ALU.add), "v",
                  mark=f"out_{X}_{ti}")

        for X in range(2):
            for hb in range(HB):
                if X == 0 and hb == 0:
                    C.wait(v, "dxnw_6")
                    C.wait(v, "dxnw_7")
                    C.wait(v, f"dx1w_{NT-2}")
                    C.wait(v, f"dx1w_{NT-1}")
                if X == 1 and hb == 0:
                    C.wait(v, "dyw_0_3_6")
                if X == 1 and hb % 5 == 4 and hb // 5 < 8:
                    tail_tile(0, hb // 5)
                C.wait(v, f"s1mm_{X}_{hb}")
                C.selfwait(v, "v")
                if hb % 2 == 0:
                    ups = [pA[:, 1024:1536], pA[:, 1536:2048]]
                else:
                    ups = [pC[:, :], pD[:, :]]
                for tg in range(2):
                    C.inc(v.tensor_scalar(
                        us_s1[:, tg * 512:(tg + 1) * 512], ups[tg],
                        subuf[:, hb:hb + 1], None, ALU.mult), "v",
                        mark=(f"uev_{X}_{hb}" if tg == 1 else None))
                C.wait(v, f"gs_{X}_{hb}")
                C.selfwait(v, "v")
                for tg in range(2):
                    C.inc(v.tensor_tensor(
                        Hcols(hb, tg), h2buf[:, tg * 512:(tg + 1) * 512],
                        us_s1[:, tg * 512:(tg + 1) * 512], ALU.mult), "v",
                        mark=(f"h_{X}_{hb}" if tg == 1 else None))
            PS2v = [pA[:, 0:512], pA[:, 512:1024], pA[:, 1024:1536],
                    pA[:, 1536:2048], pB[:, 0:512], pB[:, 512:1024],
                    pC[:, :], pD[:, :]]
            for dg in range(4):
                for ti in range(8):
                    par = (dg * 8 + ti) % 2
                    C.wait(v, f"s2f_{X}_{dg}_{ti}")
                    if X == 0:
                        k = dg * 8 + ti
                        if k >= 4:
                            pk = k - 4
                            C.wait(v, f"dyw_{X}_{pk // 8}_{pk % 8}")
                        dst = ystage4[k % 4][:, :]
                    elif ti < 6 or dg == 3:
                        dst = ybig[ti][:, dg * 512:(dg + 1) * 512]
                    else:
                        if dg == 0:
                            C.wait(v, f"dyw_0_3_{ti - 2}")
                        else:
                            C.wait(v, f"dyw_1_{dg-1}_{ti}")
                        dst = ystage[ti - 6][:, :]
                    C.inc(v.tensor_tensor(dst, PS2v[ti],
                                          sdrep[:, dg * 512:(dg + 1) * 512],
                                          ALU.mult), "v")
                    C.selfwait(v, "v")
                    C.inc(v.bn_stats(
                        ln4stats[:, ti * 24 + dg * 6:ti * 24 + dg * 6 + 6],
                        dst), "v", mark=f"ys_{X}_{dg}_{ti}")
            C.selfwait(v, "v")
            for ti in range(8):
                C.inc(v.bn_aggr(l4mv[ti], ln4stats[:, ti * 24:(ti + 1) * 24]),
                      "v", mark=f"l4aggr_{X}_{ti}")
            for ti in range(8):
                C.wait(v, f"l4sqrt_{X}_{ti}")
                C.selfwait(v, "v")
                C.inc(v.reciprocal(l4rs[ti], l4nm[ti]), "v")
                C.inc(v.tensor_scalar(c_negm, l4mv[ti][:, 0:1], -1.0, None,
                                      ALU.mult), "v")
                C.selfwait(v, "v")
                C.inc(v.tensor_tensor(l4nm[ti], c_negm, l4rs[ti], ALU.mult),
                      "v", mark=f"l4c_{X}_{ti}")
            if X == 1:
                for ti in range(8):
                    if ti == 0:
                        C.wait(v, "s2e_19")
                    if ti == 1:
                        C.wait(v, "s2e_23")
                    if ti % 2 == 1:
                        if ti == 7:
                            C.wait(v, "dyr_1_7")
                        C.selfwait(v, "v")
                        C.inc(v.tensor_scalar(ybig[ti][:, :], ybig[ti][:, :],
                                              l4nm[ti], l4rs[ti], ALU.add,
                                              ALU.mult), "v")
                    else:
                        C.wait(v, f"l4app_{ti}")
                    C.wait(v, f"dx1r_1_{ti}")
                    if ti >= 2:
                        C.wait(v, f"dout_1_{ti-2}")
                    C.selfwait(v, "v")
                    C.inc(v.tensor_tensor(outb2[ti % 2][:, :],
                                          ybig[ti][:, :],
                                          x1rb2[ti % 2][:, :], ALU.add),
                          "v", mark=f"out_1_{ti}")

    def run_pass(pn):
        C.begin(pn, sems if pn == 2 else None)
        if pn == 1:
            emit_sync(_FakeEng())
            emit_gpsimd(_FakeEng())
            emit_scalar(_FakeEng())
            emit_vector(_FakeEng())
            emit_tensor(_FakeEng())
            return
        with nc.Block() as block:
            @block.sync
            def _(q):
                emit_sync(q)

            @block.gpsimd
            def _(g):
                emit_gpsimd(g)

            @block.scalar
            def _(s):
                emit_scalar(s)

            @block.vector
            def _(v):
                emit_vector(v)

            @block.tensor
            def _(t):
                emit_tensor(t)

    run_pass(1)
    cnt1 = dict(C.cnt)
    run_pass(2)
    assert C.cnt == cnt1, (C.cnt, cnt1)
    st.close()
    return nc


def _u8view(wq_int32):
    a = np.ascontiguousarray(np.asarray(wq_int32))
    assert a.dtype == np.int32
    return np.ascontiguousarray(a.view(np.uint8)[..., ::4])


def _stage_w_hd(wq):
    a = _u8view(wq).reshape(HB, 128, 16, 128)
    a = np.ascontiguousarray(a.transpose(0, 3, 2, 1))
    return a.reshape(HB, 128, 2048)


def _stage_w_dh(wq):
    a = np.ascontiguousarray(_u8view(wq).T)
    a = a.reshape(HB, 128, 4, 512).transpose(2, 1, 0, 3)
    return np.ascontiguousarray(a).reshape(4, 128, HB * 512)


def kernel(L0_attn_u, L0_attn_Lambda_unconstrained_re, L0_attn_Lambda_im,
           L0_attn_B_re, L0_attn_B_im, L0_attn_C_re, L0_attn_C_im,
           L0_attn_D, L0_attn_log_Delta, L0_ffn_wq_gate, L0_ffn_s_gate,
           L0_ffn_wq_up, L0_ffn_s_up, L0_ffn_wq_down, L0_ffn_s_down):
    u = np.ascontiguousarray(np.asarray(L0_attn_u, dtype=np.float32))
    wg = _stage_w_hd(L0_ffn_wq_gate)
    wu = _stage_w_hd(L0_ffn_wq_up)
    wd = _stage_w_dh(L0_ffn_wq_down)
    sg = np.ascontiguousarray(
        np.asarray(L0_ffn_s_gate, np.float32).reshape(HB, 128).T)
    su = np.ascontiguousarray(
        np.asarray(L0_ffn_s_up, np.float32).reshape(HB, 128).T)
    sd = np.ascontiguousarray(np.asarray(L0_ffn_s_down,
                                         np.float32).reshape(1, D))
    bre = np.ascontiguousarray(np.asarray(L0_attn_B_re, np.float32))
    bim = np.ascontiguousarray(np.asarray(L0_attn_B_im, np.float32))
    cret = np.ascontiguousarray(np.asarray(L0_attn_C_re, np.float32).T)
    cimt = np.ascontiguousarray(np.asarray(L0_attn_C_im, np.float32).T)
    drow = np.ascontiguousarray(np.asarray(L0_attn_D,
                                           np.float32).reshape(1, D))
    lunc = np.ascontiguousarray(
        np.asarray(L0_attn_Lambda_unconstrained_re,
                   np.float32).reshape(NCH, 1))
    lim = np.ascontiguousarray(np.asarray(L0_attn_Lambda_im,
                                          np.float32).reshape(NCH, 1))
    logd = np.ascontiguousarray(
        np.asarray(L0_attn_log_Delta, np.float32)[:NCH].reshape(NCH, 1))
    zeros_u = np.zeros((T, D), np.float32)

    nc = build_program()
    in_maps = []
    for c in range(8):
        b, h = c // 2, c % 2
        mine = np.ascontiguousarray(u[b, h * T:(h + 1) * T, :])
        prev = np.ascontiguousarray(u[b, 0:T, :]) if h == 1 else zeros_u
        in_maps.append(dict(
            u=mine, uprev=prev, wg=wg, wu=wu, wd=wd, sg=sg, su=su, sd=sd,
            bre=bre, bim=bim, cret=cret, cimt=cimt, drow=drow, lunc=lunc,
            lim=lim, logd=logd))
    res = run_bass_kernel_spmd(nc, in_maps, core_ids=list(range(8)))
    out = np.empty((B, L, D), np.float32)
    for c in range(8):
        b, h = c // 2, c % 2
        out[b, h * T:(h + 1) * T, :] = np.asarray(res.results[c]["out"])
    return out
